# version 01
# baseline (speedup 1.0000x reference)
import numpy as np
import concourse.bass as bass
import concourse.mybir as mybir
from concourse.bass_utils import run_bass_kernel_spmd
from concourse.alu_op_type import AluOpType as OP

F32 = mybir.dt.float32
AF = mybir.ActivationFunctionType

L = 4096
DM = 1024
H = 4
DH = 256
TS = 512
NST = L // TS
NCH = DM // 128
CC = 64
NCK = L // CC
NBLK = L // 128
VH = 24

ENG = ("tensor", "vector", "scalar", "gpsimd", "sync")


class Sem:
    def __init__(self, h):
        self.h = h
        self.cur = 0


class K:

    def __init__(self, debug=()):
        self.nc = bass.Bass("TRN2", target_bir_lowering=False)
        self.dbg = set(debug)
        self.steps = []
        self.sems = {}
        self.SBN = 53200
        self.SB = self.nc.alloc_sbuf_tensor("SBP", [128, self.SBN], F32).ap()
        self.PB = [self.nc.alloc_psum_tensor(f"PB{i}", [128, 512], F32).ap()
                   for i in range(8)]
        self.off = 0
        self.marks = []
        for s in ("ld", "wld", "pe", "dv", "ac", "gp", "sp"):
            self.sems[s] = Sem(self.nc.alloc_semaphore(s))
        self.bguard = [("dv", 0)] * 8

    def take(self, n, align=16):
        o = (self.off + align - 1) // align * align
        self.off = o + n
        assert self.off <= self.SBN, f"SBUF overflow {self.off} > {self.SBN}"
        return o

    def push(self):
        self.marks.append(self.off)

    def pop(self):
        self.off = self.marks.pop()

    def dram(self, name, shape, kind=None):
        if name in self.dbg:
            kind = "ExternalOutput"
        if kind is None:
            return self.nc.dram_tensor(name, shape, F32).ap()
        return self.nc.dram_tensor(name, shape, F32, kind=kind).ap()

    def w(self, name, val=None):
        s = self.sems[name]
        return (s, s.cur if val is None else val)

    def bg(self, i):
        n, v = self.bguard[i]
        return (self.sems[n], v)

    def setbg(self, i, name, val=None):
        self.bguard[i] = (name, self.sems[name].cur if val is None else val)

    def op(self, engine, fn, waits=(), inc=None):
        ws = []
        for t in waits:
            if isinstance(t[0], Sem):
                ws.append(t)
            else:
                ws.append((self.sems[t[0]], t[1]))
        self.steps.append((engine, fn, tuple(ws), inc))
        if inc is not None:
            inc[0].cur += inc[1]

    def flush(self):
        steps, self.steps = self.steps, []
        if not steps:
            return

        def emitter(engine):
            def run(eng):
                for e, fn, waits, inc in steps:
                    if e != engine:
                        continue
                    for s, v in waits:
                        eng.wait_ge(s.h, v)
                    inst = fn(eng)
                    if inc is not None:
                        s, amt = inc
                        assert inst is not None
                        inst.then_inc(s.h, amt)
            return run

        used = [e for e in ENG if any(s[0] == e for s in steps)]
        with self.nc.Block() as block:
            for e in used:
                getattr(block, e)(emitter(e))

    def dma(self, q, out, in_, waits=(), sem="ld", slow=False):
        s = self.sems[sem]
        if slow:
            self.op(q, lambda e, o=out, i=in_: e.dma_start(
                out=o, in_=i, allow_slow_non_contiguous=True),
                waits=waits, inc=(s, 16))
        else:
            self.op(q, lambda e, o=out, i=in_: e.dma_start(out=o, in_=i),
                    waits=waits, inc=(s, 16))
        return s.cur

    def mm(self, out, lhsT, rhs, start, stop, waits=(), inc=False, tp=None):
        s = self.sems["pe"]
        self.op("tensor",
                lambda e, o=out, l=lhsT, r=rhs, st=start, sp=stop, t=tp:
                e.matmul(o, l, r, start=st, stop=sp, tile_position=t),
                waits=waits, inc=(s, 1) if inc else None)
        return s.cur

    def transp(self, out, in_, ident, waits=(), inc=False):
        s = self.sems["pe"]
        self.op("tensor", lambda e, o=out, i=in_, d=ident: e.transpose(o, i, d),
                waits=waits, inc=(s, 1) if inc else None)
        return s.cur

    def dve(self, fn, waits=(), inc=False):
        s = self.sems["dv"]
        self.op("vector", fn, waits=waits, inc=(s, 1) if inc else None)
        return s.cur

    def gps(self, fn, waits=(), inc=False):
        s = self.sems["gp"]
        self.op("gpsimd", fn, waits=waits, inc=(s, 1) if inc else None)
        return s.cur

    def act(self, out, in_, func, bias=None, scale=None, waits=(), inc=False):
        s = self.sems["ac"]
        kw = {}
        if bias is not None:
            kw["bias"] = bias
        if scale is not None:
            kw["scale"] = scale
        self.op("scalar", lambda e, o=out, i=in_, f=func, kk=dict(kw):
                e.activation(o, i, f, **kk),
                waits=waits, inc=(s, 1) if inc else None)
        return s.cur


def build_program(debug=(), phases=("P0", "B", "C", "D", "E")):
    k = K(debug)
    SB, PB = k.SB, k.PB
    S = k.sems

    EIn = "ExternalInput"
    x_fm = k.dram("x_fm", [DM, L], EIn)
    wq_d = k.dram("wq", [DM, DM], EIn)
    wk_d = k.dram("wk", [DM, DM], EIn)
    wv_d = k.dram("wv", [DM, DM], EIn)
    wbp_d = k.dram("wbp", [DM, 128], EIn)
    convw_d = k.dram("convw", [DM, 44], EIn)
    rw1_d = k.dram("rw1", [1088, 2176], EIn)
    rb1_d = k.dram("rb1", [2176], EIn)
    rw2_d = k.dram("rw2p", [2176, 16], EIn)
    rb2_d = k.dram("rb2p", [16], EIn)
    onormr_d = k.dram("onorm_r", [128, DM], EIn)
    mixw_d = k.dram("mixw_r", [128, DM], EIn)
    wo_d = k.dram("wo", [DM, DM], EIn)
    ffn_wT_d = k.dram("ffn_wT", [DM, DM], EIn)
    ffn_a1_d = k.dram("ffn_a1", [DM, DM], EIn)
    ffn_a2_d = k.dram("ffn_a2", [DM, DM], EIn)
    ffn_b_d = k.dram("ffn_b", [DM], EIn)
    cvec_d = k.dram("cvec", [DM], EIn)
    wmod_d = k.dram("wmod", [DM, 6 * DM], EIn)
    ident_d = k.dram("ident", [128, 128], EIn)
    onesrow_d = k.dram("onesrow", [1, 128], EIn)
    onescol_d = k.dram("onescol", [128, 1], EIn)
    indrow_d = k.dram("indrow", [128, 512], EIn)
    indhead_d = k.dram("indhead", [128, 512], EIn)
    masks_d = k.dram("masks", [128, 1024], EIn)

    h_sp = k.dram("h_sp", [NCH, 128, L])
    q_sp = k.dram("q_sp", [NCH, 128, L])
    kn_sp = k.dram("kn_sp", [NCH, 128, L])
    kb_sp = k.dram("kb_sp", [NCH, 128, L])
    knT_sp = k.dram("knT_sp", [NCK, 64, DM])
    vbT_sp = k.dram("vbT_sp", [NCK, 64, DM])
    vT_sp = k.dram("vT_sp", [NBLK, 128, DM])
    lT_sp = k.dram("lT_sp", [NBLK, 128, DM])
    mT_sp = k.dram("mT_sp", [NBLK, 128, DM])
    dT_sp = k.dram("dT_sp", [NBLK, 128, DM])
    rqT_sp = k.dram("rqT_sp", [NBLK, 128, 4])
    mod_sp = k.dram("mod_sp", [6 * DM])
    wp_sp = k.dram("wp_sp", [DM, DM])
    out_fm = k.dram("out_fm", [DM, L], "ExternalOutput")

    o_ceps = k.take(16)
    k.nc.const_aps.register(F32, 1e-5, SB[:, o_ceps:o_ceps + 1])
    k.nc.const_aps.register(F32, 1e-12, SB[:, o_ceps + 1:o_ceps + 2])
    k.dve(lambda e: e.memset(SB[:, o_ceps:o_ceps + 1], 1e-5))
    k.dve(lambda e: e.memset(SB[:, o_ceps + 1:o_ceps + 2], 1e-12), inc=True)
    o_mod = k.take(48)
    o_id = k.take(128)
    o_onesr = k.take(128)
    o_onesc = k.take(16)
    o_indr = k.take(512)
    o_indh = k.take(512)
    o_msk = k.take(256)
    o_cw = k.take(NCH * 44)

    MSL, MSU, MIU, MI64 = o_msk, o_msk + CC, o_msk + 2 * CC, o_msk + 3 * CC

    def modcol(v, c):
        return SB[:, o_mod + 8 * v + c:o_mod + 8 * v + c + 1]

    k.push()
    o_sc = k.take(16)
    o_wmb = k.take(2 * 512)
    o_mrow = k.take(512)

    k.dma("sync", SB[:, o_sc:o_sc + 8], cvec_d.rearrange("(c p) -> p c", p=128),
          slow=True)
    v_c = S["ld"].cur
    for dst, src, wdt in ((o_id, ident_d, 128), (o_onesr, onesrow_d, 128),
                         (o_onesc, onescol_d, 1), (o_indr, indrow_d, 512),
                         (o_indh, indhead_d, 512)):
        k.dma("sync", SB[:src.shape[0], dst:dst + wdt], src, sem="wld")
    k.dma("sync", SB[:, o_msk:o_msk + 256], masks_d[:, 0:256], sem="wld")
    k.dma("sync", SB[:, o_cw:o_cw + NCH * 44].rearrange("p (c w) -> p c w", c=NCH),
          convw_d.rearrange("(c p) w -> p c w", p=128), sem="wld")

    va = k.act(SB[:, o_sc:o_sc + 8], SB[:, o_sc:o_sc + 8], AF.Silu,
               waits=[("ld", v_c)], inc=True)

    pe_hist = []
    for j in range(12):
        for kc in range(NCH):
            n = j * NCH + kc
            buf = o_wmb + (n % 2) * 512
            wt = [k.w("pe", pe_hist[n - 2])] if n >= 2 else []
            vl = k.dma("sync", SB[:, buf:buf + 512],
                       wmod_d[kc * 128:(kc + 1) * 128, j * 512:(j + 1) * 512],
                       waits=wt)
            k.mm(PB[0][0:1, :], SB[:, o_sc + kc:o_sc + kc + 1],
                 SB[:, buf:buf + 512], start=(kc == 0), stop=(kc == NCH - 1),
                 waits=[("ld", vl), ("ac", va)] + ([k.w("dv")] if kc == 0 else []),
                 inc=True)
            pe_hist.append(S["pe"].cur)
        k.dve(lambda e: e.tensor_copy(SB[0:1, o_mrow:o_mrow + 512], PB[0][0:1, :]),
              waits=[k.w("pe")], inc=True)
        k.dma("sync", mod_sp[j * 512:(j + 1) * 512], SB[0:1, o_mrow:o_mrow + 512],
              waits=[k.w("dv")], sem="sp")
    v_modsp = S["sp"].cur
    k.pop()

    k.dma("sync", SB[:, o_mod:o_mod + 48].rearrange("p (v c) -> p v c", v=6),
          mod_sp.rearrange("(v c p) -> p v c", v=6, c=NCH),
          waits=[("sp", v_modsp)], slow=True)
    v_modld = S["ld"].cur

    k.push()
    o_t1, o_t2, o_t3 = k.take(16), k.take(16), k.take(16)
    for v6 in (2, 5):
        g = SB[:, o_mod + 8 * v6:o_mod + 8 * v6 + 8]
        t1, t2, t3 = (SB[:, o:o + 8] for o in (o_t1, o_t2, o_t3))
        k.dve(lambda e, g=g, t=t1: e.tensor_scalar(t, g, 0.25, None, op0=OP.max),
              waits=[("ld", v_modld)], inc=True)
        k.dve(lambda e, g=g, t=t2: e.tensor_scalar(t, g, -0.25, None, op0=OP.min),
              inc=True)
        k.dve(lambda e, g=g, t=t3: e.tensor_scalar(t, g, 0.0, None, op0=OP.is_ge),
              inc=True)
        k.dve(lambda e, a=t1, b=t2: e.tensor_tensor(a, a, b, op=OP.subtract),
              waits=[k.w("dv")], inc=True)
        k.dve(lambda e, a=t1, b=t3: e.tensor_tensor(a, a, b, op=OP.mult),
              waits=[k.w("dv")], inc=True)
        k.dve(lambda e, g=g, a=t1, b=t2: e.tensor_tensor(g, a, b, op=OP.add),
              waits=[k.w("dv")], inc=True)
    k.pop()
    v_mod_dv = S["dv"].cur
    if "P0" in phases:
        k.flush()

    k.push()
    o_wq = k.take(NCH * DM)
    o_wk = k.take(NCH * DM)
    o_wv = k.take(NCH * DM)
    o_wb = k.take(NCH * 128)
    o_x = k.take(NCH * TS)
    o_h = k.take(NCH * TS)
    o_kn = k.take(NCH * TS)
    o_v = k.take(NCH * (VH + TS))
    o_lin = k.take(TS + 32)
    o_halo = k.take(3 * NCH * 4)
    o_cb = k.take(TS)
    o_sq = k.take(TS)
    o_qc = k.take(2 * TS)
    o_kbc = k.take(2 * TS)
    o_mr = k.take(2 * TS)
    o_bs = k.take(TS)
    o_rq = k.take(TS)
    o_rk = k.take(TS)
    o_rkb = k.take(TS)
    o_tma = k.take(DM)
    o_tmb = k.take(DM)
    o_bT = k.take(16)
    o_rqTb = k.take(16)

    for dst, src in ((o_wq, wq_d), (o_wk, wk_d), (o_wv, wv_d)):
        k.dma("sync", SB[:, dst:dst + NCH * DM].rearrange("p (c m) -> p c m", c=NCH),
              src.rearrange("(c p) m -> p c m", p=128), sem="wld")
    k.dma("sync", SB[:, o_wb:o_wb + NCH * 128].rearrange("p (c m) -> p c m", c=NCH),
          wbp_d.rearrange("(c p) m -> p c m", p=128), sem="wld")
    v_wB = S["wld"].cur

    k.dve(lambda e: e.memset(SB[:, o_halo:o_halo + 3 * NCH * 4], 0.0))
    for c in range(NCH):
        k.dve(lambda e, c=c: e.memset(
            SB[:, o_v + c * (VH + TS):o_v + c * (VH + TS) + VH], 0.0),
            inc=(c == NCH - 1))

    def conv_taps(lin_off, w_off, ntap, out_off, waits0=()):
        k.dve(lambda e, lo=lin_off, wo=w_off, oo=out_off: e.tensor_scalar(
            SB[:, oo:oo + TS], SB[:, lo:lo + TS], SB[:, wo:wo + 1], None,
            op0=OP.mult), waits=waits0)
        for j in range(1, ntap):
            k.dve(lambda e, lo=lin_off + j, wo=w_off + j, oo=out_off:
                  e.scalar_tensor_tensor(
                      SB[:, oo:oo + TS], SB[:, lo:lo + TS], SB[:, wo:wo + 1],
                      SB[:, oo:oo + TS], op0=OP.mult, op1=OP.add))

    qspill = [0, 0]
    kbspill = [0, 0]
    rqt_spill = 0
    tma_spill = 0
    tmb_spill = 0
    mrow = SB[0:1, o_mr:o_mr + TS]
    rrow = SB[0:1, o_mr + TS:o_mr + 2 * TS]

    for st in range(NST):
        tok0 = st * TS
        v_x = k.dma("sync", SB[:, o_x:o_x + NCH * TS].rearrange(
            "p (c t) -> p c t", c=NCH),
            x_fm.rearrange("(c p) l -> p c l", p=128)[:, :, tok0:tok0 + TS],
            waits=[k.w("dv"), k.w("pe"), k.w("ac"), k.w("sp")])

        pe_sx2 = []
        for c in range(NCH):
            k.mm(PB[2][0:1, :], SB[:, o_onesc:o_onesc + 1],
                 SB[:, o_x + c * TS:o_x + (c + 1) * TS],
                 start=(c == 0), stop=(c == NCH - 1),
                 waits=[("ld", v_x), k.bg(2)] if c == 0 else [],
                 inc=(c == NCH - 1))
        for c in range(NCH):
            wv_ = [("ld", v_x)] if c == 0 else [k.w("pe", pe_sx2[-1])]
            va = k.act(SB[:, o_sq:o_sq + TS],
                       SB[:, o_x + c * TS:o_x + (c + 1) * TS], AF.Square,
                       waits=wv_, inc=True)
            k.mm(PB[3][0:1, :], SB[:, o_onesc:o_onesc + 1], SB[:, o_sq:o_sq + TS],
                 start=(c == 0), stop=(c == NCH - 1),
                 waits=[("ac", va)] + ([k.bg(3)] if c == 0 else []), inc=True)
            pe_sx2.append(S["pe"].cur)
        v_st = S["pe"].cur
        k.dve(lambda e: e.tensor_scalar(mrow, PB[2][0:1, :], 1.0 / DM, None,
                                        op0=OP.mult), waits=[("pe", v_st)])
        k.dve(lambda e: e.tensor_tensor(rrow, mrow, mrow, op=OP.mult))
        k.dve(lambda e: e.scalar_tensor_tensor(
            rrow, PB[3][0:1, :], 1.0 / DM, rrow, op0=OP.mult, op1=OP.subtract),
            inc=True)
        v_var = S["dv"].cur
        k.setbg(2, "dv", v_var)
        k.setbg(3, "dv", v_var)
        va = k.act(rrow, rrow, AF.Sqrt, bias=1e-5, waits=[("dv", v_var)], inc=True)
        k.dve(lambda e: e.reciprocal(rrow, rrow), waits=[("ac", va)], inc=True)
        v_r = S["dv"].cur
        k.mm(PB[4][:, :], SB[0:1, o_onesr:o_onesr + 128], mrow,
             start=True, stop=True, waits=[("dv", v_r), k.bg(4)])
        k.mm(PB[5][:, :], SB[0:1, o_onesr:o_onesr + 128], rrow,
             start=True, stop=True, waits=[k.bg(5)], inc=True)
        v_bc = S["pe"].cur

        for c in range(NCH):
            xc = SB[:, o_x + c * TS:o_x + (c + 1) * TS]
            hc = SB[:, o_h + c * TS:o_h + (c + 1) * TS]
            w0 = [("pe", v_bc)]
            if st == 0 and c == 0:
                w0 += [("ld", v_modld), ("dv", v_mod_dv)]
            k.dve(lambda e, x=xc, h=hc: e.tensor_tensor(h, x, PB[4][:, :],
                                                        op=OP.subtract), waits=w0)
            k.dve(lambda e, h=hc: e.tensor_tensor(h, h, PB[5][:, :], op=OP.mult))
            k.dve(lambda e, h=hc, c=c: e.scalar_tensor_tensor(
                h, h, modcol(1, c), h, op0=OP.mult, op1=OP.add))
            k.dve(lambda e, h=hc, c=c: e.tensor_scalar(
                h, h, modcol(0, c), None, op0=OP.add), inc=True)
        v_h = S["dv"].cur
        k.setbg(4, "dv", v_h)
        k.setbg(5, "dv", v_h)
        k.dma("sync", h_sp.rearrange("c p l -> p c l")[:, :, tok0:tok0 + TS],
              SB[:, o_h:o_h + NCH * TS].rearrange("p (c t) -> p c t", c=NCH),
              waits=[("dv", v_h)], sem="sp")

        for c in range(NCH):
            k.mm(PB[7][:, :], SB[:, o_wb + c * 128:o_wb + (c + 1) * 128],
                 SB[:, o_h + c * TS:o_h + (c + 1) * TS],
                 start=(c == 0), stop=(c == NCH - 1),
                 waits=[("dv", v_h), ("wld", v_wB), k.bg(7)] if c == 0 else [],
                 inc=(c == NCH - 1))
        v_bl = S["pe"].cur
        va = k.act(SB[:, o_bs:o_bs + TS], PB[7][:, :], AF.Sigmoid,
                   waits=[("pe", v_bl)], inc=True)
        k.setbg(7, "ac", va)

        for ti, (ow, cwo, tgt) in enumerate(((o_wq, 0, "q"), (o_wk, 4, "k"),
                                             (o_wv, 8, "v"))):
            sum_hist = []
            for c in range(NCH):
                bank = c % 2
                for kc in range(NCH):
                    k.mm(PB[bank][:, :],
                         SB[:, ow + kc * DM + c * 128:ow + kc * DM + (c + 1) * 128],
                         SB[:, o_h + kc * TS:o_h + (kc + 1) * TS],
                         start=(kc == 0), stop=(kc == NCH - 1),
                         waits=([k.bg(bank)] + ([("dv", v_h)] if (ti == 0 and c == 0) else []))
                         if kc == 0 else [],
                         inc=(kc == NCH - 1))
                v_p = S["pe"].cur
                hs = o_halo + (ti * NCH + c) * 4
                k.dve(lambda e, h=hs: e.tensor_copy(SB[:, o_lin:o_lin + 3],
                                                    SB[:, h:h + 3]),
                      waits=[("pe", v_p)])
                k.dve(lambda e, b=bank: e.tensor_copy(
                    SB[:, o_lin + 3:o_lin + 3 + TS], PB[b][:, :]))
                k.dve(lambda e, h=hs, b=bank: e.tensor_copy(
                    SB[:, h:h + 3], PB[b][:, TS - 3:TS]), inc=True)
                k.setbg(bank, "dv")
                conv_taps(o_lin, o_cw + c * 44 + cwo, 4, o_cb,
                          waits0=[k.w("ac")])
                k.dve(lambda e: e.engine_nop(), inc=True)
                v_cv = S["dv"].cur
                if tgt == "q":
                    db = c % 2
                    dst = SB[:, o_qc + db * TS:o_qc + (db + 1) * TS]
                elif tgt == "k":
                    dst = SB[:, o_kn + c * TS:o_kn + (c + 1) * TS]
                else:
                    dst = SB[:, o_v + c * (VH + TS) + VH:o_v + c * (VH + TS) + VH + TS]
                wsilu = [("dv", v_cv)]
                if tgt == "q":
                    wsilu.append(("sp", qspill[db]))
                v_silu = k.act(dst, SB[:, o_cb:o_cb + TS], AF.Silu, waits=wsilu,
                               inc=True)
                if tgt in ("q", "k"):
                    wsq = [("ac", v_silu)]
                    if sum_hist:
                        wsq.append(("pe", sum_hist[-1]))
                    elif ti == 1:
                        pass
                    va2 = k.act(SB[:, o_sq:o_sq + TS], dst, AF.Square,
                                waits=wsq, inc=True)
                    k.mm(PB[6][:, :],
                         SB[:, o_indh + (c // 2) * 128:o_indh + (c // 2) * 128 + 128],
                         SB[:, o_sq:o_sq + TS],
                         start=(c == 0), stop=(c == NCH - 1),
                         waits=[("ac", va2)] + ([k.bg(6)] if c == 0 else []),
                         inc=True)
                    sum_hist.append(S["pe"].cur)
                if tgt == "q":
                    qspill[db] = k.dma(
                        "scalar",
                        q_sp.rearrange("c p l -> p c l")[:, c:c + 1, tok0:tok0 + TS],
                        dst.unsqueeze(1), waits=[("ac", v_silu)], sem="sp")
            if tgt in ("q", "k"):
                v_sum = S["pe"].cur
                orr = o_rq if tgt == "q" else o_rk
                va = k.act(SB[:, orr:orr + TS], PB[6][:, :], AF.Sqrt, bias=1e-12,
                           waits=[("pe", v_sum)], inc=True)
                k.setbg(6, "ac", va)
                k.dve(lambda e, o=orr: e.reciprocal(SB[:, o:o + TS],
                                                    SB[:, o:o + TS]),
                      waits=[("ac", va)], inc=True)

        k.dve(lambda e: e.tensor_tensor(SB[:, o_rkb:o_rkb + TS],
                                        SB[:, o_rk:o_rk + TS],
                                        SB[:, o_bs:o_bs + TS], op=OP.mult),
              waits=[k.w("dv"), k.w("ac")], inc=True)

        v_knc = S["dv"].cur
        for hh in range(H):
            k.mm(PB[4][:, :], SB[:, o_indr + hh * 128:o_indr + hh * 128 + 128],
                 SB[:, o_rk:o_rk + TS], start=True, stop=True,
                 waits=[k.w("dv"), k.bg(4)])
            k.mm(PB[5][:, :], SB[:, o_indr + hh * 128:o_indr + hh * 128 + 128],
                 SB[:, o_rkb:o_rkb + TS], start=True, stop=True,
                 waits=[k.bg(5)], inc=True)
            v_b2 = S["pe"].cur
            for c in (2 * hh, 2 * hh + 1):
                kc_ap = SB[:, o_kn + c * TS:o_kn + (c + 1) * TS]
                db = c % 2
                k.dve(lambda e, kc=kc_ap, d=db: e.tensor_tensor(
                    SB[:, o_kbc + d * TS:o_kbc + (d + 1) * TS], kc, PB[5][:, :],
                    op=OP.mult), waits=[("pe", v_b2), ("sp", kbspill[db])])
                k.dve(lambda e, kc=kc_ap: e.tensor_tensor(kc, kc, PB[4][:, :],
                                                          op=OP.mult), inc=True)
                v_knc = S["dv"].cur
                kbspill[db] = k.dma(
                    "scalar",
                    kb_sp.rearrange("c p l -> p c l")[:, c:c + 1, tok0:tok0 + TS],
                    SB[:, o_kbc + db * TS:o_kbc + (db + 1) * TS].unsqueeze(1),
                    waits=[("dv", v_knc)], sem="sp")
                k.dma("scalar",
                      kn_sp.rearrange("c p l -> p c l")[:, c:c + 1, tok0:tok0 + TS],
                      kc_ap.unsqueeze(1), waits=[("dv", v_knc)], sem="sp")
            k.setbg(4, "dv", v_knc)
            k.setbg(5, "dv", v_knc)

        for blk in range(4):
            B_ = 4 * st + blk
            k.transp(PB[7][:, 0:128], SB[:, o_bs + blk * 128:o_bs + blk * 128 + 128],
                     SB[:, o_id:o_id + 128], waits=[k.w("dv", v_knc), k.bg(7)])
            k.transp(PB[7][:, 128:256], SB[:, o_rq + blk * 128:o_rq + blk * 128 + 128],
                     SB[:, o_id:o_id + 128], inc=True)
            v_tb = S["pe"].cur
            k.dve(lambda e: e.tensor_copy(SB[:, o_bT:o_bT + 4], PB[7][:, 0:128:32]),
                  waits=[("pe", v_tb), ("sp", rqt_spill)])
            k.dve(lambda e: e.tensor_copy(SB[:, o_rqTb:o_rqTb + 4],
                                          PB[7][:, 128:256:32]), inc=True)
            v_t2 = S["dv"].cur
            k.setbg(7, "dv", v_t2)
            rqt_spill = k.dma("scalar", rqT_sp[B_], SB[:, o_rqTb:o_rqTb + 4],
                              waits=[("dv", v_t2)], sem="sp")

            for half in range(2):
                bank = 2 + half
                for cp in range(4):
                    c = half * 4 + cp
                    src = SB[:, o_v + c * (VH + TS) + VH + blk * 128:
                             o_v + c * (VH + TS) + VH + blk * 128 + 128]
                    k.transp(PB[bank][:, cp * 128:(cp + 1) * 128], src,
                             SB[:, o_id:o_id + 128],
                             waits=[k.bg(bank)] if cp == 0 else [], inc=(cp == 3))
                v_tv = S["pe"].cur
                k.dve(lambda e, b=bank, hf=half: e.tensor_copy(
                    SB[:, o_tma + hf * 512:o_tma + hf * 512 + 512], PB[b][:, :]),
                    waits=[("pe", v_tv), ("sp", tma_spill)])
                for h2 in range(2):
                    hd = half * 2 + h2
                    k.dve(lambda e, b=bank, h2=h2, hd=hd: e.tensor_scalar(
                        SB[:, o_tmb + hd * 256:o_tmb + (hd + 1) * 256],
                        PB[b][:, h2 * 256:(h2 + 1) * 256],
                        SB[:, o_bT + hd:o_bT + hd + 1], None, op0=OP.mult),
                        waits=[("sp", tmb_spill)] if h2 == 0 else [],
                        inc=(h2 == 1))
                k.setbg(bank, "dv")
            v_vtm = S["dv"].cur
            k.dma("scalar", vT_sp[B_], SB[:, o_tma:o_tma + DM],
                  waits=[("dv", v_vtm)], sem="sp")
            tma_spill = S["sp"].cur
            for par in range(2):
                ck = 2 * B_ + par
                k.dma("scalar", vbT_sp[ck],
                      SB[64 * par:64 * par + 64, o_tmb:o_tmb + DM],
                      waits=[("dv", v_vtm)], sem="sp")
            tmb_spill = S["sp"].cur

            for half in range(2):
                bank = 2 + half
                for cp in range(4):
                    c = half * 4 + cp
                    src = SB[:, o_kn + c * TS + blk * 128:o_kn + c * TS + blk * 128 + 128]
                    k.transp(PB[bank][:, cp * 128:(cp + 1) * 128], src,
                             SB[:, o_id:o_id + 128],
                             waits=[k.bg(bank)] if cp == 0 else [], inc=(cp == 3))
                v_tk = S["pe"].cur
                k.dve(lambda e, b=bank, hf=half: e.tensor_copy(
                    SB[:, o_tma + hf * 512:o_tma + hf * 512 + 512], PB[b][:, :]),
                    waits=[("pe", v_tk), ("sp", tma_spill)] if half == 0
                    else [("pe", v_tk)], inc=True)
                k.setbg(bank, "dv")
            v_ktm = S["dv"].cur
            for par in range(2):
                ck = 2 * B_ + par
                k.dma("scalar", knT_sp[ck],
                      SB[64 * par:64 * par + 64, o_tma:o_tma + DM],
                      waits=[("dv", v_ktm)], sem="sp")
            tma_spill = S["sp"].cur

        for c in range(NCH):
            vb = o_v + c * (VH + TS)
            for o_dst, wtap, ntap, tgt_sp in (
                    (o_cb, o_cw + c * 44 + 12, 7, lT_sp),
                    (o_sq, o_cw + c * 44 + 19, 25, mT_sp)):
                lin0 = vb + VH - (ntap - 1)
                conv_taps(lin0, wtap, ntap, o_dst,
                          waits0=[k.w("pe"), k.w("ac")])
                k.dve(lambda e: e.engine_nop(), inc=True)
                v_cv2 = S["dv"].cur
                for blk in range(4):
                    k.transp(PB[2][:, blk * 128:(blk + 1) * 128],
                             SB[:, o_dst + blk * 128:o_dst + blk * 128 + 128],
                             SB[:, o_id:o_id + 128],
                             waits=[("dv", v_cv2), k.bg(2)] if blk == 0 else [],
                             inc=(blk == 3))
                v_tl = S["pe"].cur
                k.dve(lambda e: e.tensor_copy(SB[:, o_tma:o_tma + 512],
                                              PB[2][:, :]),
                      waits=[("pe", v_tl), ("sp", tma_spill)], inc=True)
                v_tlm = S["dv"].cur
                k.setbg(2, "dv", v_tlm)
                for blk in range(4):
                    k.dma("scalar", tgt_sp[4 * st + blk][:, c * 128:(c + 1) * 128],
                          SB[:, o_tma + blk * 128:o_tma + blk * 128 + 128],
                          waits=[("dv", v_tlm)], sem="sp")
                tma_spill = S["sp"].cur
            k.dve(lambda e, v=vb: e.tensor_copy(SB[:, v:v + VH],
                                                SB[:, v + TS:v + TS + VH]))

    k.pop()
    if "B" in phases:
        k.flush()

    o_pT = k.take(NBLK * 16)
    k.push()
    o_feats = k.take(4096)
    k.push()
    o_SS = [k.take(512) for _ in range(H)]
    o_qF = k.take(NCH * TS)
    o_kbF = k.take(NCH * TS)
    o_knF = k.take(NCH * TS)
    o_knT = [k.take(DM) for _ in range(2)]
    o_vbT = [k.take(DM) for _ in range(2)]
    o_rqT = k.take(16)
    o_vTb = k.take(DM)
    o_lTb = k.take(DM)
    o_mTb = k.take(DM)
    o_dta = k.take(2 * DM)
    CLS = ("P1", "Q1", "Q2", "P2", "Q3", "Q4", "P4", "Q8", "P8", "Q12",
           "Q16", "P16", "Q32", "Q48", "C1", "C2", "C3")
    o_cls = {n: k.take(256) for n in CLS}
    o_Ys = k.take(DM)
    o_Z1 = k.take(DM)
    o_Z2 = k.take(DM)
    o_Us = k.take(DM)
    o_att = k.take(256)
    o_pr = [k.take(DM) for _ in range(2)]
    o_fr = k.take(64)
    o_ft = k.take(64)
    o_sc1 = k.take(16)
    o_sc2 = k.take(16)
    o_msk4 = k.take(768)
    k.dma("sync", SB[:, o_msk4:o_msk4 + 768], masks_d[:, 256:1024], sem="wld")

    for off, n in ([(o, 512) for o in o_SS] +
                   [(o, DM) for o in o_knT] +
                   [(o_Ys, DM), (o_Z1, DM), (o_Z2, DM), (o_Us, DM),
                    (o_att, 256)] +
                   [(o_cls[n], 256) for n in CLS]):
        k.dve(lambda e, o=off, m=n: e.memset(SB[:, o:o + m], 0.0))
    k.dve(lambda e: e.engine_nop(), inc=True)

    M_SL4, M_SU4, M_IU4 = o_msk4, o_msk4 + 256, o_msk4 + 512

    def cls_ap(n, rows=128, h=None):
        o = o_cls[n]
        if h is None:
            return SB[:rows, o:o + 256]
        return SB[:rows, o + 64 * h:o + 64 * h + 64]

    knt_guard = [k.w("dv", 0), k.w("dv", 0)]
    vbt_guard = [k.w("dv", 0), k.w("dv", 0)]
    dta_spill = [0, 0]
    v_S_add = S["dv"].cur

    for ck in range(NCK):
        stl = ck // 8
        cin = ck % 8
        t0 = cin * CC
        blkrel = cin // 2
        B_ = ck // 2
        sld = ck % 2
        sdta = B_ % 2

        if cin == 0:
            w0 = [k.w("pe"), k.w("dv"), k.w("gp")]
            for dst, src in ((o_qF, q_sp), (o_kbF, kb_sp), (o_knF, kn_sp)):
                k.dma("sync", SB[:, dst:dst + NCH * TS].rearrange(
                    "p (c t) -> p c t", c=NCH),
                    src.rearrange("c p l -> p c l")[:, :, stl * TS:(stl + 1) * TS],
                    waits=w0)
                w0 = []
            k.dma("sync", SB[:, o_rqT:o_rqT + 16].rearrange("p (b hh) -> p b hh", b=4),
                  rqT_sp.rearrange("b p hh -> p b hh")[:, 4 * stl:4 * stl + 4, :])
        if cin % 2 == 0:
            for dst, src in ((o_vTb, vT_sp), (o_lTb, lT_sp), (o_mTb, mT_sp)):
                k.dma("sync", SB[:, dst:dst + DM], src[B_],
                      waits=[k.w("dv"), k.w("gp")])
        k.dma("sync", SB[0:64, o_knT[sld]:o_knT[sld] + DM], knT_sp[ck],
              waits=[knt_guard[sld]])
        k.dma("sync", SB[0:64, o_vbT[sld]:o_vbT[sld] + DM], vbT_sp[ck],
              waits=[vbt_guard[sld]])
        v_tmld = S["ld"].cur
        v_fmld = v_tmld

        def qslice(base, hd, kc):
            c = 2 * hd + kc
            return SB[:, base + c * TS + t0:base + c * TS + t0 + CC]

        for hd in range(H):
            for kc in range(2):
                k.mm(PB[4][0:64, 64 * hd:64 * hd + 64],
                     qslice(o_kbF, hd, kc), qslice(o_knF, hd, kc),
                     start=(kc == 0), stop=(kc == 1),
                     waits=[("ld", v_fmld), k.bg(4)] if hd == 0 and kc == 0 else [],
                     inc=(kc == 1))
        for hd in range(H):
            for kc in range(2):
                k.mm(PB[4][0:64, 256 + 64 * hd:256 + 64 * hd + 64],
                     qslice(o_knF, hd, kc), qslice(o_kbF, hd, kc),
                     start=(kc == 0), stop=(kc == 1), inc=(kc == 1))
        v_gr = S["pe"].cur
        k.dve(lambda e: e.scalar_tensor_tensor(
            cls_ap("P1", 64), PB[4][0:64, 0:256], -1.0,
            SB[0:64, M_SL4:M_SL4 + 256], op0=OP.mult, op1=OP.mult),
            waits=[("pe", v_gr), k.w("wld")])
        k.dve(lambda e: e.scalar_tensor_tensor(
            cls_ap("Q1", 64), PB[4][0:64, 256:512], -1.0,
            SB[0:64, M_SU4:M_SU4 + 256], op0=OP.mult, op1=OP.mult), inc=True)
        k.setbg(4, "dv")
        v_pq1 = S["dv"].cur

        rounds = [
            [("Q2", "P1", "Q1", 5), ("P2", "Q1", "P1", 6)],
            [("Q3", "P1", "Q2", 4), ("Q4", "P2", "Q2", 5), ("P4", "Q2", "P2", 6)],
            [("Q8", "P4", "Q4", 4), ("P8", "Q4", "P4", 5)],
            [("Q12", "P4", "Q8", 6), ("Q16", "P8", "Q8", 4), ("P16", "Q8", "P8", 5)],
            [("Q32", "P16", "Q16", 6)],
            [("Q48", "P16", "Q32", 4)],
        ]
        for rnd in rounds:
            v_prev = S["dv"].cur
            for dst, lh, rh, bk in rnd:
                for hd in range(H):
                    k.mm(PB[bk][0:64, 64 * hd:64 * hd + 64],
                         cls_ap(lh, 128, hd), cls_ap(rh, 128, hd),
                         start=(hd == 0), stop=(hd == H - 1),
                         waits=[("dv", v_prev), k.bg(bk)] if hd == 0 else [],
                         inc=(hd == H - 1))
            for dst, lh, rh, bk in rnd:
                k.dve(lambda e, d=dst, b=bk: e.tensor_copy(
                    cls_ap(d, 64), PB[b][0:64, 0:256]),
                    waits=[k.w("pe")], inc=True)
                k.setbg(bk, "dv")
        k.dve(lambda e: e.tensor_tensor(cls_ap("C1"), cls_ap("Q1"), cls_ap("Q2"),
                                        op=OP.add))
        k.dve(lambda e: e.tensor_tensor(cls_ap("C1"), cls_ap("C1"), cls_ap("Q3"),
                                        op=OP.add))
        k.dve(lambda e: e.tensor_tensor(cls_ap("C2"), cls_ap("Q4"), cls_ap("Q8"),
                                        op=OP.add))
        k.dve(lambda e: e.tensor_tensor(cls_ap("C2"), cls_ap("C2"), cls_ap("Q12"),
                                        op=OP.add))
        k.dve(lambda e: e.tensor_tensor(cls_ap("C3"), cls_ap("Q16"), cls_ap("Q32"),
                                        op=OP.add))
        k.dve(lambda e: e.tensor_tensor(cls_ap("C3"), cls_ap("C3"), cls_ap("Q48"),
                                        op=OP.add), inc=True)
        v_Cs = S["dv"].cur

        for hd in range(H):
            bk = hd // 2
            col = 256 * (hd % 2)
            for kc in range(2):
                k.mm(PB[bk][0:64, col:col + 256],
                     qslice(o_kbF, hd, kc),
                     SB[:, o_SS[hd] + 256 * kc:o_SS[hd] + 256 * kc + 256],
                     start=(kc == 0), stop=(kc == 1),
                     waits=([("dv", v_S_add), k.bg(bk)] if (hd % 2 == 0 and kc == 0)
                            else []),
                     inc=(kc == 1))
        v_ymm = S["pe"].cur
        for sg in range(2):
            k.dve(lambda e, s_=sg, sl=sld: e.tensor_tensor(
                SB[0:64, o_Ys + 512 * s_:o_Ys + 512 * s_ + 512],
                SB[0:64, o_vbT[sl] + 512 * s_:o_vbT[sl] + 512 * s_ + 512],
                PB[s_][0:64, 0:512], op=OP.subtract),
                waits=[("pe", v_ymm), ("ld", v_tmld)] if sg == 0 else [],
                inc=(sg == 1))
        v_ys = S["dv"].cur
        k.setbg(0, "dv", v_ys)
        k.setbg(1, "dv", v_ys)

        for lvl, (cname, src_o, dst_o) in enumerate(
                (("C3", o_Ys, o_Z1), ("C2", o_Z1, o_Z2), ("C1", o_Z2, o_Us))):
            v_src = S["dv"].cur
            for hd in range(H):
                bk = hd // 2
                col = 256 * (hd % 2)
                k.mm(PB[bk][0:64, col:col + 256],
                     cls_ap(cname, 128, hd),
                     SB[:, src_o + 256 * hd:src_o + 256 * hd + 256],
                     start=True, stop=True,
                     waits=[("dv", v_src), k.bg(bk)] if hd % 2 == 0 else [],
                     inc=(hd % 2 == 1))
            v_app = S["pe"].cur
            for sg in range(2):
                k.dve(lambda e, s_=sg, so=src_o, do=dst_o: e.tensor_tensor(
                    SB[0:64, do + 512 * s_:do + 512 * s_ + 512],
                    PB[s_][0:64, 0:512],
                    SB[0:64, so + 512 * s_:so + 512 * s_ + 512], op=OP.add),
                    waits=[("pe", v_app)] if sg == 0 else [], inc=(sg == 1))
            k.setbg(0, "dv")
            k.setbg(1, "dv")
        v_U = S["dv"].cur

        for hd in range(H):
            for kc in range(2):
                k.mm(PB[4][0:64, 64 * hd:64 * hd + 64],
                     qslice(o_knF, hd, kc), qslice(o_qF, hd, kc),
                     start=(kc == 0), stop=(kc == 1),
                     waits=[k.bg(4)] if hd == 0 and kc == 0 else [],
                     inc=(kc == 1))
        v_ag = S["pe"].cur
        k.dve(lambda e: e.tensor_tensor(
            SB[0:64, o_att:o_att + 256], PB[4][0:64, 0:256],
            SB[0:64, M_IU4:M_IU4 + 256], op=OP.mult),
            waits=[("pe", v_ag)], inc=True)
        k.setbg(4, "dv")
        v_at = S["dv"].cur

        par = ck % 2
        for hd in range(H):
            bk = 2 + hd // 2
            col = 256 * (hd % 2)
            for kc in range(2):
                k.mm(PB[bk][64 * par:64 * par + 64, col:col + 256],
                     qslice(o_qF, hd, kc),
                     SB[:, o_SS[hd] + 256 * kc:o_SS[hd] + 256 * kc + 256],
                     start=(kc == 0), stop=False, tp=(0, 64 * par),
                     waits=([("dv", v_at), k.bg(bk)] if hd % 2 == 0 and kc == 0
                            else []))
            k.mm(PB[bk][64 * par:64 * par + 64, col:col + 256],
                 SB[:, o_att + 64 * hd:o_att + 64 * hd + 64],
                 SB[:, o_Us + 256 * hd:o_Us + 256 * hd + 256],
                 start=False, stop=True, tp=(0, 64 * par),
                 inc=(hd % 2 == 1))
        v_omm = S["pe"].cur
        for hd in range(H):
            bk = 2 + hd // 2
            col = 256 * (hd % 2)
            k.dve(lambda e, b=bk, c=col, h=hd, p_=par, sd=sdta, br=blkrel:
                  e.tensor_scalar(
                      SB[64 * p_:64 * p_ + 64,
                         o_dta + sd * DM + 256 * h:o_dta + sd * DM + 256 * h + 256],
                      PB[b][64 * p_:64 * p_ + 64, c:c + 256],
                      SB[64 * p_:64 * p_ + 64,
                         o_rqT + 4 * br + h:o_rqT + 4 * br + h + 1],
                      None, op0=OP.mult),
                  waits=([("pe", v_omm), ("sp", dta_spill[sdta])] if hd == 0 else []),
                  inc=(hd == H - 1))
        v_oev = S["dv"].cur
        k.setbg(2, "dv", v_oev)
        k.setbg(3, "dv", v_oev)

        for pair in range(2):
            for hh2 in range(2):
                hd = 2 * pair + hh2
                bk = 2 + hh2
                for kc2 in range(2):
                    k.mm(PB[bk][:, 256 * kc2:256 * kc2 + 256],
                         SB[:, o_knT[sld] + 256 * hd + 128 * kc2:
                            o_knT[sld] + 256 * hd + 128 * kc2 + 128],
                         SB[:, o_Us + 256 * hd:o_Us + 256 * hd + 256],
                         start=(kc2 == 0), stop=(kc2 == 1),
                         waits=[k.bg(bk)] if kc2 == 0 else [],
                         inc=(kc2 == 1))
            v_ds = S["pe"].cur
            for hh2 in range(2):
                hd = 2 * pair + hh2
                bk = 2 + hh2
                k.dve(lambda e, b=bk, h=hd: e.tensor_tensor(
                    SB[:, o_SS[h]:o_SS[h] + 512], SB[:, o_SS[h]:o_SS[h] + 512],
                    PB[b][:, :], op=OP.add),
                    waits=[("pe", v_ds)] if hh2 == 0 else [], inc=True)
                k.setbg(bk, "dv")
        v_S_add = S["dv"].cur
        knt_guard[sld] = k.w("pe", S["pe"].cur)
        vbt_guard[sld] = k.w("dv", v_ys)

        if "cdbg" in k.dbg and ck == 0:
            cdbg = k.dram("cdbg", [16, 128, DM], "ExternalOutput")
            for di, off in enumerate((o_cls["P1"], o_cls["Q1"], o_cls["C1"],
                                      o_cls["C2"], o_cls["C3"], o_att)):
                k.dma("sync", cdbg[di][:, 0:256], SB[:, off:off + 256],
                      waits=[k.w("dv"), k.w("pe")], sem="sp")
            for di, off in enumerate((o_Ys, o_Z1, o_Z2, o_Us, o_dta)):
                k.dma("sync", cdbg[6 + di], SB[:, off:off + DM], sem="sp")
            k.dma("sync", cdbg[11][:, 0:512], SB[:, o_SS[0]:o_SS[0] + 512],
                  sem="sp")
            k.dma("sync", cdbg[12][:, 0:512], SB[:, o_SS[1]:o_SS[1] + 512],
                  sem="sp")
            k.dma("sync", cdbg[13][0:64], SB[0:64, o_knT[0]:o_knT[0] + DM],
                  sem="sp")
            k.dma("sync", cdbg[14][0:64], SB[0:64, o_vbT[0]:o_vbT[0] + DM],
                  sem="sp")
            k.dma("sync", cdbg[15][:, 0:256], SB[:, o_cls["Q2"]:o_cls["Q2"] + 256],
                  sem="sp")
            k.op("tensor", lambda e: e.nop(), waits=[k.w("sp")])

        if par == 1:
            dta_o = o_dta + sdta * DM
            tens = {"l": o_lTb, "m": o_mTb, "d": dta_o, "v": o_vTb}
            sums = [("l", None, 0), ("l", "l", 1), ("m", None, 2), ("m", "m", 3),
                    ("d", None, 4), ("d", "d", 5), ("v", None, 6), ("v", "v", 7),
                    ("l", "m", 8), ("l", "d", 9), ("l", "v", 10),
                    ("m", "d", 11), ("m", "v", 12), ("d", "v", 13)]
            v_need = v_oev
            pi = 0
            gp_hist = []
            for a, b_, idx in sums:
                if b_ is None:
                    k.dve(lambda e, ta=tens[a], ix=idx: e.tensor_reduce(
                        SB[:, o_fr + 4 * ix:o_fr + 4 * ix + 4],
                        SB[:, ta:ta + DM].rearrange("p (hh dd) -> p hh dd", hh=4),
                        axis=mybir.AxisListType.X, op=OP.add),
                        waits=[("dv", v_need)] if idx == 0 else [])
                else:
                    pb_ = o_pr[pi % 2]
                    wgp = [("dv", v_need)]
                    if len(gp_hist) >= 2:
                        wgp.append(("dv", gp_hist[-2][1]))
                    k.gps(lambda e, ta=tens[a], tb=tens[b_], po=pb_: e.tensor_mul(
                        SB[:, po:po + DM], SB[:, ta:ta + DM], SB[:, tb:tb + DM]),
                        waits=wgp, inc=True)
                    vg = S["gp"].cur
                    k.dve(lambda e, po=pb_, ix=idx: e.tensor_reduce(
                        SB[:, o_fr + 4 * ix:o_fr + 4 * ix + 4],
                        SB[:, po:po + DM].rearrange("p (hh dd) -> p hh dd", hh=4),
                        axis=mybir.AxisListType.X, op=OP.add),
                        waits=[("gp", vg)], inc=True)
                    gp_hist.append((pi, S["dv"].cur))
                    pi += 1
            k.dve(lambda e: e.engine_nop(), inc=True)
            v_sums = S["dv"].cur
            pairs_mv = [(0, 1), (2, 3), (4, 5), (6, 7)]
            for si, qi in pairs_mv:
                k.dve(lambda e, s=si: e.tensor_scalar(
                    SB[:, o_ft + 4 * s:o_ft + 4 * s + 4],
                    SB[:, o_fr + 4 * s:o_fr + 4 * s + 4], 1.0 / DH, None,
                    op0=OP.mult), waits=[("dv", v_sums)], inc=True)
                k.dve(lambda e, s=si: e.tensor_tensor(
                    SB[:, o_sc1:o_sc1 + 4], SB[:, o_fr + 4 * s:o_fr + 4 * s + 4],
                    SB[:, o_fr + 4 * s:o_fr + 4 * s + 4], op=OP.mult), inc=True)
                k.dve(lambda e: e.tensor_scalar(
                    SB[:, o_sc2:o_sc2 + 4], SB[:, o_sc1:o_sc1 + 4],
                    1.0 / (255.0 * DH), None, op0=OP.mult),
                    waits=[k.w("dv")], inc=True)
                k.dve(lambda e, q=qi: e.scalar_tensor_tensor(
                    SB[:, o_ft + 4 * q:o_ft + 4 * q + 4],
                    SB[:, o_fr + 4 * q:o_fr + 4 * q + 4], 1.0 / 255.0,
                    SB[:, o_sc2:o_sc2 + 4], op0=OP.mult, op1=OP.subtract),
                    waits=[k.w("dv")], inc=True)
            for ix in range(8, 14):
                k.dve(lambda e, x=ix: e.tensor_scalar(
                    SB[:, o_ft + 4 * x:o_ft + 4 * x + 4],
                    SB[:, o_fr + 4 * x:o_fr + 4 * x + 4], 1.0 / DH, None,
                    op0=OP.mult), inc=(ix == 13))
            v_ft = S["dv"].cur
            k.transp(PB[7][0:56, 0:128], SB[:, o_ft:o_ft + 56],
                     SB[:, o_id:o_id + 128], waits=[("dv", v_ft), k.bg(7)],
                     inc=True)
            v_ftr = S["pe"].cur
            k.dve(lambda e, b_=B_: e.tensor_copy(
                SB[0:56, o_feats + 128 * b_:o_feats + 128 * b_ + 128],
                PB[7][0:56, 0:128]), waits=[("pe", v_ftr)], inc=True)
            k.setbg(7, "dv")
            dta_spill[sdta] = k.dma("scalar", dT_sp[B_],
                                    SB[:, dta_o:dta_o + DM],
                                    waits=[("dv", v_oev)], sem="sp")
    if "feats_dbg" in k.dbg:
        fdbg = k.dram("feats_dbg", [56, L], "ExternalOutput")
        k.dma("sync", fdbg, SB[0:56, o_feats:o_feats + L],
              waits=[k.w("dv"), k.w("pe")], sem="sp")
    k.pop()
    if "C" in phases:
        k.flush()

    k.push()
    o_rw1 = k.take(9 * 2176)
    o_rw2 = k.take(17 * 16)
    o_rb1 = k.take(32)
    o_rb2 = k.take(16)
    o_hD = k.take(NCH * TS)
    o_h1 = k.take(2 * TS)
    o_lg = k.take(TS)
    o_lgT = k.take(16)
    o_sm = k.take(64)

    k.dma("sync", SB[:, o_rw1:o_rw1 + 8 * 2176].rearrange(
        "p (c m) -> p c m", c=8),
        rw1_d[0:1024].rearrange("(c p) m -> p c m", p=128), sem="wld",
        waits=[k.w("pe"), k.w("dv")])
    k.dma("sync", SB[0:64, o_rw1 + 8 * 2176:o_rw1 + 9 * 2176],
          rw1_d[1024:1088], sem="wld")
    k.dma("sync", SB[:, o_rw2:o_rw2 + 17 * 16].rearrange(
        "p (c n) -> p c n", c=17),
        rw2_d.rearrange("(c p) n -> p c n", p=128), sem="wld")
    k.dma("sync", SB[:, o_rb1:o_rb1 + 17],
          rb1_d.rearrange("(c p) -> p c", p=128), sem="wld", slow=True)
    k.dma("sync", SB[0:16, o_rb2:o_rb2 + 1], rb2_d.rearrange("(n o) -> n o", o=1),
          sem="wld")
    v_wD = S["wld"].cur

    for st in range(NST):
        tok0 = st * TS
        k.dma("sync", SB[:, o_hD:o_hD + NCH * TS].rearrange(
            "p (c t) -> p c t", c=NCH),
            h_sp.rearrange("c p l -> p c l")[:, :, tok0:tok0 + TS],
            waits=[k.w("pe"), k.w("ac"), k.w("dv")])
        v_hld = S["ld"].cur
        lg_hist = []
        for m in range(17):
            bank = m % 2
            for kc in range(9):
                if kc < 8:
                    lh = SB[:, o_rw1 + kc * 2176 + 128 * m:
                            o_rw1 + kc * 2176 + 128 * m + 128]
                    rh = SB[:, o_hD + kc * TS:o_hD + (kc + 1) * TS]
                else:
                    lh = SB[0:64, o_rw1 + 8 * 2176 + 128 * m:
                            o_rw1 + 8 * 2176 + 128 * m + 128]
                    rh = SB[0:64, o_feats + tok0:o_feats + tok0 + TS]
                k.mm(PB[bank][:, :], lh, rh, start=(kc == 0), stop=(kc == 8),
                     waits=([("ld", v_hld), ("wld", v_wD), k.bg(bank)]
                            if kc == 0 else []),
                     inc=(kc == 8))
            v_pm = S["pe"].cur
            db = m % 2
            va = k.act(SB[:, o_h1 + db * TS:o_h1 + (db + 1) * TS], PB[bank][:, :],
                       AF.Silu, bias=SB[:, o_rb1 + m:o_rb1 + m + 1],
                       waits=[("pe", v_pm)] +
                       ([("pe", lg_hist[m - 2])] if m >= 2 else []),
                       inc=True)
            k.setbg(bank, "ac", va)
            k.mm(PB[2][0:16, :], SB[:, o_rw2 + m * 16:o_rw2 + m * 16 + 16],
                 SB[:, o_h1 + db * TS:o_h1 + (db + 1) * TS],
                 start=(m == 0), stop=(m == 16),
                 waits=[("ac", va)] + ([k.bg(2)] if m == 0 else []),
                 inc=True)
            lg_hist.append(S["pe"].cur)
        v_lgm = S["pe"].cur
        va = k.act(SB[0:16, o_lg:o_lg + TS], PB[2][0:16, :], AF.Identity,
                   bias=SB[0:16, o_rb2:o_rb2 + 1], waits=[("pe", v_lgm)],
                   inc=True)
        k.setbg(2, "ac", va)
        for blk in range(4):
            B_ = 4 * st + blk
            k.transp(PB[3][0:128, 0:16], SB[0:16, o_lg + blk * 128:o_lg + blk * 128 + 128],
                     SB[0:16, o_id:o_id + 16], waits=[("ac", va), k.bg(3)], inc=True)
            v_t = S["pe"].cur
            lgT = SB[:, o_lgT:o_lgT + 16]
            k.dve(lambda e: e.tensor_copy(lgT, PB[3][:, 0:16]),
                  waits=[("pe", v_t)], inc=True)
            k.setbg(3, "dv")
            mx = SB[:, o_sm:o_sm + 4]
            ex = SB[:, o_sm + 16:o_sm + 32]
            s4 = SB[:, o_sm + 32:o_sm + 36]
            r4 = SB[:, o_sm + 48:o_sm + 52]
            k.dve(lambda e: e.tensor_tensor(mx, lgT[:, 0:4], lgT[:, 4:8],
                                            op=OP.max), waits=[k.w("dv")], inc=True)
            k.dve(lambda e: e.tensor_tensor(s4, lgT[:, 8:12], lgT[:, 12:16],
                                            op=OP.max), inc=True)
            k.dve(lambda e: e.tensor_tensor(mx, mx, s4, op=OP.max),
                  waits=[k.w("dv")], inc=True)
            for i in range(4):
                k.dve(lambda e, i=i: e.tensor_tensor(
                    ex[:, 4 * i:4 * i + 4], lgT[:, 4 * i:4 * i + 4], mx,
                    op=OP.subtract), waits=[k.w("dv")], inc=(i == 3))
            va2 = k.act(ex, ex, AF.Exp, waits=[k.w("dv")], inc=True)
            k.dve(lambda e: e.tensor_tensor(s4, ex[:, 0:4], ex[:, 4:8],
                                            op=OP.add), waits=[("ac", va2)],
                  inc=True)
            k.dve(lambda e: e.tensor_tensor(r4, ex[:, 8:12], ex[:, 12:16],
                                            op=OP.add), inc=True)
            k.dve(lambda e: e.tensor_tensor(s4, s4, r4, op=OP.add),
                  waits=[k.w("dv")], inc=True)
            k.dve(lambda e: e.reciprocal(r4, s4), waits=[k.w("dv")], inc=True)
            pdst = SB[:, o_pT + 16 * B_:o_pT + 16 * B_ + 16]
            for i in range(4):
                k.dve(lambda e, i=i, p_=pdst: e.tensor_tensor(
                    p_[:, 4 * i:4 * i + 4], ex[:, 4 * i:4 * i + 4], r4,
                    op=OP.mult), waits=[k.w("dv")], inc=(i == 3))
            k.dve(lambda e, p_=pdst: e.tensor_scalar(
                p_, p_, 1.0 - 4 * 0.01, 0.01, op0=OP.mult, op1=OP.add),
                waits=[k.w("dv")], inc=True)
    k.pop()
    k.pop()
    if "D" in phases:
        k.flush()

    k.push()
    o_A = k.take(NCH * DM)
    o_B = k.take(NCH * DM)
    o_C = k.take(NCH * DM)
    o_mixw = k.take(DM)
    o_onr = k.take(DM)
    o_xE = k.take(NCH * TS)
    o_onF = k.take(NCH * TS)
    o_x1 = k.take(NCH * TS)
    o_t4 = [k.take(DM) for _ in range(4)]
    o_mx = k.take(DM)
    o_mw2 = k.take(DM)
    o_t3 = k.take(DM)
    o_s4 = k.take(64)
    o_fb = k.take(16)
    o_fch = k.take(TS)
    o_och = k.take(2 * TS)
    o_mrE = k.take(2 * TS)
    mrow = SB[0:1, o_mrE:o_mrE + TS]
    rrow = SB[0:1, o_mrE + TS:o_mrE + 2 * TS]

    k.dma("sync", SB[:, o_A:o_A + NCH * DM].rearrange("p (c m) -> p c m", c=NCH),
          ffn_wT_d.rearrange("(c p) m -> p c m", p=128), sem="wld",
          waits=[k.w("pe"), k.w("dv"), k.w("ac")])
    k.dma("sync", SB[:, o_B:o_B + NCH * DM].rearrange("p (c m) -> p c m", c=NCH),
          ffn_a1_d.rearrange("(c p) m -> p c m", p=128), sem="wld")
    k.dma("sync", SB[:, o_mixw:o_mixw + DM], mixw_d, sem="wld")
    k.dma("sync", SB[:, o_onr:o_onr + DM], onormr_d, sem="wld")
    k.dma("sync", SB[:, o_fb:o_fb + 8].rearrange("p c -> p c"),
          ffn_b_d.rearrange("(c p) -> p c", p=128), sem="wld", slow=True)
    v_wE = S["wld"].cur

    def mm1024(dst_off, lhs_off, rhs_off, wv, lhT_colmaj=True):
        for m in range(NCH):
            bank = m % 2
            for nn in range(2):
                for kc in range(NCH):
                    k.mm(PB[bank][:, :],
                         SB[:, lhs_off + kc * DM + 128 * m:
                            lhs_off + kc * DM + 128 * m + 128],
                         SB[:, rhs_off + kc * DM + 512 * nn:
                            rhs_off + kc * DM + 512 * nn + 512],
                         start=(kc == 0), stop=(kc == NCH - 1),
                         waits=([("wld", wv), k.bg(bank)] if kc == 0 else []),
                         inc=(kc == NCH - 1))
                v_ = S["pe"].cur
                k.dve(lambda e, b=bank, mm_=m, n_=nn, d=dst_off: e.tensor_copy(
                    SB[:, d + mm_ * DM + 512 * n_:d + mm_ * DM + 512 * n_ + 512],
                    PB[b][:, :]), waits=[("pe", v_)], inc=True)
                k.setbg(bank, "dv")

    mm1024(o_C, o_A, o_B, v_wE)
    v_T1 = S["dv"].cur
    for cc in range(NCH):
        for m in range(NCH):
            bank = 2 + (m % 2)
            k.transp(PB[bank][:, 0:128],
                     SB[:, o_C + cc * DM + 128 * m:o_C + cc * DM + 128 * m + 128],
                     SB[:, o_id:o_id + 128],
                     waits=[("dv", v_T1), k.bg(bank)] if cc == 0 and m < 2 else
                     [k.bg(bank)], inc=True)
            v_t = S["pe"].cur
            k.dve(lambda e, b=bank, c_=cc, m_=m: e.tensor_copy(
                SB[:, o_A + m_ * DM + 128 * c_:o_A + m_ * DM + 128 * c_ + 128],
                PB[b][:, 0:128]), waits=[("pe", v_t)], inc=True)
            k.setbg(bank, "dv")
    v_T1T = S["dv"].cur
    k.dma("sync", SB[:, o_B:o_B + NCH * DM].rearrange("p (c m) -> p c m", c=NCH),
          ffn_a2_d.rearrange("(c p) m -> p c m", p=128), sem="wld",
          waits=[("dv", v_T1T)])
    v_wE2 = S["wld"].cur
    mm1024(o_C, o_A, o_B, v_wE2)
    k.dma("sync", SB[:, o_A:o_A + NCH * DM].rearrange("p (c m) -> p c m", c=NCH),
          wo_d.rearrange("(c p) m -> p c m", p=128), sem="wld",
          waits=[k.w("dv")])
    v_wo = S["wld"].cur
    och_spill = [0, 0]

    for st in range(NST):
        tok0 = st * TS
        k.dma("sync", SB[:, o_xE:o_xE + NCH * TS].rearrange(
            "p (c t) -> p c t", c=NCH),
            x_fm.rearrange("(c p) l -> p c l", p=128)[:, :, tok0:tok0 + TS],
            waits=[k.w("dv"), k.w("pe"), k.w("ac"), k.w("sp")])
        v_xld = S["ld"].cur
        for blk in range(4):
            B_ = 4 * st + blk
            for ti, src in enumerate((lT_sp, mT_sp, dT_sp, vT_sp)):
                k.dma("sync", SB[:, o_t4[ti]:o_t4[ti] + DM], src[B_],
                      waits=[k.w("dv")] if ti == 0 else [])
            v_tld = S["ld"].cur
            for hd in range(H):
                for i in range(4):
                    pcol = SB[:, o_pT + 16 * B_ + 4 * i + hd:
                              o_pT + 16 * B_ + 4 * i + hd + 1]
                    src = SB[:, o_t4[i] + 256 * hd:o_t4[i] + 256 * hd + 256]
                    dst = SB[:, o_mx + 256 * hd:o_mx + 256 * hd + 256]
                    if i == 0:
                        k.dve(lambda e, s_=src, d_=dst, p_=pcol: e.tensor_scalar(
                            d_, s_, p_, None, op0=OP.mult),
                            waits=[("ld", v_tld)] if hd == 0 else [])
                    else:
                        k.dve(lambda e, s_=src, d_=dst, p_=pcol:
                              e.scalar_tensor_tensor(d_, s_, p_, d_,
                                                     op0=OP.mult, op1=OP.add),
                              inc=(hd == H - 1 and i == 3))
            v_mx = S["dv"].cur
            k.dve(lambda e: e.tensor_tensor(SB[:, o_t3:o_t3 + DM],
                                            SB[:, o_mx:o_mx + DM],
                                            SB[:, o_mx:o_mx + DM], op=OP.mult),
                  waits=[("dv", v_mx)])
            k.dve(lambda e: e.tensor_reduce(
                SB[:, o_s4:o_s4 + 4],
                SB[:, o_t3:o_t3 + DM].rearrange("p (hh dd) -> p hh dd", hh=4),
                axis=mybir.AxisListType.X, op=OP.add), waits=[k.w("dv")],
                inc=True)
            va = k.act(SB[:, o_s4 + 8:o_s4 + 12], SB[:, o_s4:o_s4 + 4], AF.Sqrt,
                       scale=1.0 / DH, bias=1e-5, waits=[k.w("dv")], inc=True)
            k.dve(lambda e: e.reciprocal(SB[:, o_s4 + 16:o_s4 + 20],
                                         SB[:, o_s4 + 8:o_s4 + 12]),
                  waits=[("ac", va)], inc=True)
            k.dve(lambda e: e.tensor_tensor(SB[:, o_mw2:o_mw2 + DM],
                                            SB[:, o_mx:o_mx + DM],
                                            SB[:, o_mixw:o_mixw + DM],
                                            op=OP.mult), inc=True)
            k.dve(lambda e: e.tensor_tensor(SB[:, o_t3:o_t3 + DM],
                                            SB[:, o_mw2:o_mw2 + DM],
                                            SB[:, o_mw2:o_mw2 + DM],
                                            op=OP.mult), waits=[k.w("dv")])
            k.dve(lambda e: e.tensor_reduce(
                SB[:, o_s4 + 24:o_s4 + 28],
                SB[:, o_t3:o_t3 + DM].rearrange("p (hh dd) -> p hh dd", hh=4),
                axis=mybir.AxisListType.X, op=OP.add), waits=[k.w("dv")],
                inc=True)
            k.dve(lambda e: e.tensor_tensor(
                SB[:, o_s4 + 32:o_s4 + 36], SB[:, o_s4 + 16:o_s4 + 20],
                SB[:, o_s4 + 16:o_s4 + 20], op=OP.mult), waits=[k.w("dv")],
                inc=True)
            k.dve(lambda e: e.tensor_tensor(
                SB[:, o_s4 + 40:o_s4 + 44], SB[:, o_s4 + 24:o_s4 + 28],
                SB[:, o_s4 + 32:o_s4 + 36], op=OP.mult), waits=[k.w("dv")],
                inc=True)
            va = k.act(SB[:, o_s4 + 44:o_s4 + 48], SB[:, o_s4 + 40:o_s4 + 44],
                       AF.Sqrt, scale=1.0 / DH, bias=1e-5, waits=[k.w("dv")],
                       inc=True)
            k.dve(lambda e: e.reciprocal(SB[:, o_s4 + 48:o_s4 + 52],
                                         SB[:, o_s4 + 44:o_s4 + 48]),
                  waits=[("ac", va)], inc=True)
            k.dve(lambda e: e.tensor_tensor(
                SB[:, o_s4 + 52:o_s4 + 56], SB[:, o_s4 + 16:o_s4 + 20],
                SB[:, o_s4 + 48:o_s4 + 52], op=OP.mult), waits=[k.w("dv")],
                inc=True)
            k.dve(lambda e: e.tensor_tensor(SB[:, o_t3:o_t3 + DM],
                                            SB[:, o_mw2:o_mw2 + DM],
                                            SB[:, o_onr:o_onr + DM],
                                            op=OP.mult), waits=[k.w("dv")])
            for hd in range(H):
                k.dve(lambda e, h=hd: e.tensor_scalar(
                    SB[:, o_t3 + 256 * h:o_t3 + 256 * h + 256],
                    SB[:, o_t3 + 256 * h:o_t3 + 256 * h + 256],
                    SB[:, o_s4 + 52 + h:o_s4 + 53 + h], None, op0=OP.mult),
                    waits=[k.w("dv")] if hd == 0 else [], inc=(hd == H - 1))
            v_on = S["dv"].cur
            for cc in range(NCH):
                bank = 2 + (cc % 2)
                k.transp(PB[bank][:, 0:128],
                         SB[:, o_t3 + 128 * cc:o_t3 + 128 * cc + 128],
                         SB[:, o_id:o_id + 128],
                         waits=[("dv", v_on), k.bg(bank)], inc=True)
                v_t = S["pe"].cur
                k.dve(lambda e, b=bank, c_=cc, bl=blk: e.tensor_copy(
                    SB[:, o_onF + c_ * TS + 128 * bl:o_onF + c_ * TS + 128 * bl + 128],
                    PB[b][:, 0:128]), waits=[("pe", v_t)], inc=True)
                k.setbg(bank, "dv")
        v_onF = S["dv"].cur
        for m in range(NCH):
            bank = m % 2
            for kc in range(NCH):
                k.mm(PB[bank][:, :],
                     SB[:, o_A + kc * DM + 128 * m:o_A + kc * DM + 128 * m + 128],
                     SB[:, o_onF + kc * TS:o_onF + (kc + 1) * TS],
                     start=(kc == 0), stop=(kc == NCH - 1),
                     waits=([("dv", v_onF), ("wld", v_wo), ("ld", v_xld),
                             k.bg(bank)] if kc == 0 else []),
                     inc=(kc == NCH - 1))
            v_ = S["pe"].cur
            k.dve(lambda e, b=bank, m_=m: e.scalar_tensor_tensor(
                SB[:, o_x1 + m_ * TS:o_x1 + (m_ + 1) * TS], PB[b][:, :],
                modcol(2, m_), SB[:, o_xE + m_ * TS:o_xE + (m_ + 1) * TS],
                op0=OP.mult, op1=OP.add), waits=[("pe", v_)], inc=True)
            k.setbg(bank, "dv")
        v_x1 = S["dv"].cur
        for c in range(NCH):
            k.mm(PB[4][0:1, :], SB[:, o_onesc:o_onesc + 1],
                 SB[:, o_x1 + c * TS:o_x1 + (c + 1) * TS],
                 start=(c == 0), stop=(c == NCH - 1),
                 waits=[("dv", v_x1), k.bg(4)] if c == 0 else [],
                 inc=(c == NCH - 1))
        pe_hist2 = []
        for c in range(NCH):
            wv_ = [("dv", v_x1)] if c == 0 else [k.w("pe", pe_hist2[-1])]
            va = k.act(SB[:, o_fch:o_fch + TS],
                       SB[:, o_x1 + c * TS:o_x1 + (c + 1) * TS], AF.Square,
                       waits=wv_, inc=True)
            k.mm(PB[5][0:1, :], SB[:, o_onesc:o_onesc + 1], SB[:, o_fch:o_fch + TS],
                 start=(c == 0), stop=(c == NCH - 1),
                 waits=[("ac", va)] + ([k.bg(5)] if c == 0 else []), inc=True)
            pe_hist2.append(S["pe"].cur)
        v_st2 = S["pe"].cur
        k.dve(lambda e: e.tensor_scalar(mrow, PB[4][0:1, :], 1.0 / DM, None,
                                        op0=OP.mult), waits=[("pe", v_st2)])
        k.dve(lambda e: e.tensor_tensor(rrow, mrow, mrow, op=OP.mult))
        k.dve(lambda e: e.scalar_tensor_tensor(
            rrow, PB[5][0:1, :], 1.0 / DM, rrow, op0=OP.mult, op1=OP.subtract),
            inc=True)
        v_var2 = S["dv"].cur
        k.setbg(4, "dv", v_var2)
        k.setbg(5, "dv", v_var2)
        va = k.act(rrow, rrow, AF.Sqrt, bias=1e-5, waits=[("dv", v_var2)],
                   inc=True)
        k.dve(lambda e: e.reciprocal(rrow, rrow), waits=[("ac", va)], inc=True)
        v_r2 = S["dv"].cur
        k.mm(PB[4][:, :], SB[0:1, o_onesr:o_onesr + 128], mrow,
             start=True, stop=True, waits=[("dv", v_r2), k.bg(4)])
        k.mm(PB[5][:, :], SB[0:1, o_onesr:o_onesr + 128], rrow,
             start=True, stop=True, waits=[k.bg(5)], inc=True)
        v_bc2 = S["pe"].cur
        for c in range(NCH):
            x1c = SB[:, o_x1 + c * TS:o_x1 + (c + 1) * TS]
            hc = SB[:, o_onF + c * TS:o_onF + (c + 1) * TS]
            k.dve(lambda e, x=x1c, h=hc: e.tensor_tensor(h, x, PB[4][:, :],
                                                         op=OP.subtract),
                  waits=[("pe", v_bc2)] if c == 0 else [])
            k.dve(lambda e, h=hc: e.tensor_tensor(h, h, PB[5][:, :], op=OP.mult))
            k.dve(lambda e, h=hc, c=c: e.scalar_tensor_tensor(
                h, h, modcol(4, c), h, op0=OP.mult, op1=OP.add))
            k.dve(lambda e, h=hc, c=c: e.tensor_scalar(
                h, h, modcol(3, c), None, op0=OP.add), inc=True)
        v_h2 = S["dv"].cur
        k.setbg(4, "dv", v_h2)
        k.setbg(5, "dv", v_h2)
        for m in range(NCH):
            bank = m % 2
            for kc in range(NCH):
                k.mm(PB[bank][:, :],
                     SB[:, o_C + kc * DM + 128 * m:o_C + kc * DM + 128 * m + 128],
                     SB[:, o_onF + kc * TS:o_onF + (kc + 1) * TS],
                     start=(kc == 0), stop=(kc == NCH - 1),
                     waits=[("dv", v_h2), k.bg(bank)] if kc == 0 else [],
                     inc=(kc == NCH - 1))
            v_ = S["pe"].cur
            va = k.act(SB[:, o_fch:o_fch + TS], PB[bank][:, :], AF.Tanh,
                       bias=SB[:, o_fb + m:o_fb + m + 1],
                       waits=[("pe", v_), k.w("dv")], inc=True)
            k.setbg(bank, "ac", va)
            db = m % 2
            k.dve(lambda e, m_=m, d_=db: e.scalar_tensor_tensor(
                SB[:, o_och + d_ * TS:o_och + (d_ + 1) * TS],
                SB[:, o_fch:o_fch + TS], modcol(5, m_),
                SB[:, o_x1 + m_ * TS:o_x1 + (m_ + 1) * TS],
                op0=OP.mult, op1=OP.add),
                waits=[("ac", va), ("sp", och_spill[db])], inc=True)
            v_out = S["dv"].cur
            och_spill[db] = k.dma(
                "sync",
                out_fm.rearrange("(c p) l -> p c l", p=128)[:, m:m + 1,
                                                            tok0:tok0 + TS],
                SB[:, o_och + db * TS:o_och + (db + 1) * TS].unsqueeze(1),
                waits=[("dv", v_out)], sem="sp")
    k.pop()
    if "E" in phases:
        k.flush()

    k.op("sync", lambda e: e.nop(),
         waits=[k.w("sp"), k.w("ld"), k.w("wld")])
    k.flush()
    return k


def build_C(k, env):
    raise NotImplementedError


def build_D(k, env):
    raise NotImplementedError


def build_E(k, env):
    raise NotImplementedError



def make_consts():
    ident = np.eye(128, dtype=np.float32)
    onesrow = np.ones((1, 128), np.float32)
    onescol = np.ones((128, 1), np.float32)
    indrow = np.zeros((128, 512), np.float32)
    indhead = np.zeros((128, 512), np.float32)
    for h in range(4):
        indrow[32 * h, 128 * h:128 * (h + 1)] = 1.0
        indhead[:, 128 * h + 32 * h] = 1.0
    masks = np.zeros((128, 1024), np.float32)
    sl = np.tril(np.ones((CC, CC), np.float32), -1)
    su = np.triu(np.ones((CC, CC), np.float32), 1)
    iu = np.triu(np.ones((CC, CC), np.float32), 0)
    i64 = np.eye(CC, dtype=np.float32)
    for r in range(2):
        rs = slice(r * CC, (r + 1) * CC)
        masks[rs, 0:CC] = sl
        masks[rs, CC:2 * CC] = su
        masks[rs, 2 * CC:3 * CC] = iu
        masks[rs, 3 * CC:4 * CC] = i64
        for t in range(4):
            masks[rs, 256 + t * CC:256 + (t + 1) * CC] = sl
            masks[rs, 512 + t * CC:512 + (t + 1) * CC] = su
            masks[rs, 768 + t * CC:768 + (t + 1) * CC] = iu
    return dict(ident=ident, onesrow=onesrow, onescol=onescol,
                indrow=indrow, indhead=indhead, masks=masks)


def make_inputs(d, b):
    f32 = np.float32
    c = make_consts()
    x = np.ascontiguousarray(np.asarray(d["L0_attn_hidden_states"][b]).T).astype(f32)
    wbp = np.zeros((DM, 128), f32)
    for h in range(4):
        wbp[:, 32 * h] = np.asarray(d["L0_attn_Wb"])[:, h]
    convw = np.concatenate([
        np.asarray(d["L0_attn_conv_q_w"]), np.asarray(d["L0_attn_conv_k_w"]),
        np.asarray(d["L0_attn_conv_v_w"]), np.asarray(d["L0_attn_local_w"]),
        np.asarray(d["L0_attn_mid_w"])], axis=1).astype(f32)
    rw1 = np.zeros((1088, 2176), f32)
    rw1[:1080, :2160] = np.asarray(d["L0_attn_r_w1"])
    rb1 = np.zeros((2176,), f32)
    rb1[:2160] = np.asarray(d["L0_attn_r_b1"])
    rw2 = np.asarray(d["L0_attn_r_w2"]).astype(f32)
    rb2 = np.asarray(d["L0_attn_r_b2"]).astype(f32)
    perm = [4 * h + i for i in range(4) for h in range(4)]
    rw2p = np.zeros((2176, 16), f32)
    rw2p[:2160] = rw2[:, perm]
    rb2p = np.ascontiguousarray(rb2[perm])
    mixw = np.asarray(d["L0_attn_mix_w"]).astype(f32).reshape(-1)
    onormt = np.tile(np.asarray(d["L0_attn_onorm_w"]).astype(f32), 4)
    return {
        "x_fm": x,
        "wq": np.asarray(d["L0_attn_Wq"]).astype(f32),
        "wk": np.asarray(d["L0_attn_Wk"]).astype(f32),
        "wv": np.asarray(d["L0_attn_Wv"]).astype(f32),
        "wbp": wbp,
        "convw": convw,
        "rw1": rw1, "rb1": rb1, "rw2p": rw2p, "rb2p": rb2p,
        "onorm_r": np.broadcast_to(onormt, (128, DM)).copy(),
        "mixw_r": np.broadcast_to(mixw, (128, DM)).copy(),
        "wo": np.asarray(d["L0_attn_Wo"]).astype(f32),
        "ffn_wT": np.ascontiguousarray(np.asarray(d["L0_ffn_weight"]).T).astype(f32),
        "ffn_a1": np.asarray(d["L0_ffn_aux_weight1"]).astype(f32),
        "ffn_a2": np.asarray(d["L0_ffn_aux_weight2"]).astype(f32),
        "ffn_b": np.asarray(d["L0_ffn_bias"]).astype(f32),
        "cvec": np.asarray(d["L0_c"][b]).astype(f32),
        "wmod": np.asarray(d["L0_wmod"]).astype(f32),
        **c,
    }


def kernel(**inputs):
    kk = build_program()
    in_maps = [make_inputs(inputs, b) for b in range(8)]
    res = run_bass_kernel_spmd(kk.nc, in_maps, core_ids=list(range(8)))
    out = np.stack([np.asarray(r["out_fm"]).T for r in res.results])
    return out.astype(np.float32)


# version 02
# speedup vs baseline: 1.1044x; 1.1044x over previous
import numpy as np
import concourse.bass as bass
import concourse.mybir as mybir
from concourse.bass_utils import run_bass_kernel_spmd
from concourse.alu_op_type import AluOpType as OP

F32 = mybir.dt.float32
AF = mybir.ActivationFunctionType

L = 4096
DM = 1024
H = 4
DH = 256
TS = 512
NST = L // TS
NCH = DM // 128
CC = 64
NCK = L // CC
NBLK = L // 128
VH = 24

ENG = ("tensor", "vector", "scalar", "gpsimd", "sync")


class Sem:
    def __init__(self, h):
        self.h = h
        self.cur = 0


class K:

    def __init__(self, debug=()):
        self.nc = bass.Bass("TRN2", target_bir_lowering=False)
        self.dbg = set(debug)
        self.steps = []
        self.sems = {}
        self.SBN = 53200
        self.SB = self.nc.alloc_sbuf_tensor("SBP", [128, self.SBN], F32).ap()
        self.PB = [self.nc.alloc_psum_tensor(f"PB{i}", [128, 512], F32).ap()
                   for i in range(8)]
        self.off = 0
        self.marks = []
        for s in ("ld", "wld", "pe", "dv", "ac", "gp", "sp"):
            self.sems[s] = Sem(self.nc.alloc_semaphore(s))
        self.bguard = [("dv", 0)] * 8

    def take(self, n, align=16):
        o = (self.off + align - 1) // align * align
        self.off = o + n
        assert self.off <= self.SBN, f"SBUF overflow {self.off} > {self.SBN}"
        return o

    def push(self):
        self.marks.append(self.off)

    def pop(self):
        self.off = self.marks.pop()

    def dram(self, name, shape, kind=None):
        if name in self.dbg:
            kind = "ExternalOutput"
        if kind is None:
            return self.nc.dram_tensor(name, shape, F32).ap()
        return self.nc.dram_tensor(name, shape, F32, kind=kind).ap()

    def w(self, name, val=None):
        s = self.sems[name]
        return (s, s.cur if val is None else val)

    def bg(self, i):
        n, v = self.bguard[i]
        return (self.sems[n], v)

    def setbg(self, i, name, val=None):
        self.bguard[i] = (name, self.sems[name].cur if val is None else val)

    def op(self, engine, fn, waits=(), inc=None):
        ws = []
        for t in waits:
            if isinstance(t[0], Sem):
                ws.append(t)
            else:
                ws.append((self.sems[t[0]], t[1]))
        self.steps.append((engine, fn, tuple(ws), inc))
        if inc is not None:
            inc[0].cur += inc[1]

    def flush(self):
        steps, self.steps = self.steps, []
        if not steps:
            return

        def emitter(engine):
            def run(eng):
                for e, fn, waits, inc in steps:
                    if e != engine:
                        continue
                    for s, v in waits:
                        eng.wait_ge(s.h, v)
                    inst = fn(eng)
                    if inc is not None:
                        s, amt = inc
                        assert inst is not None
                        inst.then_inc(s.h, amt)
            return run

        used = [e for e in ENG if any(s[0] == e for s in steps)]
        with self.nc.Block() as block:
            for e in used:
                getattr(block, e)(emitter(e))

    def dma(self, q, out, in_, waits=(), sem="ld", slow=False):
        s = self.sems[sem]
        if slow:
            self.op(q, lambda e, o=out, i=in_: e.dma_start(
                out=o, in_=i, allow_slow_non_contiguous=True),
                waits=waits, inc=(s, 16))
        else:
            self.op(q, lambda e, o=out, i=in_: e.dma_start(out=o, in_=i),
                    waits=waits, inc=(s, 16))
        return s.cur

    def mm(self, out, lhsT, rhs, start, stop, waits=(), inc=False, tp=None):
        s = self.sems["pe"]
        self.op("tensor",
                lambda e, o=out, l=lhsT, r=rhs, st=start, sp=stop, t=tp:
                e.matmul(o, l, r, start=st, stop=sp, tile_position=t),
                waits=waits, inc=(s, 1) if inc else None)
        return s.cur

    def transp(self, out, in_, ident, waits=(), inc=False):
        s = self.sems["pe"]
        self.op("tensor", lambda e, o=out, i=in_, d=ident: e.transpose(o, i, d),
                waits=waits, inc=(s, 1) if inc else None)
        return s.cur

    def dve(self, fn, waits=(), inc=False):
        s = self.sems["dv"]
        self.op("vector", fn, waits=waits, inc=(s, 1) if inc else None)
        return s.cur

    def gps(self, fn, waits=(), inc=False):
        s = self.sems["gp"]
        self.op("gpsimd", fn, waits=waits, inc=(s, 1) if inc else None)
        return s.cur

    def act(self, out, in_, func, bias=None, scale=None, waits=(), inc=False):
        s = self.sems["ac"]
        kw = {}
        if bias is not None:
            kw["bias"] = bias
        if scale is not None:
            kw["scale"] = scale
        self.op("scalar", lambda e, o=out, i=in_, f=func, kk=dict(kw):
                e.activation(o, i, f, **kk),
                waits=waits, inc=(s, 1) if inc else None)
        return s.cur


def build_program(debug=(), phases=("P0", "B", "C", "D", "E")):
    k = K(debug)
    SB, PB = k.SB, k.PB
    S = k.sems

    EIn = "ExternalInput"
    x_fm = k.dram("x_fm", [DM, L], EIn)
    wq_d = k.dram("wq", [DM, DM], EIn)
    wk_d = k.dram("wk", [DM, DM], EIn)
    wv_d = k.dram("wv", [DM, DM], EIn)
    wbp_d = k.dram("wbp", [DM, 128], EIn)
    convw_d = k.dram("convw", [DM, 44], EIn)
    rw1_d = k.dram("rw1", [1088, 2176], EIn)
    rb1_d = k.dram("rb1", [2176], EIn)
    rw2_d = k.dram("rw2p", [2176, 16], EIn)
    rb2_d = k.dram("rb2p", [16], EIn)
    onormr_d = k.dram("onorm_r", [128, DM], EIn)
    mixw_d = k.dram("mixw_r", [128, DM], EIn)
    wo_d = k.dram("wo", [DM, DM], EIn)
    ffn_wT_d = k.dram("ffn_wT", [DM, DM], EIn)
    ffn_a1_d = k.dram("ffn_a1", [DM, DM], EIn)
    ffn_a2_d = k.dram("ffn_a2", [DM, DM], EIn)
    ffn_b_d = k.dram("ffn_b", [DM], EIn)
    cvec_d = k.dram("cvec", [DM], EIn)
    wmod_d = k.dram("wmod", [DM, 6 * DM], EIn)
    ident_d = k.dram("ident", [128, 128], EIn)
    onesrow_d = k.dram("onesrow", [1, 128], EIn)
    onescol_d = k.dram("onescol", [128, 1], EIn)
    indrow_d = k.dram("indrow", [128, 512], EIn)
    indhead_d = k.dram("indhead", [128, 512], EIn)
    masks_d = k.dram("masks", [128, 1536], EIn)

    h_sp = k.dram("h_sp", [NCH, 128, L])
    q_sp = k.dram("q_sp", [NCH, 128, L])
    kn_sp = k.dram("kn_sp", [NCH, 128, L])
    kb_sp = k.dram("kb_sp", [NCH, 128, L])
    knT_sp = k.dram("knT_sp", [NCK, 64, DM])
    vbT_sp = k.dram("vbT_sp", [NCK, 64, DM])
    vT_sp = k.dram("vT_sp", [NBLK, 128, DM])
    lT_sp = k.dram("lT_sp", [NBLK, 128, DM])
    mT_sp = k.dram("mT_sp", [NBLK, 128, DM])
    dT_sp = k.dram("dT_sp", [NBLK, 128, DM])
    rqT_sp = k.dram("rqT_sp", [NBLK, 128, 4])
    mod_sp = k.dram("mod_sp", [6 * DM])
    wp_sp = k.dram("wp_sp", [DM, DM])
    out_fm = k.dram("out_fm", [DM, L], "ExternalOutput")

    o_ceps = k.take(16)
    k.nc.const_aps.register(F32, 1e-5, SB[:, o_ceps:o_ceps + 1])
    k.nc.const_aps.register(F32, 1e-12, SB[:, o_ceps + 1:o_ceps + 2])
    k.dve(lambda e: e.memset(SB[:, o_ceps:o_ceps + 1], 1e-5))
    k.dve(lambda e: e.memset(SB[:, o_ceps + 1:o_ceps + 2], 1e-12), inc=True)
    o_mod = k.take(48)
    o_id = k.take(128)
    o_onesr = k.take(128)
    o_onesc = k.take(16)
    o_indr = k.take(512)
    o_indh = k.take(512)
    o_msk = k.take(256)
    o_cw = k.take(NCH * 44)

    MSL, MSU, MIU, MI64 = o_msk, o_msk + CC, o_msk + 2 * CC, o_msk + 3 * CC

    def modcol(v, c):
        return SB[:, o_mod + 8 * v + c:o_mod + 8 * v + c + 1]

    k.push()
    o_sc = k.take(16)
    o_wmb = k.take(2 * 512)
    o_mrow = k.take(512)

    k.dma("sync", SB[:, o_sc:o_sc + 8], cvec_d.rearrange("(c p) -> p c", p=128),
          slow=True)
    v_c = S["ld"].cur
    for dst, src, wdt in ((o_id, ident_d, 128), (o_onesr, onesrow_d, 128),
                         (o_onesc, onescol_d, 1), (o_indr, indrow_d, 512),
                         (o_indh, indhead_d, 512)):
        k.dma("sync", SB[:src.shape[0], dst:dst + wdt], src, sem="wld")
    k.dma("sync", SB[:, o_msk:o_msk + 256], masks_d[:, 0:256], sem="wld")
    k.dma("sync", SB[:, o_cw:o_cw + NCH * 44].rearrange("p (c w) -> p c w", c=NCH),
          convw_d.rearrange("(c p) w -> p c w", p=128), sem="wld")

    va = k.act(SB[:, o_sc:o_sc + 8], SB[:, o_sc:o_sc + 8], AF.Silu,
               waits=[("ld", v_c)], inc=True)

    pe_hist = []
    for j in range(12):
        for kc in range(NCH):
            n = j * NCH + kc
            buf = o_wmb + (n % 2) * 512
            wt = [k.w("pe", pe_hist[n - 2])] if n >= 2 else []
            vl = k.dma("sync", SB[:, buf:buf + 512],
                       wmod_d[kc * 128:(kc + 1) * 128, j * 512:(j + 1) * 512],
                       waits=wt)
            k.mm(PB[0][0:1, :], SB[:, o_sc + kc:o_sc + kc + 1],
                 SB[:, buf:buf + 512], start=(kc == 0), stop=(kc == NCH - 1),
                 waits=[("ld", vl), ("ac", va)] + ([k.w("dv")] if kc == 0 else []),
                 inc=True)
            pe_hist.append(S["pe"].cur)
        k.dve(lambda e: e.tensor_copy(SB[0:1, o_mrow:o_mrow + 512], PB[0][0:1, :]),
              waits=[k.w("pe")], inc=True)
        k.dma("sync", mod_sp[j * 512:(j + 1) * 512], SB[0:1, o_mrow:o_mrow + 512],
              waits=[k.w("dv")], sem="sp")
    v_modsp = S["sp"].cur
    k.pop()

    k.dma("sync", SB[:, o_mod:o_mod + 48].rearrange("p (v c) -> p v c", v=6),
          mod_sp.rearrange("(v c p) -> p v c", v=6, c=NCH),
          waits=[("sp", v_modsp)], slow=True)
    v_modld = S["ld"].cur

    k.push()
    o_t1, o_t2, o_t3 = k.take(16), k.take(16), k.take(16)
    for v6 in (2, 5):
        g = SB[:, o_mod + 8 * v6:o_mod + 8 * v6 + 8]
        t1, t2, t3 = (SB[:, o:o + 8] for o in (o_t1, o_t2, o_t3))
        k.dve(lambda e, g=g, t=t1: e.tensor_scalar(t, g, 0.25, None, op0=OP.max),
              waits=[("ld", v_modld)], inc=True)
        k.dve(lambda e, g=g, t=t2: e.tensor_scalar(t, g, -0.25, None, op0=OP.min),
              inc=True)
        k.dve(lambda e, g=g, t=t3: e.tensor_scalar(t, g, 0.0, None, op0=OP.is_ge),
              inc=True)
        k.dve(lambda e, a=t1, b=t2: e.tensor_tensor(a, a, b, op=OP.subtract),
              waits=[k.w("dv")], inc=True)
        k.dve(lambda e, a=t1, b=t3: e.tensor_tensor(a, a, b, op=OP.mult),
              waits=[k.w("dv")], inc=True)
        k.dve(lambda e, g=g, a=t1, b=t2: e.tensor_tensor(g, a, b, op=OP.add),
              waits=[k.w("dv")], inc=True)
    k.pop()
    v_mod_dv = S["dv"].cur
    if "P0" in phases:
        k.flush()

    k.push()
    o_wq = k.take(NCH * DM)
    o_wk = k.take(NCH * DM)
    o_wv = k.take(NCH * DM)
    o_wb = k.take(NCH * 128)
    o_x = k.take(NCH * TS)
    o_h = k.take(NCH * TS)
    o_kn = k.take(NCH * TS)
    o_v = k.take(NCH * (VH + TS))
    o_lin = k.take(TS + 16)
    o_halo = k.take(3 * NCH * 4)
    o_cb = k.take(TS)
    o_sq2 = k.take(2 * TS)
    o_qc = k.take(2 * TS)
    o_kbc = k.take(2 * TS)
    o_mr = k.take(2 * TS)
    o_bs = k.take(TS)
    o_rq = k.take(TS)
    o_rk = k.take(TS)
    o_tma = k.take(DM)
    o_tmb = k.take(DM)
    o_bT = k.take(16)
    o_rqTb = k.take(16)

    for dst, src in ((o_wq, wq_d), (o_wk, wk_d), (o_wv, wv_d)):
        k.dma("sync", SB[:, dst:dst + NCH * DM].rearrange("p (c m) -> p c m", c=NCH),
              src.rearrange("(c p) m -> p c m", p=128), sem="wld")
    k.dma("sync", SB[:, o_wb:o_wb + NCH * 128].rearrange("p (c m) -> p c m", c=NCH),
          wbp_d.rearrange("(c p) m -> p c m", p=128), sem="wld")
    v_wB = S["wld"].cur

    k.dve(lambda e: e.memset(SB[:, o_halo:o_halo + 3 * NCH * 4], 0.0))
    for c in range(NCH):
        k.dve(lambda e, c=c: e.memset(
            SB[:, o_v + c * (VH + TS):o_v + c * (VH + TS) + VH], 0.0),
            inc=(c == NCH - 1))

    def conv_taps(lin_off, w_off, ntap, out_off, waits0=()):
        k.dve(lambda e, lo=lin_off, wo=w_off, oo=out_off: e.tensor_scalar(
            SB[:, oo:oo + TS], SB[:, lo:lo + TS], SB[:, wo:wo + 1], None,
            op0=OP.mult), waits=waits0)
        for j in range(1, ntap):
            k.dve(lambda e, lo=lin_off + j, wo=w_off + j, oo=out_off:
                  e.scalar_tensor_tensor(
                      SB[:, oo:oo + TS], SB[:, lo:lo + TS], SB[:, wo:wo + 1],
                      SB[:, oo:oo + TS], op0=OP.mult, op1=OP.add))

    qspill = [0, 0]
    kbspill = [0, 0]
    rqt_spill = 0
    tma_spill = 0
    tmb_spill = 0
    sq2_guard = [("dv", 0), ("dv", 0)]
    mrow = SB[0:1, o_mr:o_mr + TS]
    rrow = SB[0:1, o_mr + TS:o_mr + 2 * TS]

    g_xfree = []
    for st in range(NST):
        tok0 = st * TS
        v_x = k.dma("sync", SB[:, o_x:o_x + NCH * TS].rearrange(
            "p (c t) -> p c t", c=NCH),
            x_fm.rearrange("(c p) l -> p c l", p=128)[:, :, tok0:tok0 + TS],
            waits=g_xfree)

        pe_sx2 = []
        for c in range(NCH):
            k.mm(PB[2][0:1, :], SB[:, o_onesc:o_onesc + 1],
                 SB[:, o_x + c * TS:o_x + (c + 1) * TS],
                 start=(c == 0), stop=(c == NCH - 1),
                 waits=[("ld", v_x), k.bg(2)] if c == 0 else [],
                 inc=(c == NCH - 1))
        for c in range(NCH):
            wv_ = ([("ld", v_x)] if c == 0 else [k.w("pe", pe_sx2[-1])]) + \
                [sq2_guard[c % 2]]
            va = k.act(SB[:, o_sq2 + (c % 2) * TS:o_sq2 + (c % 2) * TS + TS],
                       SB[:, o_x + c * TS:o_x + (c + 1) * TS], AF.Square,
                       waits=wv_, inc=True)
            k.mm(PB[3][0:1, :], SB[:, o_onesc:o_onesc + 1],
                 SB[:, o_sq2 + (c % 2) * TS:o_sq2 + (c % 2) * TS + TS],
                 start=(c == 0), stop=(c == NCH - 1),
                 waits=[("ac", va)] + ([k.bg(3)] if c == 0 else []), inc=True)
            pe_sx2.append(S["pe"].cur)
            sq2_guard[c % 2] = ("pe", S["pe"].cur)
        v_st = S["pe"].cur
        v_xsq = S["ac"].cur
        k.dve(lambda e: e.tensor_scalar(mrow, PB[2][0:1, :], 1.0 / DM, None,
                                        op0=OP.mult), waits=[("pe", v_st)])
        k.dve(lambda e: e.tensor_tensor(rrow, mrow, mrow, op=OP.mult))
        k.dve(lambda e: e.scalar_tensor_tensor(
            rrow, PB[3][0:1, :], 1.0 / DM, rrow, op0=OP.mult, op1=OP.subtract),
            inc=True)
        v_var = S["dv"].cur
        k.setbg(2, "dv", v_var)
        k.setbg(3, "dv", v_var)
        va = k.act(rrow, rrow, AF.Sqrt, bias=1e-5, waits=[("dv", v_var)], inc=True)
        k.dve(lambda e: e.reciprocal(rrow, rrow), waits=[("ac", va)], inc=True)
        v_r = S["dv"].cur
        k.mm(PB[4][:, :], SB[0:1, o_onesr:o_onesr + 128], mrow,
             start=True, stop=True, waits=[("dv", v_r), k.bg(4)])
        k.mm(PB[5][:, :], SB[0:1, o_onesr:o_onesr + 128], rrow,
             start=True, stop=True, waits=[k.bg(5)], inc=True)
        v_bc = S["pe"].cur

        for c in range(NCH):
            xc = SB[:, o_x + c * TS:o_x + (c + 1) * TS]
            hc = SB[:, o_h + c * TS:o_h + (c + 1) * TS]
            w0 = [("pe", v_bc)]
            if st == 0 and c == 0:
                w0 += [("ld", v_modld), ("dv", v_mod_dv)]
            k.dve(lambda e, x=xc, h=hc: e.tensor_tensor(h, x, PB[4][:, :],
                                                        op=OP.subtract), waits=w0)
            k.dve(lambda e, h=hc: e.tensor_tensor(h, h, PB[5][:, :], op=OP.mult))
            k.dve(lambda e, h=hc, c=c: e.scalar_tensor_tensor(
                h, h, modcol(1, c), h, op0=OP.mult, op1=OP.add))
            k.dve(lambda e, h=hc, c=c: e.tensor_scalar(
                h, h, modcol(0, c), None, op0=OP.add), inc=True)
        v_h = S["dv"].cur
        k.setbg(4, "dv", v_h)
        k.setbg(5, "dv", v_h)
        k.dma("sync", h_sp.rearrange("c p l -> p c l")[:, :, tok0:tok0 + TS],
              SB[:, o_h:o_h + NCH * TS].rearrange("p (c t) -> p c t", c=NCH),
              waits=[("dv", v_h)], sem="sp")

        for c in range(NCH):
            k.mm(PB[7][:, :], SB[:, o_wb + c * 128:o_wb + (c + 1) * 128],
                 SB[:, o_h + c * TS:o_h + (c + 1) * TS],
                 start=(c == 0), stop=(c == NCH - 1),
                 waits=[("dv", v_h), ("wld", v_wB), k.bg(7)] if c == 0 else [],
                 inc=(c == NCH - 1))
        v_bl = S["pe"].cur
        va = k.act(SB[:, o_bs:o_bs + TS], PB[7][:, :], AF.Sigmoid,
                   waits=[("pe", v_bl)], inc=True)
        k.setbg(7, "ac", va)

        g_xfree = [("pe", v_st), ("ac", v_xsq), ("dv", v_h)]
        for ti, (ow, cwo, tgt) in enumerate(((o_wq, 0, "q"), (o_wk, 4, "k"),
                                             (o_wv, 8, "v"))):
            sum_hist = []
            pend_sq = None

            def emit_sum(cw, vaw):
                k.mm(PB[6][:, :],
                     SB[:, o_indh + (cw // 2) * 128:o_indh + (cw // 2) * 128 + 128],
                     SB[:, o_sq2 + (cw % 2) * TS:o_sq2 + (cw % 2) * TS + TS],
                     start=(cw == 0), stop=(cw == NCH - 1),
                     waits=[("ac", vaw)] + ([k.bg(6)] if cw == 0 else []),
                     inc=True)
                sum_hist.append(S["pe"].cur)
                sq2_guard[cw % 2] = ("pe", S["pe"].cur)

            for c in range(NCH):
                bank = c % 2
                for kc in range(NCH):
                    k.mm(PB[bank][:, :],
                         SB[:, ow + kc * DM + c * 128:ow + kc * DM + (c + 1) * 128],
                         SB[:, o_h + kc * TS:o_h + (kc + 1) * TS],
                         start=(kc == 0), stop=(kc == NCH - 1),
                         waits=([k.bg(bank)] + ([("dv", v_h)] if (ti == 0 and c == 0) else []))
                         if kc == 0 else [],
                         inc=(kc == NCH - 1))
                v_p = S["pe"].cur
                if pend_sq is not None:
                    emit_sum(*pend_sq)
                    pend_sq = None
                hs = o_halo + (ti * NCH + c) * 4
                k.dve(lambda e, h=hs: e.tensor_copy(SB[:, o_lin:o_lin + 3],
                                                    SB[:, h:h + 3]),
                      waits=[("pe", v_p)])
                k.dve(lambda e, b=bank: e.tensor_copy(
                    SB[:, o_lin + 3:o_lin + 3 + TS], PB[b][:, :]))
                k.dve(lambda e, h=hs, b=bank: e.tensor_copy(
                    SB[:, h:h + 3], PB[b][:, TS - 3:TS]), inc=True)
                k.setbg(bank, "dv")
                conv_taps(o_lin, o_cw + c * 44 + cwo, 4, o_cb,
                          waits0=[k.w("ac")])
                k.dve(lambda e: e.engine_nop(), inc=True)
                v_cv = S["dv"].cur
                if tgt == "q":
                    db = c % 2
                    dst = SB[:, o_qc + db * TS:o_qc + (db + 1) * TS]
                elif tgt == "k":
                    dst = SB[:, o_kn + c * TS:o_kn + (c + 1) * TS]
                else:
                    dst = SB[:, o_v + c * (VH + TS) + VH:o_v + c * (VH + TS) + VH + TS]
                wsilu = [("dv", v_cv)]
                if tgt == "q":
                    wsilu.append(("sp", qspill[db]))
                v_silu = k.act(dst, SB[:, o_cb:o_cb + TS], AF.Silu, waits=wsilu,
                               inc=True)
                if tgt in ("q", "k"):
                    va2 = k.act(SB[:, o_sq2 + (c % 2) * TS:o_sq2 + (c % 2) * TS + TS],
                                dst, AF.Square,
                                waits=[("ac", v_silu), sq2_guard[c % 2]],
                                inc=True)
                    pend_sq = (c, va2)
                if tgt == "q":
                    qspill[db] = k.dma(
                        "scalar",
                        q_sp.rearrange("c p l -> p c l")[:, c:c + 1, tok0:tok0 + TS],
                        dst.unsqueeze(1), waits=[("ac", v_silu)], sem="sp")
            if pend_sq is not None:
                emit_sum(*pend_sq)
                pend_sq = None
            if tgt in ("q", "k"):
                v_sum = S["pe"].cur
                orr = o_rq if tgt == "q" else o_rk
                va = k.act(SB[:, orr:orr + TS], PB[6][:, :], AF.Sqrt, bias=1e-12,
                           waits=[("pe", v_sum)], inc=True)
                k.setbg(6, "ac", va)
                k.dve(lambda e, o=orr: e.reciprocal(SB[:, o:o + TS],
                                                    SB[:, o:o + TS]),
                      waits=[("ac", va)], inc=True)

        v_knc = S["dv"].cur
        for hh in range(H):
            k.mm(PB[4][:, :], SB[:, o_indr + hh * 128:o_indr + hh * 128 + 128],
                 SB[:, o_rk:o_rk + TS], start=True, stop=True,
                 waits=[k.w("dv"), k.bg(4)])
            k.mm(PB[5][:, :], SB[:, o_indr + hh * 128:o_indr + hh * 128 + 128],
                 SB[:, o_bs:o_bs + TS], start=True, stop=True,
                 waits=[k.bg(5)], inc=True)
            v_b2 = S["pe"].cur
            for c in (2 * hh, 2 * hh + 1):
                kc_ap = SB[:, o_kn + c * TS:o_kn + (c + 1) * TS]
                db = c % 2
                k.dve(lambda e, kc=kc_ap: e.tensor_tensor(kc, kc, PB[4][:, :],
                                                          op=OP.mult),
                      waits=[("pe", v_b2)])
                k.dve(lambda e, kc=kc_ap, d=db: e.tensor_tensor(
                    SB[:, o_kbc + d * TS:o_kbc + (d + 1) * TS], kc, PB[5][:, :],
                    op=OP.mult), waits=[("sp", kbspill[db])], inc=True)
                v_knc = S["dv"].cur
                kbspill[db] = k.dma(
                    "scalar",
                    kb_sp.rearrange("c p l -> p c l")[:, c:c + 1, tok0:tok0 + TS],
                    SB[:, o_kbc + db * TS:o_kbc + (db + 1) * TS].unsqueeze(1),
                    waits=[("dv", v_knc)], sem="sp")
                k.dma("scalar",
                      kn_sp.rearrange("c p l -> p c l")[:, c:c + 1, tok0:tok0 + TS],
                      kc_ap.unsqueeze(1), waits=[("dv", v_knc)], sem="sp")
            k.setbg(4, "dv", v_knc)
            k.setbg(5, "dv", v_knc)

        for blk in range(4):
            B_ = 4 * st + blk
            k.transp(PB[7][:, 0:128], SB[:, o_bs + blk * 128:o_bs + blk * 128 + 128],
                     SB[:, o_id:o_id + 128], waits=[k.w("dv", v_knc), k.bg(7)])
            k.transp(PB[7][:, 128:256], SB[:, o_rq + blk * 128:o_rq + blk * 128 + 128],
                     SB[:, o_id:o_id + 128], inc=True)
            v_tb = S["pe"].cur
            k.dve(lambda e: e.tensor_copy(SB[:, o_bT:o_bT + 4], PB[7][:, 0:128:32]),
                  waits=[("pe", v_tb), ("sp", rqt_spill)])
            k.dve(lambda e: e.tensor_copy(SB[:, o_rqTb:o_rqTb + 4],
                                          PB[7][:, 128:256:32]), inc=True)
            v_t2 = S["dv"].cur
            k.setbg(7, "dv", v_t2)
            rqt_spill = k.dma("scalar", rqT_sp[B_], SB[:, o_rqTb:o_rqTb + 4],
                              waits=[("dv", v_t2)], sem="sp")

            for half in range(2):
                bank = 2 + half
                for cp in range(4):
                    c = half * 4 + cp
                    src = SB[:, o_v + c * (VH + TS) + VH + blk * 128:
                             o_v + c * (VH + TS) + VH + blk * 128 + 128]
                    k.transp(PB[bank][:, cp * 128:(cp + 1) * 128], src,
                             SB[:, o_id:o_id + 128],
                             waits=[k.bg(bank)] if cp == 0 else [], inc=(cp == 3))
                v_tv = S["pe"].cur
                k.dve(lambda e, b=bank, hf=half: e.tensor_copy(
                    SB[:, o_tma + hf * 512:o_tma + hf * 512 + 512], PB[b][:, :]),
                    waits=[("pe", v_tv), ("sp", tma_spill)])
                for h2 in range(2):
                    hd = half * 2 + h2
                    k.dve(lambda e, b=bank, h2=h2, hd=hd: e.tensor_scalar(
                        SB[:, o_tmb + hd * 256:o_tmb + (hd + 1) * 256],
                        PB[b][:, h2 * 256:(h2 + 1) * 256],
                        SB[:, o_bT + hd:o_bT + hd + 1], None, op0=OP.mult),
                        waits=[("sp", tmb_spill)] if h2 == 0 else [],
                        inc=(h2 == 1))
                k.setbg(bank, "dv")
            v_vtm = S["dv"].cur
            k.dma("scalar", vT_sp[B_], SB[:, o_tma:o_tma + DM],
                  waits=[("dv", v_vtm)], sem="sp")
            tma_spill = S["sp"].cur
            for par in range(2):
                ck = 2 * B_ + par
                k.dma("scalar", vbT_sp[ck],
                      SB[64 * par:64 * par + 64, o_tmb:o_tmb + DM],
                      waits=[("dv", v_vtm)], sem="sp")
            tmb_spill = S["sp"].cur

            for half in range(2):
                bank = 2 + half
                for cp in range(4):
                    c = half * 4 + cp
                    src = SB[:, o_kn + c * TS + blk * 128:o_kn + c * TS + blk * 128 + 128]
                    k.transp(PB[bank][:, cp * 128:(cp + 1) * 128], src,
                             SB[:, o_id:o_id + 128],
                             waits=[k.bg(bank)] if cp == 0 else [], inc=(cp == 3))
                v_tk = S["pe"].cur
                k.dve(lambda e, b=bank, hf=half: e.tensor_copy(
                    SB[:, o_tma + hf * 512:o_tma + hf * 512 + 512], PB[b][:, :]),
                    waits=[("pe", v_tk), ("sp", tma_spill)] if half == 0
                    else [("pe", v_tk)], inc=True)
                k.setbg(bank, "dv")
            v_ktm = S["dv"].cur
            for par in range(2):
                ck = 2 * B_ + par
                k.dma("scalar", knT_sp[ck],
                      SB[64 * par:64 * par + 64, o_tma:o_tma + DM],
                      waits=[("dv", v_ktm)], sem="sp")
            tma_spill = S["sp"].cur

        for c in range(NCH):
            vb = o_v + c * (VH + TS)
            conv_taps(vb + VH - 6, o_cw + c * 44 + 12, 7, o_cb,
                      waits0=[k.w("pe"), k.w("ac")])
            k.dve(lambda e: e.engine_nop(), inc=True)
            v_loc = S["dv"].cur
            first = True
            for j in range(25):
                lo = vb + j
                wo = o_cw + c * 44 + 19 + j
                if first:
                    k.dve(lambda e, l=lo, w=wo: e.tensor_scalar(
                        SB[:, o_tmb:o_tmb + 512], SB[:, l:l + TS],
                        SB[:, w:w + 1], None, op0=OP.mult),
                        waits=[("sp", tmb_spill)])
                    first = False
                else:
                    k.dve(lambda e, l=lo, w=wo: e.scalar_tensor_tensor(
                        SB[:, o_tmb:o_tmb + 512], SB[:, l:l + TS],
                        SB[:, w:w + 1], SB[:, o_tmb:o_tmb + 512],
                        op0=OP.mult, op1=OP.add))
            k.dve(lambda e: e.engine_nop(), inc=True)
            v_mid = S["dv"].cur
            for o_dst, v_rdy, tgt_sp in ((o_cb, v_loc, lT_sp),
                                         (o_tmb, v_mid, mT_sp)):
                for blk in range(4):
                    k.transp(PB[2][:, blk * 128:(blk + 1) * 128],
                             SB[:, o_dst + blk * 128:o_dst + blk * 128 + 128],
                             SB[:, o_id:o_id + 128],
                             waits=[("dv", v_rdy), k.bg(2)] if blk == 0 else [],
                             inc=(blk == 3))
                v_tl = S["pe"].cur
                k.dve(lambda e: e.tensor_copy(SB[:, o_tma:o_tma + 512],
                                              PB[2][:, :]),
                      waits=[("pe", v_tl), ("sp", tma_spill)], inc=True)
                v_tlm = S["dv"].cur
                k.setbg(2, "dv", v_tlm)
                for blk in range(4):
                    k.dma("scalar", tgt_sp[4 * st + blk][:, c * 128:(c + 1) * 128],
                          SB[:, o_tma + blk * 128:o_tma + blk * 128 + 128],
                          waits=[("dv", v_tlm)], sem="sp")
                tma_spill = S["sp"].cur
            k.dve(lambda e, v=vb: e.tensor_copy(SB[:, v:v + VH],
                                                SB[:, v + TS:v + TS + VH]))

    k.pop()
    if "B" in phases:
        k.flush()

    o_pT = k.take(NBLK * 16)
    k.push()
    o_feats = k.take(4096)
    k.push()
    o_SS = [k.take(512) for _ in range(H)]
    o_qF = k.take(NCH * TS)
    o_kbF = k.take(NCH * TS)
    o_knF = k.take(NCH * TS)
    o_knT = [k.take(DM) for _ in range(4)]
    o_vbT = [k.take(DM) for _ in range(4)]
    o_rqT = k.take(32)
    o_vTb = k.take(DM)
    o_lTb = k.take(DM)
    o_mTb = k.take(DM)
    o_dta = k.take(2 * DM)
    PCLS = ("P1", "Q1", "Q2", "P2", "Q4", "P4", "Q8", "P8", "Q16",
            "P16", "Q32", "P12", "P32", "P48", "C1", "C2", "C3",
            "CP3", "CP3I", "C2I", "M2", "N2", "TMP")
    o_cls = {n: k.take(512) for n in PCLS}
    o_Tx = k.take(1024)
    o_Ys = k.take(DM)
    o_Us = k.take(DM)
    o_att = k.take(256)
    o_pr = [k.take(DM) for _ in range(2)]
    o_fr = k.take(64)
    o_ft = k.take(128)
    o_sc1 = k.take(16)
    o_sc2 = k.take(16)
    o_msk4 = k.take(768)
    o_mI = k.take(512)
    k.dma("sync", SB[:, o_msk4:o_msk4 + 768], masks_d[:, 256:1024], sem="wld")
    k.dma("sync", SB[:, o_mI:o_mI + 512], masks_d[:, 1024:1536], sem="wld")

    for off, n in ([(o, 512) for o in o_SS] +
                   [(o, DM) for o in o_knT] +
                   [(o_Ys, DM), (o_Us, DM), (o_Tx, 1024),
                    (o_att, 256), (o_feats, 4096)] +
                   [(o_cls[n], 512) for n in o_cls]):
        k.dve(lambda e, o=off, m=n: e.memset(SB[:, o:o + m], 0.0))
    k.dve(lambda e: e.engine_nop(), inc=True)

    M_SL4, M_SU4, M_IU4 = o_msk4, o_msk4 + 256, o_msk4 + 512

    def pow_ap(n, par, hd=None, rows=128):
        o = o_cls[n] + 256 * par
        if hd is None:
            return SB[:rows, o:o + 256]
        return SB[:rows, o + 64 * hd:o + 64 * hd + 64]

    def cls_full(n, rows=128):
        return SB[:rows, o_cls[n]:o_cls[n] + 512]

    def texp_ap(gpar, par, hd):
        o = o_Tx + 512 * gpar + 256 * par + 64 * hd
        return SB[:, o:o + 64]

    knt_guard = [k.w("dv", 0)] * 4
    vbt_guard = [k.w("dv", 0)] * 4
    dta_spill = [0, 0]
    texp_guard = [("pe", 0), ("pe", 0)]
    st_c = {"v_S_add": S["dv"].cur, "v_tmld": [0, 0],
            "v_fmld": 0, "scan_qf_pe": 0, "pr_guard": [0, 0],
            "ft_guard": [0, 0], "v_Mmms": 0, "v_Tx": [0, 0]}

    NG = NCK // 2

    def qsl(base, ck, hd, kc2):
        c = 2 * hd + kc2
        t0 = (ck % 8) * CC
        return SB[:, base + c * TS + t0:base + c * TS + t0 + CC]

    def emit_T0(g, boundary):
        if boundary:
            w0 = [k.w("pe"), k.w("dv"), k.w("gp")]
            for dst, src in ((o_qF, q_sp), (o_kbF, kb_sp), (o_knF, kn_sp)):
                k.dma("sync", SB[:, dst:dst + NCH * TS].rearrange(
                    "p (c t) -> p c t", c=NCH),
                    src.rearrange("c p l -> p c l")[:, :, (g // 4) * TS:
                                                    (g // 4) * TS + TS],
                    waits=w0)
                w0 = []
            stl = g // 4
            k.dma("sync",
                  SB[:, o_rqT + 16 * (stl % 2):o_rqT + 16 * (stl % 2) + 16]
                  .rearrange("p (b hh) -> p b hh", b=4),
                  rqT_sp.rearrange("b p hh -> p b hh")[:, 4 * stl:4 * stl + 4, :])
        for par in range(2):
            ck = 2 * g + par
            k.dma("sync", SB[0:64, o_knT[ck % 4]:o_knT[ck % 4] + DM], knT_sp[ck],
                  waits=[knt_guard[ck % 4]])
            k.dma("sync", SB[0:64, o_vbT[ck % 4]:o_vbT[ck % 4] + DM], vbT_sp[ck],
                  waits=[vbt_guard[ck % 4]])
        st_c["v_tmld"][g % 2] = S["ld"].cur
        if boundary:
            st_c["v_fmld"] = S["ld"].cur

    def emit_T1(g):
        v_ld = st_c["v_fmld"]
        for par in range(2):
            ck = 2 * g + par
            bk = 4 + par
            for hd in range(H):
                for kc2 in range(2):
                    k.mm(PB[bk][0:64, 64 * hd:64 * hd + 64],
                         qsl(o_kbF, ck, hd, kc2), qsl(o_knF, ck, hd, kc2),
                         start=(kc2 == 0), stop=(kc2 == 1),
                         waits=[("ld", v_ld), k.bg(bk)] if hd == 0 and kc2 == 0
                         else [], inc=(kc2 == 1))
            for hd in range(H):
                for kc2 in range(2):
                    k.mm(PB[bk][0:64, 256 + 64 * hd:256 + 64 * hd + 64],
                         qsl(o_knF, ck, hd, kc2), qsl(o_kbF, ck, hd, kc2),
                         start=(kc2 == 0), stop=(kc2 == 1), inc=(kc2 == 1))
        st_c["v_gr"] = S["pe"].cur

    def emit_T2(g):
        v_gr = st_c["v_gr"]
        for par in range(2):
            bk = 4 + par
            k.dve(lambda e, b=bk, p_=par: e.scalar_tensor_tensor(
                pow_ap("P1", p_, rows=64), PB[b][0:64, 0:256], -1.0,
                SB[0:64, M_SL4:M_SL4 + 256], op0=OP.mult, op1=OP.mult),
                waits=[("pe", v_gr), k.w("wld")] if par == 0 else [])
            k.dve(lambda e, b=bk, p_=par: e.scalar_tensor_tensor(
                pow_ap("Q1", p_, rows=64), PB[b][0:64, 256:512], -1.0,
                SB[0:64, M_SU4:M_SU4 + 256], op0=OP.mult, op1=OP.mult),
                inc=(par == 1))
            k.setbg(bk, "dv")
        st_c["v_pq1"] = S["dv"].cur

    TROUNDS = [
        [("Q2", "P1", "Q1", 4), ("P2", "Q1", "P1", 5)],
        [("C1", "P1", "Q2", 6), ("Q4", "P2", "Q2", 4), ("P4", "Q2", "P2", 5)],
        [("Q8", "P4", "Q4", 4), ("P8", "Q4", "P4", 5)],
        [("C2", "P4", "Q8", 6), ("Q16", "P8", "Q8", 4), ("P16", "Q8", "P8", 5)],
        [("Q32", "P16", "Q16", 6), ("P12", "Q4", "P8", 4),
         ("P32", "Q16", "P16", 5)],
        [("C3", "P16", "Q32", 4), ("P48", "Q16", "P32", 5)],
    ]

    def emit_Tround(g, ri):
        v_prev = S["dv"].cur
        for dst, lh, rh, bk in TROUNDS[ri]:
            for par in range(2):
                for hd in range(H):
                    k.mm(PB[bk][0:64, 256 * par + 64 * hd:256 * par + 64 * hd + 64],
                         pow_ap(lh, par, hd), pow_ap(rh, par, hd),
                         start=(par == 0 and hd == 0),
                         stop=(par == 1 and hd == H - 1),
                         waits=[("dv", v_prev), k.bg(bk)] if par == 0 and hd == 0
                         else [], inc=(par == 1 and hd == H - 1))

    def emit_Tround_evac(g, ri):
        for dst, lh, rh, bk in TROUNDS[ri]:
            if dst.startswith("C"):
                a, b_ = {"C1": ("Q1", "Q2"), "C2": ("Q4", "Q8"),
                         "C3": ("Q16", "Q32")}[dst]
                k.dve(lambda e, d=dst, x=a, y=b_: e.tensor_tensor(
                    cls_full(d), cls_full(x), cls_full(y), op=OP.add),
                    waits=[k.w("pe")])
                k.dve(lambda e, d=dst, b=bk: e.tensor_tensor(
                    cls_full(d, rows=64), cls_full(d, rows=64),
                    PB[b][0:64, :], op=OP.add), inc=True)
            else:
                k.dve(lambda e, d=dst, b=bk: e.tensor_copy(
                    SB[0:64, o_cls[d]:o_cls[d] + 512], PB[b][0:64, :]),
                    waits=[k.w("pe")], inc=True)
            k.setbg(bk, "dv")

    def emit_Mprep(g):
        k.dve(lambda e: e.tensor_tensor(
            cls_full("TMP", rows=64), cls_full("P16", rows=64),
            cls_full("P32", rows=64), op=OP.add),
            waits=[("pe", st_c["v_Mmms"])])
        k.dve(lambda e: e.tensor_tensor(
            cls_full("CP3", rows=64), cls_full("TMP", rows=64),
            cls_full("P48", rows=64), op=OP.add))
        k.dve(lambda e: e.tensor_tensor(
            cls_full("CP3I", rows=64), cls_full("CP3", rows=64),
            SB[0:64, o_mI:o_mI + 512], op=OP.add))
        k.dve(lambda e: e.tensor_tensor(
            cls_full("C2I", rows=64), cls_full("C2", rows=64),
            SB[0:64, o_mI:o_mI + 512], op=OP.add))
        k.dve(lambda e: e.tensor_tensor(
            cls_full("TMP", rows=64), cls_full("P4", rows=64),
            cls_full("P8", rows=64), op=OP.add))
        k.dve(lambda e: e.tensor_tensor(
            cls_full("TMP", rows=64), cls_full("TMP", rows=64),
            cls_full("P12", rows=64), op=OP.add))
        k.dve(lambda e: e.tensor_tensor(
            cls_full("TMP", rows=64), cls_full("TMP", rows=64),
            SB[0:64, o_mI:o_mI + 512], op=OP.add), inc=True)
        st_c["v_Mprep"] = S["dv"].cur

    def emit_M1(g):
        v_p = st_c["v_Mprep"]
        for lh, rh, bk in (("CP3I", "C2", 4), ("C2I", "CP3", 5)):
            for par in range(2):
                for hd in range(H):
                    k.mm(PB[bk][0:64, 256 * par + 64 * hd:256 * par + 64 * hd + 64],
                         pow_ap(lh, par, hd), pow_ap(rh, par, hd),
                         start=(par == 0 and hd == 0),
                         stop=(par == 1 and hd == H - 1),
                         waits=[("dv", v_p), k.bg(bk)] if par == 0 and hd == 0
                         else [], inc=(par == 1 and hd == H - 1))
        st_c["v_M1"] = S["pe"].cur

    def emit_M1_evac(g):
        v_m1 = st_c["v_M1"]
        for dst, addend, bk in (("M2", "C3", 4), ("N2", "TMP", 5)):
            k.dve(lambda e, d=dst, a=addend, b=bk: e.tensor_tensor(
                cls_full(d, rows=64), cls_full(a, rows=64),
                PB[b][0:64, :], op=OP.add),
                waits=[("pe", v_m1)] if dst == "M2" else [], inc=True)
            k.setbg(bk, "dv")
        k.dve(lambda e: e.tensor_tensor(
            cls_full("M2", rows=64), cls_full("M2", rows=64),
            SB[0:64, o_mI:o_mI + 512], op=OP.add), inc=True)
        st_c["v_M1e"] = S["dv"].cur

    def emit_M2(g):
        v_e = st_c["v_M1e"]
        for par in range(2):
            for hd in range(H):
                k.mm(PB[6][0:64, 256 * par + 64 * hd:256 * par + 64 * hd + 64],
                     pow_ap("N2", par, hd), pow_ap("C1", par, hd),
                     start=(par == 0 and hd == 0),
                     stop=(par == 1 and hd == H - 1),
                     waits=[("dv", v_e), k.bg(6)] if par == 0 and hd == 0
                     else [], inc=(par == 1 and hd == H - 1))
        st_c["v_Mmms"] = S["pe"].cur

    def emit_M2_evac(g):
        gpar = g % 2
        k.dve(lambda e, gp2=gpar: e.tensor_tensor(
            SB[0:64, o_Tx + 512 * gp2:o_Tx + 512 * gp2 + 512],
            cls_full("M2", rows=64), PB[6][0:64, :], op=OP.add),
            waits=[("pe", st_c["v_Mmms"]), texp_guard[gpar]], inc=True)
        k.setbg(6, "dv")
        st_c["v_Tx"][gpar] = S["dv"].cur

    def emit_S_a(ck):
        g = ck // 2
        par = ck % 2
        for hd in range(H):
            bk = hd // 2
            col = 256 * (hd % 2)
            for kc2 in range(2):
                k.mm(PB[bk][0:64, col:col + 256],
                     qsl(o_kbF, ck, hd, kc2),
                     SB[:, o_SS[hd] + 256 * kc2:o_SS[hd] + 256 * kc2 + 256],
                     start=(kc2 == 0), stop=(kc2 == 1),
                     waits=([("dv", st_c["v_S_add"]), k.bg(bk)]
                            if (hd % 2 == 0 and kc2 == 0) else []),
                     inc=(kc2 == 1))
        v_ymm = S["pe"].cur
        for sg in range(2):
            k.dve(lambda e, s_=sg, sl=(ck % 4): e.tensor_tensor(
                SB[0:64, o_Ys + 512 * s_:o_Ys + 512 * s_ + 512],
                SB[0:64, o_vbT[sl] + 512 * s_:o_vbT[sl] + 512 * s_ + 512],
                PB[s_][0:64, 0:512], op=OP.subtract),
                waits=[("pe", v_ymm), ("ld", st_c["v_tmld"][g % 2])]
                if sg == 0 else [], inc=(sg == 1))
        st_c["v_ys"] = S["dv"].cur
        k.setbg(0, "dv", st_c["v_ys"])
        k.setbg(1, "dv", st_c["v_ys"])

    def emit_S_u(ck):
        g = ck // 2
        gpar = g % 2
        par = ck % 2
        v_src = st_c["v_ys"]
        wT = [("dv", st_c["v_Tx"][gpar])]
        for hd in range(H):
            bk = hd // 2
            col = 256 * (hd % 2)
            k.mm(PB[bk][0:64, col:col + 256],
                 texp_ap(gpar, par, hd),
                 SB[:, o_Ys + 256 * hd:o_Ys + 256 * hd + 256],
                 start=True, stop=True,
                 waits=([("dv", v_src), k.bg(bk)] + wT) if hd % 2 == 0
                 else [], inc=(hd % 2 == 1))
        v_app = S["pe"].cur
        if par == 1:
            texp_guard[gpar] = ("pe", v_app)
        for sg in range(2):
            k.dve(lambda e, s_=sg: e.tensor_copy(
                SB[0:64, o_Us + 512 * s_:o_Us + 512 * s_ + 512],
                PB[s_][0:64, 0:512]),
                waits=[("pe", v_app)] if sg == 0 else [], inc=(sg == 1))
        k.setbg(0, "dv")
        k.setbg(1, "dv")
        st_c["v_U"] = S["dv"].cur

    def emit_S_attn_o(ck):
        g = ck // 2
        par = ck % 2
        stl = ck // 8
        blkrel = (ck % 8) // 2
        sdta = g % 2
        for hd in range(H):
            for kc2 in range(2):
                k.mm(PB[7][0:64, 64 * hd:64 * hd + 64],
                     qsl(o_knF, ck, hd, kc2), qsl(o_qF, ck, hd, kc2),
                     start=(kc2 == 0), stop=(kc2 == 1),
                     waits=[k.bg(7)] if hd == 0 and kc2 == 0 else [],
                     inc=(kc2 == 1))
        v_ag = S["pe"].cur
        k.dve(lambda e: e.tensor_tensor(
            SB[0:64, o_att:o_att + 256], PB[7][0:64, 0:256],
            SB[0:64, M_IU4:M_IU4 + 256], op=OP.mult),
            waits=[("pe", v_ag)], inc=True)
        k.setbg(7, "dv")
        v_at = S["dv"].cur
        for hd in range(H):
            bk = 2 + hd // 2
            col = 256 * (hd % 2)
            for kc2 in range(2):
                k.mm(PB[bk][64 * par:64 * par + 64, col:col + 256],
                     qsl(o_qF, ck, hd, kc2),
                     SB[:, o_SS[hd] + 256 * kc2:o_SS[hd] + 256 * kc2 + 256],
                     start=(kc2 == 0), stop=False, tp=(0, 64 * par),
                     waits=([("dv", v_at), k.bg(bk)] if hd % 2 == 0 and kc2 == 0
                            else []))
            k.mm(PB[bk][64 * par:64 * par + 64, col:col + 256],
                 SB[:, o_att + 64 * hd:o_att + 64 * hd + 64],
                 SB[:, o_Us + 256 * hd:o_Us + 256 * hd + 256],
                 start=False, stop=True, tp=(0, 64 * par),
                 inc=(hd % 2 == 1))
        st_c["scan_qf_pe"] = S["pe"].cur
        v_omm = S["pe"].cur
        for hd in range(H):
            bk = 2 + hd // 2
            col = 256 * (hd % 2)
            k.dve(lambda e, b=bk, c=col, h=hd, p_=par, sd=sdta, br=blkrel,
                  sp2=(stl % 2): e.tensor_scalar(
                      SB[64 * p_:64 * p_ + 64,
                         o_dta + sd * DM + 256 * h:o_dta + sd * DM + 256 * h + 256],
                      PB[b][64 * p_:64 * p_ + 64, c:c + 256],
                      SB[64 * p_:64 * p_ + 64,
                         o_rqT + 16 * sp2 + 4 * br + h:
                         o_rqT + 16 * sp2 + 4 * br + h + 1],
                      None, op0=OP.mult),
                  waits=([("pe", v_omm), ("sp", dta_spill[sdta])] if hd == 0
                         else []),
                  inc=(hd == H - 1))
        st_c["v_oev"] = S["dv"].cur
        k.setbg(2, "dv", st_c["v_oev"])
        k.setbg(3, "dv", st_c["v_oev"])

    def emit_S_dS(ck):
        par = ck % 2
        for pair in range(2):
            for hh2 in range(2):
                hd = 2 * pair + hh2
                bk = 2 + hh2
                for kc2 in range(2):
                    k.mm(PB[bk][:, 256 * kc2:256 * kc2 + 256],
                         SB[:, o_knT[ck % 4] + 256 * hd + 128 * kc2:
                            o_knT[ck % 4] + 256 * hd + 128 * kc2 + 128],
                         SB[:, o_Us + 256 * hd:o_Us + 256 * hd + 256],
                         start=(kc2 == 0), stop=(kc2 == 1),
                         waits=[k.bg(bk)] if kc2 == 0 else [],
                         inc=(kc2 == 1))
            v_ds = S["pe"].cur
            for hh2 in range(2):
                hd = 2 * pair + hh2
                bk = 2 + hh2
                k.dve(lambda e, b=bk, h=hd: e.tensor_tensor(
                    SB[:, o_SS[h]:o_SS[h] + 512], SB[:, o_SS[h]:o_SS[h] + 512],
                    PB[b][:, :], op=OP.add),
                    waits=[("pe", v_ds)] if hh2 == 0 else [], inc=True)
                k.setbg(bk, "dv")
        st_c["v_S_add"] = S["dv"].cur
        knt_guard[ck % 4] = k.w("pe", S["pe"].cur)
        vbt_guard[ck % 4] = k.w("dv", st_c["v_ys"])

    pend_feats = []

    def emit_feats_sums(g):
        B_ = g
        sdta = g % 2
        dta_o = o_dta + sdta * DM
        for dst, src in ((o_vTb, vT_sp), (o_lTb, lT_sp), (o_mTb, mT_sp)):
            k.dma("sync", SB[:, dst:dst + DM], src[B_],
                  waits=[k.w("dv"), k.w("gp")])
        v_tld = S["ld"].cur
        tens = {"l": o_lTb, "m": o_mTb, "d": dta_o, "v": o_vTb}
        sums = [("l", None, 0), ("l", "l", 1), ("m", None, 2), ("m", "m", 3),
                ("d", None, 4), ("d", "d", 5), ("v", None, 6), ("v", "v", 7),
                ("l", "m", 8), ("l", "d", 9), ("l", "v", 10),
                ("m", "d", 11), ("m", "v", 12), ("d", "v", 13)]
        v_need = st_c["v_oev"]
        pi = 0
        for a, b_, idx in sums:
            if b_ is None:
                k.dve(lambda e, ta=tens[a], ix=idx: e.tensor_reduce(
                    SB[:, o_fr + 4 * ix:o_fr + 4 * ix + 4],
                    SB[:, ta:ta + DM].rearrange("p (hh dd) -> p hh dd", hh=4),
                    axis=mybir.AxisListType.X, op=OP.add),
                    waits=[("dv", v_need), ("ld", v_tld)] if idx == 0 else [])
            else:
                pb_ = o_pr[pi % 2]
                wgp = [("dv", v_need), ("ld", v_tld),
                       ("dv", st_c["pr_guard"][pi % 2])]
                k.gps(lambda e, ta=tens[a], tb=tens[b_], po=pb_: e.tensor_mul(
                    SB[:, po:po + DM], SB[:, ta:ta + DM], SB[:, tb:tb + DM]),
                    waits=wgp, inc=True)
                vg = S["gp"].cur
                k.dve(lambda e, po=pb_, ix=idx: e.tensor_reduce(
                    SB[:, o_fr + 4 * ix:o_fr + 4 * ix + 4],
                    SB[:, po:po + DM].rearrange("p (hh dd) -> p hh dd", hh=4),
                    axis=mybir.AxisListType.X, op=OP.add),
                    waits=[("gp", vg)], inc=True)
                st_c["pr_guard"][pi % 2] = S["dv"].cur
                pi += 1
        k.dve(lambda e: e.engine_nop(), inc=True)
        v_sums = S["dv"].cur
        fto = o_ft + 64 * (B_ % 2)
        for si, qi in ((0, 1), (2, 3), (4, 5), (6, 7)):
            k.dve(lambda e, s=si, f=fto: e.tensor_scalar(
                SB[:, f + 4 * s:f + 4 * s + 4],
                SB[:, o_fr + 4 * s:o_fr + 4 * s + 4], 1.0 / DH, None,
                op0=OP.mult),
                waits=[("dv", v_sums), ("pe", st_c["ft_guard"][B_ % 2])],
                inc=True)
            k.dve(lambda e, s=si: e.tensor_tensor(
                SB[:, o_sc1:o_sc1 + 4], SB[:, o_fr + 4 * s:o_fr + 4 * s + 4],
                SB[:, o_fr + 4 * s:o_fr + 4 * s + 4], op=OP.mult), inc=True)
            k.dve(lambda e: e.tensor_scalar(
                SB[:, o_sc2:o_sc2 + 4], SB[:, o_sc1:o_sc1 + 4],
                1.0 / (255.0 * DH), None, op0=OP.mult),
                waits=[k.w("dv")], inc=True)
            k.dve(lambda e, q=qi, f=fto: e.scalar_tensor_tensor(
                SB[:, f + 4 * q:f + 4 * q + 4],
                SB[:, o_fr + 4 * q:o_fr + 4 * q + 4], 1.0 / 255.0,
                SB[:, o_sc2:o_sc2 + 4], op0=OP.mult, op1=OP.subtract),
                waits=[k.w("dv")], inc=True)
        for ix in range(8, 14):
            k.dve(lambda e, x=ix, f=fto: e.tensor_scalar(
                SB[:, f + 4 * x:f + 4 * x + 4],
                SB[:, o_fr + 4 * x:o_fr + 4 * x + 4], 1.0 / DH, None,
                op0=OP.mult), inc=(ix == 13))
        v_ft = S["dv"].cur
        dta_spill[sdta] = k.dma("scalar", dT_sp[B_], SB[:, dta_o:dta_o + DM],
                                waits=[("dv", st_c["v_oev"])], sem="sp")
        pend_feats.append((B_, v_ft))

    def emit_feats_fin():
        while pend_feats:
            B_, v_ft = pend_feats.pop(0)
            k.transp(PB[7][0:56, 0:128],
                     SB[:, o_ft + 64 * (B_ % 2):o_ft + 64 * (B_ % 2) + 56],
                     SB[:, o_id:o_id + 128], waits=[("dv", v_ft), k.bg(7)],
                     inc=True)
            v_ftr = S["pe"].cur
            st_c["ft_guard"][B_ % 2] = v_ftr
            k.dve(lambda e, b_=B_: e.tensor_copy(
                SB[0:56, o_feats + 128 * b_:o_feats + 128 * b_ + 128],
                PB[7][0:56, 0:128]), waits=[("pe", v_ftr)], inc=True)
            k.setbg(7, "dv")

    for g in range(NG + 1):
        boundary = (g % 4 == 0)
        if g < NG and boundary:
            if g > 0:
                for par in range(2):
                    ck = 2 * (g - 1) + par
                    emit_S_a(ck)
                    emit_S_u(ck)
                    emit_S_attn_o(ck)
                    emit_S_dS(ck)
                    if par == 1:
                        emit_feats_sums(g - 1)
            emit_feats_fin()
            emit_T0(g, True)
            emit_T1(g)
            emit_T2(g)
            for ri in range(6):
                emit_Tround(g, ri)
                emit_Tround_evac(g, ri)
            emit_Mprep(g)
            emit_M1(g)
            emit_M1_evac(g)
            emit_M2(g)
            emit_M2_evac(g)
        elif g < NG:
            cka, ckb = 2 * (g - 1), 2 * (g - 1) + 1
            emit_T0(g, False)
            emit_S_a(cka)
            emit_T1(g)
            emit_S_u(cka)
            emit_T2(g)
            emit_Tround(g, 0)
            emit_S_attn_o(cka)
            emit_Tround_evac(g, 0)
            emit_Tround(g, 1)
            emit_S_dS(cka)
            emit_Tround_evac(g, 1)
            emit_Tround(g, 2)
            emit_S_a(ckb)
            emit_Tround_evac(g, 2)
            emit_Tround(g, 3)
            emit_S_u(ckb)
            emit_Tround_evac(g, 3)
            emit_Tround(g, 4)
            emit_S_attn_o(ckb)
            emit_Tround_evac(g, 4)
            emit_Tround(g, 5)
            emit_S_dS(ckb)
            emit_Tround_evac(g, 5)
            emit_Mprep(g)
            emit_M1(g)
            emit_feats_fin()
            emit_M1_evac(g)
            emit_M2(g)
            emit_M2_evac(g)
            emit_feats_sums(g - 1)
        else:
            for par in range(2):
                ck = 2 * (g - 1) + par
                emit_S_a(ck)
                emit_S_u(ck)
                emit_S_attn_o(ck)
                emit_S_dS(ck)
                if par == 1:
                    emit_feats_sums(g - 1)
            emit_feats_fin()

    if "feats_dbg" in k.dbg:
        fdbg = k.dram("feats_dbg", [56, L], "ExternalOutput")
        k.dma("sync", fdbg, SB[0:56, o_feats:o_feats + L],
              waits=[k.w("dv"), k.w("pe")], sem="sp")
    k.pop()
    if "C" in phases:
        k.flush()

    k.push()
    o_rw1 = k.take(9 * 2176)
    o_rw2 = k.take(17 * 16)
    o_rb1 = k.take(32)
    o_rb2 = k.take(16)
    o_hD = k.take(NCH * TS)
    o_h1 = k.take(2 * TS)
    o_lg = k.take(TS)
    o_lgT = k.take(16)
    o_sm = k.take(64)

    k.dma("sync", SB[:, o_rw1:o_rw1 + 8 * 2176].rearrange(
        "p (c m) -> p c m", c=8),
        rw1_d[0:1024].rearrange("(c p) m -> p c m", p=128), sem="wld",
        waits=[k.w("pe"), k.w("dv")])
    k.dma("sync", SB[0:64, o_rw1 + 8 * 2176:o_rw1 + 9 * 2176],
          rw1_d[1024:1088], sem="wld")
    k.dma("sync", SB[:, o_rw2:o_rw2 + 17 * 16].rearrange(
        "p (c n) -> p c n", c=17),
        rw2_d.rearrange("(c p) n -> p c n", p=128), sem="wld")
    k.dma("sync", SB[:, o_rb1:o_rb1 + 17],
          rb1_d.rearrange("(c p) -> p c", p=128), sem="wld", slow=True)
    k.dma("sync", SB[0:16, o_rb2:o_rb2 + 1], rb2_d.rearrange("(n o) -> n o", o=1),
          sem="wld")
    v_wD = S["wld"].cur

    for st in range(NST):
        tok0 = st * TS
        k.dma("sync", SB[:, o_hD:o_hD + NCH * TS].rearrange(
            "p (c t) -> p c t", c=NCH),
            h_sp.rearrange("c p l -> p c l")[:, :, tok0:tok0 + TS],
            waits=[k.w("pe"), k.w("ac"), k.w("dv")])
        v_hld = S["ld"].cur
        lg_hist = []
        for m in range(17):
            bank = m % 2
            for kc in range(9):
                if kc < 8:
                    lh = SB[:, o_rw1 + kc * 2176 + 128 * m:
                            o_rw1 + kc * 2176 + 128 * m + 128]
                    rh = SB[:, o_hD + kc * TS:o_hD + (kc + 1) * TS]
                else:
                    lh = SB[0:64, o_rw1 + 8 * 2176 + 128 * m:
                            o_rw1 + 8 * 2176 + 128 * m + 128]
                    rh = SB[0:64, o_feats + tok0:o_feats + tok0 + TS]
                k.mm(PB[bank][:, :], lh, rh, start=(kc == 0), stop=(kc == 8),
                     waits=([("ld", v_hld), ("wld", v_wD), k.bg(bank)]
                            if kc == 0 else []),
                     inc=(kc == 8))
            v_pm = S["pe"].cur
            db = m % 2
            va = k.act(SB[:, o_h1 + db * TS:o_h1 + (db + 1) * TS], PB[bank][:, :],
                       AF.Silu, bias=SB[:, o_rb1 + m:o_rb1 + m + 1],
                       waits=[("pe", v_pm)] +
                       ([("pe", lg_hist[m - 2])] if m >= 2 else []),
                       inc=True)
            k.setbg(bank, "ac", va)
            k.mm(PB[2][0:16, :], SB[:, o_rw2 + m * 16:o_rw2 + m * 16 + 16],
                 SB[:, o_h1 + db * TS:o_h1 + (db + 1) * TS],
                 start=(m == 0), stop=(m == 16),
                 waits=[("ac", va)] + ([k.bg(2)] if m == 0 else []),
                 inc=True)
            lg_hist.append(S["pe"].cur)
        v_lgm = S["pe"].cur
        va = k.act(SB[0:16, o_lg:o_lg + TS], PB[2][0:16, :], AF.Identity,
                   bias=SB[0:16, o_rb2:o_rb2 + 1], waits=[("pe", v_lgm)],
                   inc=True)
        k.setbg(2, "ac", va)
        for blk in range(4):
            B_ = 4 * st + blk
            k.transp(PB[3][0:128, 0:16], SB[0:16, o_lg + blk * 128:o_lg + blk * 128 + 128],
                     SB[0:16, o_id:o_id + 16], waits=[("ac", va), k.bg(3)], inc=True)
            v_t = S["pe"].cur
            lgT = SB[:, o_lgT:o_lgT + 16]
            k.dve(lambda e: e.tensor_copy(lgT, PB[3][:, 0:16]),
                  waits=[("pe", v_t)], inc=True)
            k.setbg(3, "dv")
            mx = SB[:, o_sm:o_sm + 4]
            ex = SB[:, o_sm + 16:o_sm + 32]
            s4 = SB[:, o_sm + 32:o_sm + 36]
            r4 = SB[:, o_sm + 48:o_sm + 52]
            k.dve(lambda e: e.tensor_tensor(mx, lgT[:, 0:4], lgT[:, 4:8],
                                            op=OP.max), waits=[k.w("dv")], inc=True)
            k.dve(lambda e: e.tensor_tensor(s4, lgT[:, 8:12], lgT[:, 12:16],
                                            op=OP.max), inc=True)
            k.dve(lambda e: e.tensor_tensor(mx, mx, s4, op=OP.max),
                  waits=[k.w("dv")], inc=True)
            for i in range(4):
                k.dve(lambda e, i=i: e.tensor_tensor(
                    ex[:, 4 * i:4 * i + 4], lgT[:, 4 * i:4 * i + 4], mx,
                    op=OP.subtract), waits=[k.w("dv")], inc=(i == 3))
            va2 = k.act(ex, ex, AF.Exp, waits=[k.w("dv")], inc=True)
            k.dve(lambda e: e.tensor_tensor(s4, ex[:, 0:4], ex[:, 4:8],
                                            op=OP.add), waits=[("ac", va2)],
                  inc=True)
            k.dve(lambda e: e.tensor_tensor(r4, ex[:, 8:12], ex[:, 12:16],
                                            op=OP.add), inc=True)
            k.dve(lambda e: e.tensor_tensor(s4, s4, r4, op=OP.add),
                  waits=[k.w("dv")], inc=True)
            k.dve(lambda e: e.reciprocal(r4, s4), waits=[k.w("dv")], inc=True)
            pdst = SB[:, o_pT + 16 * B_:o_pT + 16 * B_ + 16]
            for i in range(4):
                k.dve(lambda e, i=i, p_=pdst: e.tensor_tensor(
                    p_[:, 4 * i:4 * i + 4], ex[:, 4 * i:4 * i + 4], r4,
                    op=OP.mult), waits=[k.w("dv")], inc=(i == 3))
            k.dve(lambda e, p_=pdst: e.tensor_scalar(
                p_, p_, 1.0 - 4 * 0.01, 0.01, op0=OP.mult, op1=OP.add),
                waits=[k.w("dv")], inc=True)
    k.pop()
    k.pop()
    if "D" in phases:
        k.flush()

    k.push()
    o_A = k.take(NCH * DM)
    o_B = k.take(NCH * DM)
    o_C = k.take(NCH * DM)
    o_mixw = k.take(DM)
    o_onr = k.take(DM)
    o_xE = k.take(NCH * TS)
    o_onF = k.take(NCH * TS)
    o_x1 = k.take(NCH * TS)
    o_t4 = [k.take(DM) for _ in range(4)]
    o_mx = k.take(DM)
    o_mw2 = k.take(DM)
    o_t3 = k.take(DM)
    o_s4 = k.take(64)
    o_fb = k.take(16)
    o_fch = k.take(TS)
    o_och = k.take(2 * TS)
    o_mrE = k.take(2 * TS)
    mrow = SB[0:1, o_mrE:o_mrE + TS]
    rrow = SB[0:1, o_mrE + TS:o_mrE + 2 * TS]

    k.dma("sync", SB[:, o_A:o_A + NCH * DM].rearrange("p (c m) -> p c m", c=NCH),
          ffn_wT_d.rearrange("(c p) m -> p c m", p=128), sem="wld",
          waits=[k.w("pe"), k.w("dv"), k.w("ac")])
    k.dma("sync", SB[:, o_B:o_B + NCH * DM].rearrange("p (c m) -> p c m", c=NCH),
          ffn_a1_d.rearrange("(c p) m -> p c m", p=128), sem="wld")
    k.dma("sync", SB[:, o_mixw:o_mixw + DM], mixw_d, sem="wld")
    k.dma("sync", SB[:, o_onr:o_onr + DM], onormr_d, sem="wld")
    k.dma("sync", SB[:, o_fb:o_fb + 8].rearrange("p c -> p c"),
          ffn_b_d.rearrange("(c p) -> p c", p=128), sem="wld", slow=True)
    v_wE = S["wld"].cur

    def mm1024(dst_off, lhs_off, rhs_off, wv, lhT_colmaj=True):
        for m in range(NCH):
            bank = m % 2
            for nn in range(2):
                for kc in range(NCH):
                    k.mm(PB[bank][:, :],
                         SB[:, lhs_off + kc * DM + 128 * m:
                            lhs_off + kc * DM + 128 * m + 128],
                         SB[:, rhs_off + kc * DM + 512 * nn:
                            rhs_off + kc * DM + 512 * nn + 512],
                         start=(kc == 0), stop=(kc == NCH - 1),
                         waits=([("wld", wv), k.bg(bank)] if kc == 0 else []),
                         inc=(kc == NCH - 1))
                v_ = S["pe"].cur
                k.dve(lambda e, b=bank, mm_=m, n_=nn, d=dst_off: e.tensor_copy(
                    SB[:, d + mm_ * DM + 512 * n_:d + mm_ * DM + 512 * n_ + 512],
                    PB[b][:, :]), waits=[("pe", v_)], inc=True)
                k.setbg(bank, "dv")

    mm1024(o_C, o_A, o_B, v_wE)
    v_T1 = S["dv"].cur
    for cc in range(NCH):
        for m in range(NCH):
            bank = 2 + (m % 2)
            k.transp(PB[bank][:, 0:128],
                     SB[:, o_C + cc * DM + 128 * m:o_C + cc * DM + 128 * m + 128],
                     SB[:, o_id:o_id + 128],
                     waits=[("dv", v_T1), k.bg(bank)] if cc == 0 and m < 2 else
                     [k.bg(bank)], inc=True)
            v_t = S["pe"].cur
            k.dve(lambda e, b=bank, c_=cc, m_=m: e.tensor_copy(
                SB[:, o_A + m_ * DM + 128 * c_:o_A + m_ * DM + 128 * c_ + 128],
                PB[b][:, 0:128]), waits=[("pe", v_t)], inc=True)
            k.setbg(bank, "dv")
    v_T1T = S["dv"].cur
    k.dma("sync", SB[:, o_B:o_B + NCH * DM].rearrange("p (c m) -> p c m", c=NCH),
          ffn_a2_d.rearrange("(c p) m -> p c m", p=128), sem="wld",
          waits=[("dv", v_T1T)])
    v_wE2 = S["wld"].cur
    mm1024(o_C, o_A, o_B, v_wE2)
    k.dma("sync", SB[:, o_A:o_A + NCH * DM].rearrange("p (c m) -> p c m", c=NCH),
          wo_d.rearrange("(c p) m -> p c m", p=128), sem="wld",
          waits=[k.w("dv")])
    v_wo = S["wld"].cur
    och_spill = [0, 0]

    onF_o = [o_onF, o_B]
    xE_o = [o_xE, o_B + NCH * TS]
    v_onF_p = [0, 0]
    v_x1_p = [0, 0]
    v_wp_p = [0, 0]
    v_xld_p = [0, 0]
    st_e = {"v_tr": 0}

    def e_front_load(st):
        p = st % 2
        tok0 = st * TS
        k.dma("sync", SB[:, xE_o[p]:xE_o[p] + NCH * TS].rearrange(
            "p (c t) -> p c t", c=NCH),
            x_fm.rearrange("(c p) l -> p c l", p=128)[:, :, tok0:tok0 + TS],
            waits=[("dv", v_x1_p[p])] +
            ([k.w("dv"), k.w("pe"), k.w("ac"), k.w("sp")] if st < 2 else []))
        v_xld_p[p] = S["ld"].cur

    def e_chain(st, blk):
        B_ = 4 * st + blk
        for ti, src in enumerate((lT_sp, mT_sp, dT_sp, vT_sp)):
            k.dma("sync", SB[:, o_t4[ti]:o_t4[ti] + DM], src[B_],
                  waits=[k.w("dv")] if ti == 0 else [])
        v_tld = S["ld"].cur
        for hd in range(H):
            for i in range(4):
                pcol = SB[:, o_pT + 16 * B_ + 4 * i + hd:
                          o_pT + 16 * B_ + 4 * i + hd + 1]
                src = SB[:, o_t4[i] + 256 * hd:o_t4[i] + 256 * hd + 256]
                dst = SB[:, o_mx + 256 * hd:o_mx + 256 * hd + 256]
                if i == 0:
                    k.dve(lambda e, s_=src, d_=dst, p_=pcol: e.tensor_scalar(
                        d_, s_, p_, None, op0=OP.mult),
                        waits=[("ld", v_tld)] if hd == 0 else [])
                else:
                    k.dve(lambda e, s_=src, d_=dst, p_=pcol:
                          e.scalar_tensor_tensor(d_, s_, p_, d_,
                                                 op0=OP.mult, op1=OP.add),
                          inc=(hd == H - 1 and i == 3))
        v_mx = S["dv"].cur
        k.dve(lambda e: e.tensor_tensor(SB[:, o_t3:o_t3 + DM],
                                        SB[:, o_mx:o_mx + DM],
                                        SB[:, o_mx:o_mx + DM], op=OP.mult),
              waits=[("dv", v_mx), ("pe", st_e["v_tr"])])
        k.dve(lambda e: e.tensor_reduce(
            SB[:, o_s4:o_s4 + 4],
            SB[:, o_t3:o_t3 + DM].rearrange("p (hh dd) -> p hh dd", hh=4),
            axis=mybir.AxisListType.X, op=OP.add), waits=[k.w("dv")],
            inc=True)
        va = k.act(SB[:, o_s4 + 8:o_s4 + 12], SB[:, o_s4:o_s4 + 4], AF.Sqrt,
                   scale=1.0 / DH, bias=1e-5, waits=[k.w("dv")], inc=True)
        k.dve(lambda e: e.reciprocal(SB[:, o_s4 + 16:o_s4 + 20],
                                     SB[:, o_s4 + 8:o_s4 + 12]),
              waits=[("ac", va)], inc=True)
        k.dve(lambda e: e.tensor_tensor(SB[:, o_mw2:o_mw2 + DM],
                                        SB[:, o_mx:o_mx + DM],
                                        SB[:, o_mixw:o_mixw + DM],
                                        op=OP.mult), inc=True)
        k.dve(lambda e: e.tensor_tensor(SB[:, o_t3:o_t3 + DM],
                                        SB[:, o_mw2:o_mw2 + DM],
                                        SB[:, o_mw2:o_mw2 + DM],
                                        op=OP.mult), waits=[k.w("dv")])
        k.dve(lambda e: e.tensor_reduce(
            SB[:, o_s4 + 24:o_s4 + 28],
            SB[:, o_t3:o_t3 + DM].rearrange("p (hh dd) -> p hh dd", hh=4),
            axis=mybir.AxisListType.X, op=OP.add), waits=[k.w("dv")],
            inc=True)
        k.dve(lambda e: e.tensor_tensor(
            SB[:, o_s4 + 32:o_s4 + 36], SB[:, o_s4 + 16:o_s4 + 20],
            SB[:, o_s4 + 16:o_s4 + 20], op=OP.mult), waits=[k.w("dv")],
            inc=True)
        k.dve(lambda e: e.tensor_tensor(
            SB[:, o_s4 + 40:o_s4 + 44], SB[:, o_s4 + 24:o_s4 + 28],
            SB[:, o_s4 + 32:o_s4 + 36], op=OP.mult), waits=[k.w("dv")],
            inc=True)
        va = k.act(SB[:, o_s4 + 44:o_s4 + 48], SB[:, o_s4 + 40:o_s4 + 44],
                   AF.Sqrt, scale=1.0 / DH, bias=1e-5, waits=[k.w("dv")],
                   inc=True)
        k.dve(lambda e: e.reciprocal(SB[:, o_s4 + 48:o_s4 + 52],
                                     SB[:, o_s4 + 44:o_s4 + 48]),
              waits=[("ac", va)], inc=True)
        k.dve(lambda e: e.tensor_tensor(
            SB[:, o_s4 + 52:o_s4 + 56], SB[:, o_s4 + 16:o_s4 + 20],
            SB[:, o_s4 + 48:o_s4 + 52], op=OP.mult), waits=[k.w("dv")],
            inc=True)
        k.dve(lambda e: e.tensor_tensor(SB[:, o_t3:o_t3 + DM],
                                        SB[:, o_mw2:o_mw2 + DM],
                                        SB[:, o_onr:o_onr + DM],
                                        op=OP.mult), waits=[k.w("dv")])
        for hd in range(H):
            k.dve(lambda e, h=hd: e.tensor_scalar(
                SB[:, o_t3 + 256 * h:o_t3 + 256 * h + 256],
                SB[:, o_t3 + 256 * h:o_t3 + 256 * h + 256],
                SB[:, o_s4 + 52 + h:o_s4 + 53 + h], None, op0=OP.mult),
                waits=[k.w("dv")] if hd == 0 else [], inc=(hd == H - 1))
        st_e["v_on"] = S["dv"].cur

    def e_transp(st, blk):
        p = st % 2
        v_on = st_e["v_on"]
        for cc in range(NCH):
            bank = 6 + (cc % 2)
            k.transp(PB[bank][:, 0:128],
                     SB[:, o_t3 + 128 * cc:o_t3 + 128 * cc + 128],
                     SB[:, o_id:o_id + 128],
                     waits=[("dv", v_on), k.bg(bank)], inc=True)
            v_t = S["pe"].cur
            k.dve(lambda e, b=bank, c_=cc, bl=blk, p_=p: e.tensor_copy(
                SB[:, onF_o[p_] + c_ * TS + 128 * bl:
                   onF_o[p_] + c_ * TS + 128 * bl + 128],
                PB[b][:, 0:128]),
                waits=[("pe", v_t)] +
                ([("pe", v_wp_p[p])] if cc == 0 and blk == 0 else []),
                inc=True)
            k.setbg(bank, "dv")
        st_e["v_tr"] = S["pe"].cur
        if blk == 3:
            v_onF_p[p] = S["dv"].cur

    def e_back_wo(stb, ms):
        p = stb % 2
        for m in ms:
            bank = m % 4
            for kc in range(NCH):
                k.mm(PB[bank][:, :],
                     SB[:, o_A + kc * DM + 128 * m:o_A + kc * DM + 128 * m + 128],
                     SB[:, onF_o[p] + kc * TS:onF_o[p] + (kc + 1) * TS],
                     start=(kc == 0), stop=(kc == NCH - 1),
                     waits=([("dv", v_onF_p[p]), ("wld", v_wo), k.bg(bank)]
                            if kc == 0 else []),
                     inc=(kc == NCH - 1))
            v_ = S["pe"].cur
            k.dve(lambda e, b=bank, m_=m, p_=p: e.scalar_tensor_tensor(
                SB[:, o_x1 + m_ * TS:o_x1 + (m_ + 1) * TS], PB[b][:, :],
                modcol(2, m_), SB[:, xE_o[p_] + m_ * TS:xE_o[p_] + (m_ + 1) * TS],
                op0=OP.mult, op1=OP.add),
                waits=[("pe", v_)] + ([("ld", v_xld_p[p])] if m == ms[0] else []),
                inc=True)
            k.setbg(bank, "dv")
        if ms[-1] == NCH - 1:
            v_x1_p[p] = S["dv"].cur
            st_e["v_x1"] = S["dv"].cur

    def e_back_ln2(stb):
        p = stb % 2
        v_x1 = st_e["v_x1"]
        for c in range(NCH):
            k.mm(PB[4][0:1, :], SB[:, o_onesc:o_onesc + 1],
                 SB[:, o_x1 + c * TS:o_x1 + (c + 1) * TS],
                 start=(c == 0), stop=(c == NCH - 1),
                 waits=[("dv", v_x1), k.bg(4)] if c == 0 else [],
                 inc=(c == NCH - 1))
        pe_hist2 = []
        for c in range(NCH):
            wv_ = ([("dv", v_x1), ("dv", st_e["v_outlast"])] if c == 0
                   else [k.w("pe", pe_hist2[-1])])
            va = k.act(SB[:, o_fch:o_fch + TS],
                       SB[:, o_x1 + c * TS:o_x1 + (c + 1) * TS], AF.Square,
                       waits=wv_, inc=True)
            k.mm(PB[5][0:1, :], SB[:, o_onesc:o_onesc + 1], SB[:, o_fch:o_fch + TS],
                 start=(c == 0), stop=(c == NCH - 1),
                 waits=[("ac", va)] + ([k.bg(5)] if c == 0 else []), inc=True)
            pe_hist2.append(S["pe"].cur)
        v_st2 = S["pe"].cur
        k.dve(lambda e: e.tensor_scalar(mrow, PB[4][0:1, :], 1.0 / DM, None,
                                        op0=OP.mult), waits=[("pe", v_st2)])
        k.dve(lambda e: e.tensor_tensor(rrow, mrow, mrow, op=OP.mult))
        k.dve(lambda e: e.scalar_tensor_tensor(
            rrow, PB[5][0:1, :], 1.0 / DM, rrow, op0=OP.mult, op1=OP.subtract),
            inc=True)
        v_var2 = S["dv"].cur
        k.setbg(4, "dv", v_var2)
        k.setbg(5, "dv", v_var2)
        va = k.act(rrow, rrow, AF.Sqrt, bias=1e-5, waits=[("dv", v_var2)],
                   inc=True)
        k.dve(lambda e: e.reciprocal(rrow, rrow), waits=[("ac", va)], inc=True)
        v_r2 = S["dv"].cur
        k.mm(PB[4][:, :], SB[0:1, o_onesr:o_onesr + 128], mrow,
             start=True, stop=True, waits=[("dv", v_r2), k.bg(4)])
        k.mm(PB[5][:, :], SB[0:1, o_onesr:o_onesr + 128], rrow,
             start=True, stop=True, waits=[k.bg(5)], inc=True)
        v_bc2 = S["pe"].cur
        for c in range(NCH):
            x1c = SB[:, o_x1 + c * TS:o_x1 + (c + 1) * TS]
            hc = SB[:, onF_o[p] + c * TS:onF_o[p] + (c + 1) * TS]
            k.dve(lambda e, x=x1c, h=hc: e.tensor_tensor(h, x, PB[4][:, :],
                                                         op=OP.subtract),
                  waits=[("pe", v_bc2)] if c == 0 else [])
            k.dve(lambda e, h=hc: e.tensor_tensor(h, h, PB[5][:, :], op=OP.mult))
            k.dve(lambda e, h=hc, c=c: e.scalar_tensor_tensor(
                h, h, modcol(4, c), h, op0=OP.mult, op1=OP.add))
            k.dve(lambda e, h=hc, c=c: e.tensor_scalar(
                h, h, modcol(3, c), None, op0=OP.add), inc=True)
        st_e["v_h2"] = S["dv"].cur
        k.setbg(4, "dv", st_e["v_h2"])
        k.setbg(5, "dv", st_e["v_h2"])

    def e_back_ffn(stb, ms):
        p = stb % 2
        tok0 = stb * TS
        v_h2 = st_e["v_h2"]
        for m in ms:
            bank = m % 4
            for kc in range(NCH):
                k.mm(PB[bank][:, :],
                     SB[:, o_C + kc * DM + 128 * m:o_C + kc * DM + 128 * m + 128],
                     SB[:, onF_o[p] + kc * TS:onF_o[p] + (kc + 1) * TS],
                     start=(kc == 0), stop=(kc == NCH - 1),
                     waits=[("dv", v_h2), k.bg(bank)] if kc == 0 else [],
                     inc=(kc == NCH - 1))
            v_ = S["pe"].cur
            va = k.act(SB[:, o_fch:o_fch + TS], PB[bank][:, :], AF.Tanh,
                       bias=SB[:, o_fb + m:o_fb + m + 1],
                       waits=[("pe", v_), k.w("dv")], inc=True)
            k.setbg(bank, "ac", va)
            db = m % 2
            k.dve(lambda e, m_=m, d_=db: e.scalar_tensor_tensor(
                SB[:, o_och + d_ * TS:o_och + (d_ + 1) * TS],
                SB[:, o_fch:o_fch + TS], modcol(5, m_),
                SB[:, o_x1 + m_ * TS:o_x1 + (m_ + 1) * TS],
                op0=OP.mult, op1=OP.add),
                waits=[("ac", va), ("sp", och_spill[db])], inc=True)
            v_out = S["dv"].cur
            och_spill[db] = k.dma(
                "sync",
                out_fm.rearrange("(c p) l -> p c l", p=128)[:, m:m + 1,
                                                            tok0:tok0 + TS],
                SB[:, o_och + db * TS:o_och + (db + 1) * TS].unsqueeze(1),
                waits=[("dv", v_out)], sem="sp")
        if ms[-1] == NCH - 1:
            v_wp_p[p] = S["pe"].cur
            st_e["v_outlast"] = S["dv"].cur

    st_e["v_x1"] = S["dv"].cur
    st_e["v_outlast"] = S["dv"].cur
    for st in range(NST + 1):
        back = None
        if st > 0:
            stb = st - 1
            back = [lambda s=stb: e_back_wo(s, (0, 1, 2, 3)),
                    lambda s=stb: e_back_wo(s, (4, 5, 6, 7)),
                    lambda s=stb: e_back_ln2(s),
                    lambda s=stb: e_back_ffn(s, (0, 1, 2, 3, 4, 5, 6, 7))]
        if st < NST:
            e_front_load(st)
            for blk in range(4):
                e_chain(st, blk)
                if back is not None:
                    back[blk]()
                e_transp(st, blk)
        else:
            for pc in back:
                pc()
    k.pop()
    if "E" in phases:
        k.flush()

    k.op("sync", lambda e: e.nop(),
         waits=[k.w("sp"), k.w("ld"), k.w("wld")])
    k.flush()
    return k


def build_C(k, env):
    raise NotImplementedError


def build_D(k, env):
    raise NotImplementedError


def build_E(k, env):
    raise NotImplementedError



def make_consts():
    ident = np.eye(128, dtype=np.float32)
    onesrow = np.ones((1, 128), np.float32)
    onescol = np.ones((128, 1), np.float32)
    indrow = np.zeros((128, 512), np.float32)
    indhead = np.zeros((128, 512), np.float32)
    for h in range(4):
        indrow[32 * h, 128 * h:128 * (h + 1)] = 1.0
        indhead[:, 128 * h + 32 * h] = 1.0
    masks = np.zeros((128, 1536), np.float32)
    sl = np.tril(np.ones((CC, CC), np.float32), -1)
    su = np.triu(np.ones((CC, CC), np.float32), 1)
    iu = np.triu(np.ones((CC, CC), np.float32), 0)
    i64 = np.eye(CC, dtype=np.float32)
    for r in range(2):
        rs = slice(r * CC, (r + 1) * CC)
        masks[rs, 0:CC] = sl
        masks[rs, CC:2 * CC] = su
        masks[rs, 2 * CC:3 * CC] = iu
        masks[rs, 3 * CC:4 * CC] = i64
        for t in range(4):
            masks[rs, 256 + t * CC:256 + (t + 1) * CC] = sl
            masks[rs, 512 + t * CC:512 + (t + 1) * CC] = su
            masks[rs, 768 + t * CC:768 + (t + 1) * CC] = iu
        for t in range(8):
            masks[rs, 1024 + t * CC:1024 + (t + 1) * CC] = i64
    return dict(ident=ident, onesrow=onesrow, onescol=onescol,
                indrow=indrow, indhead=indhead, masks=masks)


def make_inputs(d, b):
    f32 = np.float32
    c = make_consts()
    x = np.ascontiguousarray(np.asarray(d["L0_attn_hidden_states"][b]).T).astype(f32)
    wbp = np.zeros((DM, 128), f32)
    for h in range(4):
        wbp[:, 32 * h] = np.asarray(d["L0_attn_Wb"])[:, h]
    convw = np.concatenate([
        np.asarray(d["L0_attn_conv_q_w"]), np.asarray(d["L0_attn_conv_k_w"]),
        np.asarray(d["L0_attn_conv_v_w"]), np.asarray(d["L0_attn_local_w"]),
        np.asarray(d["L0_attn_mid_w"])], axis=1).astype(f32)
    rw1 = np.zeros((1088, 2176), f32)
    rw1[:1080, :2160] = np.asarray(d["L0_attn_r_w1"])
    rb1 = np.zeros((2176,), f32)
    rb1[:2160] = np.asarray(d["L0_attn_r_b1"])
    rw2 = np.asarray(d["L0_attn_r_w2"]).astype(f32)
    rb2 = np.asarray(d["L0_attn_r_b2"]).astype(f32)
    perm = [4 * h + i for i in range(4) for h in range(4)]
    rw2p = np.zeros((2176, 16), f32)
    rw2p[:2160] = rw2[:, perm]
    rb2p = np.ascontiguousarray(rb2[perm])
    mixw = np.asarray(d["L0_attn_mix_w"]).astype(f32).reshape(-1)
    onormt = np.tile(np.asarray(d["L0_attn_onorm_w"]).astype(f32), 4)
    return {
        "x_fm": x,
        "wq": np.asarray(d["L0_attn_Wq"]).astype(f32),
        "wk": np.asarray(d["L0_attn_Wk"]).astype(f32),
        "wv": np.asarray(d["L0_attn_Wv"]).astype(f32),
        "wbp": wbp,
        "convw": convw,
        "rw1": rw1, "rb1": rb1, "rw2p": rw2p, "rb2p": rb2p,
        "onorm_r": np.broadcast_to(onormt, (128, DM)).copy(),
        "mixw_r": np.broadcast_to(mixw, (128, DM)).copy(),
        "wo": np.asarray(d["L0_attn_Wo"]).astype(f32),
        "ffn_wT": np.ascontiguousarray(np.asarray(d["L0_ffn_weight"]).T).astype(f32),
        "ffn_a1": np.asarray(d["L0_ffn_aux_weight1"]).astype(f32),
        "ffn_a2": np.asarray(d["L0_ffn_aux_weight2"]).astype(f32),
        "ffn_b": np.asarray(d["L0_ffn_bias"]).astype(f32),
        "cvec": np.asarray(d["L0_c"][b]).astype(f32),
        "wmod": np.asarray(d["L0_wmod"]).astype(f32),
        **c,
    }


def kernel(**inputs):
    kk = build_program()
    in_maps = [make_inputs(inputs, b) for b in range(8)]
    res = run_bass_kernel_spmd(kk.nc, in_maps, core_ids=list(range(8)))
    out = np.stack([np.asarray(r["out_fm"]).T for r in res.results])
    return out.astype(np.float32)


# version 04
# speedup vs baseline: 1.1410x; 1.0331x over previous
import numpy as np
import concourse.bass as bass
import concourse.mybir as mybir
from concourse.bass_utils import run_bass_kernel_spmd
from concourse.alu_op_type import AluOpType as OP

F32 = mybir.dt.float32
AF = mybir.ActivationFunctionType

L = 4096
DM = 1024
H = 4
DH = 256
TS = 512
NST = L // TS
NCH = DM // 128
CC = 64
NCK = L // CC
NBLK = L // 128
VH = 24

ENG = ("tensor", "vector", "scalar", "gpsimd", "sync")


class Sem:
    def __init__(self, h):
        self.h = h
        self.cur = 0


class K:

    def __init__(self, debug=()):
        self.nc = bass.Bass("TRN2", target_bir_lowering=False)
        self.dbg = set(debug)
        self.steps = []
        self.sems = {}
        self.SBN = 53200
        self.SB = self.nc.alloc_sbuf_tensor("SBP", [128, self.SBN], F32).ap()
        self.PB = [self.nc.alloc_psum_tensor(f"PB{i}", [128, 512], F32).ap()
                   for i in range(8)]
        self.off = 0
        self.marks = []
        for s in ("ld", "wld", "pe", "dv", "ac", "gp", "sp"):
            self.sems[s] = Sem(self.nc.alloc_semaphore(s))
        self.bguard = [("dv", 0)] * 8

    def take(self, n, align=16):
        o = (self.off + align - 1) // align * align
        self.off = o + n
        assert self.off <= self.SBN, f"SBUF overflow {self.off} > {self.SBN}"
        return o

    def push(self):
        self.marks.append(self.off)

    def pop(self):
        self.off = self.marks.pop()

    def dram(self, name, shape, kind=None):
        if name in self.dbg:
            kind = "ExternalOutput"
        if kind is None:
            return self.nc.dram_tensor(name, shape, F32).ap()
        return self.nc.dram_tensor(name, shape, F32, kind=kind).ap()

    def w(self, name, val=None):
        s = self.sems[name]
        return (s, s.cur if val is None else val)

    def bg(self, i):
        n, v = self.bguard[i]
        return (self.sems[n], v)

    def setbg(self, i, name, val=None):
        self.bguard[i] = (name, self.sems[name].cur if val is None else val)

    def op(self, engine, fn, waits=(), inc=None):
        ws = []
        for t in waits:
            if isinstance(t[0], Sem):
                ws.append(t)
            else:
                ws.append((self.sems[t[0]], t[1]))
        self.steps.append((engine, fn, tuple(ws), inc))
        if inc is not None:
            inc[0].cur += inc[1]

    def flush(self):
        steps, self.steps = self.steps, []
        if not steps:
            return

        def emitter(engine):
            def run(eng):
                for e, fn, waits, inc in steps:
                    if e != engine:
                        continue
                    for s, v in waits:
                        eng.wait_ge(s.h, v)
                    inst = fn(eng)
                    if inc is not None:
                        s, amt = inc
                        assert inst is not None
                        inst.then_inc(s.h, amt)
            return run

        used = [e for e in ENG if any(s[0] == e for s in steps)]
        with self.nc.Block() as block:
            for e in used:
                getattr(block, e)(emitter(e))

    def dma(self, q, out, in_, waits=(), sem="ld", slow=False):
        s = self.sems[sem]
        if slow:
            self.op(q, lambda e, o=out, i=in_: e.dma_start(
                out=o, in_=i, allow_slow_non_contiguous=True),
                waits=waits, inc=(s, 16))
        else:
            self.op(q, lambda e, o=out, i=in_: e.dma_start(out=o, in_=i),
                    waits=waits, inc=(s, 16))
        return s.cur

    def mm(self, out, lhsT, rhs, start, stop, waits=(), inc=False, tp=None):
        s = self.sems["pe"]
        self.op("tensor",
                lambda e, o=out, l=lhsT, r=rhs, st=start, sp=stop, t=tp:
                e.matmul(o, l, r, start=st, stop=sp, tile_position=t),
                waits=waits, inc=(s, 1) if inc else None)
        return s.cur

    def transp(self, out, in_, ident, waits=(), inc=False):
        s = self.sems["pe"]
        self.op("tensor", lambda e, o=out, i=in_, d=ident: e.transpose(o, i, d),
                waits=waits, inc=(s, 1) if inc else None)
        return s.cur

    def dve(self, fn, waits=(), inc=False):
        s = self.sems["dv"]
        self.op("vector", fn, waits=waits, inc=(s, 1) if inc else None)
        return s.cur

    def gps(self, fn, waits=(), inc=False):
        s = self.sems["gp"]
        self.op("gpsimd", fn, waits=waits, inc=(s, 1) if inc else None)
        return s.cur

    def act(self, out, in_, func, bias=None, scale=None, waits=(), inc=False):
        s = self.sems["ac"]
        kw = {}
        if bias is not None:
            kw["bias"] = bias
        if scale is not None:
            kw["scale"] = scale
        self.op("scalar", lambda e, o=out, i=in_, f=func, kk=dict(kw):
                e.activation(o, i, f, **kk),
                waits=waits, inc=(s, 1) if inc else None)
        return s.cur


def build_program(debug=(), phases=("P0", "B", "C", "D", "E")):
    k = K(debug)
    SB, PB = k.SB, k.PB
    S = k.sems

    EIn = "ExternalInput"
    x_fm = k.dram("x_fm", [DM, L], EIn)
    wq_d = k.dram("wq", [DM, DM], EIn)
    wk_d = k.dram("wk", [DM, DM], EIn)
    wv_d = k.dram("wv", [DM, DM], EIn)
    wbp_d = k.dram("wbp", [DM, 128], EIn)
    convw_d = k.dram("convw", [DM, 44], EIn)
    rw1_d = k.dram("rw1", [1088, 2176], EIn)
    rb1_d = k.dram("rb1", [2176], EIn)
    rw2_d = k.dram("rw2p", [2176, 16], EIn)
    rb2_d = k.dram("rb2p", [16], EIn)
    onormr_d = k.dram("onorm_r", [128, DM], EIn)
    mixw_d = k.dram("mixw_r", [128, DM], EIn)
    wo_d = k.dram("wo", [DM, DM], EIn)
    ffn_wT_d = k.dram("ffn_wT", [DM, DM], EIn)
    ffn_a1_d = k.dram("ffn_a1", [DM, DM], EIn)
    ffn_a2_d = k.dram("ffn_a2", [DM, DM], EIn)
    ffn_b_d = k.dram("ffn_b", [DM], EIn)
    cvec_d = k.dram("cvec", [DM], EIn)
    wmod_d = k.dram("wmod", [DM, 6 * DM], EIn)
    ident_d = k.dram("ident", [128, 128], EIn)
    onesrow_d = k.dram("onesrow", [1, 128], EIn)
    onescol_d = k.dram("onescol", [128, 1], EIn)
    indrow_d = k.dram("indrow", [128, 512], EIn)
    indhead_d = k.dram("indhead", [128, 512], EIn)
    masks_d = k.dram("masks", [128, 1536], EIn)

    h_sp = k.dram("h_sp", [NCH, 128, L])
    q_sp = k.dram("q_sp", [NCH, 128, L])
    kn_sp = k.dram("kn_sp", [NCH, 128, L])
    kb_sp = k.dram("kb_sp", [NCH, 128, L])
    knT_sp = k.dram("knT_sp", [NCK, 64, DM])
    vbT_sp = k.dram("vbT_sp", [NCK, 64, DM])
    vT_sp = k.dram("vT_sp", [NBLK, 128, DM])
    lT_sp = k.dram("lT_sp", [NBLK, 128, DM])
    mT_sp = k.dram("mT_sp", [NBLK, 128, DM])
    dT_sp = k.dram("dT_sp", [NBLK, 128, DM])
    rqT_sp = k.dram("rqT_sp", [NBLK, 128, 4])
    mod_sp = k.dram("mod_sp", [6 * DM])
    wp_sp = k.dram("wp_sp", [DM, DM])
    out_fm = k.dram("out_fm", [DM, L], "ExternalOutput")

    o_ceps = k.take(16)
    k.nc.const_aps.register(F32, 1e-5, SB[:, o_ceps:o_ceps + 1])
    k.nc.const_aps.register(F32, 1e-12, SB[:, o_ceps + 1:o_ceps + 2])
    k.dve(lambda e: e.memset(SB[:, o_ceps:o_ceps + 1], 1e-5))
    k.dve(lambda e: e.memset(SB[:, o_ceps + 1:o_ceps + 2], 1e-12), inc=True)
    o_mod = k.take(48)
    o_id = k.take(128)
    o_onesr = k.take(128)
    o_onesc = k.take(16)
    o_indr = k.take(512)
    o_indh = k.take(512)
    o_msk = k.take(256)
    o_cw = k.take(NCH * 44)

    MSL, MSU, MIU, MI64 = o_msk, o_msk + CC, o_msk + 2 * CC, o_msk + 3 * CC

    def modcol(v, c):
        return SB[:, o_mod + 8 * v + c:o_mod + 8 * v + c + 1]

    k.push()
    o_sc = k.take(16)
    o_wmb = k.take(2 * 512)
    o_mrow = k.take(512)

    k.dma("sync", SB[:, o_sc:o_sc + 8], cvec_d.rearrange("(c p) -> p c", p=128),
          slow=True)
    v_c = S["ld"].cur
    for dst, src, wdt in ((o_id, ident_d, 128), (o_onesr, onesrow_d, 128),
                         (o_onesc, onescol_d, 1), (o_indr, indrow_d, 512),
                         (o_indh, indhead_d, 512)):
        k.dma("sync", SB[:src.shape[0], dst:dst + wdt], src, sem="wld")
    k.dma("sync", SB[:, o_msk:o_msk + 256], masks_d[:, 0:256], sem="wld")
    k.dma("sync", SB[:, o_cw:o_cw + NCH * 44].rearrange("p (c w) -> p c w", c=NCH),
          convw_d.rearrange("(c p) w -> p c w", p=128), sem="wld")

    va = k.act(SB[:, o_sc:o_sc + 8], SB[:, o_sc:o_sc + 8], AF.Silu,
               waits=[("ld", v_c)], inc=True)

    pe_hist = []
    for j in range(12):
        for kc in range(NCH):
            n = j * NCH + kc
            buf = o_wmb + (n % 2) * 512
            wt = [k.w("pe", pe_hist[n - 2])] if n >= 2 else []
            vl = k.dma("sync", SB[:, buf:buf + 512],
                       wmod_d[kc * 128:(kc + 1) * 128, j * 512:(j + 1) * 512],
                       waits=wt)
            k.mm(PB[0][0:1, :], SB[:, o_sc + kc:o_sc + kc + 1],
                 SB[:, buf:buf + 512], start=(kc == 0), stop=(kc == NCH - 1),
                 waits=[("ld", vl), ("ac", va)] + ([k.w("dv")] if kc == 0 else []),
                 inc=True)
            pe_hist.append(S["pe"].cur)
        k.dve(lambda e: e.tensor_copy(SB[0:1, o_mrow:o_mrow + 512], PB[0][0:1, :]),
              waits=[k.w("pe")], inc=True)
        k.dma("sync", mod_sp[j * 512:(j + 1) * 512], SB[0:1, o_mrow:o_mrow + 512],
              waits=[k.w("dv")], sem="sp")
    v_modsp = S["sp"].cur
    k.pop()

    k.dma("sync", SB[:, o_mod:o_mod + 48].rearrange("p (v c) -> p v c", v=6),
          mod_sp.rearrange("(v c p) -> p v c", v=6, c=NCH),
          waits=[("sp", v_modsp)], slow=True)
    v_modld = S["ld"].cur

    k.push()
    o_t1, o_t2, o_t3 = k.take(16), k.take(16), k.take(16)
    for v6 in (2, 5):
        g = SB[:, o_mod + 8 * v6:o_mod + 8 * v6 + 8]
        t1, t2, t3 = (SB[:, o:o + 8] for o in (o_t1, o_t2, o_t3))
        k.dve(lambda e, g=g, t=t1: e.tensor_scalar(t, g, 0.25, None, op0=OP.max),
              waits=[("ld", v_modld)], inc=True)
        k.dve(lambda e, g=g, t=t2: e.tensor_scalar(t, g, -0.25, None, op0=OP.min),
              inc=True)
        k.dve(lambda e, g=g, t=t3: e.tensor_scalar(t, g, 0.0, None, op0=OP.is_ge),
              inc=True)
        k.dve(lambda e, a=t1, b=t2: e.tensor_tensor(a, a, b, op=OP.subtract),
              waits=[k.w("dv")], inc=True)
        k.dve(lambda e, a=t1, b=t3: e.tensor_tensor(a, a, b, op=OP.mult),
              waits=[k.w("dv")], inc=True)
        k.dve(lambda e, g=g, a=t1, b=t2: e.tensor_tensor(g, a, b, op=OP.add),
              waits=[k.w("dv")], inc=True)
    k.pop()
    v_mod_dv = S["dv"].cur
    if "P0" in phases:
        k.flush()

    k.push()
    VSEG = NCH * (VH + TS)
    o_wq = k.take(NCH * DM)
    o_wk = k.take(NCH * DM)
    o_wv = k.take(2 * 1024)
    o_wb = k.take(NCH * 128)
    o_x = k.take(NCH * TS)
    o_h = k.take(NCH * TS)
    o_kn = k.take(NCH * TS)
    o_v = k.take(2 * VSEG)
    o_lin = k.take(TS + 16)
    o_halo = k.take(3 * NCH * 4)
    o_cb = k.take(TS)
    o_cb5 = k.take(2 * TS)
    o_sq2 = k.take(2 * TS)
    o_qc = k.take(2 * TS)
    o_kbc = k.take(2 * TS)
    o_mr = k.take(2 * TS)
    o_bs = k.take(TS)
    o_rq = k.take(TS)
    o_rk = k.take(TS)
    o_tma = k.take(DM)
    o_tmb = k.take(DM)
    o_bT = k.take(16)
    o_rqTb = k.take(16)

    for dst, src in ((o_wq, wq_d), (o_wk, wk_d)):
        k.dma("sync", SB[:, dst:dst + NCH * DM].rearrange("p (c m) -> p c m", c=NCH),
              src.rearrange("(c p) m -> p c m", p=128), sem="wld")
    k.dma("sync", SB[:, o_wb:o_wb + NCH * 128].rearrange("p (c m) -> p c m", c=NCH),
          wbp_d.rearrange("(c p) m -> p c m", p=128), sem="wld")
    v_wB = S["wld"].cur

    wv_ldv = [0, 0]
    wv_pe_guard = [0, 0]

    def load_wv(c):
        sl = c % 2
        k.dma("sync",
              SB[:, o_wv + sl * 1024:o_wv + sl * 1024 + 1024].rearrange(
                  "p (kc m) -> p kc m", kc=NCH),
              wv_d.rearrange("(kc p) m -> p kc m", p=128)[:, :,
                                                          c * 128:(c + 1) * 128],
              waits=[("pe", wv_pe_guard[sl])], sem="wld")
        wv_ldv[sl] = S["wld"].cur

    load_wv(0)
    load_wv(1)

    k.dve(lambda e: e.memset(SB[:, o_halo:o_halo + 3 * NCH * 4], 0.0))
    for vp in range(2):
        for c in range(NCH):
            k.dve(lambda e, vp=vp, c=c: e.memset(
                SB[:, o_v + vp * VSEG + c * (VH + TS):
                   o_v + vp * VSEG + c * (VH + TS) + VH], 0.0),
                inc=(vp == 1 and c == NCH - 1))

    def conv_taps(lin_off, w_off, ntap, out_off, waits0=()):
        k.dve(lambda e, lo=lin_off, wo=w_off, oo=out_off: e.tensor_scalar(
            SB[:, oo:oo + TS], SB[:, lo:lo + TS], SB[:, wo:wo + 1], None,
            op0=OP.mult), waits=waits0)
        for j in range(1, ntap):
            k.dve(lambda e, lo=lin_off + j, wo=w_off + j, oo=out_off:
                  e.scalar_tensor_tensor(
                      SB[:, oo:oo + TS], SB[:, lo:lo + TS], SB[:, wo:wo + 1],
                      SB[:, oo:oo + TS], op0=OP.mult, op1=OP.add))

    qspill = [0, 0]
    kbspill = [0, 0]
    rqt_spill = 0
    tma_spill = 0
    tmb_spill = 0

    u_loc = {}
    u_mid = {}
    u_tput_loc = [0, 0]
    u_tput_mid = [0]
    st_b = {"ac_vend": 0}

    def conv_unit(sp, c):
        vpar = sp % 2
        vb = o_v + vpar * VSEG + c * (VH + TS)
        cb5 = o_cb5 + (c % 2) * TS
        w0 = [("pe", u_tput_loc[c % 2])]
        if c == 0:
            w0.append(("ac", st_b["ac_vend"]))
        conv_taps(vb + VH - 6, o_cw + c * 44 + 12, 7, cb5, waits0=w0)
        k.dve(lambda e: e.engine_nop(), inc=True)
        u_loc[c] = S["dv"].cur
        first = True
        for j in range(25):
            lo = vb + j
            wo = o_cw + c * 44 + 19 + j
            if first:
                k.dve(lambda e, l=lo, w=wo: e.tensor_scalar(
                    SB[:, o_tmb:o_tmb + 512], SB[:, l:l + TS],
                    SB[:, w:w + 1], None, op0=OP.mult),
                    waits=[("sp", tmb_spill), ("pe", u_tput_mid[0])])
                first = False
            else:
                k.dve(lambda e, l=lo, w=wo: e.scalar_tensor_tensor(
                    SB[:, o_tmb:o_tmb + 512], SB[:, l:l + TS],
                    SB[:, w:w + 1], SB[:, o_tmb:o_tmb + 512],
                    op0=OP.mult, op1=OP.add))
        k.dve(lambda e: e.engine_nop(), inc=True)
        u_mid[c] = S["dv"].cur
        dhl = o_v + (1 - vpar) * VSEG + c * (VH + TS)
        k.dve(lambda e, s_=vb + TS, d_=dhl: e.tensor_copy(
            SB[:, d_:d_ + VH], SB[:, s_:s_ + VH]))

    def tput_unit(sp, c):
        nonlocal tma_spill
        cb5 = o_cb5 + (c % 2) * TS
        for src_o, v_rdy, tgt_sp, kind in ((cb5, u_loc[c], lT_sp, "loc"),
                                           (o_tmb, u_mid[c], mT_sp, "mid")):
            for blk in range(4):
                k.transp(PB[2][:, blk * 128:(blk + 1) * 128],
                         SB[:, src_o + blk * 128:src_o + blk * 128 + 128],
                         SB[:, o_id:o_id + 128],
                         waits=[("dv", v_rdy), k.bg(2)] if blk == 0 else [],
                         inc=(blk == 3))
            v_tl = S["pe"].cur
            if kind == "loc":
                u_tput_loc[c % 2] = v_tl
            else:
                u_tput_mid[0] = v_tl
            k.dve(lambda e: e.tensor_copy(SB[:, o_tma:o_tma + 512],
                                          PB[2][:, :]),
                  waits=[("pe", v_tl), ("sp", tma_spill)], inc=True)
            v_tlm = S["dv"].cur
            k.setbg(2, "dv", v_tlm)
            for blk in range(4):
                k.dma("scalar", tgt_sp[4 * sp + blk][:, c * 128:(c + 1) * 128],
                      SB[:, o_tma + blk * 128:o_tma + blk * 128 + 128],
                      waits=[("dv", v_tlm)], sem="sp")
            tma_spill = S["sp"].cur

    def emit_unit_slot(sp, sidx):
        if sp < 0 or sidx < 2 or (sidx - 2) % 3 != 0:
            return
        u = (sidx - 2) // 3
        if u >= NCH:
            return
        if u > 0:
            tput_unit(sp, u - 1)
        conv_unit(sp, u)
    sq2_guard = [("dv", 0), ("dv", 0)]
    mrow = SB[0:1, o_mr:o_mr + TS]
    rrow = SB[0:1, o_mr + TS:o_mr + 2 * TS]

    g_xfree = []
    for st in range(NST):
        tok0 = st * TS
        v_x = k.dma("sync", SB[:, o_x:o_x + NCH * TS].rearrange(
            "p (c t) -> p c t", c=NCH),
            x_fm.rearrange("(c p) l -> p c l", p=128)[:, :, tok0:tok0 + TS],
            waits=g_xfree)

        pe_sx2 = []
        for c in range(NCH):
            k.mm(PB[2][0:1, :], SB[:, o_onesc:o_onesc + 1],
                 SB[:, o_x + c * TS:o_x + (c + 1) * TS],
                 start=(c == 0), stop=(c == NCH - 1),
                 waits=[("ld", v_x), k.bg(2)] if c == 0 else [],
                 inc=(c == NCH - 1))
        for c in range(NCH):
            wv_ = ([("ld", v_x)] if c == 0 else [k.w("pe", pe_sx2[-1])]) + \
                [sq2_guard[c % 2]]
            va = k.act(SB[:, o_sq2 + (c % 2) * TS:o_sq2 + (c % 2) * TS + TS],
                       SB[:, o_x + c * TS:o_x + (c + 1) * TS], AF.Square,
                       waits=wv_, inc=True)
            k.mm(PB[3][0:1, :], SB[:, o_onesc:o_onesc + 1],
                 SB[:, o_sq2 + (c % 2) * TS:o_sq2 + (c % 2) * TS + TS],
                 start=(c == 0), stop=(c == NCH - 1),
                 waits=[("ac", va)] + ([k.bg(3)] if c == 0 else []), inc=True)
            pe_sx2.append(S["pe"].cur)
            sq2_guard[c % 2] = ("pe", S["pe"].cur)
        v_st = S["pe"].cur
        v_xsq = S["ac"].cur
        k.dve(lambda e: e.tensor_scalar(mrow, PB[2][0:1, :], 1.0 / DM, None,
                                        op0=OP.mult), waits=[("pe", v_st)])
        k.dve(lambda e: e.tensor_tensor(rrow, mrow, mrow, op=OP.mult))
        k.dve(lambda e: e.scalar_tensor_tensor(
            rrow, PB[3][0:1, :], 1.0 / DM, rrow, op0=OP.mult, op1=OP.subtract),
            inc=True)
        v_var = S["dv"].cur
        k.setbg(2, "dv", v_var)
        k.setbg(3, "dv", v_var)
        va = k.act(rrow, rrow, AF.Sqrt, bias=1e-5, waits=[("dv", v_var)], inc=True)
        k.dve(lambda e: e.reciprocal(rrow, rrow), waits=[("ac", va)], inc=True)
        v_r = S["dv"].cur
        k.mm(PB[4][:, :], SB[0:1, o_onesr:o_onesr + 128], mrow,
             start=True, stop=True, waits=[("dv", v_r), k.bg(4)])
        k.mm(PB[5][:, :], SB[0:1, o_onesr:o_onesr + 128], rrow,
             start=True, stop=True, waits=[k.bg(5)], inc=True)
        v_bc = S["pe"].cur

        for c in range(NCH):
            xc = SB[:, o_x + c * TS:o_x + (c + 1) * TS]
            hc = SB[:, o_h + c * TS:o_h + (c + 1) * TS]
            w0 = [("pe", v_bc)]
            if c == 0:
                w0.append(("sp", st_b.get("h_spill", 0)))
            if st == 0 and c == 0:
                w0 += [("ld", v_modld), ("dv", v_mod_dv)]
            k.dve(lambda e, x=xc, h=hc: e.tensor_tensor(h, x, PB[4][:, :],
                                                        op=OP.subtract), waits=w0)
            k.dve(lambda e, h=hc: e.tensor_tensor(h, h, PB[5][:, :], op=OP.mult))
            k.dve(lambda e, h=hc, c=c: e.scalar_tensor_tensor(
                h, h, modcol(1, c), h, op0=OP.mult, op1=OP.add))
            k.dve(lambda e, h=hc, c=c: e.tensor_scalar(
                h, h, modcol(0, c), None, op0=OP.add), inc=True)
        v_h = S["dv"].cur
        k.setbg(4, "dv", v_h)
        k.setbg(5, "dv", v_h)
        st_b["h_spill"] = k.dma(
            "sync", h_sp.rearrange("c p l -> p c l")[:, :, tok0:tok0 + TS],
            SB[:, o_h:o_h + NCH * TS].rearrange("p (c t) -> p c t", c=NCH),
            waits=[("dv", v_h)], sem="sp")

        for c in range(NCH):
            k.mm(PB[7][:, :], SB[:, o_wb + c * 128:o_wb + (c + 1) * 128],
                 SB[:, o_h + c * TS:o_h + (c + 1) * TS],
                 start=(c == 0), stop=(c == NCH - 1),
                 waits=[("dv", v_h), ("wld", v_wB), k.bg(7)] if c == 0 else [],
                 inc=(c == NCH - 1))
        v_bl = S["pe"].cur
        va = k.act(SB[:, o_bs:o_bs + TS], PB[7][:, :], AF.Sigmoid,
                   waits=[("pe", v_bl)], inc=True)
        k.setbg(7, "ac", va)

        g_xfree = [("pe", v_st), ("ac", v_xsq), ("dv", v_h)]
        vpar = st % 2
        for ti, (ow, cwo, tgt) in enumerate(((o_wq, 0, "q"), (o_wk, 4, "k"),
                                             (None, 8, "v"))):
            sum_hist = []
            pend_sq = None

            def emit_sum(cw, vaw):
                k.mm(PB[6][:, :],
                     SB[:, o_indh + (cw // 2) * 128:o_indh + (cw // 2) * 128 + 128],
                     SB[:, o_sq2 + (cw % 2) * TS:o_sq2 + (cw % 2) * TS + TS],
                     start=(cw == 0), stop=(cw == NCH - 1),
                     waits=[("ac", vaw)] + ([k.bg(6)] if cw == 0 else []),
                     inc=True)
                sum_hist.append(S["pe"].cur)
                sq2_guard[cw % 2] = ("pe", S["pe"].cur)

            if tgt == "k" and st > 0:
                load_wv(0)
                load_wv(1)
            for c in range(NCH):
                bank = c % 2
                for kc in range(NCH):
                    if tgt == "v":
                        lh = SB[:, o_wv + (c % 2) * 1024 + kc * 128:
                                o_wv + (c % 2) * 1024 + kc * 128 + 128]
                    else:
                        lh = SB[:, ow + kc * DM + c * 128:
                                ow + kc * DM + (c + 1) * 128]
                    k.mm(PB[bank][:, :], lh,
                         SB[:, o_h + kc * TS:o_h + (kc + 1) * TS],
                         start=(kc == 0), stop=(kc == NCH - 1),
                         waits=([k.bg(bank)] +
                                ([("dv", v_h)] if (ti == 0 and c == 0) else []) +
                                ([("wld", wv_ldv[c % 2])] if tgt == "v" else []))
                         if kc == 0 else [],
                         inc=(kc == NCH - 1))
                v_p = S["pe"].cur
                if tgt == "v":
                    wv_pe_guard[c % 2] = v_p
                    if c + 2 < NCH:
                        load_wv(c + 2)
                if pend_sq is not None:
                    emit_sum(*pend_sq)
                    pend_sq = None
                hs = o_halo + (ti * NCH + c) * 4
                k.dve(lambda e, h=hs: e.tensor_copy(SB[:, o_lin:o_lin + 3],
                                                    SB[:, h:h + 3]),
                      waits=[("pe", v_p)])
                k.dve(lambda e, b=bank: e.tensor_copy(
                    SB[:, o_lin + 3:o_lin + 3 + TS], PB[b][:, :]))
                k.dve(lambda e, h=hs, b=bank: e.tensor_copy(
                    SB[:, h:h + 3], PB[b][:, TS - 3:TS]), inc=True)
                k.setbg(bank, "dv")
                conv_taps(o_lin, o_cw + c * 44 + cwo, 4, o_cb,
                          waits0=[k.w("ac")])
                k.dve(lambda e: e.engine_nop(), inc=True)
                v_cv = S["dv"].cur
                if tgt == "q":
                    db = c % 2
                    dst = SB[:, o_qc + db * TS:o_qc + (db + 1) * TS]
                elif tgt == "k":
                    dst = SB[:, o_kn + c * TS:o_kn + (c + 1) * TS]
                else:
                    vb0 = o_v + vpar * VSEG + c * (VH + TS) + VH
                    dst = SB[:, vb0:vb0 + TS]
                wsilu = [("dv", v_cv)]
                if tgt == "q":
                    wsilu.append(("sp", qspill[db]))
                if tgt == "k" and c == 0:
                    wsilu.append(("sp", st_b.get("kn_spill", 0)))
                v_silu = k.act(dst, SB[:, o_cb:o_cb + TS], AF.Silu, waits=wsilu,
                               inc=True)
                if tgt in ("q", "k"):
                    va2 = k.act(SB[:, o_sq2 + (c % 2) * TS:o_sq2 + (c % 2) * TS + TS],
                                dst, AF.Square,
                                waits=[("ac", v_silu), sq2_guard[c % 2]],
                                inc=True)
                    pend_sq = (c, va2)
                if tgt == "q":
                    qspill[db] = k.dma(
                        "scalar",
                        q_sp.rearrange("c p l -> p c l")[:, c:c + 1, tok0:tok0 + TS],
                        dst.unsqueeze(1), waits=[("ac", v_silu)], sem="sp")
                emit_unit_slot(st - 1, ti * NCH + c)
            if pend_sq is not None:
                emit_sum(*pend_sq)
                pend_sq = None
            if tgt in ("q", "k"):
                v_sum = S["pe"].cur
                orr = o_rq if tgt == "q" else o_rk
                va = k.act(SB[:, orr:orr + TS], PB[6][:, :], AF.Sqrt, bias=1e-12,
                           waits=[("pe", v_sum)], inc=True)
                k.setbg(6, "ac", va)
                k.dve(lambda e, o=orr: e.reciprocal(SB[:, o:o + TS],
                                                    SB[:, o:o + TS]),
                      waits=[("ac", va)], inc=True)
        st_b["ac_vend"] = S["ac"].cur
        if st > 0:
            tput_unit(st - 1, NCH - 1)

        v_knc = S["dv"].cur
        for hh in range(H):
            k.mm(PB[4][:, :], SB[:, o_indr + hh * 128:o_indr + hh * 128 + 128],
                 SB[:, o_rk:o_rk + TS], start=True, stop=True,
                 waits=[k.w("dv"), k.bg(4)])
            k.mm(PB[5][:, :], SB[:, o_indr + hh * 128:o_indr + hh * 128 + 128],
                 SB[:, o_bs:o_bs + TS], start=True, stop=True,
                 waits=[k.bg(5)], inc=True)
            v_b2 = S["pe"].cur
            for c in (2 * hh, 2 * hh + 1):
                kc_ap = SB[:, o_kn + c * TS:o_kn + (c + 1) * TS]
                db = c % 2
                k.dve(lambda e, kc=kc_ap: e.tensor_tensor(kc, kc, PB[4][:, :],
                                                          op=OP.mult),
                      waits=[("pe", v_b2)])
                k.dve(lambda e, kc=kc_ap, d=db: e.tensor_tensor(
                    SB[:, o_kbc + d * TS:o_kbc + (d + 1) * TS], kc, PB[5][:, :],
                    op=OP.mult), waits=[("sp", kbspill[db])], inc=True)
                v_knc = S["dv"].cur
                kbspill[db] = k.dma(
                    "scalar",
                    kb_sp.rearrange("c p l -> p c l")[:, c:c + 1, tok0:tok0 + TS],
                    SB[:, o_kbc + db * TS:o_kbc + (db + 1) * TS].unsqueeze(1),
                    waits=[("dv", v_knc)], sem="sp")
                k.dma("scalar",
                      kn_sp.rearrange("c p l -> p c l")[:, c:c + 1, tok0:tok0 + TS],
                      kc_ap.unsqueeze(1), waits=[("dv", v_knc)], sem="sp")
            k.setbg(4, "dv", v_knc)
            k.setbg(5, "dv", v_knc)
        st_b["kn_spill"] = S["sp"].cur

        for blk in range(4):
            B_ = 4 * st + blk
            k.transp(PB[7][:, 0:128], SB[:, o_bs + blk * 128:o_bs + blk * 128 + 128],
                     SB[:, o_id:o_id + 128], waits=[k.w("dv", v_knc), k.bg(7)])
            k.transp(PB[7][:, 128:256], SB[:, o_rq + blk * 128:o_rq + blk * 128 + 128],
                     SB[:, o_id:o_id + 128], inc=True)
            v_tb = S["pe"].cur
            k.dve(lambda e: e.tensor_copy(SB[:, o_bT:o_bT + 4], PB[7][:, 0:128:32]),
                  waits=[("pe", v_tb), ("sp", rqt_spill)])
            k.dve(lambda e: e.tensor_copy(SB[:, o_rqTb:o_rqTb + 4],
                                          PB[7][:, 128:256:32]), inc=True)
            v_t2 = S["dv"].cur
            k.setbg(7, "dv", v_t2)
            rqt_spill = k.dma("scalar", rqT_sp[B_], SB[:, o_rqTb:o_rqTb + 4],
                              waits=[("dv", v_t2)], sem="sp")

            for half in range(2):
                bank = 2 + half
                for cp in range(4):
                    c = half * 4 + cp
                    src = SB[:, o_v + vpar * VSEG + c * (VH + TS) + VH + blk * 128:
                             o_v + vpar * VSEG + c * (VH + TS) + VH + blk * 128 + 128]
                    k.transp(PB[bank][:, cp * 128:(cp + 1) * 128], src,
                             SB[:, o_id:o_id + 128],
                             waits=[k.bg(bank)] if cp == 0 else [], inc=(cp == 3))
                v_tv = S["pe"].cur
                k.dve(lambda e, b=bank, hf=half: e.tensor_copy(
                    SB[:, o_tma + hf * 512:o_tma + hf * 512 + 512], PB[b][:, :]),
                    waits=[("pe", v_tv), ("sp", tma_spill)])
                for h2 in range(2):
                    hd = half * 2 + h2
                    k.dve(lambda e, b=bank, h2=h2, hd=hd: e.tensor_scalar(
                        SB[:, o_tmb + hd * 256:o_tmb + (hd + 1) * 256],
                        PB[b][:, h2 * 256:(h2 + 1) * 256],
                        SB[:, o_bT + hd:o_bT + hd + 1], None, op0=OP.mult),
                        waits=[("sp", tmb_spill)] if h2 == 0 else [],
                        inc=(h2 == 1))
                k.setbg(bank, "dv")
            v_vtm = S["dv"].cur
            k.dma("scalar", vT_sp[B_], SB[:, o_tma:o_tma + DM],
                  waits=[("dv", v_vtm)], sem="sp")
            tma_spill = S["sp"].cur
            for par in range(2):
                ck = 2 * B_ + par
                k.dma("scalar", vbT_sp[ck],
                      SB[64 * par:64 * par + 64, o_tmb:o_tmb + DM],
                      waits=[("dv", v_vtm)], sem="sp")
            tmb_spill = S["sp"].cur

            for half in range(2):
                bank = 2 + half
                for cp in range(4):
                    c = half * 4 + cp
                    src = SB[:, o_kn + c * TS + blk * 128:o_kn + c * TS + blk * 128 + 128]
                    k.transp(PB[bank][:, cp * 128:(cp + 1) * 128], src,
                             SB[:, o_id:o_id + 128],
                             waits=[k.bg(bank)] if cp == 0 else [], inc=(cp == 3))
                v_tk = S["pe"].cur
                k.dve(lambda e, b=bank, hf=half: e.tensor_copy(
                    SB[:, o_tma + hf * 512:o_tma + hf * 512 + 512], PB[b][:, :]),
                    waits=[("pe", v_tk), ("sp", tma_spill)] if half == 0
                    else [("pe", v_tk)], inc=True)
                k.setbg(bank, "dv")
            v_ktm = S["dv"].cur
            for par in range(2):
                ck = 2 * B_ + par
                k.dma("scalar", knT_sp[ck],
                      SB[64 * par:64 * par + 64, o_tma:o_tma + DM],
                      waits=[("dv", v_ktm)], sem="sp")
            tma_spill = S["sp"].cur

    sp_last = NST - 1
    for u in range(NCH):
        if u > 0:
            tput_unit(sp_last, u - 1)
        conv_unit(sp_last, u)
    tput_unit(sp_last, NCH - 1)

    k.pop()
    if "B" in phases:
        k.flush()

    o_pT = k.take(NBLK * 16)
    k.push()
    o_feats = k.take(4096)
    k.push()
    o_SS = [k.take(512) for _ in range(H)]
    o_qF = k.take(NCH * TS)
    o_kbF = k.take(NCH * TS)
    o_knF = k.take(NCH * TS)
    o_knT = [k.take(DM) for _ in range(4)]
    o_vbT = [k.take(DM) for _ in range(4)]
    o_rqT = k.take(32)
    o_vTb = k.take(DM)
    o_lTb = k.take(DM)
    o_mTb = k.take(DM)
    o_dta = k.take(2 * DM)
    PCLS = ("P1", "Q1", "Q2", "P2", "Q4", "P4", "Q8", "P8", "Q16",
            "P16", "Q32", "P12", "P32", "P48", "C1", "C2", "C3",
            "CP3", "CP3I", "C2I", "M2", "N2", "TMP")
    o_cls = {n: k.take(512) for n in PCLS}
    o_Tx = k.take(1024)
    o_Ys = k.take(DM)
    o_Us = k.take(DM)
    o_att = k.take(256)
    o_pr = [k.take(DM) for _ in range(2)]
    o_fr = k.take(64)
    o_ft = k.take(128)
    o_sc1 = k.take(16)
    o_sc2 = k.take(16)
    o_msk4 = k.take(768)
    o_mI = k.take(512)
    k.dma("sync", SB[:, o_msk4:o_msk4 + 768], masks_d[:, 256:1024], sem="wld",
          waits=[k.w("sp")])
    k.dma("sync", SB[:, o_mI:o_mI + 512], masks_d[:, 1024:1536], sem="wld")

    first_ms = [k.w("sp")]
    for off, n in ([(o, 512) for o in o_SS] +
                   [(o, DM) for o in o_knT] +
                   [(o_Ys, DM), (o_Us, DM), (o_Tx, 1024),
                    (o_att, 256), (o_feats, 4096)] +
                   [(o_cls[n], 512) for n in o_cls]):
        k.dve(lambda e, o=off, m=n: e.memset(SB[:, o:o + m], 0.0),
              waits=first_ms)
        first_ms = []
    k.dve(lambda e: e.engine_nop(), inc=True)

    M_SL4, M_SU4, M_IU4 = o_msk4, o_msk4 + 256, o_msk4 + 512

    def pow_ap(n, par, hd=None, rows=128):
        o = o_cls[n] + 256 * par
        if hd is None:
            return SB[:rows, o:o + 256]
        return SB[:rows, o + 64 * hd:o + 64 * hd + 64]

    def cls_full(n, rows=128):
        return SB[:rows, o_cls[n]:o_cls[n] + 512]

    def texp_ap(gpar, par, hd):
        o = o_Tx + 512 * gpar + 256 * par + 64 * hd
        return SB[:, o:o + 64]

    knt_guard = [k.w("dv", 0)] * 4
    vbt_guard = [k.w("dv", 0)] * 4
    dta_spill = [0, 0]
    texp_guard = [("pe", 0), ("pe", 0)]
    st_c = {"v_S_add": S["dv"].cur, "v_tmld": [0, 0],
            "v_fmld": 0, "scan_qf_pe": 0, "pr_guard": [0, 0],
            "ft_guard": [0, 0], "v_Mmms": 0, "v_Tx": [0, 0], "pi": 0}

    NG = NCK // 2

    def qsl(base, ck, hd, kc2):
        c = 2 * hd + kc2
        t0 = (ck % 8) * CC
        return SB[:, base + c * TS + t0:base + c * TS + t0 + CC]

    def emit_T0(g, boundary):
        if boundary:
            w0 = [k.w("pe"), k.w("dv"), k.w("gp")]
            for dst, src in ((o_qF, q_sp), (o_kbF, kb_sp), (o_knF, kn_sp)):
                k.dma("sync", SB[:, dst:dst + NCH * TS].rearrange(
                    "p (c t) -> p c t", c=NCH),
                    src.rearrange("c p l -> p c l")[:, :, (g // 4) * TS:
                                                    (g // 4) * TS + TS],
                    waits=w0)
                w0 = []
            stl = g // 4
            k.dma("sync",
                  SB[:, o_rqT + 16 * (stl % 2):o_rqT + 16 * (stl % 2) + 16]
                  .rearrange("p (b hh) -> p b hh", b=4),
                  rqT_sp.rearrange("b p hh -> p b hh")[:, 4 * stl:4 * stl + 4, :])
        for par in range(2):
            ck = 2 * g + par
            k.dma("sync", SB[0:64, o_knT[ck % 4]:o_knT[ck % 4] + DM], knT_sp[ck],
                  waits=[knt_guard[ck % 4]])
            k.dma("sync", SB[0:64, o_vbT[ck % 4]:o_vbT[ck % 4] + DM], vbT_sp[ck],
                  waits=[vbt_guard[ck % 4]])
        st_c["v_tmld"][g % 2] = S["ld"].cur
        if boundary:
            st_c["v_fmld"] = S["ld"].cur

    def emit_T1(g):
        v_ld = st_c["v_fmld"]
        for par in range(2):
            ck = 2 * g + par
            bk = 4 + par
            for hd in range(H):
                for kc2 in range(2):
                    k.mm(PB[bk][0:64, 64 * hd:64 * hd + 64],
                         qsl(o_kbF, ck, hd, kc2), qsl(o_knF, ck, hd, kc2),
                         start=(kc2 == 0), stop=(kc2 == 1),
                         waits=[("ld", v_ld), k.bg(bk)] if hd == 0 and kc2 == 0
                         else [], inc=(kc2 == 1))
            for hd in range(H):
                for kc2 in range(2):
                    k.mm(PB[bk][0:64, 256 + 64 * hd:256 + 64 * hd + 64],
                         qsl(o_knF, ck, hd, kc2), qsl(o_kbF, ck, hd, kc2),
                         start=(kc2 == 0), stop=(kc2 == 1), inc=(kc2 == 1))
        st_c["v_gr"] = S["pe"].cur

    def emit_T2(g):
        v_gr = st_c["v_gr"]
        for par in range(2):
            bk = 4 + par
            k.dve(lambda e, b=bk, p_=par: e.scalar_tensor_tensor(
                pow_ap("P1", p_, rows=64), PB[b][0:64, 0:256], -1.0,
                SB[0:64, M_SL4:M_SL4 + 256], op0=OP.mult, op1=OP.mult),
                waits=[("pe", v_gr), k.w("wld")] if par == 0 else [])
            k.dve(lambda e, b=bk, p_=par: e.scalar_tensor_tensor(
                pow_ap("Q1", p_, rows=64), PB[b][0:64, 256:512], -1.0,
                SB[0:64, M_SU4:M_SU4 + 256], op0=OP.mult, op1=OP.mult),
                inc=(par == 1))
            k.setbg(bk, "dv")
        st_c["v_pq1"] = S["dv"].cur

    TROUNDS = [
        [("Q2", "P1", "Q1", 4), ("P2", "Q1", "P1", 5)],
        [("C1", "P1", "Q2", 6), ("Q4", "P2", "Q2", 4), ("P4", "Q2", "P2", 5)],
        [("Q8", "P4", "Q4", 4), ("P8", "Q4", "P4", 5)],
        [("C2", "P4", "Q8", 6), ("Q16", "P8", "Q8", 4), ("P16", "Q8", "P8", 5)],
        [("Q32", "P16", "Q16", 6), ("P12", "Q4", "P8", 4),
         ("P32", "Q16", "P16", 5)],
        [("C3", "P16", "Q32", 4), ("P48", "Q16", "P32", 5)],
    ]

    def emit_Tround(g, ri):
        v_prev = S["dv"].cur
        for dst, lh, rh, bk in TROUNDS[ri]:
            for par in range(2):
                for hd in range(H):
                    k.mm(PB[bk][0:64, 256 * par + 64 * hd:256 * par + 64 * hd + 64],
                         pow_ap(lh, par, hd), pow_ap(rh, par, hd),
                         start=(par == 0 and hd == 0),
                         stop=(par == 1 and hd == H - 1),
                         waits=[("dv", v_prev), k.bg(bk)] if par == 0 and hd == 0
                         else [], inc=(par == 1 and hd == H - 1))

    def emit_Tround_evac(g, ri):
        for dst, lh, rh, bk in TROUNDS[ri]:
            if dst.startswith("C"):
                a, b_ = {"C1": ("Q1", "Q2"), "C2": ("Q4", "Q8"),
                         "C3": ("Q16", "Q32")}[dst]
                k.dve(lambda e, d=dst, x=a, y=b_: e.tensor_tensor(
                    cls_full(d), cls_full(x), cls_full(y), op=OP.add),
                    waits=[k.w("pe")])
                k.dve(lambda e, d=dst, b=bk: e.tensor_tensor(
                    cls_full(d, rows=64), cls_full(d, rows=64),
                    PB[b][0:64, :], op=OP.add), inc=True)
            else:
                k.dve(lambda e, d=dst, b=bk: e.tensor_copy(
                    SB[0:64, o_cls[d]:o_cls[d] + 512], PB[b][0:64, :]),
                    waits=[k.w("pe")], inc=True)
            k.setbg(bk, "dv")

    def emit_Mprep(g):
        k.dve(lambda e: e.tensor_tensor(
            cls_full("TMP", rows=64), cls_full("P16", rows=64),
            cls_full("P32", rows=64), op=OP.add),
            waits=[("pe", st_c["v_Mmms"])])
        k.dve(lambda e: e.tensor_tensor(
            cls_full("CP3", rows=64), cls_full("TMP", rows=64),
            cls_full("P48", rows=64), op=OP.add))
        k.dve(lambda e: e.tensor_tensor(
            cls_full("CP3I", rows=64), cls_full("CP3", rows=64),
            SB[0:64, o_mI:o_mI + 512], op=OP.add))
        k.dve(lambda e: e.tensor_tensor(
            cls_full("C2I", rows=64), cls_full("C2", rows=64),
            SB[0:64, o_mI:o_mI + 512], op=OP.add))
        k.dve(lambda e: e.tensor_tensor(
            cls_full("TMP", rows=64), cls_full("P4", rows=64),
            cls_full("P8", rows=64), op=OP.add))
        k.dve(lambda e: e.tensor_tensor(
            cls_full("TMP", rows=64), cls_full("TMP", rows=64),
            cls_full("P12", rows=64), op=OP.add))
        k.dve(lambda e: e.tensor_tensor(
            cls_full("TMP", rows=64), cls_full("TMP", rows=64),
            SB[0:64, o_mI:o_mI + 512], op=OP.add), inc=True)
        st_c["v_Mprep"] = S["dv"].cur

    def emit_M1(g):
        v_p = st_c["v_Mprep"]
        for lh, rh, bk in (("CP3I", "C2", 4), ("C2I", "CP3", 5)):
            for par in range(2):
                for hd in range(H):
                    k.mm(PB[bk][0:64, 256 * par + 64 * hd:256 * par + 64 * hd + 64],
                         pow_ap(lh, par, hd), pow_ap(rh, par, hd),
                         start=(par == 0 and hd == 0),
                         stop=(par == 1 and hd == H - 1),
                         waits=[("dv", v_p), k.bg(bk)] if par == 0 and hd == 0
                         else [], inc=(par == 1 and hd == H - 1))
        st_c["v_M1"] = S["pe"].cur

    def emit_M1_evac(g):
        v_m1 = st_c["v_M1"]
        for dst, addend, bk in (("M2", "C3", 4), ("N2", "TMP", 5)):
            k.dve(lambda e, d=dst, a=addend, b=bk: e.tensor_tensor(
                cls_full(d, rows=64), cls_full(a, rows=64),
                PB[b][0:64, :], op=OP.add),
                waits=[("pe", v_m1)] if dst == "M2" else [], inc=True)
            k.setbg(bk, "dv")
        k.dve(lambda e: e.tensor_tensor(
            cls_full("M2", rows=64), cls_full("M2", rows=64),
            SB[0:64, o_mI:o_mI + 512], op=OP.add), inc=True)
        st_c["v_M1e"] = S["dv"].cur

    def emit_M2(g):
        v_e = st_c["v_M1e"]
        for par in range(2):
            for hd in range(H):
                k.mm(PB[6][0:64, 256 * par + 64 * hd:256 * par + 64 * hd + 64],
                     pow_ap("N2", par, hd), pow_ap("C1", par, hd),
                     start=(par == 0 and hd == 0),
                     stop=(par == 1 and hd == H - 1),
                     waits=[("dv", v_e), k.bg(6)] if par == 0 and hd == 0
                     else [], inc=(par == 1 and hd == H - 1))
        st_c["v_Mmms"] = S["pe"].cur

    def emit_M2_evac(g):
        gpar = g % 2
        k.dve(lambda e, gp2=gpar: e.tensor_tensor(
            SB[0:64, o_Tx + 512 * gp2:o_Tx + 512 * gp2 + 512],
            cls_full("M2", rows=64), PB[6][0:64, :], op=OP.add),
            waits=[("pe", st_c["v_Mmms"]), texp_guard[gpar]], inc=True)
        k.setbg(6, "dv")
        st_c["v_Tx"][gpar] = S["dv"].cur

    def emit_S_a(ck):
        g = ck // 2
        par = ck % 2
        for hd in range(H):
            bk = hd // 2
            col = 256 * (hd % 2)
            for kc2 in range(2):
                k.mm(PB[bk][0:64, col:col + 256],
                     qsl(o_kbF, ck, hd, kc2),
                     SB[:, o_SS[hd] + 256 * kc2:o_SS[hd] + 256 * kc2 + 256],
                     start=(kc2 == 0), stop=(kc2 == 1),
                     waits=([("dv", st_c["v_S_add"]), k.bg(bk)]
                            if (hd % 2 == 0 and kc2 == 0) else []),
                     inc=(kc2 == 1))
        v_ymm = S["pe"].cur
        for sg in range(2):
            k.dve(lambda e, s_=sg, sl=(ck % 4): e.tensor_tensor(
                SB[0:64, o_Ys + 512 * s_:o_Ys + 512 * s_ + 512],
                SB[0:64, o_vbT[sl] + 512 * s_:o_vbT[sl] + 512 * s_ + 512],
                PB[s_][0:64, 0:512], op=OP.subtract),
                waits=[("pe", v_ymm), ("ld", st_c["v_tmld"][g % 2])]
                if sg == 0 else [], inc=(sg == 1))
        st_c["v_ys"] = S["dv"].cur
        k.setbg(0, "dv", st_c["v_ys"])
        k.setbg(1, "dv", st_c["v_ys"])

    def emit_S_u(ck):
        g = ck // 2
        gpar = g % 2
        par = ck % 2
        v_src = st_c["v_ys"]
        wT = [("dv", st_c["v_Tx"][gpar])]
        for hd in range(H):
            bk = hd // 2
            col = 256 * (hd % 2)
            k.mm(PB[bk][0:64, col:col + 256],
                 texp_ap(gpar, par, hd),
                 SB[:, o_Ys + 256 * hd:o_Ys + 256 * hd + 256],
                 start=True, stop=True,
                 waits=([("dv", v_src), k.bg(bk)] + wT) if hd % 2 == 0
                 else [], inc=(hd % 2 == 1))
        v_app = S["pe"].cur
        if par == 1:
            texp_guard[gpar] = ("pe", v_app)
        for sg in range(2):
            k.dve(lambda e, s_=sg: e.tensor_copy(
                SB[0:64, o_Us + 512 * s_:o_Us + 512 * s_ + 512],
                PB[s_][0:64, 0:512]),
                waits=[("pe", v_app)] if sg == 0 else [], inc=(sg == 1))
        k.setbg(0, "dv")
        k.setbg(1, "dv")
        st_c["v_U"] = S["dv"].cur

    def emit_S_attn_o(ck):
        g = ck // 2
        par = ck % 2
        stl = ck // 8
        blkrel = (ck % 8) // 2
        sdta = g % 2
        for hd in range(H):
            for kc2 in range(2):
                k.mm(PB[7][0:64, 64 * hd:64 * hd + 64],
                     qsl(o_knF, ck, hd, kc2), qsl(o_qF, ck, hd, kc2),
                     start=(kc2 == 0), stop=(kc2 == 1),
                     waits=[k.bg(7)] if hd == 0 and kc2 == 0 else [],
                     inc=(kc2 == 1))
        v_ag = S["pe"].cur
        k.dve(lambda e: e.tensor_tensor(
            SB[0:64, o_att:o_att + 256], PB[7][0:64, 0:256],
            SB[0:64, M_IU4:M_IU4 + 256], op=OP.mult),
            waits=[("pe", v_ag)], inc=True)
        k.setbg(7, "dv")
        v_at = S["dv"].cur
        for hd in range(H):
            bk = 2 + hd // 2
            col = 256 * (hd % 2)
            for kc2 in range(2):
                k.mm(PB[bk][64 * par:64 * par + 64, col:col + 256],
                     qsl(o_qF, ck, hd, kc2),
                     SB[:, o_SS[hd] + 256 * kc2:o_SS[hd] + 256 * kc2 + 256],
                     start=(kc2 == 0), stop=False, tp=(0, 64 * par),
                     waits=([("dv", v_at), k.bg(bk)] if hd % 2 == 0 and kc2 == 0
                            else []))
            k.mm(PB[bk][64 * par:64 * par + 64, col:col + 256],
                 SB[:, o_att + 64 * hd:o_att + 64 * hd + 64],
                 SB[:, o_Us + 256 * hd:o_Us + 256 * hd + 256],
                 start=False, stop=True, tp=(0, 64 * par),
                 inc=(hd % 2 == 1))
        st_c["scan_qf_pe"] = S["pe"].cur
        v_omm = S["pe"].cur
        for hd in range(H):
            bk = 2 + hd // 2
            col = 256 * (hd % 2)
            k.dve(lambda e, b=bk, c=col, h=hd, p_=par, sd=sdta, br=blkrel,
                  sp2=(stl % 2): e.tensor_scalar(
                      SB[64 * p_:64 * p_ + 64,
                         o_dta + sd * DM + 256 * h:o_dta + sd * DM + 256 * h + 256],
                      PB[b][64 * p_:64 * p_ + 64, c:c + 256],
                      SB[64 * p_:64 * p_ + 64,
                         o_rqT + 16 * sp2 + 4 * br + h:
                         o_rqT + 16 * sp2 + 4 * br + h + 1],
                      None, op0=OP.mult),
                  waits=([("pe", v_omm), ("sp", dta_spill[sdta])] if hd == 0
                         else []),
                  inc=(hd == H - 1))
        st_c["v_oev"] = S["dv"].cur
        k.setbg(2, "dv", st_c["v_oev"])
        k.setbg(3, "dv", st_c["v_oev"])

    def emit_S_dS(ck):
        par = ck % 2
        for pair in range(2):
            for hh2 in range(2):
                hd = 2 * pair + hh2
                bk = 2 + hh2
                for kc2 in range(2):
                    k.mm(PB[bk][:, 256 * kc2:256 * kc2 + 256],
                         SB[:, o_knT[ck % 4] + 256 * hd + 128 * kc2:
                            o_knT[ck % 4] + 256 * hd + 128 * kc2 + 128],
                         SB[:, o_Us + 256 * hd:o_Us + 256 * hd + 256],
                         start=(kc2 == 0), stop=(kc2 == 1),
                         waits=[k.bg(bk)] if kc2 == 0 else [],
                         inc=(kc2 == 1))
            v_ds = S["pe"].cur
            for hh2 in range(2):
                hd = 2 * pair + hh2
                bk = 2 + hh2
                k.dve(lambda e, b=bk, h=hd: e.tensor_tensor(
                    SB[:, o_SS[h]:o_SS[h] + 512], SB[:, o_SS[h]:o_SS[h] + 512],
                    PB[b][:, :], op=OP.add),
                    waits=[("pe", v_ds)] if hh2 == 0 else [], inc=True)
                k.setbg(bk, "dv")
        st_c["v_S_add"] = S["dv"].cur
        knt_guard[ck % 4] = k.w("pe", S["pe"].cur)
        vbt_guard[ck % 4] = k.w("dv", st_c["v_ys"])

    pend_feats = []

    def emit_one_sum(a, b_, idx, tens, w0):
        if b_ is None:
            k.dve(lambda e, ta=tens[a], ix=idx: e.tensor_reduce(
                SB[:, o_fr + 4 * ix:o_fr + 4 * ix + 4],
                SB[:, ta:ta + DM].rearrange("p (hh dd) -> p hh dd", hh=4),
                axis=mybir.AxisListType.X, op=OP.add), waits=w0)
        else:
            pi = st_c["pi"]
            pb_ = o_pr[pi % 2]
            wgp = w0 + [("dv", st_c["pr_guard"][pi % 2])]
            k.gps(lambda e, ta=tens[a], tb=tens[b_], po=pb_: e.tensor_mul(
                SB[:, po:po + DM], SB[:, ta:ta + DM], SB[:, tb:tb + DM]),
                waits=wgp, inc=True)
            vg = S["gp"].cur
            k.dve(lambda e, po=pb_, ix=idx: e.tensor_reduce(
                SB[:, o_fr + 4 * ix:o_fr + 4 * ix + 4],
                SB[:, po:po + DM].rearrange("p (hh dd) -> p hh dd", hh=4),
                axis=mybir.AxisListType.X, op=OP.add),
                waits=[("gp", vg)], inc=True)
            st_c["pr_guard"][pi % 2] = S["dv"].cur
            st_c["pi"] = pi + 1

    def emit_feats_pre(g):
        B_ = g
        for dst, src in ((o_vTb, vT_sp), (o_lTb, lT_sp), (o_mTb, mT_sp)):
            k.dma("sync", SB[:, dst:dst + DM], src[B_],
                  waits=[k.w("dv"), k.w("gp")])
        v_tld = S["ld"].cur
        st_c["v_ftld"] = v_tld
        tens = {"l": o_lTb, "m": o_mTb, "v": o_vTb}
        first = True
        for a, idx in (("l", 0), ("m", 2), ("v", 6)):
            emit_one_sum(a, None, idx, tens, [("ld", v_tld)] if first else [])
            first = False

    def emit_feats_prod(g):
        tens = {"l": o_lTb, "m": o_mTb, "v": o_vTb}
        first = True
        for a, b_, idx in (("l", "l", 1), ("m", "m", 3), ("v", "v", 7),
                           ("l", "m", 8), ("l", "v", 10), ("m", "v", 12)):
            emit_one_sum(a, b_, idx, tens,
                         [("ld", st_c["v_ftld"])] if first else [])
            first = False

    def emit_feats_sums(g):
        B_ = g
        sdta = g % 2
        dta_o = o_dta + sdta * DM
        tens = {"l": o_lTb, "m": o_mTb, "d": dta_o, "v": o_vTb}
        v_need = st_c["v_oev"]
        fdv, fgp = True, True
        for a, b_, idx in (("d", None, 4), ("d", "d", 5), ("l", "d", 9),
                           ("m", "d", 11), ("d", "v", 13)):
            need = fdv if b_ is None else fgp
            emit_one_sum(a, b_, idx, tens, [("dv", v_need)] if need else [])
            if b_ is None:
                fdv = False
            else:
                fgp = False
        k.dve(lambda e: e.engine_nop(), inc=True)
        v_sums = S["dv"].cur
        fto = o_ft + 64 * (B_ % 2)
        for si, qi in ((0, 1), (2, 3), (4, 5), (6, 7)):
            k.dve(lambda e, s=si, f=fto: e.tensor_scalar(
                SB[:, f + 4 * s:f + 4 * s + 4],
                SB[:, o_fr + 4 * s:o_fr + 4 * s + 4], 1.0 / DH, None,
                op0=OP.mult),
                waits=[("dv", v_sums), ("pe", st_c["ft_guard"][B_ % 2])],
                inc=True)
            k.dve(lambda e, s=si: e.tensor_tensor(
                SB[:, o_sc1:o_sc1 + 4], SB[:, o_fr + 4 * s:o_fr + 4 * s + 4],
                SB[:, o_fr + 4 * s:o_fr + 4 * s + 4], op=OP.mult), inc=True)
            k.dve(lambda e: e.tensor_scalar(
                SB[:, o_sc2:o_sc2 + 4], SB[:, o_sc1:o_sc1 + 4],
                1.0 / (255.0 * DH), None, op0=OP.mult),
                waits=[k.w("dv")], inc=True)
            k.dve(lambda e, q=qi, f=fto: e.scalar_tensor_tensor(
                SB[:, f + 4 * q:f + 4 * q + 4],
                SB[:, o_fr + 4 * q:o_fr + 4 * q + 4], 1.0 / 255.0,
                SB[:, o_sc2:o_sc2 + 4], op0=OP.mult, op1=OP.subtract),
                waits=[k.w("dv")], inc=True)
        for ix in range(8, 14):
            k.dve(lambda e, x=ix, f=fto: e.tensor_scalar(
                SB[:, f + 4 * x:f + 4 * x + 4],
                SB[:, o_fr + 4 * x:o_fr + 4 * x + 4], 1.0 / DH, None,
                op0=OP.mult), inc=(ix == 13))
        v_ft = S["dv"].cur
        dta_spill[sdta] = k.dma("scalar", dT_sp[B_], SB[:, dta_o:dta_o + DM],
                                waits=[("dv", st_c["v_oev"])], sem="sp")
        pend_feats.append((B_, v_ft))

    def emit_feats_fin():
        while pend_feats:
            B_, v_ft = pend_feats.pop(0)
            k.transp(PB[7][0:56, 0:128],
                     SB[:, o_ft + 64 * (B_ % 2):o_ft + 64 * (B_ % 2) + 56],
                     SB[:, o_id:o_id + 128], waits=[("dv", v_ft), k.bg(7)],
                     inc=True)
            v_ftr = S["pe"].cur
            st_c["ft_guard"][B_ % 2] = v_ftr
            k.dve(lambda e, b_=B_: e.tensor_copy(
                SB[0:56, o_feats + 128 * b_:o_feats + 128 * b_ + 128],
                PB[7][0:56, 0:128]), waits=[("pe", v_ftr)], inc=True)
            k.setbg(7, "dv")

    for g in range(NG + 1):
        boundary = (g % 4 == 0)
        if g < NG and boundary:
            if g > 0:
                emit_feats_pre(g - 1)
                emit_feats_prod(g - 1)
                for par in range(2):
                    ck = 2 * (g - 1) + par
                    emit_S_a(ck)
                    emit_S_u(ck)
                    emit_S_attn_o(ck)
                    emit_S_dS(ck)
                    if par == 1:
                        emit_feats_sums(g - 1)
            emit_feats_fin()
            emit_T0(g, True)
            emit_T1(g)
            emit_T2(g)
            for ri in range(6):
                emit_Tround(g, ri)
                emit_Tround_evac(g, ri)
            emit_Mprep(g)
            emit_M1(g)
            emit_M1_evac(g)
            emit_M2(g)
            emit_M2_evac(g)
        elif g < NG:
            cka, ckb = 2 * (g - 1), 2 * (g - 1) + 1
            emit_T0(g, False)
            emit_S_a(cka)
            emit_feats_pre(g - 1)
            emit_T1(g)
            emit_S_u(cka)
            emit_T2(g)
            emit_Tround(g, 0)
            emit_S_attn_o(cka)
            emit_feats_prod(g - 1)
            emit_Tround_evac(g, 0)
            emit_Tround(g, 1)
            emit_S_dS(cka)
            emit_Tround_evac(g, 1)
            emit_Tround(g, 2)
            emit_S_a(ckb)
            emit_Tround_evac(g, 2)
            emit_Tround(g, 3)
            emit_S_u(ckb)
            emit_Tround_evac(g, 3)
            emit_Tround(g, 4)
            emit_S_attn_o(ckb)
            emit_Tround_evac(g, 4)
            emit_Tround(g, 5)
            emit_S_dS(ckb)
            emit_Tround_evac(g, 5)
            emit_Mprep(g)
            emit_M1(g)
            emit_feats_fin()
            emit_M1_evac(g)
            emit_M2(g)
            emit_M2_evac(g)
            emit_feats_sums(g - 1)
        else:
            emit_feats_pre(g - 1)
            emit_feats_prod(g - 1)
            for par in range(2):
                ck = 2 * (g - 1) + par
                emit_S_a(ck)
                emit_S_u(ck)
                emit_S_attn_o(ck)
                emit_S_dS(ck)
                if par == 1:
                    emit_feats_sums(g - 1)
            emit_feats_fin()

    if "feats_dbg" in k.dbg:
        fdbg = k.dram("feats_dbg", [56, L], "ExternalOutput")
        k.dma("sync", fdbg, SB[0:56, o_feats:o_feats + L],
              waits=[k.w("dv"), k.w("pe")], sem="sp")
    k.pop()
    if "C" in phases:
        k.flush()

    k.push()
    o_rw1 = k.take(9 * 2176)
    o_rw2 = k.take(17 * 16)
    o_rb1 = k.take(32)
    o_rb2 = k.take(16)
    o_hD = k.take(NCH * TS)
    o_h1 = k.take(2 * TS)
    o_lg = k.take(TS)
    o_lgT = k.take(16)
    o_sm = k.take(64)

    k.dma("sync", SB[:, o_rw1:o_rw1 + 8 * 2176].rearrange(
        "p (c m) -> p c m", c=8),
        rw1_d[0:1024].rearrange("(c p) m -> p c m", p=128), sem="wld",
        waits=[k.w("pe"), k.w("dv"), k.w("sp")])
    k.dma("sync", SB[0:64, o_rw1 + 8 * 2176:o_rw1 + 9 * 2176],
          rw1_d[1024:1088], sem="wld")
    k.dma("sync", SB[:, o_rw2:o_rw2 + 17 * 16].rearrange(
        "p (c n) -> p c n", c=17),
        rw2_d.rearrange("(c p) n -> p c n", p=128), sem="wld")
    k.dma("sync", SB[:, o_rb1:o_rb1 + 17],
          rb1_d.rearrange("(c p) -> p c", p=128), sem="wld", slow=True)
    k.dma("sync", SB[0:16, o_rb2:o_rb2 + 1], rb2_d.rearrange("(n o) -> n o", o=1),
          sem="wld")
    v_wD = S["wld"].cur

    for st in range(NST):
        tok0 = st * TS
        k.dma("sync", SB[:, o_hD:o_hD + NCH * TS].rearrange(
            "p (c t) -> p c t", c=NCH),
            h_sp.rearrange("c p l -> p c l")[:, :, tok0:tok0 + TS],
            waits=[k.w("pe"), k.w("ac"), k.w("dv")])
        v_hld = S["ld"].cur
        lg_hist = []
        for m in range(17):
            bank = m % 2
            for kc in range(9):
                if kc < 8:
                    lh = SB[:, o_rw1 + kc * 2176 + 128 * m:
                            o_rw1 + kc * 2176 + 128 * m + 128]
                    rh = SB[:, o_hD + kc * TS:o_hD + (kc + 1) * TS]
                else:
                    lh = SB[0:64, o_rw1 + 8 * 2176 + 128 * m:
                            o_rw1 + 8 * 2176 + 128 * m + 128]
                    rh = SB[0:64, o_feats + tok0:o_feats + tok0 + TS]
                k.mm(PB[bank][:, :], lh, rh, start=(kc == 0), stop=(kc == 8),
                     waits=([("ld", v_hld), ("wld", v_wD), k.bg(bank)]
                            if kc == 0 else []),
                     inc=(kc == 8))
            v_pm = S["pe"].cur
            db = m % 2
            va = k.act(SB[:, o_h1 + db * TS:o_h1 + (db + 1) * TS], PB[bank][:, :],
                       AF.Silu, bias=SB[:, o_rb1 + m:o_rb1 + m + 1],
                       waits=[("pe", v_pm)] +
                       ([("pe", lg_hist[m - 2])] if m >= 2 else []),
                       inc=True)
            k.setbg(bank, "ac", va)
            k.mm(PB[2][0:16, :], SB[:, o_rw2 + m * 16:o_rw2 + m * 16 + 16],
                 SB[:, o_h1 + db * TS:o_h1 + (db + 1) * TS],
                 start=(m == 0), stop=(m == 16),
                 waits=[("ac", va)] + ([k.bg(2)] if m == 0 else []),
                 inc=True)
            lg_hist.append(S["pe"].cur)
        v_lgm = S["pe"].cur
        va = k.act(SB[0:16, o_lg:o_lg + TS], PB[2][0:16, :], AF.Identity,
                   bias=SB[0:16, o_rb2:o_rb2 + 1], waits=[("pe", v_lgm)],
                   inc=True)
        k.setbg(2, "ac", va)
        for blk in range(4):
            B_ = 4 * st + blk
            k.transp(PB[3][0:128, 0:16], SB[0:16, o_lg + blk * 128:o_lg + blk * 128 + 128],
                     SB[0:16, o_id:o_id + 16], waits=[("ac", va), k.bg(3)], inc=True)
            v_t = S["pe"].cur
            lgT = SB[:, o_lgT:o_lgT + 16]
            k.dve(lambda e: e.tensor_copy(lgT, PB[3][:, 0:16]),
                  waits=[("pe", v_t)], inc=True)
            k.setbg(3, "dv")
            mx = SB[:, o_sm:o_sm + 4]
            ex = SB[:, o_sm + 16:o_sm + 32]
            s4 = SB[:, o_sm + 32:o_sm + 36]
            r4 = SB[:, o_sm + 48:o_sm + 52]
            k.dve(lambda e: e.tensor_tensor(mx, lgT[:, 0:4], lgT[:, 4:8],
                                            op=OP.max), waits=[k.w("dv")], inc=True)
            k.dve(lambda e: e.tensor_tensor(s4, lgT[:, 8:12], lgT[:, 12:16],
                                            op=OP.max), inc=True)
            k.dve(lambda e: e.tensor_tensor(mx, mx, s4, op=OP.max),
                  waits=[k.w("dv")], inc=True)
            for i in range(4):
                k.dve(lambda e, i=i: e.tensor_tensor(
                    ex[:, 4 * i:4 * i + 4], lgT[:, 4 * i:4 * i + 4], mx,
                    op=OP.subtract), waits=[k.w("dv")], inc=(i == 3))
            va2 = k.act(ex, ex, AF.Exp, waits=[k.w("dv")], inc=True)
            k.dve(lambda e: e.tensor_tensor(s4, ex[:, 0:4], ex[:, 4:8],
                                            op=OP.add), waits=[("ac", va2)],
                  inc=True)
            k.dve(lambda e: e.tensor_tensor(r4, ex[:, 8:12], ex[:, 12:16],
                                            op=OP.add), inc=True)
            k.dve(lambda e: e.tensor_tensor(s4, s4, r4, op=OP.add),
                  waits=[k.w("dv")], inc=True)
            k.dve(lambda e: e.reciprocal(r4, s4), waits=[k.w("dv")], inc=True)
            pdst = SB[:, o_pT + 16 * B_:o_pT + 16 * B_ + 16]
            for i in range(4):
                k.dve(lambda e, i=i, p_=pdst: e.tensor_tensor(
                    p_[:, 4 * i:4 * i + 4], ex[:, 4 * i:4 * i + 4], r4,
                    op=OP.mult), waits=[k.w("dv")], inc=(i == 3))
            k.dve(lambda e, p_=pdst: e.tensor_scalar(
                p_, p_, 1.0 - 4 * 0.01, 0.01, op0=OP.mult, op1=OP.add),
                waits=[k.w("dv")], inc=True)
    k.pop()
    k.pop()
    if "D" in phases:
        k.flush()

    k.push()
    o_A = k.take(NCH * DM)
    o_B = k.take(NCH * DM)
    o_C = k.take(NCH * DM)
    o_mixw = k.take(DM)
    o_onr = k.take(DM)
    o_xE = k.take(NCH * TS)
    o_onF = k.take(NCH * TS)
    o_x1 = k.take(NCH * TS)
    o_t4 = [k.take(DM) for _ in range(4)]
    o_mx = k.take(DM)
    o_mw2 = k.take(DM)
    o_t3 = k.take(DM)
    o_s4 = k.take(64)
    o_fb = k.take(16)
    o_fch = k.take(TS)
    o_och = k.take(2 * TS)
    o_mrE = k.take(2 * TS)
    mrow = SB[0:1, o_mrE:o_mrE + TS]
    rrow = SB[0:1, o_mrE + TS:o_mrE + 2 * TS]

    k.dma("sync", SB[:, o_A:o_A + NCH * DM].rearrange("p (c m) -> p c m", c=NCH),
          ffn_wT_d.rearrange("(c p) m -> p c m", p=128), sem="wld",
          waits=[k.w("pe"), k.w("dv"), k.w("ac"), k.w("sp")])
    k.dma("sync", SB[:, o_B:o_B + NCH * DM].rearrange("p (c m) -> p c m", c=NCH),
          ffn_a1_d.rearrange("(c p) m -> p c m", p=128), sem="wld")
    k.dma("sync", SB[:, o_mixw:o_mixw + DM], mixw_d, sem="wld")
    k.dma("sync", SB[:, o_onr:o_onr + DM], onormr_d, sem="wld")
    k.dma("sync", SB[:, o_fb:o_fb + 8].rearrange("p c -> p c"),
          ffn_b_d.rearrange("(c p) -> p c", p=128), sem="wld", slow=True)
    v_wE = S["wld"].cur

    def mm1024(dst_off, lhs_off, rhs_off, wv, lhT_colmaj=True):
        for m in range(NCH):
            bank = m % 2
            for nn in range(2):
                for kc in range(NCH):
                    k.mm(PB[bank][:, :],
                         SB[:, lhs_off + kc * DM + 128 * m:
                            lhs_off + kc * DM + 128 * m + 128],
                         SB[:, rhs_off + kc * DM + 512 * nn:
                            rhs_off + kc * DM + 512 * nn + 512],
                         start=(kc == 0), stop=(kc == NCH - 1),
                         waits=([("wld", wv), k.bg(bank)] if kc == 0 else []),
                         inc=(kc == NCH - 1))
                v_ = S["pe"].cur
                k.dve(lambda e, b=bank, mm_=m, n_=nn, d=dst_off: e.tensor_copy(
                    SB[:, d + mm_ * DM + 512 * n_:d + mm_ * DM + 512 * n_ + 512],
                    PB[b][:, :]), waits=[("pe", v_)], inc=True)
                k.setbg(bank, "dv")

    mm1024(o_C, o_A, o_B, v_wE)
    v_T1 = S["dv"].cur
    for cc in range(NCH):
        for m in range(NCH):
            bank = 2 + (m % 2)
            k.transp(PB[bank][:, 0:128],
                     SB[:, o_C + cc * DM + 128 * m:o_C + cc * DM + 128 * m + 128],
                     SB[:, o_id:o_id + 128],
                     waits=[("dv", v_T1), k.bg(bank)] if cc == 0 and m < 2 else
                     [k.bg(bank)], inc=True)
            v_t = S["pe"].cur
            k.dve(lambda e, b=bank, c_=cc, m_=m: e.tensor_copy(
                SB[:, o_A + m_ * DM + 128 * c_:o_A + m_ * DM + 128 * c_ + 128],
                PB[b][:, 0:128]), waits=[("pe", v_t)], inc=True)
            k.setbg(bank, "dv")
    v_T1T = S["dv"].cur
    k.dma("sync", SB[:, o_B:o_B + NCH * DM].rearrange("p (c m) -> p c m", c=NCH),
          ffn_a2_d.rearrange("(c p) m -> p c m", p=128), sem="wld",
          waits=[("dv", v_T1T)])
    v_wE2 = S["wld"].cur
    mm1024(o_C, o_A, o_B, v_wE2)
    k.dma("sync", SB[:, o_A:o_A + NCH * DM].rearrange("p (c m) -> p c m", c=NCH),
          wo_d.rearrange("(c p) m -> p c m", p=128), sem="wld",
          waits=[k.w("dv")])
    v_wo = S["wld"].cur
    och_spill = [0, 0]

    onF_o = [o_onF, o_B]
    xE_o = [o_xE, o_B + NCH * TS]
    v_onF_p = [0, 0]
    v_x1_p = [0, 0]
    v_wp_p = [0, 0]
    v_xld_p = [0, 0]
    st_e = {"v_tr": 0}

    def e_front_load(st):
        p = st % 2
        tok0 = st * TS
        k.dma("sync", SB[:, xE_o[p]:xE_o[p] + NCH * TS].rearrange(
            "p (c t) -> p c t", c=NCH),
            x_fm.rearrange("(c p) l -> p c l", p=128)[:, :, tok0:tok0 + TS],
            waits=[("dv", v_x1_p[p])] +
            ([k.w("dv"), k.w("pe"), k.w("ac"), k.w("sp")] if st < 2 else []))
        v_xld_p[p] = S["ld"].cur

    def e_chain(st, blk):
        B_ = 4 * st + blk
        for ti, src in enumerate((lT_sp, mT_sp, dT_sp, vT_sp)):
            k.dma("sync", SB[:, o_t4[ti]:o_t4[ti] + DM], src[B_],
                  waits=[k.w("dv")] if ti == 0 else [])
        v_tld = S["ld"].cur
        for hd in range(H):
            for i in range(4):
                pcol = SB[:, o_pT + 16 * B_ + 4 * i + hd:
                          o_pT + 16 * B_ + 4 * i + hd + 1]
                src = SB[:, o_t4[i] + 256 * hd:o_t4[i] + 256 * hd + 256]
                dst = SB[:, o_mx + 256 * hd:o_mx + 256 * hd + 256]
                if i == 0:
                    k.dve(lambda e, s_=src, d_=dst, p_=pcol: e.tensor_scalar(
                        d_, s_, p_, None, op0=OP.mult),
                        waits=[("ld", v_tld)] if hd == 0 else [])
                else:
                    k.dve(lambda e, s_=src, d_=dst, p_=pcol:
                          e.scalar_tensor_tensor(d_, s_, p_, d_,
                                                 op0=OP.mult, op1=OP.add),
                          inc=(hd == H - 1 and i == 3))
        v_mx = S["dv"].cur
        k.dve(lambda e: e.tensor_tensor(SB[:, o_t3:o_t3 + DM],
                                        SB[:, o_mx:o_mx + DM],
                                        SB[:, o_mx:o_mx + DM], op=OP.mult),
              waits=[("dv", v_mx), ("pe", st_e["v_tr"])])
        k.dve(lambda e: e.tensor_reduce(
            SB[:, o_s4:o_s4 + 4],
            SB[:, o_t3:o_t3 + DM].rearrange("p (hh dd) -> p hh dd", hh=4),
            axis=mybir.AxisListType.X, op=OP.add), waits=[k.w("dv")],
            inc=True)
        va = k.act(SB[:, o_s4 + 8:o_s4 + 12], SB[:, o_s4:o_s4 + 4], AF.Sqrt,
                   scale=1.0 / DH, bias=1e-5, waits=[k.w("dv")], inc=True)
        k.dve(lambda e: e.reciprocal(SB[:, o_s4 + 16:o_s4 + 20],
                                     SB[:, o_s4 + 8:o_s4 + 12]),
              waits=[("ac", va)], inc=True)
        k.dve(lambda e: e.tensor_tensor(SB[:, o_mw2:o_mw2 + DM],
                                        SB[:, o_mx:o_mx + DM],
                                        SB[:, o_mixw:o_mixw + DM],
                                        op=OP.mult), inc=True)
        k.dve(lambda e: e.tensor_tensor(SB[:, o_t3:o_t3 + DM],
                                        SB[:, o_mw2:o_mw2 + DM],
                                        SB[:, o_mw2:o_mw2 + DM],
                                        op=OP.mult), waits=[k.w("dv")])
        k.dve(lambda e: e.tensor_reduce(
            SB[:, o_s4 + 24:o_s4 + 28],
            SB[:, o_t3:o_t3 + DM].rearrange("p (hh dd) -> p hh dd", hh=4),
            axis=mybir.AxisListType.X, op=OP.add), waits=[k.w("dv")],
            inc=True)
        k.dve(lambda e: e.tensor_tensor(
            SB[:, o_s4 + 32:o_s4 + 36], SB[:, o_s4 + 16:o_s4 + 20],
            SB[:, o_s4 + 16:o_s4 + 20], op=OP.mult), waits=[k.w("dv")],
            inc=True)
        k.dve(lambda e: e.tensor_tensor(
            SB[:, o_s4 + 40:o_s4 + 44], SB[:, o_s4 + 24:o_s4 + 28],
            SB[:, o_s4 + 32:o_s4 + 36], op=OP.mult), waits=[k.w("dv")],
            inc=True)
        va = k.act(SB[:, o_s4 + 44:o_s4 + 48], SB[:, o_s4 + 40:o_s4 + 44],
                   AF.Sqrt, scale=1.0 / DH, bias=1e-5, waits=[k.w("dv")],
                   inc=True)
        k.dve(lambda e: e.reciprocal(SB[:, o_s4 + 48:o_s4 + 52],
                                     SB[:, o_s4 + 44:o_s4 + 48]),
              waits=[("ac", va)], inc=True)
        k.dve(lambda e: e.tensor_tensor(
            SB[:, o_s4 + 52:o_s4 + 56], SB[:, o_s4 + 16:o_s4 + 20],
            SB[:, o_s4 + 48:o_s4 + 52], op=OP.mult), waits=[k.w("dv")],
            inc=True)
        k.dve(lambda e: e.tensor_tensor(SB[:, o_t3:o_t3 + DM],
                                        SB[:, o_mw2:o_mw2 + DM],
                                        SB[:, o_onr:o_onr + DM],
                                        op=OP.mult), waits=[k.w("dv")])
        for hd in range(H):
            k.dve(lambda e, h=hd: e.tensor_scalar(
                SB[:, o_t3 + 256 * h:o_t3 + 256 * h + 256],
                SB[:, o_t3 + 256 * h:o_t3 + 256 * h + 256],
                SB[:, o_s4 + 52 + h:o_s4 + 53 + h], None, op0=OP.mult),
                waits=[k.w("dv")] if hd == 0 else [], inc=(hd == H - 1))
        st_e["v_on"] = S["dv"].cur

    def e_transp(st, blk):
        p = st % 2
        v_on = st_e["v_on"]
        for cc in range(NCH):
            bank = 6 + (cc % 2)
            k.transp(PB[bank][:, 0:128],
                     SB[:, o_t3 + 128 * cc:o_t3 + 128 * cc + 128],
                     SB[:, o_id:o_id + 128],
                     waits=[("dv", v_on), k.bg(bank)], inc=True)
            v_t = S["pe"].cur
            k.dve(lambda e, b=bank, c_=cc, bl=blk, p_=p: e.tensor_copy(
                SB[:, onF_o[p_] + c_ * TS + 128 * bl:
                   onF_o[p_] + c_ * TS + 128 * bl + 128],
                PB[b][:, 0:128]),
                waits=[("pe", v_t)] +
                ([("pe", v_wp_p[p])] if cc == 0 and blk == 0 else []),
                inc=True)
            k.setbg(bank, "dv")
        st_e["v_tr"] = S["pe"].cur
        if blk == 3:
            v_onF_p[p] = S["dv"].cur

    def e_back_wo(stb, ms):
        p = stb % 2
        for m in ms:
            bank = m % 4
            for kc in range(NCH):
                k.mm(PB[bank][:, :],
                     SB[:, o_A + kc * DM + 128 * m:o_A + kc * DM + 128 * m + 128],
                     SB[:, onF_o[p] + kc * TS:onF_o[p] + (kc + 1) * TS],
                     start=(kc == 0), stop=(kc == NCH - 1),
                     waits=([("dv", v_onF_p[p]), ("wld", v_wo), k.bg(bank)]
                            if kc == 0 else []),
                     inc=(kc == NCH - 1))
            v_ = S["pe"].cur
            k.dve(lambda e, b=bank, m_=m, p_=p: e.scalar_tensor_tensor(
                SB[:, o_x1 + m_ * TS:o_x1 + (m_ + 1) * TS], PB[b][:, :],
                modcol(2, m_), SB[:, xE_o[p_] + m_ * TS:xE_o[p_] + (m_ + 1) * TS],
                op0=OP.mult, op1=OP.add),
                waits=[("pe", v_)] + ([("ld", v_xld_p[p])] if m == ms[0] else []),
                inc=True)
            k.setbg(bank, "dv")
        if ms[-1] == NCH - 1:
            v_x1_p[p] = S["dv"].cur
            st_e["v_x1"] = S["dv"].cur

    def e_back_ln2(stb):
        p = stb % 2
        v_x1 = st_e["v_x1"]
        for c in range(NCH):
            k.mm(PB[4][0:1, :], SB[:, o_onesc:o_onesc + 1],
                 SB[:, o_x1 + c * TS:o_x1 + (c + 1) * TS],
                 start=(c == 0), stop=(c == NCH - 1),
                 waits=[("dv", v_x1), k.bg(4)] if c == 0 else [],
                 inc=(c == NCH - 1))
        pe_hist2 = []
        for c in range(NCH):
            wv_ = ([("dv", v_x1), ("dv", st_e["v_outlast"])] if c == 0
                   else [k.w("pe", pe_hist2[-1])])
            va = k.act(SB[:, o_fch:o_fch + TS],
                       SB[:, o_x1 + c * TS:o_x1 + (c + 1) * TS], AF.Square,
                       waits=wv_, inc=True)
            k.mm(PB[5][0:1, :], SB[:, o_onesc:o_onesc + 1], SB[:, o_fch:o_fch + TS],
                 start=(c == 0), stop=(c == NCH - 1),
                 waits=[("ac", va)] + ([k.bg(5)] if c == 0 else []), inc=True)
            pe_hist2.append(S["pe"].cur)
        v_st2 = S["pe"].cur
        k.dve(lambda e: e.tensor_scalar(mrow, PB[4][0:1, :], 1.0 / DM, None,
                                        op0=OP.mult), waits=[("pe", v_st2)])
        k.dve(lambda e: e.tensor_tensor(rrow, mrow, mrow, op=OP.mult))
        k.dve(lambda e: e.scalar_tensor_tensor(
            rrow, PB[5][0:1, :], 1.0 / DM, rrow, op0=OP.mult, op1=OP.subtract),
            inc=True)
        v_var2 = S["dv"].cur
        k.setbg(4, "dv", v_var2)
        k.setbg(5, "dv", v_var2)
        va = k.act(rrow, rrow, AF.Sqrt, bias=1e-5, waits=[("dv", v_var2)],
                   inc=True)
        k.dve(lambda e: e.reciprocal(rrow, rrow), waits=[("ac", va)], inc=True)
        v_r2 = S["dv"].cur
        k.mm(PB[4][:, :], SB[0:1, o_onesr:o_onesr + 128], mrow,
             start=True, stop=True, waits=[("dv", v_r2), k.bg(4)])
        k.mm(PB[5][:, :], SB[0:1, o_onesr:o_onesr + 128], rrow,
             start=True, stop=True, waits=[k.bg(5)], inc=True)
        v_bc2 = S["pe"].cur
        for c in range(NCH):
            x1c = SB[:, o_x1 + c * TS:o_x1 + (c + 1) * TS]
            hc = SB[:, onF_o[p] + c * TS:onF_o[p] + (c + 1) * TS]
            k.dve(lambda e, x=x1c, h=hc: e.tensor_tensor(h, x, PB[4][:, :],
                                                         op=OP.subtract),
                  waits=[("pe", v_bc2)] if c == 0 else [])
            k.dve(lambda e, h=hc: e.tensor_tensor(h, h, PB[5][:, :], op=OP.mult))
            k.dve(lambda e, h=hc, c=c: e.scalar_tensor_tensor(
                h, h, modcol(4, c), h, op0=OP.mult, op1=OP.add))
            k.dve(lambda e, h=hc, c=c: e.tensor_scalar(
                h, h, modcol(3, c), None, op0=OP.add), inc=True)
        st_e["v_h2"] = S["dv"].cur
        k.setbg(4, "dv", st_e["v_h2"])
        k.setbg(5, "dv", st_e["v_h2"])

    def e_back_ffn(stb, ms):
        p = stb % 2
        tok0 = stb * TS
        v_h2 = st_e["v_h2"]
        for m in ms:
            bank = m % 4
            for kc in range(NCH):
                k.mm(PB[bank][:, :],
                     SB[:, o_C + kc * DM + 128 * m:o_C + kc * DM + 128 * m + 128],
                     SB[:, onF_o[p] + kc * TS:onF_o[p] + (kc + 1) * TS],
                     start=(kc == 0), stop=(kc == NCH - 1),
                     waits=[("dv", v_h2), k.bg(bank)] if kc == 0 else [],
                     inc=(kc == NCH - 1))
            v_ = S["pe"].cur
            va = k.act(SB[:, o_fch:o_fch + TS], PB[bank][:, :], AF.Tanh,
                       bias=SB[:, o_fb + m:o_fb + m + 1],
                       waits=[("pe", v_), k.w("dv")], inc=True)
            k.setbg(bank, "ac", va)
            db = m % 2
            k.dve(lambda e, m_=m, d_=db: e.scalar_tensor_tensor(
                SB[:, o_och + d_ * TS:o_och + (d_ + 1) * TS],
                SB[:, o_fch:o_fch + TS], modcol(5, m_),
                SB[:, o_x1 + m_ * TS:o_x1 + (m_ + 1) * TS],
                op0=OP.mult, op1=OP.add),
                waits=[("ac", va), ("sp", och_spill[db])], inc=True)
            v_out = S["dv"].cur
            och_spill[db] = k.dma(
                "sync",
                out_fm.rearrange("(c p) l -> p c l", p=128)[:, m:m + 1,
                                                            tok0:tok0 + TS],
                SB[:, o_och + db * TS:o_och + (db + 1) * TS].unsqueeze(1),
                waits=[("dv", v_out)], sem="sp")
        if ms[-1] == NCH - 1:
            v_wp_p[p] = S["pe"].cur
            st_e["v_outlast"] = S["dv"].cur

    st_e["v_x1"] = S["dv"].cur
    st_e["v_outlast"] = S["dv"].cur
    for st in range(NST + 1):
        back = None
        if st > 0:
            stb = st - 1
            back = [lambda s=stb: e_back_wo(s, (0, 1, 2, 3)),
                    lambda s=stb: e_back_wo(s, (4, 5, 6, 7)),
                    lambda s=stb: e_back_ln2(s),
                    lambda s=stb: e_back_ffn(s, (0, 1, 2, 3, 4, 5, 6, 7))]
        if st < NST:
            e_front_load(st)
            for blk in range(4):
                e_chain(st, blk)
                if back is not None:
                    back[blk]()
                e_transp(st, blk)
        else:
            for pc in back:
                pc()
    k.pop()
    if "E" in phases:
        k.flush()

    k.op("sync", lambda e: e.nop(),
         waits=[k.w("sp"), k.w("ld"), k.w("wld")])
    k.flush()
    return k


def build_C(k, env):
    raise NotImplementedError


def build_D(k, env):
    raise NotImplementedError


def build_E(k, env):
    raise NotImplementedError



def make_consts():
    ident = np.eye(128, dtype=np.float32)
    onesrow = np.ones((1, 128), np.float32)
    onescol = np.ones((128, 1), np.float32)
    indrow = np.zeros((128, 512), np.float32)
    indhead = np.zeros((128, 512), np.float32)
    for h in range(4):
        indrow[32 * h, 128 * h:128 * (h + 1)] = 1.0
        indhead[:, 128 * h + 32 * h] = 1.0
    masks = np.zeros((128, 1536), np.float32)
    sl = np.tril(np.ones((CC, CC), np.float32), -1)
    su = np.triu(np.ones((CC, CC), np.float32), 1)
    iu = np.triu(np.ones((CC, CC), np.float32), 0)
    i64 = np.eye(CC, dtype=np.float32)
    for r in range(2):
        rs = slice(r * CC, (r + 1) * CC)
        masks[rs, 0:CC] = sl
        masks[rs, CC:2 * CC] = su
        masks[rs, 2 * CC:3 * CC] = iu
        masks[rs, 3 * CC:4 * CC] = i64
        for t in range(4):
            masks[rs, 256 + t * CC:256 + (t + 1) * CC] = sl
            masks[rs, 512 + t * CC:512 + (t + 1) * CC] = su
            masks[rs, 768 + t * CC:768 + (t + 1) * CC] = iu
        for t in range(8):
            masks[rs, 1024 + t * CC:1024 + (t + 1) * CC] = i64
    return dict(ident=ident, onesrow=onesrow, onescol=onescol,
                indrow=indrow, indhead=indhead, masks=masks)


def make_inputs(d, b):
    f32 = np.float32
    c = make_consts()
    x = np.ascontiguousarray(np.asarray(d["L0_attn_hidden_states"][b]).T).astype(f32)
    wbp = np.zeros((DM, 128), f32)
    for h in range(4):
        wbp[:, 32 * h] = np.asarray(d["L0_attn_Wb"])[:, h]
    convw = np.concatenate([
        np.asarray(d["L0_attn_conv_q_w"]), np.asarray(d["L0_attn_conv_k_w"]),
        np.asarray(d["L0_attn_conv_v_w"]), np.asarray(d["L0_attn_local_w"]),
        np.asarray(d["L0_attn_mid_w"])], axis=1).astype(f32)
    rw1 = np.zeros((1088, 2176), f32)
    rw1[:1080, :2160] = np.asarray(d["L0_attn_r_w1"])
    rb1 = np.zeros((2176,), f32)
    rb1[:2160] = np.asarray(d["L0_attn_r_b1"])
    rw2 = np.asarray(d["L0_attn_r_w2"]).astype(f32)
    rb2 = np.asarray(d["L0_attn_r_b2"]).astype(f32)
    perm = [4 * h + i for i in range(4) for h in range(4)]
    rw2p = np.zeros((2176, 16), f32)
    rw2p[:2160] = rw2[:, perm]
    rb2p = np.ascontiguousarray(rb2[perm])
    mixw = np.asarray(d["L0_attn_mix_w"]).astype(f32).reshape(-1)
    onormt = np.tile(np.asarray(d["L0_attn_onorm_w"]).astype(f32), 4)
    return {
        "x_fm": x,
        "wq": np.asarray(d["L0_attn_Wq"]).astype(f32),
        "wk": np.asarray(d["L0_attn_Wk"]).astype(f32),
        "wv": np.asarray(d["L0_attn_Wv"]).astype(f32),
        "wbp": wbp,
        "convw": convw,
        "rw1": rw1, "rb1": rb1, "rw2p": rw2p, "rb2p": rb2p,
        "onorm_r": np.broadcast_to(onormt, (128, DM)).copy(),
        "mixw_r": np.broadcast_to(mixw, (128, DM)).copy(),
        "wo": np.asarray(d["L0_attn_Wo"]).astype(f32),
        "ffn_wT": np.ascontiguousarray(np.asarray(d["L0_ffn_weight"]).T).astype(f32),
        "ffn_a1": np.asarray(d["L0_ffn_aux_weight1"]).astype(f32),
        "ffn_a2": np.asarray(d["L0_ffn_aux_weight2"]).astype(f32),
        "ffn_b": np.asarray(d["L0_ffn_bias"]).astype(f32),
        "cvec": np.asarray(d["L0_c"][b]).astype(f32),
        "wmod": np.asarray(d["L0_wmod"]).astype(f32),
        **c,
    }


def kernel(**inputs):
    kk = build_program()
    in_maps = [make_inputs(inputs, b) for b in range(8)]
    res = run_bass_kernel_spmd(kk.nc, in_maps, core_ids=list(range(8)))
    out = np.stack([np.asarray(r["out_fm"]).T for r in res.results])
    return out.astype(np.float32)


# version 05
# speedup vs baseline: 1.1516x; 1.0093x over previous
import numpy as np
import concourse.bass as bass
import concourse.mybir as mybir
from concourse.bass_utils import run_bass_kernel_spmd
from concourse.alu_op_type import AluOpType as OP

F32 = mybir.dt.float32
AF = mybir.ActivationFunctionType

L = 4096
DM = 1024
H = 4
DH = 256
TS = 512
NST = L // TS
NCH = DM // 128
CC = 64
NCK = L // CC
NBLK = L // 128
VH = 24

ENG = ("tensor", "vector", "scalar", "gpsimd", "sync")


class Sem:
    def __init__(self, h):
        self.h = h
        self.cur = 0


class K:

    def __init__(self, debug=()):
        self.nc = bass.Bass("TRN2", target_bir_lowering=False)
        self.dbg = set(debug)
        self.steps = []
        self.sems = {}
        self.SBN = 53200
        self.SB = self.nc.alloc_sbuf_tensor("SBP", [128, self.SBN], F32).ap()
        self.PB = [self.nc.alloc_psum_tensor(f"PB{i}", [128, 512], F32).ap()
                   for i in range(8)]
        self.off = 0
        self.marks = []
        for s in ("ld", "wld", "pe", "dv", "ac", "gp", "sp"):
            self.sems[s] = Sem(self.nc.alloc_semaphore(s))
        self.bguard = [("dv", 0)] * 8

    def take(self, n, align=16):
        o = (self.off + align - 1) // align * align
        self.off = o + n
        assert self.off <= self.SBN, f"SBUF overflow {self.off} > {self.SBN}"
        return o

    def push(self):
        self.marks.append(self.off)

    def pop(self):
        self.off = self.marks.pop()

    def dram(self, name, shape, kind=None):
        if name in self.dbg:
            kind = "ExternalOutput"
        if kind is None:
            return self.nc.dram_tensor(name, shape, F32).ap()
        return self.nc.dram_tensor(name, shape, F32, kind=kind).ap()

    def w(self, name, val=None):
        s = self.sems[name]
        return (s, s.cur if val is None else val)

    def bg(self, i):
        n, v = self.bguard[i]
        return (self.sems[n], v)

    def setbg(self, i, name, val=None):
        self.bguard[i] = (name, self.sems[name].cur if val is None else val)

    def op(self, engine, fn, waits=(), inc=None):
        ws = []
        for t in waits:
            if isinstance(t[0], Sem):
                ws.append(t)
            else:
                ws.append((self.sems[t[0]], t[1]))
        self.steps.append((engine, fn, tuple(ws), inc))
        if inc is not None:
            inc[0].cur += inc[1]

    def flush(self):
        steps, self.steps = self.steps, []
        if not steps:
            return

        def emitter(engine):
            def run(eng):
                for e, fn, waits, inc in steps:
                    if e != engine:
                        continue
                    for s, v in waits:
                        eng.wait_ge(s.h, v)
                    inst = fn(eng)
                    if inc is not None:
                        s, amt = inc
                        assert inst is not None
                        inst.then_inc(s.h, amt)
            return run

        used = [e for e in ENG if any(s[0] == e for s in steps)]
        with self.nc.Block() as block:
            for e in used:
                getattr(block, e)(emitter(e))

    def dma(self, q, out, in_, waits=(), sem="ld", slow=False):
        s = self.sems[sem]
        if slow:
            self.op(q, lambda e, o=out, i=in_: e.dma_start(
                out=o, in_=i, allow_slow_non_contiguous=True),
                waits=waits, inc=(s, 16))
        else:
            self.op(q, lambda e, o=out, i=in_: e.dma_start(out=o, in_=i),
                    waits=waits, inc=(s, 16))
        return s.cur

    def mm(self, out, lhsT, rhs, start, stop, waits=(), inc=False, tp=None):
        s = self.sems["pe"]
        self.op("tensor",
                lambda e, o=out, l=lhsT, r=rhs, st=start, sp=stop, t=tp:
                e.matmul(o, l, r, start=st, stop=sp, tile_position=t),
                waits=waits, inc=(s, 1) if inc else None)
        return s.cur

    def transp(self, out, in_, ident, waits=(), inc=False):
        s = self.sems["pe"]
        self.op("tensor", lambda e, o=out, i=in_, d=ident: e.transpose(o, i, d),
                waits=waits, inc=(s, 1) if inc else None)
        return s.cur

    def dve(self, fn, waits=(), inc=False):
        s = self.sems["dv"]
        self.op("vector", fn, waits=waits, inc=(s, 1) if inc else None)
        return s.cur

    def gps(self, fn, waits=(), inc=False):
        s = self.sems["gp"]
        self.op("gpsimd", fn, waits=waits, inc=(s, 1) if inc else None)
        return s.cur

    def act(self, out, in_, func, bias=None, scale=None, waits=(), inc=False):
        s = self.sems["ac"]
        kw = {}
        if bias is not None:
            kw["bias"] = bias
        if scale is not None:
            kw["scale"] = scale
        self.op("scalar", lambda e, o=out, i=in_, f=func, kk=dict(kw):
                e.activation(o, i, f, **kk),
                waits=waits, inc=(s, 1) if inc else None)
        return s.cur


def build_program(debug=(), phases=("P0", "B", "C", "D", "E")):
    k = K(debug)
    SB, PB = k.SB, k.PB
    S = k.sems

    EIn = "ExternalInput"
    x_fm = k.dram("x_fm", [DM, L], EIn)
    wq_d = k.dram("wq", [DM, DM], EIn)
    wk_d = k.dram("wk", [DM, DM], EIn)
    wv_d = k.dram("wv", [DM, DM], EIn)
    wbp_d = k.dram("wbp", [DM, 128], EIn)
    convw_d = k.dram("convw", [DM, 44], EIn)
    rw1_d = k.dram("rw1", [1088, 2176], EIn)
    rb1_d = k.dram("rb1", [2176], EIn)
    rw2_d = k.dram("rw2p", [2176, 16], EIn)
    rb2_d = k.dram("rb2p", [16], EIn)
    onormr_d = k.dram("onorm_r", [128, DM], EIn)
    mixw_d = k.dram("mixw_r", [128, DM], EIn)
    wo_d = k.dram("wo", [DM, DM], EIn)
    ffn_wT_d = k.dram("ffn_wT", [DM, DM], EIn)
    ffn_a1_d = k.dram("ffn_a1", [DM, DM], EIn)
    ffn_a2_d = k.dram("ffn_a2", [DM, DM], EIn)
    ffn_b_d = k.dram("ffn_b", [DM], EIn)
    cvec_d = k.dram("cvec", [DM], EIn)
    wmod_d = k.dram("wmod", [DM, 6 * DM], EIn)
    ident_d = k.dram("ident", [128, 128], EIn)
    onesrow_d = k.dram("onesrow", [1, 128], EIn)
    onescol_d = k.dram("onescol", [128, 1], EIn)
    indrow_d = k.dram("indrow", [128, 512], EIn)
    indhead_d = k.dram("indhead", [128, 512], EIn)
    masks_d = k.dram("masks", [128, 1536], EIn)

    h_sp = k.dram("h_sp", [NCH, 128, L])
    q_sp = k.dram("q_sp", [NCH, 128, L])
    kn_sp = k.dram("kn_sp", [NCH, 128, L])
    kb_sp = k.dram("kb_sp", [NCH, 128, L])
    knT_sp = k.dram("knT_sp", [NCK, 64, DM])
    vbT_sp = k.dram("vbT_sp", [NCK, 64, DM])
    vT_sp = k.dram("vT_sp", [NBLK, 128, DM])
    lT_sp = k.dram("lT_sp", [NBLK, 128, DM])
    mT_sp = k.dram("mT_sp", [NBLK, 128, DM])
    dT_sp = k.dram("dT_sp", [NBLK, 128, DM])
    rqT_sp = k.dram("rqT_sp", [NBLK, 128, 4])
    mod_sp = k.dram("mod_sp", [6 * DM])
    wp_sp = k.dram("wp_sp", [DM, DM])
    out_fm = k.dram("out_fm", [DM, L], "ExternalOutput")

    o_ceps = k.take(16)
    k.nc.const_aps.register(F32, 1e-5, SB[:, o_ceps:o_ceps + 1])
    k.nc.const_aps.register(F32, 1e-12, SB[:, o_ceps + 1:o_ceps + 2])
    k.dve(lambda e: e.memset(SB[:, o_ceps:o_ceps + 1], 1e-5))
    k.dve(lambda e: e.memset(SB[:, o_ceps + 1:o_ceps + 2], 1e-12), inc=True)
    o_mod = k.take(48)
    o_id = k.take(128)
    o_onesr = k.take(128)
    o_onesc = k.take(16)
    o_indr = k.take(512)
    o_indh = k.take(512)
    o_msk = k.take(256)
    o_cw = k.take(NCH * 44)

    MSL, MSU, MIU, MI64 = o_msk, o_msk + CC, o_msk + 2 * CC, o_msk + 3 * CC

    def modcol(v, c):
        return SB[:, o_mod + 8 * v + c:o_mod + 8 * v + c + 1]

    k.push()
    o_sc = k.take(16)
    o_wmb = k.take(2 * 512)
    o_mrow = k.take(512)

    k.dma("sync", SB[:, o_sc:o_sc + 8], cvec_d.rearrange("(c p) -> p c", p=128),
          slow=True)
    v_c = S["ld"].cur
    for dst, src, wdt in ((o_id, ident_d, 128), (o_onesr, onesrow_d, 128),
                         (o_onesc, onescol_d, 1), (o_indr, indrow_d, 512),
                         (o_indh, indhead_d, 512)):
        k.dma("sync", SB[:src.shape[0], dst:dst + wdt], src, sem="wld")
    k.dma("sync", SB[:, o_msk:o_msk + 256], masks_d[:, 0:256], sem="wld")
    k.dma("sync", SB[:, o_cw:o_cw + NCH * 44].rearrange("p (c w) -> p c w", c=NCH),
          convw_d.rearrange("(c p) w -> p c w", p=128), sem="wld")

    va = k.act(SB[:, o_sc:o_sc + 8], SB[:, o_sc:o_sc + 8], AF.Silu,
               waits=[("ld", v_c)], inc=True)

    pe_hist = []
    for j in range(12):
        for kc in range(NCH):
            n = j * NCH + kc
            buf = o_wmb + (n % 2) * 512
            wt = [k.w("pe", pe_hist[n - 2])] if n >= 2 else []
            vl = k.dma("sync", SB[:, buf:buf + 512],
                       wmod_d[kc * 128:(kc + 1) * 128, j * 512:(j + 1) * 512],
                       waits=wt)
            k.mm(PB[0][0:1, :], SB[:, o_sc + kc:o_sc + kc + 1],
                 SB[:, buf:buf + 512], start=(kc == 0), stop=(kc == NCH - 1),
                 waits=[("ld", vl), ("ac", va)] + ([k.w("dv")] if kc == 0 else []),
                 inc=True)
            pe_hist.append(S["pe"].cur)
        k.dve(lambda e: e.tensor_copy(SB[0:1, o_mrow:o_mrow + 512], PB[0][0:1, :]),
              waits=[k.w("pe")], inc=True)
        k.dma("sync", mod_sp[j * 512:(j + 1) * 512], SB[0:1, o_mrow:o_mrow + 512],
              waits=[k.w("dv")], sem="sp")
    v_modsp = S["sp"].cur
    k.pop()

    k.dma("sync", SB[:, o_mod:o_mod + 48].rearrange("p (v c) -> p v c", v=6),
          mod_sp.rearrange("(v c p) -> p v c", v=6, c=NCH),
          waits=[("sp", v_modsp)], slow=True)
    v_modld = S["ld"].cur

    k.push()
    o_t1, o_t2, o_t3 = k.take(16), k.take(16), k.take(16)
    for v6 in (2, 5):
        g = SB[:, o_mod + 8 * v6:o_mod + 8 * v6 + 8]
        t1, t2, t3 = (SB[:, o:o + 8] for o in (o_t1, o_t2, o_t3))
        k.dve(lambda e, g=g, t=t1: e.tensor_scalar(t, g, 0.25, None, op0=OP.max),
              waits=[("ld", v_modld)], inc=True)
        k.dve(lambda e, g=g, t=t2: e.tensor_scalar(t, g, -0.25, None, op0=OP.min),
              inc=True)
        k.dve(lambda e, g=g, t=t3: e.tensor_scalar(t, g, 0.0, None, op0=OP.is_ge),
              inc=True)
        k.dve(lambda e, a=t1, b=t2: e.tensor_tensor(a, a, b, op=OP.subtract),
              waits=[k.w("dv")], inc=True)
        k.dve(lambda e, a=t1, b=t3: e.tensor_tensor(a, a, b, op=OP.mult),
              waits=[k.w("dv")], inc=True)
        k.dve(lambda e, g=g, a=t1, b=t2: e.tensor_tensor(g, a, b, op=OP.add),
              waits=[k.w("dv")], inc=True)
    k.pop()
    v_mod_dv = S["dv"].cur
    if "P0" in phases:
        k.flush()

    k.push()
    VSEG = NCH * (VH + TS)
    o_wq = k.take(NCH * DM)
    o_wk = k.take(NCH * DM)
    o_wv = k.take(2 * 1024)
    o_wb = k.take(NCH * 128)
    o_x = k.take(NCH * TS)
    o_h = k.take(NCH * TS)
    o_kn = k.take(NCH * TS)
    o_v = k.take(2 * VSEG)
    o_lin = k.take(TS + 16)
    o_halo = k.take(3 * NCH * 4)
    o_cb = k.take(TS)
    o_cb5 = k.take(2 * TS)
    o_sq2 = k.take(2 * TS)
    o_qc = k.take(2 * TS)
    o_kbc = k.take(2 * TS)
    o_mr = k.take(2 * TS)
    o_bs = k.take(TS)
    o_rq = k.take(TS)
    o_rk = k.take(TS)
    o_tma = k.take(DM)
    o_tmb = k.take(DM)
    o_bT = k.take(16)
    o_rqTb = k.take(16)

    for dst, src in ((o_wq, wq_d), (o_wk, wk_d)):
        k.dma("sync", SB[:, dst:dst + NCH * DM].rearrange("p (c m) -> p c m", c=NCH),
              src.rearrange("(c p) m -> p c m", p=128), sem="wld")
    k.dma("sync", SB[:, o_wb:o_wb + NCH * 128].rearrange("p (c m) -> p c m", c=NCH),
          wbp_d.rearrange("(c p) m -> p c m", p=128), sem="wld")
    v_wB = S["wld"].cur

    wv_ldv = [0, 0]
    wv_pe_guard = [0, 0]

    def load_wv(c):
        sl = c % 2
        k.dma("sync",
              SB[:, o_wv + sl * 1024:o_wv + sl * 1024 + 1024].rearrange(
                  "p (kc m) -> p kc m", kc=NCH),
              wv_d.rearrange("(kc p) m -> p kc m", p=128)[:, :,
                                                          c * 128:(c + 1) * 128],
              waits=[("pe", wv_pe_guard[sl])], sem="wld")
        wv_ldv[sl] = S["wld"].cur

    load_wv(0)
    load_wv(1)

    k.dve(lambda e: e.memset(SB[:, o_halo:o_halo + 3 * NCH * 4], 0.0))
    for vp in range(2):
        for c in range(NCH):
            k.dve(lambda e, vp=vp, c=c: e.memset(
                SB[:, o_v + vp * VSEG + c * (VH + TS):
                   o_v + vp * VSEG + c * (VH + TS) + VH], 0.0),
                inc=(vp == 1 and c == NCH - 1))

    def conv_taps(lin_off, w_off, ntap, out_off, waits0=()):
        k.dve(lambda e, lo=lin_off, wo=w_off, oo=out_off: e.tensor_scalar(
            SB[:, oo:oo + TS], SB[:, lo:lo + TS], SB[:, wo:wo + 1], None,
            op0=OP.mult), waits=waits0)
        for j in range(1, ntap):
            k.dve(lambda e, lo=lin_off + j, wo=w_off + j, oo=out_off:
                  e.scalar_tensor_tensor(
                      SB[:, oo:oo + TS], SB[:, lo:lo + TS], SB[:, wo:wo + 1],
                      SB[:, oo:oo + TS], op0=OP.mult, op1=OP.add))

    qspill = [0, 0]
    kbspill = [0, 0]
    rqt_spill = 0
    tma_spill = 0
    tmb_spill = 0

    u_loc = {}
    u_mid = {}
    u_tput_loc = [0, 0]
    u_tput_mid = [0]
    st_b = {"ac_vend": 0}

    def conv_unit(sp, c):
        vpar = sp % 2
        vb = o_v + vpar * VSEG + c * (VH + TS)
        cb5 = o_cb5 + (c % 2) * TS
        w0 = [("pe", u_tput_loc[c % 2])]
        if c == 0:
            w0.append(("ac", st_b["ac_vend"]))
        conv_taps(vb + VH - 6, o_cw + c * 44 + 12, 7, cb5, waits0=w0)
        k.dve(lambda e: e.engine_nop(), inc=True)
        u_loc[c] = S["dv"].cur
        first = True
        for j in range(25):
            lo = vb + j
            wo = o_cw + c * 44 + 19 + j
            if first:
                k.dve(lambda e, l=lo, w=wo: e.tensor_scalar(
                    SB[:, o_tmb:o_tmb + 512], SB[:, l:l + TS],
                    SB[:, w:w + 1], None, op0=OP.mult),
                    waits=[("sp", tmb_spill), ("pe", u_tput_mid[0])])
                first = False
            else:
                k.dve(lambda e, l=lo, w=wo: e.scalar_tensor_tensor(
                    SB[:, o_tmb:o_tmb + 512], SB[:, l:l + TS],
                    SB[:, w:w + 1], SB[:, o_tmb:o_tmb + 512],
                    op0=OP.mult, op1=OP.add))
        k.dve(lambda e: e.engine_nop(), inc=True)
        u_mid[c] = S["dv"].cur
        dhl = o_v + (1 - vpar) * VSEG + c * (VH + TS)
        k.dve(lambda e, s_=vb + TS, d_=dhl: e.tensor_copy(
            SB[:, d_:d_ + VH], SB[:, s_:s_ + VH]))

    def tput_unit(sp, c):
        nonlocal tma_spill
        cb5 = o_cb5 + (c % 2) * TS
        for src_o, v_rdy, tgt_sp, kind in ((cb5, u_loc[c], lT_sp, "loc"),
                                           (o_tmb, u_mid[c], mT_sp, "mid")):
            for blk in range(4):
                k.transp(PB[2][:, blk * 128:(blk + 1) * 128],
                         SB[:, src_o + blk * 128:src_o + blk * 128 + 128],
                         SB[:, o_id:o_id + 128],
                         waits=[("dv", v_rdy), k.bg(2)] if blk == 0 else [],
                         inc=(blk == 3))
            v_tl = S["pe"].cur
            if kind == "loc":
                u_tput_loc[c % 2] = v_tl
            else:
                u_tput_mid[0] = v_tl
            k.dve(lambda e: e.tensor_copy(SB[:, o_tma:o_tma + 512],
                                          PB[2][:, :]),
                  waits=[("pe", v_tl), ("sp", tma_spill)], inc=True)
            v_tlm = S["dv"].cur
            k.setbg(2, "dv", v_tlm)
            for blk in range(4):
                k.dma("scalar", tgt_sp[4 * sp + blk][:, c * 128:(c + 1) * 128],
                      SB[:, o_tma + blk * 128:o_tma + blk * 128 + 128],
                      waits=[("dv", v_tlm)], sem="sp")
            tma_spill = S["sp"].cur

    def emit_unit_slot(sp, sidx):
        if sp < 0 or sidx < 2 or (sidx - 2) % 3 != 0:
            return
        u = (sidx - 2) // 3
        if u >= NCH:
            return
        if u > 0:
            tput_unit(sp, u - 1)
        conv_unit(sp, u)
    sq2_guard = [("dv", 0), ("dv", 0)]
    mrow = SB[0:1, o_mr:o_mr + TS]
    rrow = SB[0:1, o_mr + TS:o_mr + 2 * TS]

    g_xfree = []
    for st in range(NST):
        tok0 = st * TS
        v_x = k.dma("sync", SB[:, o_x:o_x + NCH * TS].rearrange(
            "p (c t) -> p c t", c=NCH),
            x_fm.rearrange("(c p) l -> p c l", p=128)[:, :, tok0:tok0 + TS],
            waits=g_xfree)

        pe_sx2 = []
        for c in range(NCH):
            k.mm(PB[2][0:1, :], SB[:, o_onesc:o_onesc + 1],
                 SB[:, o_x + c * TS:o_x + (c + 1) * TS],
                 start=(c == 0), stop=(c == NCH - 1),
                 waits=[("ld", v_x), k.bg(2)] if c == 0 else [],
                 inc=(c == NCH - 1))
        for c in range(NCH):
            wv_ = ([("ld", v_x)] if c == 0 else [k.w("pe", pe_sx2[-1])]) + \
                [sq2_guard[c % 2]]
            va = k.act(SB[:, o_sq2 + (c % 2) * TS:o_sq2 + (c % 2) * TS + TS],
                       SB[:, o_x + c * TS:o_x + (c + 1) * TS], AF.Square,
                       waits=wv_, inc=True)
            k.mm(PB[3][0:1, :], SB[:, o_onesc:o_onesc + 1],
                 SB[:, o_sq2 + (c % 2) * TS:o_sq2 + (c % 2) * TS + TS],
                 start=(c == 0), stop=(c == NCH - 1),
                 waits=[("ac", va)] + ([k.bg(3)] if c == 0 else []), inc=True)
            pe_sx2.append(S["pe"].cur)
            sq2_guard[c % 2] = ("pe", S["pe"].cur)
        v_st = S["pe"].cur
        v_xsq = S["ac"].cur
        k.dve(lambda e: e.tensor_scalar(mrow, PB[2][0:1, :], 1.0 / DM, None,
                                        op0=OP.mult), waits=[("pe", v_st)])
        k.dve(lambda e: e.tensor_tensor(rrow, mrow, mrow, op=OP.mult))
        k.dve(lambda e: e.scalar_tensor_tensor(
            rrow, PB[3][0:1, :], 1.0 / DM, rrow, op0=OP.mult, op1=OP.subtract),
            inc=True)
        v_var = S["dv"].cur
        k.setbg(2, "dv", v_var)
        k.setbg(3, "dv", v_var)
        va = k.act(rrow, rrow, AF.Sqrt, bias=1e-5, waits=[("dv", v_var)], inc=True)
        k.dve(lambda e: e.reciprocal(rrow, rrow), waits=[("ac", va)], inc=True)
        v_r = S["dv"].cur
        k.mm(PB[4][:, :], SB[0:1, o_onesr:o_onesr + 128], mrow,
             start=True, stop=True, waits=[("dv", v_r), k.bg(4)])
        k.mm(PB[5][:, :], SB[0:1, o_onesr:o_onesr + 128], rrow,
             start=True, stop=True, waits=[k.bg(5)], inc=True)
        v_bc = S["pe"].cur

        for c in range(NCH):
            xc = SB[:, o_x + c * TS:o_x + (c + 1) * TS]
            hc = SB[:, o_h + c * TS:o_h + (c + 1) * TS]
            w0 = [("pe", v_bc)]
            if c == 0:
                w0.append(("sp", st_b.get("h_spill", 0)))
            if st == 0 and c == 0:
                w0 += [("ld", v_modld), ("dv", v_mod_dv)]
            k.dve(lambda e, x=xc, h=hc: e.tensor_tensor(h, x, PB[4][:, :],
                                                        op=OP.subtract), waits=w0)
            k.dve(lambda e, h=hc: e.tensor_tensor(h, h, PB[5][:, :], op=OP.mult))
            k.dve(lambda e, h=hc, c=c: e.scalar_tensor_tensor(
                h, h, modcol(1, c), h, op0=OP.mult, op1=OP.add))
            k.dve(lambda e, h=hc, c=c: e.tensor_scalar(
                h, h, modcol(0, c), None, op0=OP.add), inc=True)
        v_h = S["dv"].cur
        k.setbg(4, "dv", v_h)
        k.setbg(5, "dv", v_h)
        st_b["h_spill"] = k.dma(
            "sync", h_sp.rearrange("c p l -> p c l")[:, :, tok0:tok0 + TS],
            SB[:, o_h:o_h + NCH * TS].rearrange("p (c t) -> p c t", c=NCH),
            waits=[("dv", v_h)], sem="sp")

        for c in range(NCH):
            k.mm(PB[7][:, :], SB[:, o_wb + c * 128:o_wb + (c + 1) * 128],
                 SB[:, o_h + c * TS:o_h + (c + 1) * TS],
                 start=(c == 0), stop=(c == NCH - 1),
                 waits=[("dv", v_h), ("wld", v_wB), k.bg(7)] if c == 0 else [],
                 inc=(c == NCH - 1))
        v_bl = S["pe"].cur
        va = k.act(SB[:, o_bs:o_bs + TS], PB[7][:, :], AF.Sigmoid,
                   waits=[("pe", v_bl)], inc=True)
        k.setbg(7, "ac", va)

        g_xfree = [("pe", v_st), ("ac", v_xsq), ("dv", v_h)]
        vpar = st % 2
        for ti, (ow, cwo, tgt) in enumerate(((o_wq, 0, "q"), (o_wk, 4, "k"),
                                             (None, 8, "v"))):
            sum_hist = []
            pend_sq = None

            def emit_sum(cw, vaw):
                k.mm(PB[6][:, :],
                     SB[:, o_indh + (cw // 2) * 128:o_indh + (cw // 2) * 128 + 128],
                     SB[:, o_sq2 + (cw % 2) * TS:o_sq2 + (cw % 2) * TS + TS],
                     start=(cw == 0), stop=(cw == NCH - 1),
                     waits=[("ac", vaw)] + ([k.bg(6)] if cw == 0 else []),
                     inc=True)
                sum_hist.append(S["pe"].cur)
                sq2_guard[cw % 2] = ("pe", S["pe"].cur)

            if tgt == "k" and st > 0:
                load_wv(0)
                load_wv(1)
            for c in range(NCH):
                bank = c % 2
                for kc in range(NCH):
                    if tgt == "v":
                        lh = SB[:, o_wv + (c % 2) * 1024 + kc * 128:
                                o_wv + (c % 2) * 1024 + kc * 128 + 128]
                    else:
                        lh = SB[:, ow + kc * DM + c * 128:
                                ow + kc * DM + (c + 1) * 128]
                    k.mm(PB[bank][:, :], lh,
                         SB[:, o_h + kc * TS:o_h + (kc + 1) * TS],
                         start=(kc == 0), stop=(kc == NCH - 1),
                         waits=([k.bg(bank)] +
                                ([("dv", v_h)] if (ti == 0 and c == 0) else []) +
                                ([("wld", wv_ldv[c % 2])] if tgt == "v" else []))
                         if kc == 0 else [],
                         inc=(kc == NCH - 1))
                v_p = S["pe"].cur
                if tgt == "v":
                    wv_pe_guard[c % 2] = v_p
                    if c + 2 < NCH:
                        load_wv(c + 2)
                if pend_sq is not None:
                    emit_sum(*pend_sq)
                    pend_sq = None
                hs = o_halo + (ti * NCH + c) * 4
                k.dve(lambda e, h=hs: e.tensor_copy(SB[:, o_lin:o_lin + 3],
                                                    SB[:, h:h + 3]),
                      waits=[("pe", v_p)])
                k.dve(lambda e, b=bank: e.tensor_copy(
                    SB[:, o_lin + 3:o_lin + 3 + TS], PB[b][:, :]))
                k.dve(lambda e, h=hs, b=bank: e.tensor_copy(
                    SB[:, h:h + 3], PB[b][:, TS - 3:TS]), inc=True)
                k.setbg(bank, "dv")
                conv_taps(o_lin, o_cw + c * 44 + cwo, 4, o_cb,
                          waits0=[k.w("ac")])
                k.dve(lambda e: e.engine_nop(), inc=True)
                v_cv = S["dv"].cur
                if tgt == "q":
                    db = c % 2
                    dst = SB[:, o_qc + db * TS:o_qc + (db + 1) * TS]
                elif tgt == "k":
                    dst = SB[:, o_kn + c * TS:o_kn + (c + 1) * TS]
                else:
                    vb0 = o_v + vpar * VSEG + c * (VH + TS) + VH
                    dst = SB[:, vb0:vb0 + TS]
                wsilu = [("dv", v_cv)]
                if tgt == "q":
                    wsilu.append(("sp", qspill[db]))
                if tgt == "k" and c == 0:
                    wsilu.append(("sp", st_b.get("kn_spill", 0)))
                v_silu = k.act(dst, SB[:, o_cb:o_cb + TS], AF.Silu, waits=wsilu,
                               inc=True)
                if tgt in ("q", "k"):
                    va2 = k.act(SB[:, o_sq2 + (c % 2) * TS:o_sq2 + (c % 2) * TS + TS],
                                dst, AF.Square,
                                waits=[("ac", v_silu), sq2_guard[c % 2]],
                                inc=True)
                    pend_sq = (c, va2)
                if tgt == "q":
                    qspill[db] = k.dma(
                        "scalar",
                        q_sp.rearrange("c p l -> p c l")[:, c:c + 1, tok0:tok0 + TS],
                        dst.unsqueeze(1), waits=[("ac", v_silu)], sem="sp")
                emit_unit_slot(st - 1, ti * NCH + c)
            if pend_sq is not None:
                emit_sum(*pend_sq)
                pend_sq = None
            if tgt in ("q", "k"):
                v_sum = S["pe"].cur
                orr = o_rq if tgt == "q" else o_rk
                va = k.act(SB[:, orr:orr + TS], PB[6][:, :], AF.Sqrt, bias=1e-12,
                           waits=[("pe", v_sum)], inc=True)
                k.setbg(6, "ac", va)
                k.dve(lambda e, o=orr: e.reciprocal(SB[:, o:o + TS],
                                                    SB[:, o:o + TS]),
                      waits=[("ac", va)], inc=True)
        st_b["ac_vend"] = S["ac"].cur
        if st > 0:
            tput_unit(st - 1, NCH - 1)

        v_knc = S["dv"].cur
        for hh in range(H):
            k.mm(PB[4][:, :], SB[:, o_indr + hh * 128:o_indr + hh * 128 + 128],
                 SB[:, o_rk:o_rk + TS], start=True, stop=True,
                 waits=[k.w("dv"), k.bg(4)])
            k.mm(PB[5][:, :], SB[:, o_indr + hh * 128:o_indr + hh * 128 + 128],
                 SB[:, o_bs:o_bs + TS], start=True, stop=True,
                 waits=[k.bg(5)], inc=True)
            v_b2 = S["pe"].cur
            for c in (2 * hh, 2 * hh + 1):
                kc_ap = SB[:, o_kn + c * TS:o_kn + (c + 1) * TS]
                db = c % 2
                k.dve(lambda e, kc=kc_ap: e.tensor_tensor(kc, kc, PB[4][:, :],
                                                          op=OP.mult),
                      waits=[("pe", v_b2)])
                k.dve(lambda e, kc=kc_ap, d=db: e.tensor_tensor(
                    SB[:, o_kbc + d * TS:o_kbc + (d + 1) * TS], kc, PB[5][:, :],
                    op=OP.mult), waits=[("sp", kbspill[db])], inc=True)
                v_knc = S["dv"].cur
                kbspill[db] = k.dma(
                    "scalar",
                    kb_sp.rearrange("c p l -> p c l")[:, c:c + 1, tok0:tok0 + TS],
                    SB[:, o_kbc + db * TS:o_kbc + (db + 1) * TS].unsqueeze(1),
                    waits=[("dv", v_knc)], sem="sp")
                k.dma("scalar",
                      kn_sp.rearrange("c p l -> p c l")[:, c:c + 1, tok0:tok0 + TS],
                      kc_ap.unsqueeze(1), waits=[("dv", v_knc)], sem="sp")
            k.setbg(4, "dv", v_knc)
            k.setbg(5, "dv", v_knc)
        st_b["kn_spill"] = S["sp"].cur

        for blk in range(4):
            B_ = 4 * st + blk
            k.transp(PB[7][:, 0:128], SB[:, o_bs + blk * 128:o_bs + blk * 128 + 128],
                     SB[:, o_id:o_id + 128], waits=[k.w("dv", v_knc), k.bg(7)])
            k.transp(PB[7][:, 128:256], SB[:, o_rq + blk * 128:o_rq + blk * 128 + 128],
                     SB[:, o_id:o_id + 128], inc=True)
            v_tb = S["pe"].cur
            k.dve(lambda e: e.tensor_copy(SB[:, o_bT:o_bT + 4], PB[7][:, 0:128:32]),
                  waits=[("pe", v_tb), ("sp", rqt_spill)])
            k.dve(lambda e: e.tensor_copy(SB[:, o_rqTb:o_rqTb + 4],
                                          PB[7][:, 128:256:32]), inc=True)
            v_t2 = S["dv"].cur
            k.setbg(7, "dv", v_t2)
            rqt_spill = k.dma("scalar", rqT_sp[B_], SB[:, o_rqTb:o_rqTb + 4],
                              waits=[("dv", v_t2)], sem="sp")

            for half in range(2):
                bank = 2 + half
                for cp in range(4):
                    c = half * 4 + cp
                    src = SB[:, o_v + vpar * VSEG + c * (VH + TS) + VH + blk * 128:
                             o_v + vpar * VSEG + c * (VH + TS) + VH + blk * 128 + 128]
                    k.transp(PB[bank][:, cp * 128:(cp + 1) * 128], src,
                             SB[:, o_id:o_id + 128],
                             waits=[k.bg(bank)] if cp == 0 else [], inc=(cp == 3))
                v_tv = S["pe"].cur
                k.dve(lambda e, b=bank, hf=half: e.tensor_copy(
                    SB[:, o_tma + hf * 512:o_tma + hf * 512 + 512], PB[b][:, :]),
                    waits=[("pe", v_tv), ("sp", tma_spill)])
                for h2 in range(2):
                    hd = half * 2 + h2
                    k.dve(lambda e, b=bank, h2=h2, hd=hd: e.tensor_scalar(
                        SB[:, o_tmb + hd * 256:o_tmb + (hd + 1) * 256],
                        PB[b][:, h2 * 256:(h2 + 1) * 256],
                        SB[:, o_bT + hd:o_bT + hd + 1], None, op0=OP.mult),
                        waits=[("sp", tmb_spill)] if h2 == 0 else [],
                        inc=(h2 == 1))
                k.setbg(bank, "dv")
            v_vtm = S["dv"].cur
            k.dma("scalar", vT_sp[B_], SB[:, o_tma:o_tma + DM],
                  waits=[("dv", v_vtm)], sem="sp")
            tma_spill = S["sp"].cur
            for par in range(2):
                ck = 2 * B_ + par
                k.dma("scalar", vbT_sp[ck],
                      SB[64 * par:64 * par + 64, o_tmb:o_tmb + DM],
                      waits=[("dv", v_vtm)], sem="sp")
            tmb_spill = S["sp"].cur

            for half in range(2):
                bank = 2 + half
                for cp in range(4):
                    c = half * 4 + cp
                    src = SB[:, o_kn + c * TS + blk * 128:o_kn + c * TS + blk * 128 + 128]
                    k.transp(PB[bank][:, cp * 128:(cp + 1) * 128], src,
                             SB[:, o_id:o_id + 128],
                             waits=[k.bg(bank)] if cp == 0 else [], inc=(cp == 3))
                v_tk = S["pe"].cur
                k.dve(lambda e, b=bank, hf=half: e.tensor_copy(
                    SB[:, o_tma + hf * 512:o_tma + hf * 512 + 512], PB[b][:, :]),
                    waits=[("pe", v_tk), ("sp", tma_spill)] if half == 0
                    else [("pe", v_tk)], inc=True)
                k.setbg(bank, "dv")
            v_ktm = S["dv"].cur
            for par in range(2):
                ck = 2 * B_ + par
                k.dma("scalar", knT_sp[ck],
                      SB[64 * par:64 * par + 64, o_tma:o_tma + DM],
                      waits=[("dv", v_ktm)], sem="sp")
            tma_spill = S["sp"].cur

    sp_last = NST - 1
    for u in range(NCH):
        if u > 0:
            tput_unit(sp_last, u - 1)
        conv_unit(sp_last, u)
    tput_unit(sp_last, NCH - 1)

    k.pop()
    if "B" in phases:
        k.flush()

    o_pT = k.take(NBLK * 16)
    k.push()
    o_feats = k.take(4096)
    k.push()
    o_SS = [k.take(512) for _ in range(H)]
    o_qF = k.take(NCH * TS)
    o_kbF = k.take(NCH * TS)
    o_knF = k.take(NCH * TS)
    o_knT = [k.take(DM) for _ in range(4)]
    o_vbT = [k.take(DM) for _ in range(4)]
    o_rqT = k.take(32)
    o_vTb = k.take(DM)
    o_lTb = k.take(DM)
    o_mTb = k.take(DM)
    o_dta = k.take(2 * DM)
    PCLS = ("P1", "Q1", "Q2", "P2", "Q4", "P4", "Q8", "P8", "Q16",
            "P16", "Q32", "P12", "P32", "P48", "C1", "C2", "C3",
            "CP3", "CP3I", "C2I", "M2", "N2", "TMP")
    o_cls = {n: k.take(512) for n in PCLS}
    o_Tx = k.take(1024)
    o_Ys = k.take(DM)
    o_Us = k.take(DM)
    o_att = k.take(256)
    o_pr = [k.take(DM) for _ in range(2)]
    o_fr = k.take(64)
    o_ft = k.take(128)
    o_sc1 = k.take(16)
    o_sc2 = k.take(16)
    o_msk4 = k.take(768)
    o_mI = k.take(512)
    k.dma("sync", SB[:, o_msk4:o_msk4 + 768], masks_d[:, 256:1024], sem="wld",
          waits=[k.w("sp")])
    k.dma("sync", SB[:, o_mI:o_mI + 512], masks_d[:, 1024:1536], sem="wld")

    first_ms = [k.w("sp")]
    for off, n in ([(o, 512) for o in o_SS] +
                   [(o, DM) for o in o_knT] +
                   [(o_Ys, DM), (o_Us, DM), (o_Tx, 1024),
                    (o_att, 256), (o_feats, 4096)] +
                   [(o_cls[n], 512) for n in o_cls]):
        k.dve(lambda e, o=off, m=n: e.memset(SB[:, o:o + m], 0.0),
              waits=first_ms)
        first_ms = []
    k.dve(lambda e: e.engine_nop(), inc=True)

    M_SL4, M_SU4, M_IU4 = o_msk4, o_msk4 + 256, o_msk4 + 512

    def pow_ap(n, par, hd=None, rows=128):
        o = o_cls[n] + 256 * par
        if hd is None:
            return SB[:rows, o:o + 256]
        return SB[:rows, o + 64 * hd:o + 64 * hd + 64]

    def cls_full(n, rows=128):
        return SB[:rows, o_cls[n]:o_cls[n] + 512]

    def texp_ap(gpar, par, hd):
        o = o_Tx + 512 * gpar + 256 * par + 64 * hd
        return SB[:, o:o + 64]

    knt_guard = [k.w("dv", 0)] * 4
    vbt_guard = [k.w("dv", 0)] * 4
    dta_spill = [0, 0]
    texp_guard = [("pe", 0), ("pe", 0)]
    st_c = {"v_S_add": S["dv"].cur, "v_tmld": [0, 0],
            "v_fmld": 0, "scan_qf_pe": 0, "pr_guard": [0, 0],
            "ft_guard": [0, 0], "v_Mmms": 0, "v_Tx": [0, 0], "pi": 0}

    NG = NCK // 2

    def qsl(base, ck, hd, kc2):
        c = 2 * hd + kc2
        t0 = (ck % 8) * CC
        return SB[:, base + c * TS + t0:base + c * TS + t0 + CC]

    def emit_T0(g, boundary):
        if boundary:
            w0 = [k.w("pe"), k.w("dv"), k.w("gp")]
            for dst, src in ((o_qF, q_sp), (o_kbF, kb_sp), (o_knF, kn_sp)):
                k.dma("sync", SB[:, dst:dst + NCH * TS].rearrange(
                    "p (c t) -> p c t", c=NCH),
                    src.rearrange("c p l -> p c l")[:, :, (g // 4) * TS:
                                                    (g // 4) * TS + TS],
                    waits=w0)
                w0 = []
            stl = g // 4
            k.dma("sync",
                  SB[:, o_rqT + 16 * (stl % 2):o_rqT + 16 * (stl % 2) + 16]
                  .rearrange("p (b hh) -> p b hh", b=4),
                  rqT_sp.rearrange("b p hh -> p b hh")[:, 4 * stl:4 * stl + 4, :])
        for par in range(2):
            ck = 2 * g + par
            k.dma("sync", SB[0:64, o_knT[ck % 4]:o_knT[ck % 4] + DM], knT_sp[ck],
                  waits=[knt_guard[ck % 4]])
            k.dma("sync", SB[0:64, o_vbT[ck % 4]:o_vbT[ck % 4] + DM], vbT_sp[ck],
                  waits=[vbt_guard[ck % 4]])
        st_c["v_tmld"][g % 2] = S["ld"].cur
        if boundary:
            st_c["v_fmld"] = S["ld"].cur

    def emit_T1(g):
        v_ld = st_c["v_fmld"]
        for par in range(2):
            ck = 2 * g + par
            bk = 4 + par
            for hd in range(H):
                for kc2 in range(2):
                    k.mm(PB[bk][0:64, 64 * hd:64 * hd + 64],
                         qsl(o_kbF, ck, hd, kc2), qsl(o_knF, ck, hd, kc2),
                         start=(kc2 == 0), stop=(kc2 == 1),
                         waits=[("ld", v_ld), k.bg(bk)] if hd == 0 and kc2 == 0
                         else [], inc=(kc2 == 1))
            for hd in range(H):
                for kc2 in range(2):
                    k.mm(PB[bk][0:64, 256 + 64 * hd:256 + 64 * hd + 64],
                         qsl(o_knF, ck, hd, kc2), qsl(o_kbF, ck, hd, kc2),
                         start=(kc2 == 0), stop=(kc2 == 1), inc=(kc2 == 1))
        st_c["v_gr"] = S["pe"].cur

    def emit_T2(g):
        v_gr = st_c["v_gr"]
        for par in range(2):
            bk = 4 + par
            k.dve(lambda e, b=bk, p_=par: e.scalar_tensor_tensor(
                pow_ap("P1", p_, rows=64), PB[b][0:64, 0:256], -1.0,
                SB[0:64, M_SL4:M_SL4 + 256], op0=OP.mult, op1=OP.mult),
                waits=[("pe", v_gr), k.w("wld")] if par == 0 else [])
            k.dve(lambda e, b=bk, p_=par: e.scalar_tensor_tensor(
                pow_ap("Q1", p_, rows=64), PB[b][0:64, 256:512], -1.0,
                SB[0:64, M_SU4:M_SU4 + 256], op0=OP.mult, op1=OP.mult),
                inc=(par == 1))
            k.setbg(bk, "dv")
        st_c["v_pq1"] = S["dv"].cur

    TROUNDS = [
        [("Q2", "P1", "Q1", 4), ("P2", "Q1", "P1", 5)],
        [("C1", "P1", "Q2", 6), ("Q4", "P2", "Q2", 4), ("P4", "Q2", "P2", 5)],
        [("Q8", "P4", "Q4", 4), ("P8", "Q4", "P4", 5)],
        [("C2", "P4", "Q8", 6), ("Q16", "P8", "Q8", 4), ("P16", "Q8", "P8", 5)],
        [("Q32", "P16", "Q16", 6), ("P12", "Q4", "P8", 4),
         ("P32", "Q16", "P16", 5)],
        [("C3", "P16", "Q32", 4), ("P48", "Q16", "P32", 5)],
    ]

    def emit_Tround(g, ri):
        v_prev = S["dv"].cur
        for dst, lh, rh, bk in TROUNDS[ri]:
            for par in range(2):
                for hd in range(H):
                    k.mm(PB[bk][0:64, 256 * par + 64 * hd:256 * par + 64 * hd + 64],
                         pow_ap(lh, par, hd), pow_ap(rh, par, hd),
                         start=(par == 0 and hd == 0),
                         stop=(par == 1 and hd == H - 1),
                         waits=[("dv", v_prev), k.bg(bk)] if par == 0 and hd == 0
                         else [], inc=(par == 1 and hd == H - 1))

    def emit_Tround_evac(g, ri):
        for dst, lh, rh, bk in TROUNDS[ri]:
            if dst.startswith("C"):
                a, b_ = {"C1": ("Q1", "Q2"), "C2": ("Q4", "Q8"),
                         "C3": ("Q16", "Q32")}[dst]
                k.dve(lambda e, d=dst, x=a, y=b_: e.tensor_tensor(
                    cls_full(d), cls_full(x), cls_full(y), op=OP.add),
                    waits=[k.w("pe")])
                k.dve(lambda e, d=dst, b=bk: e.tensor_tensor(
                    cls_full(d, rows=64), cls_full(d, rows=64),
                    PB[b][0:64, :], op=OP.add), inc=True)
            else:
                k.dve(lambda e, d=dst, b=bk: e.tensor_copy(
                    SB[0:64, o_cls[d]:o_cls[d] + 512], PB[b][0:64, :]),
                    waits=[k.w("pe")], inc=True)
            k.setbg(bk, "dv")

    def emit_Mprep(g):
        k.dve(lambda e: e.tensor_tensor(
            cls_full("TMP", rows=64), cls_full("P16", rows=64),
            cls_full("P32", rows=64), op=OP.add),
            waits=[("pe", st_c["v_Mmms"])])
        k.dve(lambda e: e.tensor_tensor(
            cls_full("CP3", rows=64), cls_full("TMP", rows=64),
            cls_full("P48", rows=64), op=OP.add))
        k.dve(lambda e: e.tensor_tensor(
            cls_full("CP3I", rows=64), cls_full("CP3", rows=64),
            SB[0:64, o_mI:o_mI + 512], op=OP.add))
        k.dve(lambda e: e.tensor_tensor(
            cls_full("C2I", rows=64), cls_full("C2", rows=64),
            SB[0:64, o_mI:o_mI + 512], op=OP.add))
        k.dve(lambda e: e.tensor_tensor(
            cls_full("TMP", rows=64), cls_full("P4", rows=64),
            cls_full("P8", rows=64), op=OP.add))
        k.dve(lambda e: e.tensor_tensor(
            cls_full("TMP", rows=64), cls_full("TMP", rows=64),
            cls_full("P12", rows=64), op=OP.add))
        k.dve(lambda e: e.tensor_tensor(
            cls_full("TMP", rows=64), cls_full("TMP", rows=64),
            SB[0:64, o_mI:o_mI + 512], op=OP.add), inc=True)
        st_c["v_Mprep"] = S["dv"].cur

    def emit_M1(g):
        v_p = st_c["v_Mprep"]
        for lh, rh, bk in (("CP3I", "C2", 4), ("C2I", "CP3", 5)):
            for par in range(2):
                for hd in range(H):
                    k.mm(PB[bk][0:64, 256 * par + 64 * hd:256 * par + 64 * hd + 64],
                         pow_ap(lh, par, hd), pow_ap(rh, par, hd),
                         start=(par == 0 and hd == 0),
                         stop=(par == 1 and hd == H - 1),
                         waits=[("dv", v_p), k.bg(bk)] if par == 0 and hd == 0
                         else [], inc=(par == 1 and hd == H - 1))
        st_c["v_M1"] = S["pe"].cur

    def emit_M1_evac(g):
        v_m1 = st_c["v_M1"]
        for dst, addend, bk in (("M2", "C3", 4), ("N2", "TMP", 5)):
            k.dve(lambda e, d=dst, a=addend, b=bk: e.tensor_tensor(
                cls_full(d, rows=64), cls_full(a, rows=64),
                PB[b][0:64, :], op=OP.add),
                waits=[("pe", v_m1)] if dst == "M2" else [], inc=True)
            k.setbg(bk, "dv")
        k.dve(lambda e: e.tensor_tensor(
            cls_full("M2", rows=64), cls_full("M2", rows=64),
            SB[0:64, o_mI:o_mI + 512], op=OP.add), inc=True)
        st_c["v_M1e"] = S["dv"].cur

    def emit_M2(g):
        v_e = st_c["v_M1e"]
        for par in range(2):
            for hd in range(H):
                k.mm(PB[6][0:64, 256 * par + 64 * hd:256 * par + 64 * hd + 64],
                     pow_ap("N2", par, hd), pow_ap("C1", par, hd),
                     start=(par == 0 and hd == 0),
                     stop=(par == 1 and hd == H - 1),
                     waits=[("dv", v_e), k.bg(6)] if par == 0 and hd == 0
                     else [], inc=(par == 1 and hd == H - 1))
        st_c["v_Mmms"] = S["pe"].cur

    def emit_M2_evac(g):
        gpar = g % 2
        k.dve(lambda e, gp2=gpar: e.tensor_tensor(
            SB[0:64, o_Tx + 512 * gp2:o_Tx + 512 * gp2 + 512],
            cls_full("M2", rows=64), PB[6][0:64, :], op=OP.add),
            waits=[("pe", st_c["v_Mmms"]), texp_guard[gpar]], inc=True)
        k.setbg(6, "dv")
        st_c["v_Tx"][gpar] = S["dv"].cur

    def emit_S_a(ck):
        g = ck // 2
        par = ck % 2
        for hd in range(H):
            bk = hd // 2
            col = 256 * (hd % 2)
            for kc2 in range(2):
                k.mm(PB[bk][0:64, col:col + 256],
                     qsl(o_kbF, ck, hd, kc2),
                     SB[:, o_SS[hd] + 256 * kc2:o_SS[hd] + 256 * kc2 + 256],
                     start=(kc2 == 0), stop=(kc2 == 1),
                     waits=([("dv", st_c["v_S_add"]), k.bg(bk)]
                            if (hd % 2 == 0 and kc2 == 0) else []),
                     inc=(kc2 == 1))
        v_ymm = S["pe"].cur
        for sg in range(2):
            k.dve(lambda e, s_=sg, sl=(ck % 4): e.tensor_tensor(
                SB[0:64, o_Ys + 512 * s_:o_Ys + 512 * s_ + 512],
                SB[0:64, o_vbT[sl] + 512 * s_:o_vbT[sl] + 512 * s_ + 512],
                PB[s_][0:64, 0:512], op=OP.subtract),
                waits=[("pe", v_ymm), ("ld", st_c["v_tmld"][g % 2])]
                if sg == 0 else [], inc=(sg == 1))
        st_c["v_ys"] = S["dv"].cur
        k.setbg(0, "dv", st_c["v_ys"])
        k.setbg(1, "dv", st_c["v_ys"])

    def emit_S_u(ck):
        g = ck // 2
        gpar = g % 2
        par = ck % 2
        v_src = st_c["v_ys"]
        wT = [("dv", st_c["v_Tx"][gpar])]
        for hd in range(H):
            bk = hd // 2
            col = 256 * (hd % 2)
            k.mm(PB[bk][0:64, col:col + 256],
                 texp_ap(gpar, par, hd),
                 SB[:, o_Ys + 256 * hd:o_Ys + 256 * hd + 256],
                 start=True, stop=True,
                 waits=([("dv", v_src), k.bg(bk)] + wT) if hd % 2 == 0
                 else [], inc=(hd % 2 == 1))
        v_app = S["pe"].cur
        if par == 1:
            texp_guard[gpar] = ("pe", v_app)
        for sg in range(2):
            k.dve(lambda e, s_=sg: e.tensor_copy(
                SB[0:64, o_Us + 512 * s_:o_Us + 512 * s_ + 512],
                PB[s_][0:64, 0:512]),
                waits=[("pe", v_app)] if sg == 0 else [], inc=(sg == 1))
        k.setbg(0, "dv")
        k.setbg(1, "dv")
        st_c["v_U"] = S["dv"].cur

    def emit_S_attn_o(ck):
        g = ck // 2
        par = ck % 2
        stl = ck // 8
        blkrel = (ck % 8) // 2
        sdta = g % 2
        for hd in range(H):
            for kc2 in range(2):
                k.mm(PB[7][0:64, 64 * hd:64 * hd + 64],
                     qsl(o_knF, ck, hd, kc2), qsl(o_qF, ck, hd, kc2),
                     start=(kc2 == 0), stop=(kc2 == 1),
                     waits=[k.bg(7)] if hd == 0 and kc2 == 0 else [],
                     inc=(kc2 == 1))
        v_ag = S["pe"].cur
        k.dve(lambda e: e.tensor_tensor(
            SB[0:64, o_att:o_att + 256], PB[7][0:64, 0:256],
            SB[0:64, M_IU4:M_IU4 + 256], op=OP.mult),
            waits=[("pe", v_ag)], inc=True)
        k.setbg(7, "dv")
        v_at = S["dv"].cur
        for hd in range(H):
            bk = 2 + hd // 2
            col = 256 * (hd % 2)
            for kc2 in range(2):
                k.mm(PB[bk][64 * par:64 * par + 64, col:col + 256],
                     qsl(o_qF, ck, hd, kc2),
                     SB[:, o_SS[hd] + 256 * kc2:o_SS[hd] + 256 * kc2 + 256],
                     start=(kc2 == 0), stop=False, tp=(0, 64 * par),
                     waits=([("dv", v_at), k.bg(bk)] if hd % 2 == 0 and kc2 == 0
                            else []))
            k.mm(PB[bk][64 * par:64 * par + 64, col:col + 256],
                 SB[:, o_att + 64 * hd:o_att + 64 * hd + 64],
                 SB[:, o_Us + 256 * hd:o_Us + 256 * hd + 256],
                 start=False, stop=True, tp=(0, 64 * par),
                 inc=(hd % 2 == 1))
        st_c["scan_qf_pe"] = S["pe"].cur
        v_omm = S["pe"].cur
        for hd in range(H):
            bk = 2 + hd // 2
            col = 256 * (hd % 2)
            k.dve(lambda e, b=bk, c=col, h=hd, p_=par, sd=sdta, br=blkrel,
                  sp2=(stl % 2): e.tensor_scalar(
                      SB[64 * p_:64 * p_ + 64,
                         o_dta + sd * DM + 256 * h:o_dta + sd * DM + 256 * h + 256],
                      PB[b][64 * p_:64 * p_ + 64, c:c + 256],
                      SB[64 * p_:64 * p_ + 64,
                         o_rqT + 16 * sp2 + 4 * br + h:
                         o_rqT + 16 * sp2 + 4 * br + h + 1],
                      None, op0=OP.mult),
                  waits=([("pe", v_omm), ("sp", dta_spill[sdta])] if hd == 0
                         else []),
                  inc=(hd == H - 1))
        st_c["v_oev"] = S["dv"].cur
        k.setbg(2, "dv", st_c["v_oev"])
        k.setbg(3, "dv", st_c["v_oev"])

    def emit_S_dS(ck):
        par = ck % 2
        for pair in range(2):
            for hh2 in range(2):
                hd = 2 * pair + hh2
                bk = 2 + hh2
                for kc2 in range(2):
                    k.mm(PB[bk][:, 256 * kc2:256 * kc2 + 256],
                         SB[:, o_knT[ck % 4] + 256 * hd + 128 * kc2:
                            o_knT[ck % 4] + 256 * hd + 128 * kc2 + 128],
                         SB[:, o_Us + 256 * hd:o_Us + 256 * hd + 256],
                         start=(kc2 == 0), stop=(kc2 == 1),
                         waits=[k.bg(bk)] if kc2 == 0 else [],
                         inc=(kc2 == 1))
            v_ds = S["pe"].cur
            for hh2 in range(2):
                hd = 2 * pair + hh2
                bk = 2 + hh2
                k.dve(lambda e, b=bk, h=hd: e.tensor_tensor(
                    SB[:, o_SS[h]:o_SS[h] + 512], SB[:, o_SS[h]:o_SS[h] + 512],
                    PB[b][:, :], op=OP.add),
                    waits=[("pe", v_ds)] if hh2 == 0 else [], inc=True)
                k.setbg(bk, "dv")
        st_c["v_S_add"] = S["dv"].cur
        knt_guard[ck % 4] = k.w("pe", S["pe"].cur)
        vbt_guard[ck % 4] = k.w("dv", st_c["v_ys"])

    pend_feats = []

    def emit_one_sum(a, b_, idx, tens, w0):
        if b_ is None:
            k.dve(lambda e, ta=tens[a], ix=idx: e.tensor_reduce(
                SB[:, o_fr + 4 * ix:o_fr + 4 * ix + 4],
                SB[:, ta:ta + DM].rearrange("p (hh dd) -> p hh dd", hh=4),
                axis=mybir.AxisListType.X, op=OP.add), waits=w0)
        else:
            pi = st_c["pi"]
            pb_ = o_pr[pi % 2]
            wgp = w0 + [("dv", st_c["pr_guard"][pi % 2])]
            k.gps(lambda e, ta=tens[a], tb=tens[b_], po=pb_: e.tensor_mul(
                SB[:, po:po + DM], SB[:, ta:ta + DM], SB[:, tb:tb + DM]),
                waits=wgp, inc=True)
            vg = S["gp"].cur
            k.dve(lambda e, po=pb_, ix=idx: e.tensor_reduce(
                SB[:, o_fr + 4 * ix:o_fr + 4 * ix + 4],
                SB[:, po:po + DM].rearrange("p (hh dd) -> p hh dd", hh=4),
                axis=mybir.AxisListType.X, op=OP.add),
                waits=[("gp", vg)], inc=True)
            st_c["pr_guard"][pi % 2] = S["dv"].cur
            st_c["pi"] = pi + 1

    def emit_feats_pre(g):
        B_ = g
        for dst, src in ((o_vTb, vT_sp), (o_lTb, lT_sp), (o_mTb, mT_sp)):
            k.dma("sync", SB[:, dst:dst + DM], src[B_],
                  waits=[k.w("dv"), k.w("gp")])
        v_tld = S["ld"].cur
        st_c["v_ftld"] = v_tld
        tens = {"l": o_lTb, "m": o_mTb, "v": o_vTb}
        first = True
        for a, idx in (("l", 0), ("m", 2), ("v", 6)):
            emit_one_sum(a, None, idx, tens, [("ld", v_tld)] if first else [])
            first = False

    def emit_feats_prod(g):
        tens = {"l": o_lTb, "m": o_mTb, "v": o_vTb}
        first = True
        for a, b_, idx in (("l", "l", 1), ("m", "m", 3), ("v", "v", 7),
                           ("l", "m", 8), ("l", "v", 10), ("m", "v", 12)):
            emit_one_sum(a, b_, idx, tens,
                         [("ld", st_c["v_ftld"])] if first else [])
            first = False

    def emit_feats_sums(g):
        B_ = g
        sdta = g % 2
        dta_o = o_dta + sdta * DM
        tens = {"l": o_lTb, "m": o_mTb, "d": dta_o, "v": o_vTb}
        v_need = st_c["v_oev"]
        fdv, fgp = True, True
        for a, b_, idx in (("d", None, 4), ("d", "d", 5), ("l", "d", 9),
                           ("m", "d", 11), ("d", "v", 13)):
            need = fdv if b_ is None else fgp
            emit_one_sum(a, b_, idx, tens, [("dv", v_need)] if need else [])
            if b_ is None:
                fdv = False
            else:
                fgp = False
        k.dve(lambda e: e.engine_nop(), inc=True)
        v_sums = S["dv"].cur
        fto = o_ft + 64 * (B_ % 2)
        for si, qi in ((0, 1), (2, 3), (4, 5), (6, 7)):
            k.dve(lambda e, s=si, f=fto: e.tensor_scalar(
                SB[:, f + 4 * s:f + 4 * s + 4],
                SB[:, o_fr + 4 * s:o_fr + 4 * s + 4], 1.0 / DH, None,
                op0=OP.mult),
                waits=[("dv", v_sums), ("pe", st_c["ft_guard"][B_ % 2])],
                inc=True)
            k.dve(lambda e, s=si: e.tensor_tensor(
                SB[:, o_sc1:o_sc1 + 4], SB[:, o_fr + 4 * s:o_fr + 4 * s + 4],
                SB[:, o_fr + 4 * s:o_fr + 4 * s + 4], op=OP.mult), inc=True)
            k.dve(lambda e: e.tensor_scalar(
                SB[:, o_sc2:o_sc2 + 4], SB[:, o_sc1:o_sc1 + 4],
                1.0 / (255.0 * DH), None, op0=OP.mult),
                waits=[k.w("dv")], inc=True)
            k.dve(lambda e, q=qi, f=fto: e.scalar_tensor_tensor(
                SB[:, f + 4 * q:f + 4 * q + 4],
                SB[:, o_fr + 4 * q:o_fr + 4 * q + 4], 1.0 / 255.0,
                SB[:, o_sc2:o_sc2 + 4], op0=OP.mult, op1=OP.subtract),
                waits=[k.w("dv")], inc=True)
        for ix in range(8, 14):
            k.dve(lambda e, x=ix, f=fto: e.tensor_scalar(
                SB[:, f + 4 * x:f + 4 * x + 4],
                SB[:, o_fr + 4 * x:o_fr + 4 * x + 4], 1.0 / DH, None,
                op0=OP.mult), inc=(ix == 13))
        v_ft = S["dv"].cur
        dta_spill[sdta] = k.dma("scalar", dT_sp[B_], SB[:, dta_o:dta_o + DM],
                                waits=[("dv", st_c["v_oev"])], sem="sp")
        pend_feats.append((B_, v_ft))

    def emit_feats_fin():
        while pend_feats:
            B_, v_ft = pend_feats.pop(0)
            k.transp(PB[7][0:56, 0:128],
                     SB[:, o_ft + 64 * (B_ % 2):o_ft + 64 * (B_ % 2) + 56],
                     SB[:, o_id:o_id + 128], waits=[("dv", v_ft), k.bg(7)],
                     inc=True)
            v_ftr = S["pe"].cur
            st_c["ft_guard"][B_ % 2] = v_ftr
            k.dve(lambda e, b_=B_: e.tensor_copy(
                SB[0:56, o_feats + 128 * b_:o_feats + 128 * b_ + 128],
                PB[7][0:56, 0:128]), waits=[("pe", v_ftr)], inc=True)
            k.setbg(7, "dv")

    for g in range(NG + 1):
        boundary = (g % 4 == 0)
        if g < NG and boundary:
            if g > 0:
                emit_feats_pre(g - 1)
                emit_feats_prod(g - 1)
                for par in range(2):
                    ck = 2 * (g - 1) + par
                    emit_S_a(ck)
                    emit_S_u(ck)
                    emit_S_attn_o(ck)
                    emit_S_dS(ck)
                    if par == 1:
                        emit_feats_sums(g - 1)
            emit_feats_fin()
            emit_T0(g, True)
            emit_T1(g)
            emit_T2(g)
            for ri in range(6):
                emit_Tround(g, ri)
                emit_Tround_evac(g, ri)
            emit_Mprep(g)
            emit_M1(g)
            emit_M1_evac(g)
            emit_M2(g)
            emit_M2_evac(g)
        elif g < NG:
            cka, ckb = 2 * (g - 1), 2 * (g - 1) + 1
            emit_T0(g, False)
            emit_S_a(cka)
            emit_feats_pre(g - 1)
            emit_T1(g)
            emit_S_u(cka)
            emit_T2(g)
            emit_Tround(g, 0)
            emit_Tround_evac(g, 0)
            emit_S_attn_o(cka)
            emit_feats_prod(g - 1)
            emit_Tround(g, 1)
            emit_Tround_evac(g, 1)
            emit_S_dS(cka)
            emit_Tround(g, 2)
            emit_Tround_evac(g, 2)
            emit_S_a(ckb)
            emit_Tround(g, 3)
            emit_Tround_evac(g, 3)
            emit_S_u(ckb)
            emit_Tround(g, 4)
            emit_Tround_evac(g, 4)
            emit_S_attn_o(ckb)
            emit_Tround(g, 5)
            emit_Tround_evac(g, 5)
            emit_S_dS(ckb)
            emit_Mprep(g)
            emit_M1(g)
            emit_feats_fin()
            emit_M1_evac(g)
            emit_M2(g)
            emit_M2_evac(g)
            emit_feats_sums(g - 1)
        else:
            emit_feats_pre(g - 1)
            emit_feats_prod(g - 1)
            for par in range(2):
                ck = 2 * (g - 1) + par
                emit_S_a(ck)
                emit_S_u(ck)
                emit_S_attn_o(ck)
                emit_S_dS(ck)
                if par == 1:
                    emit_feats_sums(g - 1)
            emit_feats_fin()

    if "feats_dbg" in k.dbg:
        fdbg = k.dram("feats_dbg", [56, L], "ExternalOutput")
        k.dma("sync", fdbg, SB[0:56, o_feats:o_feats + L],
              waits=[k.w("dv"), k.w("pe")], sem="sp")
    k.pop()
    if "C" in phases:
        k.flush()

    k.push()
    o_rw1 = k.take(9 * 2176)
    o_rw2 = k.take(17 * 16)
    o_rb1 = k.take(32)
    o_rb2 = k.take(16)
    o_hD = k.take(NCH * TS)
    o_h1 = k.take(2 * TS)
    o_lg = k.take(TS)
    o_lgT = k.take(16)
    o_sm = k.take(64)

    k.dma("sync", SB[:, o_rw1:o_rw1 + 8 * 2176].rearrange(
        "p (c m) -> p c m", c=8),
        rw1_d[0:1024].rearrange("(c p) m -> p c m", p=128), sem="wld",
        waits=[k.w("pe"), k.w("dv"), k.w("sp")])
    k.dma("sync", SB[0:64, o_rw1 + 8 * 2176:o_rw1 + 9 * 2176],
          rw1_d[1024:1088], sem="wld")
    k.dma("sync", SB[:, o_rw2:o_rw2 + 17 * 16].rearrange(
        "p (c n) -> p c n", c=17),
        rw2_d.rearrange("(c p) n -> p c n", p=128), sem="wld")
    k.dma("sync", SB[:, o_rb1:o_rb1 + 17],
          rb1_d.rearrange("(c p) -> p c", p=128), sem="wld", slow=True)
    k.dma("sync", SB[0:16, o_rb2:o_rb2 + 1], rb2_d.rearrange("(n o) -> n o", o=1),
          sem="wld")
    v_wD = S["wld"].cur

    for st in range(NST):
        tok0 = st * TS
        k.dma("sync", SB[:, o_hD:o_hD + NCH * TS].rearrange(
            "p (c t) -> p c t", c=NCH),
            h_sp.rearrange("c p l -> p c l")[:, :, tok0:tok0 + TS],
            waits=[k.w("pe"), k.w("ac"), k.w("dv")])
        v_hld = S["ld"].cur
        lg_hist = []
        for m in range(17):
            bank = m % 2
            for kc in range(9):
                if kc < 8:
                    lh = SB[:, o_rw1 + kc * 2176 + 128 * m:
                            o_rw1 + kc * 2176 + 128 * m + 128]
                    rh = SB[:, o_hD + kc * TS:o_hD + (kc + 1) * TS]
                else:
                    lh = SB[0:64, o_rw1 + 8 * 2176 + 128 * m:
                            o_rw1 + 8 * 2176 + 128 * m + 128]
                    rh = SB[0:64, o_feats + tok0:o_feats + tok0 + TS]
                k.mm(PB[bank][:, :], lh, rh, start=(kc == 0), stop=(kc == 8),
                     waits=([("ld", v_hld), ("wld", v_wD), k.bg(bank)]
                            if kc == 0 else []),
                     inc=(kc == 8))
            v_pm = S["pe"].cur
            db = m % 2
            va = k.act(SB[:, o_h1 + db * TS:o_h1 + (db + 1) * TS], PB[bank][:, :],
                       AF.Silu, bias=SB[:, o_rb1 + m:o_rb1 + m + 1],
                       waits=[("pe", v_pm)] +
                       ([("pe", lg_hist[m - 2])] if m >= 2 else []),
                       inc=True)
            k.setbg(bank, "ac", va)
            k.mm(PB[2][0:16, :], SB[:, o_rw2 + m * 16:o_rw2 + m * 16 + 16],
                 SB[:, o_h1 + db * TS:o_h1 + (db + 1) * TS],
                 start=(m == 0), stop=(m == 16),
                 waits=[("ac", va)] + ([k.bg(2)] if m == 0 else []),
                 inc=True)
            lg_hist.append(S["pe"].cur)
        v_lgm = S["pe"].cur
        va = k.act(SB[0:16, o_lg:o_lg + TS], PB[2][0:16, :], AF.Identity,
                   bias=SB[0:16, o_rb2:o_rb2 + 1], waits=[("pe", v_lgm)],
                   inc=True)
        k.setbg(2, "ac", va)
        for blk in range(4):
            B_ = 4 * st + blk
            k.transp(PB[3][0:128, 0:16], SB[0:16, o_lg + blk * 128:o_lg + blk * 128 + 128],
                     SB[0:16, o_id:o_id + 16], waits=[("ac", va), k.bg(3)], inc=True)
            v_t = S["pe"].cur
            lgT = SB[:, o_lgT:o_lgT + 16]
            k.dve(lambda e: e.tensor_copy(lgT, PB[3][:, 0:16]),
                  waits=[("pe", v_t)], inc=True)
            k.setbg(3, "dv")
            mx = SB[:, o_sm:o_sm + 4]
            ex = SB[:, o_sm + 16:o_sm + 32]
            s4 = SB[:, o_sm + 32:o_sm + 36]
            r4 = SB[:, o_sm + 48:o_sm + 52]
            k.dve(lambda e: e.tensor_tensor(mx, lgT[:, 0:4], lgT[:, 4:8],
                                            op=OP.max), waits=[k.w("dv")], inc=True)
            k.dve(lambda e: e.tensor_tensor(s4, lgT[:, 8:12], lgT[:, 12:16],
                                            op=OP.max), inc=True)
            k.dve(lambda e: e.tensor_tensor(mx, mx, s4, op=OP.max),
                  waits=[k.w("dv")], inc=True)
            for i in range(4):
                k.dve(lambda e, i=i: e.tensor_tensor(
                    ex[:, 4 * i:4 * i + 4], lgT[:, 4 * i:4 * i + 4], mx,
                    op=OP.subtract), waits=[k.w("dv")], inc=(i == 3))
            va2 = k.act(ex, ex, AF.Exp, waits=[k.w("dv")], inc=True)
            k.dve(lambda e: e.tensor_tensor(s4, ex[:, 0:4], ex[:, 4:8],
                                            op=OP.add), waits=[("ac", va2)],
                  inc=True)
            k.dve(lambda e: e.tensor_tensor(r4, ex[:, 8:12], ex[:, 12:16],
                                            op=OP.add), inc=True)
            k.dve(lambda e: e.tensor_tensor(s4, s4, r4, op=OP.add),
                  waits=[k.w("dv")], inc=True)
            k.dve(lambda e: e.reciprocal(r4, s4), waits=[k.w("dv")], inc=True)
            pdst = SB[:, o_pT + 16 * B_:o_pT + 16 * B_ + 16]
            for i in range(4):
                k.dve(lambda e, i=i, p_=pdst: e.tensor_tensor(
                    p_[:, 4 * i:4 * i + 4], ex[:, 4 * i:4 * i + 4], r4,
                    op=OP.mult), waits=[k.w("dv")], inc=(i == 3))
            k.dve(lambda e, p_=pdst: e.tensor_scalar(
                p_, p_, 1.0 - 4 * 0.01, 0.01, op0=OP.mult, op1=OP.add),
                waits=[k.w("dv")], inc=True)
    k.pop()
    k.pop()
    if "D" in phases:
        k.flush()

    k.push()
    o_A = k.take(NCH * DM)
    o_B = k.take(NCH * DM)
    o_C = k.take(NCH * DM)
    o_mixw = k.take(DM)
    o_onr = k.take(DM)
    o_xE = k.take(NCH * TS)
    o_onF = k.take(NCH * TS)
    o_x1 = k.take(NCH * TS)
    o_t4 = [k.take(DM) for _ in range(4)]
    o_mx = k.take(DM)
    o_mw2 = k.take(DM)
    o_t3 = k.take(DM)
    o_s4 = k.take(64)
    o_fb = k.take(16)
    o_fch = k.take(TS)
    o_och = k.take(2 * TS)
    o_mrE = k.take(2 * TS)
    mrow = SB[0:1, o_mrE:o_mrE + TS]
    rrow = SB[0:1, o_mrE + TS:o_mrE + 2 * TS]

    k.dma("sync", SB[:, o_A:o_A + NCH * DM].rearrange("p (c m) -> p c m", c=NCH),
          ffn_wT_d.rearrange("(c p) m -> p c m", p=128), sem="wld",
          waits=[k.w("pe"), k.w("dv"), k.w("ac"), k.w("sp")])
    k.dma("sync", SB[:, o_B:o_B + NCH * DM].rearrange("p (c m) -> p c m", c=NCH),
          ffn_a1_d.rearrange("(c p) m -> p c m", p=128), sem="wld")
    k.dma("sync", SB[:, o_mixw:o_mixw + DM], mixw_d, sem="wld")
    k.dma("sync", SB[:, o_onr:o_onr + DM], onormr_d, sem="wld")
    k.dma("sync", SB[:, o_fb:o_fb + 8].rearrange("p c -> p c"),
          ffn_b_d.rearrange("(c p) -> p c", p=128), sem="wld", slow=True)
    v_wE = S["wld"].cur

    def mm1024(dst_off, lhs_off, rhs_off, wv, lhT_colmaj=True):
        for m in range(NCH):
            bank = m % 2
            for nn in range(2):
                for kc in range(NCH):
                    k.mm(PB[bank][:, :],
                         SB[:, lhs_off + kc * DM + 128 * m:
                            lhs_off + kc * DM + 128 * m + 128],
                         SB[:, rhs_off + kc * DM + 512 * nn:
                            rhs_off + kc * DM + 512 * nn + 512],
                         start=(kc == 0), stop=(kc == NCH - 1),
                         waits=([("wld", wv), k.bg(bank)] if kc == 0 else []),
                         inc=(kc == NCH - 1))
                v_ = S["pe"].cur
                k.dve(lambda e, b=bank, mm_=m, n_=nn, d=dst_off: e.tensor_copy(
                    SB[:, d + mm_ * DM + 512 * n_:d + mm_ * DM + 512 * n_ + 512],
                    PB[b][:, :]), waits=[("pe", v_)], inc=True)
                k.setbg(bank, "dv")

    mm1024(o_C, o_A, o_B, v_wE)
    v_T1 = S["dv"].cur
    for cc in range(NCH):
        for m in range(NCH):
            bank = 2 + (m % 2)
            k.transp(PB[bank][:, 0:128],
                     SB[:, o_C + cc * DM + 128 * m:o_C + cc * DM + 128 * m + 128],
                     SB[:, o_id:o_id + 128],
                     waits=[("dv", v_T1), k.bg(bank)] if cc == 0 and m < 2 else
                     [k.bg(bank)], inc=True)
            v_t = S["pe"].cur
            k.dve(lambda e, b=bank, c_=cc, m_=m: e.tensor_copy(
                SB[:, o_A + m_ * DM + 128 * c_:o_A + m_ * DM + 128 * c_ + 128],
                PB[b][:, 0:128]), waits=[("pe", v_t)], inc=True)
            k.setbg(bank, "dv")
    v_T1T = S["dv"].cur
    k.dma("sync", SB[:, o_B:o_B + NCH * DM].rearrange("p (c m) -> p c m", c=NCH),
          ffn_a2_d.rearrange("(c p) m -> p c m", p=128), sem="wld",
          waits=[("dv", v_T1T)])
    v_wE2 = S["wld"].cur
    mm1024(o_C, o_A, o_B, v_wE2)
    k.dma("sync", SB[:, o_A:o_A + NCH * DM].rearrange("p (c m) -> p c m", c=NCH),
          wo_d.rearrange("(c p) m -> p c m", p=128), sem="wld",
          waits=[k.w("dv")])
    v_wo = S["wld"].cur
    och_spill = [0, 0]

    onF_o = [o_onF, o_B]
    xE_o = [o_xE, o_B + NCH * TS]
    v_onF_p = [0, 0]
    v_x1_p = [0, 0]
    v_wp_p = [0, 0]
    v_xld_p = [0, 0]
    st_e = {"v_tr": 0}

    def e_front_load(st):
        p = st % 2
        tok0 = st * TS
        k.dma("sync", SB[:, xE_o[p]:xE_o[p] + NCH * TS].rearrange(
            "p (c t) -> p c t", c=NCH),
            x_fm.rearrange("(c p) l -> p c l", p=128)[:, :, tok0:tok0 + TS],
            waits=[("dv", v_x1_p[p])] +
            ([k.w("dv"), k.w("pe"), k.w("ac"), k.w("sp")] if st < 2 else []))
        v_xld_p[p] = S["ld"].cur

    def e_chain(st, blk):
        B_ = 4 * st + blk
        for ti, src in enumerate((lT_sp, mT_sp, dT_sp, vT_sp)):
            k.dma("sync", SB[:, o_t4[ti]:o_t4[ti] + DM], src[B_],
                  waits=[k.w("dv")] if ti == 0 else [])
        v_tld = S["ld"].cur
        for hd in range(H):
            for i in range(4):
                pcol = SB[:, o_pT + 16 * B_ + 4 * i + hd:
                          o_pT + 16 * B_ + 4 * i + hd + 1]
                src = SB[:, o_t4[i] + 256 * hd:o_t4[i] + 256 * hd + 256]
                dst = SB[:, o_mx + 256 * hd:o_mx + 256 * hd + 256]
                if i == 0:
                    k.dve(lambda e, s_=src, d_=dst, p_=pcol: e.tensor_scalar(
                        d_, s_, p_, None, op0=OP.mult),
                        waits=[("ld", v_tld)] if hd == 0 else [])
                else:
                    k.dve(lambda e, s_=src, d_=dst, p_=pcol:
                          e.scalar_tensor_tensor(d_, s_, p_, d_,
                                                 op0=OP.mult, op1=OP.add),
                          inc=(hd == H - 1 and i == 3))
        v_mx = S["dv"].cur
        k.dve(lambda e: e.tensor_tensor(SB[:, o_t3:o_t3 + DM],
                                        SB[:, o_mx:o_mx + DM],
                                        SB[:, o_mx:o_mx + DM], op=OP.mult),
              waits=[("dv", v_mx), ("pe", st_e["v_tr"])])
        k.dve(lambda e: e.tensor_reduce(
            SB[:, o_s4:o_s4 + 4],
            SB[:, o_t3:o_t3 + DM].rearrange("p (hh dd) -> p hh dd", hh=4),
            axis=mybir.AxisListType.X, op=OP.add), waits=[k.w("dv")],
            inc=True)
        va = k.act(SB[:, o_s4 + 8:o_s4 + 12], SB[:, o_s4:o_s4 + 4], AF.Sqrt,
                   scale=1.0 / DH, bias=1e-5, waits=[k.w("dv")], inc=True)
        k.dve(lambda e: e.reciprocal(SB[:, o_s4 + 16:o_s4 + 20],
                                     SB[:, o_s4 + 8:o_s4 + 12]),
              waits=[("ac", va)], inc=True)
        k.dve(lambda e: e.tensor_tensor(SB[:, o_mw2:o_mw2 + DM],
                                        SB[:, o_mx:o_mx + DM],
                                        SB[:, o_mixw:o_mixw + DM],
                                        op=OP.mult), inc=True)
        k.dve(lambda e: e.tensor_tensor(SB[:, o_t3:o_t3 + DM],
                                        SB[:, o_mw2:o_mw2 + DM],
                                        SB[:, o_mw2:o_mw2 + DM],
                                        op=OP.mult), waits=[k.w("dv")])
        k.dve(lambda e: e.tensor_reduce(
            SB[:, o_s4 + 24:o_s4 + 28],
            SB[:, o_t3:o_t3 + DM].rearrange("p (hh dd) -> p hh dd", hh=4),
            axis=mybir.AxisListType.X, op=OP.add), waits=[k.w("dv")],
            inc=True)
        k.dve(lambda e: e.tensor_tensor(
            SB[:, o_s4 + 32:o_s4 + 36], SB[:, o_s4 + 16:o_s4 + 20],
            SB[:, o_s4 + 16:o_s4 + 20], op=OP.mult), waits=[k.w("dv")],
            inc=True)
        k.dve(lambda e: e.tensor_tensor(
            SB[:, o_s4 + 40:o_s4 + 44], SB[:, o_s4 + 24:o_s4 + 28],
            SB[:, o_s4 + 32:o_s4 + 36], op=OP.mult), waits=[k.w("dv")],
            inc=True)
        va = k.act(SB[:, o_s4 + 44:o_s4 + 48], SB[:, o_s4 + 40:o_s4 + 44],
                   AF.Sqrt, scale=1.0 / DH, bias=1e-5, waits=[k.w("dv")],
                   inc=True)
        k.dve(lambda e: e.reciprocal(SB[:, o_s4 + 48:o_s4 + 52],
                                     SB[:, o_s4 + 44:o_s4 + 48]),
              waits=[("ac", va)], inc=True)
        k.dve(lambda e: e.tensor_tensor(
            SB[:, o_s4 + 52:o_s4 + 56], SB[:, o_s4 + 16:o_s4 + 20],
            SB[:, o_s4 + 48:o_s4 + 52], op=OP.mult), waits=[k.w("dv")],
            inc=True)
        k.dve(lambda e: e.tensor_tensor(SB[:, o_t3:o_t3 + DM],
                                        SB[:, o_mw2:o_mw2 + DM],
                                        SB[:, o_onr:o_onr + DM],
                                        op=OP.mult), waits=[k.w("dv")])
        for hd in range(H):
            k.dve(lambda e, h=hd: e.tensor_scalar(
                SB[:, o_t3 + 256 * h:o_t3 + 256 * h + 256],
                SB[:, o_t3 + 256 * h:o_t3 + 256 * h + 256],
                SB[:, o_s4 + 52 + h:o_s4 + 53 + h], None, op0=OP.mult),
                waits=[k.w("dv")] if hd == 0 else [], inc=(hd == H - 1))
        st_e["v_on"] = S["dv"].cur

    def e_transp(st, blk):
        p = st % 2
        v_on = st_e["v_on"]
        for cc in range(NCH):
            bank = 6 + (cc % 2)
            k.transp(PB[bank][:, 0:128],
                     SB[:, o_t3 + 128 * cc:o_t3 + 128 * cc + 128],
                     SB[:, o_id:o_id + 128],
                     waits=[("dv", v_on), k.bg(bank)], inc=True)
            v_t = S["pe"].cur
            k.dve(lambda e, b=bank, c_=cc, bl=blk, p_=p: e.tensor_copy(
                SB[:, onF_o[p_] + c_ * TS + 128 * bl:
                   onF_o[p_] + c_ * TS + 128 * bl + 128],
                PB[b][:, 0:128]),
                waits=[("pe", v_t)] +
                ([("pe", v_wp_p[p])] if cc == 0 and blk == 0 else []),
                inc=True)
            k.setbg(bank, "dv")
        st_e["v_tr"] = S["pe"].cur
        if blk == 3:
            v_onF_p[p] = S["dv"].cur

    def e_back_wo(stb, ms):
        p = stb % 2
        for m in ms:
            bank = m % 4
            for kc in range(NCH):
                k.mm(PB[bank][:, :],
                     SB[:, o_A + kc * DM + 128 * m:o_A + kc * DM + 128 * m + 128],
                     SB[:, onF_o[p] + kc * TS:onF_o[p] + (kc + 1) * TS],
                     start=(kc == 0), stop=(kc == NCH - 1),
                     waits=([("dv", v_onF_p[p]), ("wld", v_wo), k.bg(bank)]
                            if kc == 0 else []),
                     inc=(kc == NCH - 1))
            v_ = S["pe"].cur
            k.dve(lambda e, b=bank, m_=m, p_=p: e.scalar_tensor_tensor(
                SB[:, o_x1 + m_ * TS:o_x1 + (m_ + 1) * TS], PB[b][:, :],
                modcol(2, m_), SB[:, xE_o[p_] + m_ * TS:xE_o[p_] + (m_ + 1) * TS],
                op0=OP.mult, op1=OP.add),
                waits=[("pe", v_)] + ([("ld", v_xld_p[p])] if m == ms[0] else []),
                inc=True)
            k.setbg(bank, "dv")
        if ms[-1] == NCH - 1:
            v_x1_p[p] = S["dv"].cur
            st_e["v_x1"] = S["dv"].cur

    def e_back_ln2(stb):
        p = stb % 2
        v_x1 = st_e["v_x1"]
        for c in range(NCH):
            k.mm(PB[4][0:1, :], SB[:, o_onesc:o_onesc + 1],
                 SB[:, o_x1 + c * TS:o_x1 + (c + 1) * TS],
                 start=(c == 0), stop=(c == NCH - 1),
                 waits=[("dv", v_x1), k.bg(4)] if c == 0 else [],
                 inc=(c == NCH - 1))
        pe_hist2 = []
        for c in range(NCH):
            wv_ = ([("dv", v_x1), ("dv", st_e["v_outlast"])] if c == 0
                   else [k.w("pe", pe_hist2[-1])])
            va = k.act(SB[:, o_fch:o_fch + TS],
                       SB[:, o_x1 + c * TS:o_x1 + (c + 1) * TS], AF.Square,
                       waits=wv_, inc=True)
            k.mm(PB[5][0:1, :], SB[:, o_onesc:o_onesc + 1], SB[:, o_fch:o_fch + TS],
                 start=(c == 0), stop=(c == NCH - 1),
                 waits=[("ac", va)] + ([k.bg(5)] if c == 0 else []), inc=True)
            pe_hist2.append(S["pe"].cur)
        v_st2 = S["pe"].cur
        k.dve(lambda e: e.tensor_scalar(mrow, PB[4][0:1, :], 1.0 / DM, None,
                                        op0=OP.mult), waits=[("pe", v_st2)])
        k.dve(lambda e: e.tensor_tensor(rrow, mrow, mrow, op=OP.mult))
        k.dve(lambda e: e.scalar_tensor_tensor(
            rrow, PB[5][0:1, :], 1.0 / DM, rrow, op0=OP.mult, op1=OP.subtract),
            inc=True)
        v_var2 = S["dv"].cur
        k.setbg(4, "dv", v_var2)
        k.setbg(5, "dv", v_var2)
        va = k.act(rrow, rrow, AF.Sqrt, bias=1e-5, waits=[("dv", v_var2)],
                   inc=True)
        k.dve(lambda e: e.reciprocal(rrow, rrow), waits=[("ac", va)], inc=True)
        v_r2 = S["dv"].cur
        k.mm(PB[4][:, :], SB[0:1, o_onesr:o_onesr + 128], mrow,
             start=True, stop=True, waits=[("dv", v_r2), k.bg(4)])
        k.mm(PB[5][:, :], SB[0:1, o_onesr:o_onesr + 128], rrow,
             start=True, stop=True, waits=[k.bg(5)], inc=True)
        v_bc2 = S["pe"].cur
        for c in range(NCH):
            x1c = SB[:, o_x1 + c * TS:o_x1 + (c + 1) * TS]
            hc = SB[:, onF_o[p] + c * TS:onF_o[p] + (c + 1) * TS]
            k.dve(lambda e, x=x1c, h=hc: e.tensor_tensor(h, x, PB[4][:, :],
                                                         op=OP.subtract),
                  waits=[("pe", v_bc2)] if c == 0 else [])
            k.dve(lambda e, h=hc: e.tensor_tensor(h, h, PB[5][:, :], op=OP.mult))
            k.dve(lambda e, h=hc, c=c: e.scalar_tensor_tensor(
                h, h, modcol(4, c), h, op0=OP.mult, op1=OP.add))
            k.dve(lambda e, h=hc, c=c: e.tensor_scalar(
                h, h, modcol(3, c), None, op0=OP.add), inc=True)
        st_e["v_h2"] = S["dv"].cur
        k.setbg(4, "dv", st_e["v_h2"])
        k.setbg(5, "dv", st_e["v_h2"])

    def e_back_ffn(stb, ms):
        p = stb % 2
        tok0 = stb * TS
        v_h2 = st_e["v_h2"]
        for m in ms:
            bank = m % 4
            for kc in range(NCH):
                k.mm(PB[bank][:, :],
                     SB[:, o_C + kc * DM + 128 * m:o_C + kc * DM + 128 * m + 128],
                     SB[:, onF_o[p] + kc * TS:onF_o[p] + (kc + 1) * TS],
                     start=(kc == 0), stop=(kc == NCH - 1),
                     waits=[("dv", v_h2), k.bg(bank)] if kc == 0 else [],
                     inc=(kc == NCH - 1))
            v_ = S["pe"].cur
            va = k.act(SB[:, o_fch:o_fch + TS], PB[bank][:, :], AF.Tanh,
                       bias=SB[:, o_fb + m:o_fb + m + 1],
                       waits=[("pe", v_), k.w("dv")], inc=True)
            k.setbg(bank, "ac", va)
            db = m % 2
            k.dve(lambda e, m_=m, d_=db: e.scalar_tensor_tensor(
                SB[:, o_och + d_ * TS:o_och + (d_ + 1) * TS],
                SB[:, o_fch:o_fch + TS], modcol(5, m_),
                SB[:, o_x1 + m_ * TS:o_x1 + (m_ + 1) * TS],
                op0=OP.mult, op1=OP.add),
                waits=[("ac", va), ("sp", och_spill[db])], inc=True)
            v_out = S["dv"].cur
            och_spill[db] = k.dma(
                "sync",
                out_fm.rearrange("(c p) l -> p c l", p=128)[:, m:m + 1,
                                                            tok0:tok0 + TS],
                SB[:, o_och + db * TS:o_och + (db + 1) * TS].unsqueeze(1),
                waits=[("dv", v_out)], sem="sp")
        if ms[-1] == NCH - 1:
            v_wp_p[p] = S["pe"].cur
            st_e["v_outlast"] = S["dv"].cur

    st_e["v_x1"] = S["dv"].cur
    st_e["v_outlast"] = S["dv"].cur
    for st in range(NST + 1):
        back = None
        if st > 0:
            stb = st - 1
            back = [lambda s=stb: e_back_wo(s, (0, 1, 2, 3)),
                    lambda s=stb: e_back_wo(s, (4, 5, 6, 7)),
                    lambda s=stb: e_back_ln2(s),
                    lambda s=stb: e_back_ffn(s, (0, 1, 2, 3, 4, 5, 6, 7))]
        if st < NST:
            e_front_load(st)
            for blk in range(4):
                e_chain(st, blk)
                if back is not None:
                    back[blk]()
                e_transp(st, blk)
        else:
            for pc in back:
                pc()
    k.pop()
    if "E" in phases:
        k.flush()

    k.op("sync", lambda e: e.nop(),
         waits=[k.w("sp"), k.w("ld"), k.w("wld")])
    k.flush()
    return k


def build_C(k, env):
    raise NotImplementedError


def build_D(k, env):
    raise NotImplementedError


def build_E(k, env):
    raise NotImplementedError



def make_consts():
    ident = np.eye(128, dtype=np.float32)
    onesrow = np.ones((1, 128), np.float32)
    onescol = np.ones((128, 1), np.float32)
    indrow = np.zeros((128, 512), np.float32)
    indhead = np.zeros((128, 512), np.float32)
    for h in range(4):
        indrow[32 * h, 128 * h:128 * (h + 1)] = 1.0
        indhead[:, 128 * h + 32 * h] = 1.0
    masks = np.zeros((128, 1536), np.float32)
    sl = np.tril(np.ones((CC, CC), np.float32), -1)
    su = np.triu(np.ones((CC, CC), np.float32), 1)
    iu = np.triu(np.ones((CC, CC), np.float32), 0)
    i64 = np.eye(CC, dtype=np.float32)
    for r in range(2):
        rs = slice(r * CC, (r + 1) * CC)
        masks[rs, 0:CC] = sl
        masks[rs, CC:2 * CC] = su
        masks[rs, 2 * CC:3 * CC] = iu
        masks[rs, 3 * CC:4 * CC] = i64
        for t in range(4):
            masks[rs, 256 + t * CC:256 + (t + 1) * CC] = sl
            masks[rs, 512 + t * CC:512 + (t + 1) * CC] = su
            masks[rs, 768 + t * CC:768 + (t + 1) * CC] = iu
        for t in range(8):
            masks[rs, 1024 + t * CC:1024 + (t + 1) * CC] = i64
    return dict(ident=ident, onesrow=onesrow, onescol=onescol,
                indrow=indrow, indhead=indhead, masks=masks)


def make_inputs(d, b):
    f32 = np.float32
    c = make_consts()
    x = np.ascontiguousarray(np.asarray(d["L0_attn_hidden_states"][b]).T).astype(f32)
    wbp = np.zeros((DM, 128), f32)
    for h in range(4):
        wbp[:, 32 * h] = np.asarray(d["L0_attn_Wb"])[:, h]
    convw = np.concatenate([
        np.asarray(d["L0_attn_conv_q_w"]), np.asarray(d["L0_attn_conv_k_w"]),
        np.asarray(d["L0_attn_conv_v_w"]), np.asarray(d["L0_attn_local_w"]),
        np.asarray(d["L0_attn_mid_w"])], axis=1).astype(f32)
    rw1 = np.zeros((1088, 2176), f32)
    rw1[:1080, :2160] = np.asarray(d["L0_attn_r_w1"])
    rb1 = np.zeros((2176,), f32)
    rb1[:2160] = np.asarray(d["L0_attn_r_b1"])
    rw2 = np.asarray(d["L0_attn_r_w2"]).astype(f32)
    rb2 = np.asarray(d["L0_attn_r_b2"]).astype(f32)
    perm = [4 * h + i for i in range(4) for h in range(4)]
    rw2p = np.zeros((2176, 16), f32)
    rw2p[:2160] = rw2[:, perm]
    rb2p = np.ascontiguousarray(rb2[perm])
    mixw = np.asarray(d["L0_attn_mix_w"]).astype(f32).reshape(-1)
    onormt = np.tile(np.asarray(d["L0_attn_onorm_w"]).astype(f32), 4)
    return {
        "x_fm": x,
        "wq": np.asarray(d["L0_attn_Wq"]).astype(f32),
        "wk": np.asarray(d["L0_attn_Wk"]).astype(f32),
        "wv": np.asarray(d["L0_attn_Wv"]).astype(f32),
        "wbp": wbp,
        "convw": convw,
        "rw1": rw1, "rb1": rb1, "rw2p": rw2p, "rb2p": rb2p,
        "onorm_r": np.broadcast_to(onormt, (128, DM)).copy(),
        "mixw_r": np.broadcast_to(mixw, (128, DM)).copy(),
        "wo": np.asarray(d["L0_attn_Wo"]).astype(f32),
        "ffn_wT": np.ascontiguousarray(np.asarray(d["L0_ffn_weight"]).T).astype(f32),
        "ffn_a1": np.asarray(d["L0_ffn_aux_weight1"]).astype(f32),
        "ffn_a2": np.asarray(d["L0_ffn_aux_weight2"]).astype(f32),
        "ffn_b": np.asarray(d["L0_ffn_bias"]).astype(f32),
        "cvec": np.asarray(d["L0_c"][b]).astype(f32),
        "wmod": np.asarray(d["L0_wmod"]).astype(f32),
        **c,
    }


def kernel(**inputs):
    kk = build_program()
    in_maps = [make_inputs(inputs, b) for b in range(8)]
    res = run_bass_kernel_spmd(kk.nc, in_maps, core_ids=list(range(8)))
    out = np.stack([np.asarray(r["out_fm"]).T for r in res.results])
    return out.astype(np.float32)


# version 06
# speedup vs baseline: 1.2103x; 1.0510x over previous
import numpy as np
import concourse.bass as bass
import concourse.mybir as mybir
from concourse.bass_utils import run_bass_kernel_spmd
from concourse.alu_op_type import AluOpType as OP

F32 = mybir.dt.float32
AF = mybir.ActivationFunctionType

L = 4096
DM = 1024
H = 4
DH = 256
TS = 512
NST = L // TS
NCH = DM // 128
CC = 64
NCK = L // CC
NBLK = L // 128
VH = 24

ENG = ("tensor", "vector", "scalar", "gpsimd", "sync")


class Sem:
    def __init__(self, h):
        self.h = h
        self.cur = 0


class K:

    def __init__(self, debug=()):
        self.nc = bass.Bass("TRN2", target_bir_lowering=False)
        self.dbg = set(debug)
        self.steps = []
        self.sems = {}
        self.SBN = 53200
        self.SB = self.nc.alloc_sbuf_tensor("SBP", [128, self.SBN], F32).ap()
        self.PB = [self.nc.alloc_psum_tensor(f"PB{i}", [128, 512], F32).ap()
                   for i in range(8)]
        self.off = 0
        self.marks = []
        for s in ("ld", "wld", "pe", "dv", "ac", "gp", "sp"):
            self.sems[s] = Sem(self.nc.alloc_semaphore(s))
        self.bguard = [("dv", 0)] * 8

    def take(self, n, align=16):
        o = (self.off + align - 1) // align * align
        self.off = o + n
        assert self.off <= self.SBN, f"SBUF overflow {self.off} > {self.SBN}"
        return o

    def push(self):
        self.marks.append(self.off)

    def pop(self):
        self.off = self.marks.pop()

    def dram(self, name, shape, kind=None):
        if name in self.dbg:
            kind = "ExternalOutput"
        if kind is None:
            return self.nc.dram_tensor(name, shape, F32).ap()
        return self.nc.dram_tensor(name, shape, F32, kind=kind).ap()

    def w(self, name, val=None):
        s = self.sems[name]
        return (s, s.cur if val is None else val)

    def bg(self, i):
        n, v = self.bguard[i]
        return (self.sems[n], v)

    def setbg(self, i, name, val=None):
        self.bguard[i] = (name, self.sems[name].cur if val is None else val)

    def op(self, engine, fn, waits=(), inc=None):
        ws = []
        for t in waits:
            if isinstance(t[0], Sem):
                ws.append(t)
            else:
                ws.append((self.sems[t[0]], t[1]))
        self.steps.append((engine, fn, tuple(ws), inc))
        if inc is not None:
            inc[0].cur += inc[1]

    def flush(self):
        steps, self.steps = self.steps, []
        if not steps:
            return

        def emitter(engine):
            def run(eng):
                for e, fn, waits, inc in steps:
                    if e != engine:
                        continue
                    for s, v in waits:
                        eng.wait_ge(s.h, v)
                    inst = fn(eng)
                    if inc is not None:
                        s, amt = inc
                        assert inst is not None
                        inst.then_inc(s.h, amt)
            return run

        used = [e for e in ENG if any(s[0] == e for s in steps)]
        with self.nc.Block() as block:
            for e in used:
                getattr(block, e)(emitter(e))

    def dma(self, q, out, in_, waits=(), sem="ld", slow=False):
        s = self.sems[sem]
        if slow:
            self.op(q, lambda e, o=out, i=in_: e.dma_start(
                out=o, in_=i, allow_slow_non_contiguous=True),
                waits=waits, inc=(s, 16))
        else:
            self.op(q, lambda e, o=out, i=in_: e.dma_start(out=o, in_=i),
                    waits=waits, inc=(s, 16))
        return s.cur

    def mm(self, out, lhsT, rhs, start, stop, waits=(), inc=False, tp=None):
        s = self.sems["pe"]
        self.op("tensor",
                lambda e, o=out, l=lhsT, r=rhs, st=start, sp=stop, t=tp:
                e.matmul(o, l, r, start=st, stop=sp, tile_position=t),
                waits=waits, inc=(s, 1) if inc else None)
        return s.cur

    def transp(self, out, in_, ident, waits=(), inc=False):
        s = self.sems["pe"]
        self.op("tensor", lambda e, o=out, i=in_, d=ident: e.transpose(o, i, d),
                waits=waits, inc=(s, 1) if inc else None)
        return s.cur

    def dve(self, fn, waits=(), inc=False):
        s = self.sems["dv"]
        self.op("vector", fn, waits=waits, inc=(s, 1) if inc else None)
        return s.cur

    def gps(self, fn, waits=(), inc=False):
        s = self.sems["gp"]
        self.op("gpsimd", fn, waits=waits, inc=(s, 1) if inc else None)
        return s.cur

    def act(self, out, in_, func, bias=None, scale=None, waits=(), inc=False):
        s = self.sems["ac"]
        kw = {}
        if bias is not None:
            kw["bias"] = bias
        if scale is not None:
            kw["scale"] = scale
        self.op("scalar", lambda e, o=out, i=in_, f=func, kk=dict(kw):
                e.activation(o, i, f, **kk),
                waits=waits, inc=(s, 1) if inc else None)
        return s.cur


def build_program(debug=(), phases=("P0", "B", "C", "D", "E")):
    k = K(debug)
    SB, PB = k.SB, k.PB
    S = k.sems

    EIn = "ExternalInput"
    x_fm = k.dram("x_fm", [DM, L], EIn)
    wq_d = k.dram("wq", [DM, DM], EIn)
    wk_d = k.dram("wk", [DM, DM], EIn)
    wv_d = k.dram("wv", [DM, DM], EIn)
    wbp_d = k.dram("wbp", [DM, 128], EIn)
    convw_d = k.dram("convw", [DM, 44], EIn)
    rw1_d = k.dram("rw1", [1088, 2176], EIn)
    rb1_d = k.dram("rb1", [2176], EIn)
    rw2_d = k.dram("rw2p", [2176, 16], EIn)
    rb2_d = k.dram("rb2p", [16], EIn)
    onormr_d = k.dram("onorm_r", [128, DM], EIn)
    mixw_d = k.dram("mixw_r", [128, DM], EIn)
    wo_d = k.dram("wo", [DM, DM], EIn)
    ffn_wT_d = k.dram("ffn_wT", [DM, DM], EIn)
    ffn_a1_d = k.dram("ffn_a1", [DM, DM], EIn)
    ffn_a2_d = k.dram("ffn_a2", [DM, DM], EIn)
    ffn_b_d = k.dram("ffn_b", [DM], EIn)
    cvec_d = k.dram("cvec", [DM], EIn)
    wmod_d = k.dram("wmod", [DM, 6 * DM], EIn)
    ident_d = k.dram("ident", [128, 128], EIn)
    onesrow_d = k.dram("onesrow", [1, 128], EIn)
    onescol_d = k.dram("onescol", [128, 1], EIn)
    indrow_d = k.dram("indrow", [128, 512], EIn)
    indhead_d = k.dram("indhead", [128, 512], EIn)
    masks_d = k.dram("masks", [128, 1536], EIn)

    h_sp = k.dram("h_sp", [NCH, 128, L])
    q_sp = k.dram("q_sp", [NCH, 128, L])
    kn_sp = k.dram("kn_sp", [NCH, 128, L])
    kb_sp = k.dram("kb_sp", [NCH, 128, L])
    knT_sp = k.dram("knT_sp", [NCK, 64, DM])
    vbT_sp = k.dram("vbT_sp", [NCK, 64, DM])
    vT_sp = k.dram("vT_sp", [NBLK, 128, DM])
    lT_sp = k.dram("lT_sp", [NBLK, 128, DM])
    mT_sp = k.dram("mT_sp", [NBLK, 128, DM])
    dT_sp = k.dram("dT_sp", [NBLK, 128, DM])
    rqT_sp = k.dram("rqT_sp", [NBLK, 128, 4])
    mod_sp = k.dram("mod_sp", [6 * DM])
    wp_sp = k.dram("wp_sp", [DM, DM])
    out_fm = k.dram("out_fm", [DM, L], "ExternalOutput")

    o_ceps = k.take(16)
    k.nc.const_aps.register(F32, 1e-5, SB[:, o_ceps:o_ceps + 1])
    k.nc.const_aps.register(F32, 1e-12, SB[:, o_ceps + 1:o_ceps + 2])
    k.dve(lambda e: e.memset(SB[:, o_ceps:o_ceps + 1], 1e-5))
    k.dve(lambda e: e.memset(SB[:, o_ceps + 1:o_ceps + 2], 1e-12), inc=True)
    o_mod = k.take(48)
    o_id = k.take(128)
    o_onesr = k.take(128)
    o_onesc = k.take(16)
    o_indr = k.take(512)
    o_indh = k.take(512)
    o_msk = k.take(256)
    o_cw = k.take(NCH * 44)

    MSL, MSU, MIU, MI64 = o_msk, o_msk + CC, o_msk + 2 * CC, o_msk + 3 * CC

    def modcol(v, c):
        return SB[:, o_mod + 8 * v + c:o_mod + 8 * v + c + 1]

    k.push()
    o_sc = k.take(16)
    o_wmb = k.take(2 * 512)
    o_mrow = k.take(512)

    k.dma("sync", SB[:, o_sc:o_sc + 8], cvec_d.rearrange("(c p) -> p c", p=128),
          slow=True)
    v_c = S["ld"].cur
    for dst, src, wdt in ((o_id, ident_d, 128), (o_onesr, onesrow_d, 128),
                         (o_onesc, onescol_d, 1), (o_indr, indrow_d, 512),
                         (o_indh, indhead_d, 512)):
        k.dma("sync", SB[:src.shape[0], dst:dst + wdt], src, sem="wld")
    k.dma("sync", SB[:, o_msk:o_msk + 256], masks_d[:, 0:256], sem="wld")
    k.dma("sync", SB[:, o_cw:o_cw + NCH * 44].rearrange("p (c w) -> p c w", c=NCH),
          convw_d.rearrange("(c p) w -> p c w", p=128), sem="wld")

    va = k.act(SB[:, o_sc:o_sc + 8], SB[:, o_sc:o_sc + 8], AF.Silu,
               waits=[("ld", v_c)], inc=True)

    pe_hist = []
    for j in range(12):
        for kc in range(NCH):
            n = j * NCH + kc
            buf = o_wmb + (n % 2) * 512
            wt = [k.w("pe", pe_hist[n - 2])] if n >= 2 else []
            vl = k.dma("sync", SB[:, buf:buf + 512],
                       wmod_d[kc * 128:(kc + 1) * 128, j * 512:(j + 1) * 512],
                       waits=wt)
            k.mm(PB[0][0:1, :], SB[:, o_sc + kc:o_sc + kc + 1],
                 SB[:, buf:buf + 512], start=(kc == 0), stop=(kc == NCH - 1),
                 waits=[("ld", vl), ("ac", va)] + ([k.w("dv")] if kc == 0 else []),
                 inc=True)
            pe_hist.append(S["pe"].cur)
        k.dve(lambda e: e.tensor_copy(SB[0:1, o_mrow:o_mrow + 512], PB[0][0:1, :]),
              waits=[k.w("pe")], inc=True)
        k.dma("sync", mod_sp[j * 512:(j + 1) * 512], SB[0:1, o_mrow:o_mrow + 512],
              waits=[k.w("dv")], sem="sp")
    v_modsp = S["sp"].cur
    k.pop()

    k.dma("sync", SB[:, o_mod:o_mod + 48].rearrange("p (v c) -> p v c", v=6),
          mod_sp.rearrange("(v c p) -> p v c", v=6, c=NCH),
          waits=[("sp", v_modsp)], slow=True)
    v_modld = S["ld"].cur

    k.push()
    o_t1, o_t2, o_t3 = k.take(16), k.take(16), k.take(16)
    for v6 in (2, 5):
        g = SB[:, o_mod + 8 * v6:o_mod + 8 * v6 + 8]
        t1, t2, t3 = (SB[:, o:o + 8] for o in (o_t1, o_t2, o_t3))
        k.dve(lambda e, g=g, t=t1: e.tensor_scalar(t, g, 0.25, None, op0=OP.max),
              waits=[("ld", v_modld)], inc=True)
        k.dve(lambda e, g=g, t=t2: e.tensor_scalar(t, g, -0.25, None, op0=OP.min),
              inc=True)
        k.dve(lambda e, g=g, t=t3: e.tensor_scalar(t, g, 0.0, None, op0=OP.is_ge),
              inc=True)
        k.dve(lambda e, a=t1, b=t2: e.tensor_tensor(a, a, b, op=OP.subtract),
              waits=[k.w("dv")], inc=True)
        k.dve(lambda e, a=t1, b=t3: e.tensor_tensor(a, a, b, op=OP.mult),
              waits=[k.w("dv")], inc=True)
        k.dve(lambda e, g=g, a=t1, b=t2: e.tensor_tensor(g, a, b, op=OP.add),
              waits=[k.w("dv")], inc=True)
    k.pop()
    v_mod_dv = S["dv"].cur
    if "P0" in phases:
        k.flush()

    k.push()
    VSEG = NCH * (VH + TS)
    o_wq = k.take(NCH * DM)
    o_wk = k.take(NCH * DM)
    o_wv = k.take(2 * 1024)
    o_wb = k.take(NCH * 128)
    o_x = k.take(NCH * TS)
    o_h = k.take(NCH * TS)
    o_kn = k.take(NCH * TS)
    o_v = k.take(2 * VSEG)
    o_lin = k.take(TS + 16)
    o_halo = k.take(3 * NCH * 4)
    o_cb = k.take(TS)
    o_cb5 = k.take(2 * TS)
    o_sq2 = k.take(2 * TS)
    o_qc = k.take(2 * TS)
    o_kbc = k.take(2 * TS)
    o_mr = k.take(2 * TS)
    o_bs = k.take(TS)
    o_rq = k.take(TS)
    o_rk = k.take(TS)
    o_tma = k.take(DM)
    o_tmb = k.take(DM)
    o_bT = k.take(16)
    o_rqTb = k.take(16)

    for dst, src in ((o_wq, wq_d), (o_wk, wk_d)):
        k.dma("sync", SB[:, dst:dst + NCH * DM].rearrange("p (c m) -> p c m", c=NCH),
              src.rearrange("(c p) m -> p c m", p=128), sem="wld")
    k.dma("sync", SB[:, o_wb:o_wb + NCH * 128].rearrange("p (c m) -> p c m", c=NCH),
          wbp_d.rearrange("(c p) m -> p c m", p=128), sem="wld")
    v_wB = S["wld"].cur

    wv_ldv = [0, 0]
    wv_pe_guard = [0, 0]

    def load_wv(c):
        sl = c % 2
        k.dma("sync",
              SB[:, o_wv + sl * 1024:o_wv + sl * 1024 + 1024].rearrange(
                  "p (kc m) -> p kc m", kc=NCH),
              wv_d.rearrange("(kc p) m -> p kc m", p=128)[:, :,
                                                          c * 128:(c + 1) * 128],
              waits=[("pe", wv_pe_guard[sl])], sem="wld")
        wv_ldv[sl] = S["wld"].cur

    load_wv(0)
    load_wv(1)

    k.dve(lambda e: e.memset(SB[:, o_halo:o_halo + 3 * NCH * 4], 0.0))
    for vp in range(2):
        for c in range(NCH):
            k.dve(lambda e, vp=vp, c=c: e.memset(
                SB[:, o_v + vp * VSEG + c * (VH + TS):
                   o_v + vp * VSEG + c * (VH + TS) + VH], 0.0),
                inc=(vp == 1 and c == NCH - 1))

    def conv_taps(lin_off, w_off, ntap, out_off, waits0=()):
        k.dve(lambda e, lo=lin_off, wo=w_off, oo=out_off: e.tensor_scalar(
            SB[:, oo:oo + TS], SB[:, lo:lo + TS], SB[:, wo:wo + 1], None,
            op0=OP.mult), waits=waits0)
        for j in range(1, ntap):
            k.dve(lambda e, lo=lin_off + j, wo=w_off + j, oo=out_off:
                  e.scalar_tensor_tensor(
                      SB[:, oo:oo + TS], SB[:, lo:lo + TS], SB[:, wo:wo + 1],
                      SB[:, oo:oo + TS], op0=OP.mult, op1=OP.add))

    qspill = [0, 0]
    kbspill = [0, 0]
    rqt_spill = 0
    tma_spill = 0
    tmb_spill = 0

    u_loc = {}
    u_mid = {}
    u_tput_loc = [0, 0]
    u_tput_mid = [0]
    st_b = {"ac_vend": 0}

    def conv_unit(sp, c):
        vpar = sp % 2
        vb = o_v + vpar * VSEG + c * (VH + TS)
        cb5 = o_cb5 + (c % 2) * TS
        w0 = [("pe", u_tput_loc[c % 2])]
        if c == 0:
            w0.append(("ac", st_b["ac_vend"]))
        conv_taps(vb + VH - 6, o_cw + c * 44 + 12, 7, cb5, waits0=w0)
        k.dve(lambda e: e.engine_nop(), inc=True)
        u_loc[c] = S["dv"].cur
        first = True
        for j in range(25):
            lo = vb + j
            wo = o_cw + c * 44 + 19 + j
            if first:
                k.dve(lambda e, l=lo, w=wo: e.tensor_scalar(
                    SB[:, o_tmb:o_tmb + 512], SB[:, l:l + TS],
                    SB[:, w:w + 1], None, op0=OP.mult),
                    waits=[("sp", tmb_spill), ("pe", u_tput_mid[0])])
                first = False
            else:
                k.dve(lambda e, l=lo, w=wo: e.scalar_tensor_tensor(
                    SB[:, o_tmb:o_tmb + 512], SB[:, l:l + TS],
                    SB[:, w:w + 1], SB[:, o_tmb:o_tmb + 512],
                    op0=OP.mult, op1=OP.add))
        k.dve(lambda e: e.engine_nop(), inc=True)
        u_mid[c] = S["dv"].cur
        dhl = o_v + (1 - vpar) * VSEG + c * (VH + TS)
        k.dve(lambda e, s_=vb + TS, d_=dhl: e.tensor_copy(
            SB[:, d_:d_ + VH], SB[:, s_:s_ + VH]))

    def tput_unit(sp, c):
        nonlocal tma_spill
        cb5 = o_cb5 + (c % 2) * TS
        for src_o, v_rdy, tgt_sp, kind in ((cb5, u_loc[c], lT_sp, "loc"),
                                           (o_tmb, u_mid[c], mT_sp, "mid")):
            for blk in range(4):
                k.transp(PB[2][:, blk * 128:(blk + 1) * 128],
                         SB[:, src_o + blk * 128:src_o + blk * 128 + 128],
                         SB[:, o_id:o_id + 128],
                         waits=[("dv", v_rdy), k.bg(2)] if blk == 0 else [],
                         inc=(blk == 3))
            v_tl = S["pe"].cur
            if kind == "loc":
                u_tput_loc[c % 2] = v_tl
            else:
                u_tput_mid[0] = v_tl
            k.dve(lambda e: e.tensor_copy(SB[:, o_tma:o_tma + 512],
                                          PB[2][:, :]),
                  waits=[("pe", v_tl), ("sp", tma_spill)], inc=True)
            v_tlm = S["dv"].cur
            k.setbg(2, "dv", v_tlm)
            for blk in range(4):
                k.dma("scalar", tgt_sp[4 * sp + blk][:, c * 128:(c + 1) * 128],
                      SB[:, o_tma + blk * 128:o_tma + blk * 128 + 128],
                      waits=[("dv", v_tlm)], sem="sp")
            tma_spill = S["sp"].cur

    def emit_unit_slot(sp, sidx):
        if sp < 0 or sidx < 2 or (sidx - 2) % 3 != 0:
            return
        u = (sidx - 2) // 3
        if u >= NCH:
            return
        if u > 0:
            tput_unit(sp, u - 1)
        conv_unit(sp, u)
    sq2_guard = [("dv", 0), ("dv", 0)]
    mrow = SB[0:1, o_mr:o_mr + TS]
    rrow = SB[0:1, o_mr + TS:o_mr + 2 * TS]

    g_xfree = []
    for st in range(NST):
        tok0 = st * TS
        v_x = k.dma("sync", SB[:, o_x:o_x + NCH * TS].rearrange(
            "p (c t) -> p c t", c=NCH),
            x_fm.rearrange("(c p) l -> p c l", p=128)[:, :, tok0:tok0 + TS],
            waits=g_xfree)

        pe_sx2 = []
        for c in range(NCH):
            k.mm(PB[2][0:1, :], SB[:, o_onesc:o_onesc + 1],
                 SB[:, o_x + c * TS:o_x + (c + 1) * TS],
                 start=(c == 0), stop=(c == NCH - 1),
                 waits=[("ld", v_x), k.bg(2)] if c == 0 else [],
                 inc=(c == NCH - 1))
        for c in range(NCH):
            wv_ = ([("ld", v_x)] if c == 0 else [k.w("pe", pe_sx2[-1])]) + \
                [sq2_guard[c % 2]]
            va = k.act(SB[:, o_sq2 + (c % 2) * TS:o_sq2 + (c % 2) * TS + TS],
                       SB[:, o_x + c * TS:o_x + (c + 1) * TS], AF.Square,
                       waits=wv_, inc=True)
            k.mm(PB[3][0:1, :], SB[:, o_onesc:o_onesc + 1],
                 SB[:, o_sq2 + (c % 2) * TS:o_sq2 + (c % 2) * TS + TS],
                 start=(c == 0), stop=(c == NCH - 1),
                 waits=[("ac", va)] + ([k.bg(3)] if c == 0 else []), inc=True)
            pe_sx2.append(S["pe"].cur)
            sq2_guard[c % 2] = ("pe", S["pe"].cur)
        v_st = S["pe"].cur
        v_xsq = S["ac"].cur
        k.dve(lambda e: e.tensor_scalar(mrow, PB[2][0:1, :], 1.0 / DM, None,
                                        op0=OP.mult), waits=[("pe", v_st)])
        k.dve(lambda e: e.tensor_tensor(rrow, mrow, mrow, op=OP.mult))
        k.dve(lambda e: e.scalar_tensor_tensor(
            rrow, PB[3][0:1, :], 1.0 / DM, rrow, op0=OP.mult, op1=OP.subtract),
            inc=True)
        v_var = S["dv"].cur
        k.setbg(2, "dv", v_var)
        k.setbg(3, "dv", v_var)
        va = k.act(rrow, rrow, AF.Sqrt, bias=1e-5, waits=[("dv", v_var)], inc=True)
        k.dve(lambda e: e.reciprocal(rrow, rrow), waits=[("ac", va)], inc=True)
        v_r = S["dv"].cur
        k.mm(PB[4][:, :], SB[0:1, o_onesr:o_onesr + 128], mrow,
             start=True, stop=True, waits=[("dv", v_r), k.bg(4)])
        k.mm(PB[5][:, :], SB[0:1, o_onesr:o_onesr + 128], rrow,
             start=True, stop=True, waits=[k.bg(5)], inc=True)
        v_bc = S["pe"].cur

        for c in range(NCH):
            xc = SB[:, o_x + c * TS:o_x + (c + 1) * TS]
            hc = SB[:, o_h + c * TS:o_h + (c + 1) * TS]
            w0 = [("pe", v_bc)]
            if c == 0:
                w0.append(("sp", st_b.get("h_spill", 0)))
            if st == 0 and c == 0:
                w0 += [("ld", v_modld), ("dv", v_mod_dv)]
            k.dve(lambda e, x=xc, h=hc: e.tensor_tensor(h, x, PB[4][:, :],
                                                        op=OP.subtract), waits=w0)
            k.dve(lambda e, h=hc: e.tensor_tensor(h, h, PB[5][:, :], op=OP.mult))
            k.dve(lambda e, h=hc, c=c: e.scalar_tensor_tensor(
                h, h, modcol(1, c), h, op0=OP.mult, op1=OP.add))
            k.dve(lambda e, h=hc, c=c: e.tensor_scalar(
                h, h, modcol(0, c), None, op0=OP.add), inc=True)
        v_h = S["dv"].cur
        k.setbg(4, "dv", v_h)
        k.setbg(5, "dv", v_h)
        st_b["h_spill"] = k.dma(
            "sync", h_sp.rearrange("c p l -> p c l")[:, :, tok0:tok0 + TS],
            SB[:, o_h:o_h + NCH * TS].rearrange("p (c t) -> p c t", c=NCH),
            waits=[("dv", v_h)], sem="sp")

        for c in range(NCH):
            k.mm(PB[7][:, :], SB[:, o_wb + c * 128:o_wb + (c + 1) * 128],
                 SB[:, o_h + c * TS:o_h + (c + 1) * TS],
                 start=(c == 0), stop=(c == NCH - 1),
                 waits=[("dv", v_h), ("wld", v_wB), k.bg(7)] if c == 0 else [],
                 inc=(c == NCH - 1))
        v_bl = S["pe"].cur
        va = k.act(SB[:, o_bs:o_bs + TS], PB[7][:, :], AF.Sigmoid,
                   waits=[("pe", v_bl)], inc=True)
        k.setbg(7, "ac", va)

        g_xfree = [("pe", v_st), ("ac", v_xsq), ("dv", v_h)]
        vpar = st % 2
        for ti, (ow, cwo, tgt) in enumerate(((o_wq, 0, "q"), (o_wk, 4, "k"),
                                             (None, 8, "v"))):
            sum_hist = []
            pend_sq = None

            def emit_sum(cw, vaw):
                k.mm(PB[6][:, :],
                     SB[:, o_indh + (cw // 2) * 128:o_indh + (cw // 2) * 128 + 128],
                     SB[:, o_sq2 + (cw % 2) * TS:o_sq2 + (cw % 2) * TS + TS],
                     start=(cw == 0), stop=(cw == NCH - 1),
                     waits=[("ac", vaw)] + ([k.bg(6)] if cw == 0 else []),
                     inc=True)
                sum_hist.append(S["pe"].cur)
                sq2_guard[cw % 2] = ("pe", S["pe"].cur)

            if tgt == "k" and st > 0:
                load_wv(0)
                load_wv(1)
            for c in range(NCH):
                bank = c % 2
                for kc in range(NCH):
                    if tgt == "v":
                        lh = SB[:, o_wv + (c % 2) * 1024 + kc * 128:
                                o_wv + (c % 2) * 1024 + kc * 128 + 128]
                    else:
                        lh = SB[:, ow + kc * DM + c * 128:
                                ow + kc * DM + (c + 1) * 128]
                    k.mm(PB[bank][:, :], lh,
                         SB[:, o_h + kc * TS:o_h + (kc + 1) * TS],
                         start=(kc == 0), stop=(kc == NCH - 1),
                         waits=([k.bg(bank)] +
                                ([("dv", v_h)] if (ti == 0 and c == 0) else []) +
                                ([("wld", wv_ldv[c % 2])] if tgt == "v" else []))
                         if kc == 0 else [],
                         inc=(kc == NCH - 1))
                v_p = S["pe"].cur
                if tgt == "v":
                    wv_pe_guard[c % 2] = v_p
                    if c + 2 < NCH:
                        load_wv(c + 2)
                if pend_sq is not None:
                    emit_sum(*pend_sq)
                    pend_sq = None
                hs = o_halo + (ti * NCH + c) * 4
                k.dve(lambda e, h=hs: e.tensor_copy(SB[:, o_lin:o_lin + 3],
                                                    SB[:, h:h + 3]),
                      waits=[("pe", v_p)])
                k.dve(lambda e, b=bank: e.tensor_copy(
                    SB[:, o_lin + 3:o_lin + 3 + TS], PB[b][:, :]))
                k.dve(lambda e, h=hs, b=bank: e.tensor_copy(
                    SB[:, h:h + 3], PB[b][:, TS - 3:TS]), inc=True)
                k.setbg(bank, "dv")
                conv_taps(o_lin, o_cw + c * 44 + cwo, 4, o_cb,
                          waits0=[k.w("ac")])
                k.dve(lambda e: e.engine_nop(), inc=True)
                v_cv = S["dv"].cur
                if tgt == "q":
                    db = c % 2
                    dst = SB[:, o_qc + db * TS:o_qc + (db + 1) * TS]
                elif tgt == "k":
                    dst = SB[:, o_kn + c * TS:o_kn + (c + 1) * TS]
                else:
                    vb0 = o_v + vpar * VSEG + c * (VH + TS) + VH
                    dst = SB[:, vb0:vb0 + TS]
                wsilu = [("dv", v_cv)]
                if tgt == "q":
                    wsilu.append(("sp", qspill[db]))
                if tgt == "k" and c == 0:
                    wsilu.append(("sp", st_b.get("kn_spill", 0)))
                v_silu = k.act(dst, SB[:, o_cb:o_cb + TS], AF.Silu, waits=wsilu,
                               inc=True)
                if tgt in ("q", "k"):
                    va2 = k.act(SB[:, o_sq2 + (c % 2) * TS:o_sq2 + (c % 2) * TS + TS],
                                dst, AF.Square,
                                waits=[("ac", v_silu), sq2_guard[c % 2]],
                                inc=True)
                    pend_sq = (c, va2)
                if tgt == "q":
                    qspill[db] = k.dma(
                        "scalar",
                        q_sp.rearrange("c p l -> p c l")[:, c:c + 1, tok0:tok0 + TS],
                        dst.unsqueeze(1), waits=[("ac", v_silu)], sem="sp")
                emit_unit_slot(st - 1, ti * NCH + c)
            if pend_sq is not None:
                emit_sum(*pend_sq)
                pend_sq = None
            if tgt in ("q", "k"):
                v_sum = S["pe"].cur
                orr = o_rq if tgt == "q" else o_rk
                va = k.act(SB[:, orr:orr + TS], PB[6][:, :], AF.Sqrt, bias=1e-12,
                           waits=[("pe", v_sum)], inc=True)
                k.setbg(6, "ac", va)
                k.dve(lambda e, o=orr: e.reciprocal(SB[:, o:o + TS],
                                                    SB[:, o:o + TS]),
                      waits=[("ac", va)], inc=True)
        st_b["ac_vend"] = S["ac"].cur
        if st > 0:
            tput_unit(st - 1, NCH - 1)

        v_knc = S["dv"].cur
        for hh in range(H):
            k.mm(PB[4][:, :], SB[:, o_indr + hh * 128:o_indr + hh * 128 + 128],
                 SB[:, o_rk:o_rk + TS], start=True, stop=True,
                 waits=[k.w("dv"), k.bg(4)])
            k.mm(PB[5][:, :], SB[:, o_indr + hh * 128:o_indr + hh * 128 + 128],
                 SB[:, o_bs:o_bs + TS], start=True, stop=True,
                 waits=[k.bg(5)], inc=True)
            v_b2 = S["pe"].cur
            for c in (2 * hh, 2 * hh + 1):
                kc_ap = SB[:, o_kn + c * TS:o_kn + (c + 1) * TS]
                db = c % 2
                k.dve(lambda e, kc=kc_ap: e.tensor_tensor(kc, kc, PB[4][:, :],
                                                          op=OP.mult),
                      waits=[("pe", v_b2)])
                k.dve(lambda e, kc=kc_ap, d=db: e.tensor_tensor(
                    SB[:, o_kbc + d * TS:o_kbc + (d + 1) * TS], kc, PB[5][:, :],
                    op=OP.mult), waits=[("sp", kbspill[db])], inc=True)
                v_knc = S["dv"].cur
                kbspill[db] = k.dma(
                    "scalar",
                    kb_sp.rearrange("c p l -> p c l")[:, c:c + 1, tok0:tok0 + TS],
                    SB[:, o_kbc + db * TS:o_kbc + (db + 1) * TS].unsqueeze(1),
                    waits=[("dv", v_knc)], sem="sp")
                k.dma("scalar",
                      kn_sp.rearrange("c p l -> p c l")[:, c:c + 1, tok0:tok0 + TS],
                      kc_ap.unsqueeze(1), waits=[("dv", v_knc)], sem="sp")
            k.setbg(4, "dv", v_knc)
            k.setbg(5, "dv", v_knc)
        st_b["kn_spill"] = S["sp"].cur

        for blk in range(4):
            B_ = 4 * st + blk
            k.transp(PB[7][:, 0:128], SB[:, o_bs + blk * 128:o_bs + blk * 128 + 128],
                     SB[:, o_id:o_id + 128], waits=[k.w("dv", v_knc), k.bg(7)])
            k.transp(PB[7][:, 128:256], SB[:, o_rq + blk * 128:o_rq + blk * 128 + 128],
                     SB[:, o_id:o_id + 128], inc=True)
            v_tb = S["pe"].cur
            k.dve(lambda e: e.tensor_copy(SB[:, o_bT:o_bT + 4], PB[7][:, 0:128:32]),
                  waits=[("pe", v_tb), ("sp", rqt_spill)])
            k.dve(lambda e: e.tensor_copy(SB[:, o_rqTb:o_rqTb + 4],
                                          PB[7][:, 128:256:32]), inc=True)
            v_t2 = S["dv"].cur
            k.setbg(7, "dv", v_t2)
            rqt_spill = k.dma("scalar", rqT_sp[B_], SB[:, o_rqTb:o_rqTb + 4],
                              waits=[("dv", v_t2)], sem="sp")

            for half in range(2):
                bank = 2 + half
                for cp in range(4):
                    c = half * 4 + cp
                    src = SB[:, o_v + vpar * VSEG + c * (VH + TS) + VH + blk * 128:
                             o_v + vpar * VSEG + c * (VH + TS) + VH + blk * 128 + 128]
                    k.transp(PB[bank][:, cp * 128:(cp + 1) * 128], src,
                             SB[:, o_id:o_id + 128],
                             waits=[k.bg(bank)] if cp == 0 else [], inc=(cp == 3))
                v_tv = S["pe"].cur
                k.dve(lambda e, b=bank, hf=half: e.tensor_copy(
                    SB[:, o_tma + hf * 512:o_tma + hf * 512 + 512], PB[b][:, :]),
                    waits=[("pe", v_tv), ("sp", tma_spill)])
                for h2 in range(2):
                    hd = half * 2 + h2
                    k.dve(lambda e, b=bank, h2=h2, hd=hd: e.tensor_scalar(
                        SB[:, o_tmb + hd * 256:o_tmb + (hd + 1) * 256],
                        PB[b][:, h2 * 256:(h2 + 1) * 256],
                        SB[:, o_bT + hd:o_bT + hd + 1], None, op0=OP.mult),
                        waits=[("sp", tmb_spill)] if h2 == 0 else [],
                        inc=(h2 == 1))
                k.setbg(bank, "dv")
            v_vtm = S["dv"].cur
            k.dma("scalar", vT_sp[B_], SB[:, o_tma:o_tma + DM],
                  waits=[("dv", v_vtm)], sem="sp")
            tma_spill = S["sp"].cur
            for par in range(2):
                ck = 2 * B_ + par
                k.dma("scalar", vbT_sp[ck],
                      SB[64 * par:64 * par + 64, o_tmb:o_tmb + DM],
                      waits=[("dv", v_vtm)], sem="sp")
            tmb_spill = S["sp"].cur

            for half in range(2):
                bank = 2 + half
                for cp in range(4):
                    c = half * 4 + cp
                    src = SB[:, o_kn + c * TS + blk * 128:o_kn + c * TS + blk * 128 + 128]
                    k.transp(PB[bank][:, cp * 128:(cp + 1) * 128], src,
                             SB[:, o_id:o_id + 128],
                             waits=[k.bg(bank)] if cp == 0 else [], inc=(cp == 3))
                v_tk = S["pe"].cur
                k.dve(lambda e, b=bank, hf=half: e.tensor_copy(
                    SB[:, o_tma + hf * 512:o_tma + hf * 512 + 512], PB[b][:, :]),
                    waits=[("pe", v_tk), ("sp", tma_spill)] if half == 0
                    else [("pe", v_tk)], inc=True)
                k.setbg(bank, "dv")
            v_ktm = S["dv"].cur
            for par in range(2):
                ck = 2 * B_ + par
                k.dma("scalar", knT_sp[ck],
                      SB[64 * par:64 * par + 64, o_tma:o_tma + DM],
                      waits=[("dv", v_ktm)], sem="sp")
            tma_spill = S["sp"].cur

    sp_last = NST - 1
    for u in range(NCH):
        if u > 0:
            tput_unit(sp_last, u - 1)
        conv_unit(sp_last, u)
    tput_unit(sp_last, NCH - 1)

    k.pop()
    if "B" in phases:
        k.flush()

    o_pT = k.take(NBLK * 16)
    k.push()
    o_feats = k.take(4096)
    k.push()
    o_SS = [k.take(512) for _ in range(H)]
    o_qF = k.take(NCH * TS)
    o_kbF = k.take(NCH * TS)
    o_knF = k.take(NCH * TS)
    o_knT = [k.take(DM) for _ in range(4)]
    o_vbT = [k.take(DM) for _ in range(4)]
    o_rqT = k.take(32)
    o_vTb = k.take(DM)
    o_lTb = k.take(DM)
    o_mTb = k.take(DM)
    o_dta = k.take(2 * DM)
    PCLS = ("P1", "Q1", "Q2", "P2", "Q4", "P4", "Q8", "P8", "Q16",
            "P16", "Q32", "P12", "P32", "P48", "C1", "C2", "C3",
            "CP3", "CP3I", "C2I", "M2", "N2", "TMP")
    o_cls = {n: k.take(512) for n in PCLS}
    o_Tx = k.take(1024)
    o_Ys = k.take(DM)
    o_Us = k.take(DM)
    o_att = k.take(256)
    o_pr = [k.take(DM) for _ in range(2)]
    o_fr = k.take(64)
    o_ft = k.take(128)
    o_sc1 = k.take(16)
    o_sc2 = k.take(16)
    o_msk4 = k.take(768)
    o_mI = k.take(512)
    k.dma("sync", SB[:, o_msk4:o_msk4 + 768], masks_d[:, 256:1024], sem="wld",
          waits=[k.w("sp")])
    k.dma("sync", SB[:, o_mI:o_mI + 512], masks_d[:, 1024:1536], sem="wld")

    first_ms = [k.w("sp")]
    for off, n in ([(o, 512) for o in o_SS] +
                   [(o, DM) for o in o_knT] +
                   [(o_Ys, DM), (o_Us, DM), (o_Tx, 1024),
                    (o_att, 256), (o_feats, 4096)] +
                   [(o_cls[n], 512) for n in o_cls]):
        k.dve(lambda e, o=off, m=n: e.memset(SB[:, o:o + m], 0.0),
              waits=first_ms)
        first_ms = []
    k.dve(lambda e: e.engine_nop(), inc=True)

    M_SL4, M_SU4, M_IU4 = o_msk4, o_msk4 + 256, o_msk4 + 512

    def pow_ap(n, par, hd=None, rows=128):
        o = o_cls[n] + 256 * par
        if hd is None:
            return SB[:rows, o:o + 256]
        return SB[:rows, o + 64 * hd:o + 64 * hd + 64]

    def cls_full(n, rows=128):
        return SB[:rows, o_cls[n]:o_cls[n] + 512]

    def texp_ap(gpar, par, hd):
        o = o_Tx + 512 * gpar + 256 * par + 64 * hd
        return SB[:, o:o + 64]

    knt_guard = [k.w("dv", 0)] * 4
    vbt_guard = [k.w("dv", 0)] * 4
    dta_spill = [0, 0]
    texp_guard = [("pe", 0), ("pe", 0)]
    st_c = {"v_S_add": S["dv"].cur, "v_tmld": [0, 0],
            "v_fmld": 0, "scan_qf_pe": 0, "pr_guard": [0, 0],
            "ft_guard": [0, 0], "v_Mmms": 0, "v_Tx": [0, 0], "pi": 0}

    NG = NCK // 2

    def qsl(base, ck, hd, kc2):
        c = 2 * hd + kc2
        t0 = (ck % 8) * CC
        return SB[:, base + c * TS + t0:base + c * TS + t0 + CC]

    def emit_T0(g, boundary):
        if boundary:
            w0 = [k.w("pe"), k.w("dv"), k.w("gp")]
            for dst, src in ((o_qF, q_sp), (o_kbF, kb_sp), (o_knF, kn_sp)):
                k.dma("sync", SB[:, dst:dst + NCH * TS].rearrange(
                    "p (c t) -> p c t", c=NCH),
                    src.rearrange("c p l -> p c l")[:, :, (g // 4) * TS:
                                                    (g // 4) * TS + TS],
                    waits=w0)
                w0 = []
            stl = g // 4
            k.dma("sync",
                  SB[:, o_rqT + 16 * (stl % 2):o_rqT + 16 * (stl % 2) + 16]
                  .rearrange("p (b hh) -> p b hh", b=4),
                  rqT_sp.rearrange("b p hh -> p b hh")[:, 4 * stl:4 * stl + 4, :])
        for par in range(2):
            ck = 2 * g + par
            k.dma("sync", SB[0:64, o_knT[ck % 4]:o_knT[ck % 4] + DM], knT_sp[ck],
                  waits=[knt_guard[ck % 4]])
            k.dma("sync", SB[0:64, o_vbT[ck % 4]:o_vbT[ck % 4] + DM], vbT_sp[ck],
                  waits=[vbt_guard[ck % 4]])
        st_c["v_tmld"][g % 2] = S["ld"].cur
        if boundary:
            st_c["v_fmld"] = S["ld"].cur

    def emit_T1(g):
        v_ld = st_c["v_fmld"]
        for par in range(2):
            ck = 2 * g + par
            bk = 4 + par
            for hd in range(H):
                for kc2 in range(2):
                    k.mm(PB[bk][0:64, 64 * hd:64 * hd + 64],
                         qsl(o_kbF, ck, hd, kc2), qsl(o_knF, ck, hd, kc2),
                         start=(kc2 == 0), stop=(kc2 == 1),
                         waits=[("ld", v_ld), k.bg(bk)] if hd == 0 and kc2 == 0
                         else [], inc=(kc2 == 1))
            for hd in range(H):
                for kc2 in range(2):
                    k.mm(PB[bk][0:64, 256 + 64 * hd:256 + 64 * hd + 64],
                         qsl(o_knF, ck, hd, kc2), qsl(o_kbF, ck, hd, kc2),
                         start=(kc2 == 0), stop=(kc2 == 1), inc=(kc2 == 1))
        st_c["v_gr"] = S["pe"].cur

    def emit_T2(g):
        v_gr = st_c["v_gr"]
        for par in range(2):
            bk = 4 + par
            k.dve(lambda e, b=bk, p_=par: e.scalar_tensor_tensor(
                pow_ap("P1", p_, rows=64), PB[b][0:64, 0:256], -1.0,
                SB[0:64, M_SL4:M_SL4 + 256], op0=OP.mult, op1=OP.mult),
                waits=[("pe", v_gr), k.w("wld")] if par == 0 else [])
            k.dve(lambda e, b=bk, p_=par: e.scalar_tensor_tensor(
                pow_ap("Q1", p_, rows=64), PB[b][0:64, 256:512], -1.0,
                SB[0:64, M_SU4:M_SU4 + 256], op0=OP.mult, op1=OP.mult),
                inc=(par == 1))
            k.setbg(bk, "dv")
        st_c["v_pq1"] = S["dv"].cur

    TROUNDS = [
        [("Q2", "P1", "Q1", 4), ("P2", "Q1", "P1", 5)],
        [("C1", "P1", "Q2", 6), ("Q4", "P2", "Q2", 4), ("P4", "Q2", "P2", 5)],
        [("Q8", "P4", "Q4", 4), ("P8", "Q4", "P4", 5)],
        [("C2", "P4", "Q8", 6), ("Q16", "P8", "Q8", 4), ("P16", "Q8", "P8", 5)],
        [("Q32", "P16", "Q16", 6), ("P12", "Q4", "P8", 4),
         ("P32", "Q16", "P16", 5)],
        [("C3", "P16", "Q32", 4), ("P48", "Q16", "P32", 5)],
    ]

    def emit_Tround(g, ri):
        v_prev = S["dv"].cur
        for dst, lh, rh, bk in TROUNDS[ri]:
            for par in range(2):
                for hd in range(H):
                    k.mm(PB[bk][0:64, 256 * par + 64 * hd:256 * par + 64 * hd + 64],
                         pow_ap(lh, par, hd), pow_ap(rh, par, hd),
                         start=(par == 0 and hd == 0),
                         stop=(par == 1 and hd == H - 1),
                         waits=[("dv", v_prev), k.bg(bk)] if par == 0 and hd == 0
                         else [], inc=(par == 1 and hd == H - 1))

    def emit_Tround_evac(g, ri):
        for dst, lh, rh, bk in TROUNDS[ri]:
            if dst.startswith("C"):
                a, b_ = {"C1": ("Q1", "Q2"), "C2": ("Q4", "Q8"),
                         "C3": ("Q16", "Q32")}[dst]
                k.dve(lambda e, d=dst, x=a, y=b_: e.tensor_tensor(
                    cls_full(d), cls_full(x), cls_full(y), op=OP.add),
                    waits=[k.w("pe")])
                k.dve(lambda e, d=dst, b=bk: e.tensor_tensor(
                    cls_full(d, rows=64), cls_full(d, rows=64),
                    PB[b][0:64, :], op=OP.add), inc=True)
            else:
                k.dve(lambda e, d=dst, b=bk: e.tensor_copy(
                    SB[0:64, o_cls[d]:o_cls[d] + 512], PB[b][0:64, :]),
                    waits=[k.w("pe")], inc=True)
            k.setbg(bk, "dv")

    def emit_Mprep(g):
        k.dve(lambda e: e.tensor_tensor(
            cls_full("TMP", rows=64), cls_full("P16", rows=64),
            cls_full("P32", rows=64), op=OP.add),
            waits=[("pe", st_c["v_Mmms"])])
        k.dve(lambda e: e.tensor_tensor(
            cls_full("CP3", rows=64), cls_full("TMP", rows=64),
            cls_full("P48", rows=64), op=OP.add))
        k.dve(lambda e: e.tensor_tensor(
            cls_full("CP3I", rows=64), cls_full("CP3", rows=64),
            SB[0:64, o_mI:o_mI + 512], op=OP.add))
        k.dve(lambda e: e.tensor_tensor(
            cls_full("C2I", rows=64), cls_full("C2", rows=64),
            SB[0:64, o_mI:o_mI + 512], op=OP.add))
        k.dve(lambda e: e.tensor_tensor(
            cls_full("TMP", rows=64), cls_full("P4", rows=64),
            cls_full("P8", rows=64), op=OP.add))
        k.dve(lambda e: e.tensor_tensor(
            cls_full("TMP", rows=64), cls_full("TMP", rows=64),
            cls_full("P12", rows=64), op=OP.add))
        k.dve(lambda e: e.tensor_tensor(
            cls_full("TMP", rows=64), cls_full("TMP", rows=64),
            SB[0:64, o_mI:o_mI + 512], op=OP.add), inc=True)
        st_c["v_Mprep"] = S["dv"].cur

    def emit_M1(g):
        v_p = st_c["v_Mprep"]
        for lh, rh, bk in (("CP3I", "C2", 4), ("C2I", "CP3", 5)):
            for par in range(2):
                for hd in range(H):
                    k.mm(PB[bk][0:64, 256 * par + 64 * hd:256 * par + 64 * hd + 64],
                         pow_ap(lh, par, hd), pow_ap(rh, par, hd),
                         start=(par == 0 and hd == 0),
                         stop=(par == 1 and hd == H - 1),
                         waits=[("dv", v_p), k.bg(bk)] if par == 0 and hd == 0
                         else [], inc=(par == 1 and hd == H - 1))
        st_c["v_M1"] = S["pe"].cur

    def emit_M1_evac(g):
        v_m1 = st_c["v_M1"]
        for dst, addend, bk in (("M2", "C3", 4), ("N2", "TMP", 5)):
            k.dve(lambda e, d=dst, a=addend, b=bk: e.tensor_tensor(
                cls_full(d, rows=64), cls_full(a, rows=64),
                PB[b][0:64, :], op=OP.add),
                waits=[("pe", v_m1)] if dst == "M2" else [], inc=True)
            k.setbg(bk, "dv")
        k.dve(lambda e: e.tensor_tensor(
            cls_full("M2", rows=64), cls_full("M2", rows=64),
            SB[0:64, o_mI:o_mI + 512], op=OP.add), inc=True)
        st_c["v_M1e"] = S["dv"].cur

    def emit_M2(g):
        v_e = st_c["v_M1e"]
        for par in range(2):
            for hd in range(H):
                k.mm(PB[6][0:64, 256 * par + 64 * hd:256 * par + 64 * hd + 64],
                     pow_ap("N2", par, hd), pow_ap("C1", par, hd),
                     start=(par == 0 and hd == 0),
                     stop=(par == 1 and hd == H - 1),
                     waits=[("dv", v_e), k.bg(6)] if par == 0 and hd == 0
                     else [], inc=(par == 1 and hd == H - 1))
        st_c["v_Mmms"] = S["pe"].cur

    def emit_M2_evac(g):
        gpar = g % 2
        k.dve(lambda e, gp2=gpar: e.tensor_tensor(
            SB[0:64, o_Tx + 512 * gp2:o_Tx + 512 * gp2 + 512],
            cls_full("M2", rows=64), PB[6][0:64, :], op=OP.add),
            waits=[("pe", st_c["v_Mmms"]), texp_guard[gpar]], inc=True)
        k.setbg(6, "dv")
        st_c["v_Tx"][gpar] = S["dv"].cur

    def emit_S_a(ck):
        g = ck // 2
        par = ck % 2
        for hd in range(H):
            bk = hd // 2
            col = 256 * (hd % 2)
            for kc2 in range(2):
                k.mm(PB[bk][0:64, col:col + 256],
                     qsl(o_kbF, ck, hd, kc2),
                     SB[:, o_SS[hd] + 256 * kc2:o_SS[hd] + 256 * kc2 + 256],
                     start=(kc2 == 0), stop=(kc2 == 1),
                     waits=([("dv", st_c["v_S_add"]), k.bg(bk)]
                            if (hd % 2 == 0 and kc2 == 0) else []),
                     inc=(kc2 == 1))
        v_ymm = S["pe"].cur
        for sg in range(2):
            k.dve(lambda e, s_=sg, sl=(ck % 4): e.tensor_tensor(
                SB[0:64, o_Ys + 512 * s_:o_Ys + 512 * s_ + 512],
                SB[0:64, o_vbT[sl] + 512 * s_:o_vbT[sl] + 512 * s_ + 512],
                PB[s_][0:64, 0:512], op=OP.subtract),
                waits=[("pe", v_ymm), ("ld", st_c["v_tmld"][g % 2])]
                if sg == 0 else [], inc=(sg == 1))
        st_c["v_ys"] = S["dv"].cur
        k.setbg(0, "dv", st_c["v_ys"])
        k.setbg(1, "dv", st_c["v_ys"])

    def emit_S_u(ck):
        g = ck // 2
        gpar = g % 2
        par = ck % 2
        v_src = st_c["v_ys"]
        wT = [("dv", st_c["v_Tx"][gpar])]
        for hd in range(H):
            bk = hd // 2
            col = 256 * (hd % 2)
            k.mm(PB[bk][0:64, col:col + 256],
                 texp_ap(gpar, par, hd),
                 SB[:, o_Ys + 256 * hd:o_Ys + 256 * hd + 256],
                 start=True, stop=True,
                 waits=([("dv", v_src), k.bg(bk)] + wT) if hd % 2 == 0
                 else [], inc=(hd % 2 == 1))
        v_app = S["pe"].cur
        if par == 1:
            texp_guard[gpar] = ("pe", v_app)
        for sg in range(2):
            k.dve(lambda e, s_=sg: e.tensor_copy(
                SB[0:64, o_Us + 512 * s_:o_Us + 512 * s_ + 512],
                PB[s_][0:64, 0:512]),
                waits=[("pe", v_app)] if sg == 0 else [], inc=(sg == 1))
        k.setbg(0, "dv")
        k.setbg(1, "dv")
        st_c["v_U"] = S["dv"].cur

    def emit_S_attn_o(ck):
        g = ck // 2
        par = ck % 2
        stl = ck // 8
        blkrel = (ck % 8) // 2
        sdta = g % 2
        for hd in range(H):
            for kc2 in range(2):
                k.mm(PB[7][0:64, 64 * hd:64 * hd + 64],
                     qsl(o_knF, ck, hd, kc2), qsl(o_qF, ck, hd, kc2),
                     start=(kc2 == 0), stop=(kc2 == 1),
                     waits=[k.bg(7)] if hd == 0 and kc2 == 0 else [],
                     inc=(kc2 == 1))
        v_ag = S["pe"].cur
        k.dve(lambda e: e.tensor_tensor(
            SB[0:64, o_att:o_att + 256], PB[7][0:64, 0:256],
            SB[0:64, M_IU4:M_IU4 + 256], op=OP.mult),
            waits=[("pe", v_ag)], inc=True)
        k.setbg(7, "dv")
        v_at = S["dv"].cur
        for hd in range(H):
            bk = 2 + hd // 2
            col = 256 * (hd % 2)
            for kc2 in range(2):
                k.mm(PB[bk][64 * par:64 * par + 64, col:col + 256],
                     qsl(o_qF, ck, hd, kc2),
                     SB[:, o_SS[hd] + 256 * kc2:o_SS[hd] + 256 * kc2 + 256],
                     start=(kc2 == 0), stop=False, tp=(0, 64 * par),
                     waits=([("dv", v_at), k.bg(bk)] if hd % 2 == 0 and kc2 == 0
                            else []))
            k.mm(PB[bk][64 * par:64 * par + 64, col:col + 256],
                 SB[:, o_att + 64 * hd:o_att + 64 * hd + 64],
                 SB[:, o_Us + 256 * hd:o_Us + 256 * hd + 256],
                 start=False, stop=True, tp=(0, 64 * par),
                 inc=(hd % 2 == 1))
        st_c["scan_qf_pe"] = S["pe"].cur
        v_omm = S["pe"].cur
        for hd in range(H):
            bk = 2 + hd // 2
            col = 256 * (hd % 2)
            k.dve(lambda e, b=bk, c=col, h=hd, p_=par, sd=sdta, br=blkrel,
                  sp2=(stl % 2): e.tensor_scalar(
                      SB[64 * p_:64 * p_ + 64,
                         o_dta + sd * DM + 256 * h:o_dta + sd * DM + 256 * h + 256],
                      PB[b][64 * p_:64 * p_ + 64, c:c + 256],
                      SB[64 * p_:64 * p_ + 64,
                         o_rqT + 16 * sp2 + 4 * br + h:
                         o_rqT + 16 * sp2 + 4 * br + h + 1],
                      None, op0=OP.mult),
                  waits=([("pe", v_omm), ("sp", dta_spill[sdta])] if hd == 0
                         else []),
                  inc=(hd == H - 1))
        st_c["v_oev"] = S["dv"].cur
        k.setbg(2, "dv", st_c["v_oev"])
        k.setbg(3, "dv", st_c["v_oev"])

    def emit_S_dS(ck):
        par = ck % 2
        for pair in range(2):
            for hh2 in range(2):
                hd = 2 * pair + hh2
                bk = 2 + hh2
                for kc2 in range(2):
                    k.mm(PB[bk][:, 256 * kc2:256 * kc2 + 256],
                         SB[:, o_knT[ck % 4] + 256 * hd + 128 * kc2:
                            o_knT[ck % 4] + 256 * hd + 128 * kc2 + 128],
                         SB[:, o_Us + 256 * hd:o_Us + 256 * hd + 256],
                         start=(kc2 == 0), stop=(kc2 == 1),
                         waits=[k.bg(bk)] if kc2 == 0 else [],
                         inc=(kc2 == 1))
            v_ds = S["pe"].cur
            for hh2 in range(2):
                hd = 2 * pair + hh2
                bk = 2 + hh2
                k.dve(lambda e, b=bk, h=hd: e.tensor_tensor(
                    SB[:, o_SS[h]:o_SS[h] + 512], SB[:, o_SS[h]:o_SS[h] + 512],
                    PB[b][:, :], op=OP.add),
                    waits=[("pe", v_ds)] if hh2 == 0 else [], inc=True)
                k.setbg(bk, "dv")
        st_c["v_S_add"] = S["dv"].cur
        knt_guard[ck % 4] = k.w("pe", S["pe"].cur)
        vbt_guard[ck % 4] = k.w("dv", st_c["v_ys"])

    pend_feats = []

    def emit_one_sum(a, b_, idx, tens, w0):
        if b_ is None:
            k.dve(lambda e, ta=tens[a], ix=idx: e.tensor_reduce(
                SB[:, o_fr + 4 * ix:o_fr + 4 * ix + 4],
                SB[:, ta:ta + DM].rearrange("p (hh dd) -> p hh dd", hh=4),
                axis=mybir.AxisListType.X, op=OP.add), waits=w0)
        else:
            pi = st_c["pi"]
            pb_ = o_pr[pi % 2]
            wgp = w0 + [("dv", st_c["pr_guard"][pi % 2])]
            k.gps(lambda e, ta=tens[a], tb=tens[b_], po=pb_: e.tensor_mul(
                SB[:, po:po + DM], SB[:, ta:ta + DM], SB[:, tb:tb + DM]),
                waits=wgp, inc=True)
            vg = S["gp"].cur
            k.dve(lambda e, po=pb_, ix=idx: e.tensor_reduce(
                SB[:, o_fr + 4 * ix:o_fr + 4 * ix + 4],
                SB[:, po:po + DM].rearrange("p (hh dd) -> p hh dd", hh=4),
                axis=mybir.AxisListType.X, op=OP.add),
                waits=[("gp", vg)], inc=True)
            st_c["pr_guard"][pi % 2] = S["dv"].cur
            st_c["pi"] = pi + 1

    def emit_feats_pre(g):
        B_ = g
        for dst, src in ((o_vTb, vT_sp), (o_lTb, lT_sp), (o_mTb, mT_sp)):
            k.dma("sync", SB[:, dst:dst + DM], src[B_],
                  waits=[k.w("dv"), k.w("gp")])
        v_tld = S["ld"].cur
        st_c["v_ftld"] = v_tld
        tens = {"l": o_lTb, "m": o_mTb, "v": o_vTb}
        first = True
        for a, idx in (("l", 0), ("m", 2), ("v", 6)):
            emit_one_sum(a, None, idx, tens, [("ld", v_tld)] if first else [])
            first = False

    def emit_feats_prod(g):
        tens = {"l": o_lTb, "m": o_mTb, "v": o_vTb}
        first = True
        for a, b_, idx in (("l", "l", 1), ("m", "m", 3), ("v", "v", 7),
                           ("l", "m", 8), ("l", "v", 10), ("m", "v", 12)):
            emit_one_sum(a, b_, idx, tens,
                         [("ld", st_c["v_ftld"])] if first else [])
            first = False

    def emit_feats_sums(g):
        B_ = g
        sdta = g % 2
        dta_o = o_dta + sdta * DM
        tens = {"l": o_lTb, "m": o_mTb, "d": dta_o, "v": o_vTb}
        v_need = st_c["v_oev"]
        fdv, fgp = True, True
        for a, b_, idx in (("d", None, 4), ("d", "d", 5), ("l", "d", 9),
                           ("m", "d", 11), ("d", "v", 13)):
            need = fdv if b_ is None else fgp
            emit_one_sum(a, b_, idx, tens, [("dv", v_need)] if need else [])
            if b_ is None:
                fdv = False
            else:
                fgp = False
        k.dve(lambda e: e.engine_nop(), inc=True)
        v_sums = S["dv"].cur
        fto = o_ft + 64 * (B_ % 2)
        for si, qi in ((0, 1), (2, 3), (4, 5), (6, 7)):
            k.dve(lambda e, s=si, f=fto: e.tensor_scalar(
                SB[:, f + 4 * s:f + 4 * s + 4],
                SB[:, o_fr + 4 * s:o_fr + 4 * s + 4], 1.0 / DH, None,
                op0=OP.mult),
                waits=[("dv", v_sums), ("pe", st_c["ft_guard"][B_ % 2])],
                inc=True)
            k.dve(lambda e, s=si: e.tensor_tensor(
                SB[:, o_sc1:o_sc1 + 4], SB[:, o_fr + 4 * s:o_fr + 4 * s + 4],
                SB[:, o_fr + 4 * s:o_fr + 4 * s + 4], op=OP.mult), inc=True)
            k.dve(lambda e: e.tensor_scalar(
                SB[:, o_sc2:o_sc2 + 4], SB[:, o_sc1:o_sc1 + 4],
                1.0 / (255.0 * DH), None, op0=OP.mult),
                waits=[k.w("dv")], inc=True)
            k.dve(lambda e, q=qi, f=fto: e.scalar_tensor_tensor(
                SB[:, f + 4 * q:f + 4 * q + 4],
                SB[:, o_fr + 4 * q:o_fr + 4 * q + 4], 1.0 / 255.0,
                SB[:, o_sc2:o_sc2 + 4], op0=OP.mult, op1=OP.subtract),
                waits=[k.w("dv")], inc=True)
        for ix in range(8, 14):
            k.dve(lambda e, x=ix, f=fto: e.tensor_scalar(
                SB[:, f + 4 * x:f + 4 * x + 4],
                SB[:, o_fr + 4 * x:o_fr + 4 * x + 4], 1.0 / DH, None,
                op0=OP.mult), inc=(ix == 13))
        v_ft = S["dv"].cur
        dta_spill[sdta] = k.dma("scalar", dT_sp[B_], SB[:, dta_o:dta_o + DM],
                                waits=[("dv", st_c["v_oev"])], sem="sp")
        pend_feats.append((B_, v_ft))

    def emit_feats_fin():
        while pend_feats:
            B_, v_ft = pend_feats.pop(0)
            k.transp(PB[7][0:56, 0:128],
                     SB[:, o_ft + 64 * (B_ % 2):o_ft + 64 * (B_ % 2) + 56],
                     SB[:, o_id:o_id + 128], waits=[("dv", v_ft), k.bg(7)],
                     inc=True)
            v_ftr = S["pe"].cur
            st_c["ft_guard"][B_ % 2] = v_ftr
            k.dve(lambda e, b_=B_: e.tensor_copy(
                SB[0:56, o_feats + 128 * b_:o_feats + 128 * b_ + 128],
                PB[7][0:56, 0:128]), waits=[("pe", v_ftr)], inc=True)
            k.setbg(7, "dv")

    for g in range(NG + 1):
        boundary = (g % 4 == 0)
        if g < NG and boundary:
            if g > 0:
                emit_feats_pre(g - 1)
                emit_feats_prod(g - 1)
                for par in range(2):
                    ck = 2 * (g - 1) + par
                    emit_S_a(ck)
                    emit_S_u(ck)
                    emit_S_attn_o(ck)
                    emit_S_dS(ck)
                    if par == 1:
                        emit_feats_sums(g - 1)
            emit_feats_fin()
            emit_T0(g, True)
            emit_T1(g)
            emit_T2(g)
            for ri in range(6):
                emit_Tround(g, ri)
                emit_Tround_evac(g, ri)
            emit_Mprep(g)
            emit_M1(g)
            emit_M1_evac(g)
            emit_M2(g)
            emit_M2_evac(g)
        elif g < NG:
            cka, ckb = 2 * (g - 1), 2 * (g - 1) + 1
            emit_T0(g, False)
            emit_S_a(cka)
            emit_feats_pre(g - 1)
            emit_T1(g)
            emit_S_u(cka)
            emit_T2(g)
            emit_Tround(g, 0)
            emit_Tround_evac(g, 0)
            emit_S_attn_o(cka)
            emit_feats_prod(g - 1)
            emit_Tround(g, 1)
            emit_Tround_evac(g, 1)
            emit_S_dS(cka)
            emit_Tround(g, 2)
            emit_Tround_evac(g, 2)
            emit_S_a(ckb)
            emit_Tround(g, 3)
            emit_Tround_evac(g, 3)
            emit_S_u(ckb)
            emit_Tround(g, 4)
            emit_Tround_evac(g, 4)
            emit_S_attn_o(ckb)
            emit_Tround(g, 5)
            emit_Tround_evac(g, 5)
            emit_S_dS(ckb)
            emit_Mprep(g)
            emit_M1(g)
            emit_feats_fin()
            emit_M1_evac(g)
            emit_M2(g)
            emit_M2_evac(g)
            emit_feats_sums(g - 1)
        else:
            emit_feats_pre(g - 1)
            emit_feats_prod(g - 1)
            for par in range(2):
                ck = 2 * (g - 1) + par
                emit_S_a(ck)
                emit_S_u(ck)
                emit_S_attn_o(ck)
                emit_S_dS(ck)
                if par == 1:
                    emit_feats_sums(g - 1)
            emit_feats_fin()

    if "feats_dbg" in k.dbg:
        fdbg = k.dram("feats_dbg", [56, L], "ExternalOutput")
        k.dma("sync", fdbg, SB[0:56, o_feats:o_feats + L],
              waits=[k.w("dv"), k.w("pe")], sem="sp")
    k.pop()
    if "C" in phases:
        k.flush()

    k.push()
    o_rw1 = k.take(9 * 2176)
    o_rw1h = k.take(8 * 1088)
    o_rw1l = k.take(8 * 1088)
    o_rw2 = k.take(17 * 16)
    o_rb1 = k.take(32)
    o_rb2 = k.take(16)
    o_hD = k.take(2 * 1024)
    o_hh = k.take(NCH * TS // 2)
    o_hl = k.take(NCH * TS // 2)
    o_h1 = k.take(2 * TS)
    o_lg = k.take(TS)
    o_lgT = k.take(16)
    o_sm = k.take(64)

    F16 = mybir.dt.float16

    k.dma("sync", SB[:, o_rw1:o_rw1 + 8 * 2176].rearrange(
        "p (c m) -> p c m", c=8),
        rw1_d[0:1024].rearrange("(c p) m -> p c m", p=128), sem="wld",
        waits=[k.w("pe"), k.w("dv"), k.w("sp")])
    k.dma("sync", SB[0:64, o_rw1 + 8 * 2176:o_rw1 + 9 * 2176],
          rw1_d[1024:1088], sem="wld")
    k.dma("sync", SB[:, o_rw2:o_rw2 + 17 * 16].rearrange(
        "p (c n) -> p c n", c=17),
        rw2_d.rearrange("(c p) n -> p c n", p=128), sem="wld")
    k.dma("sync", SB[:, o_rb1:o_rb1 + 17],
          rb1_d.rearrange("(c p) -> p c", p=128), sem="wld", slow=True)
    k.dma("sync", SB[0:16, o_rb2:o_rb2 + 1], rb2_d.rearrange("(n o) -> n o", o=1),
          sem="wld")
    v_wD = S["wld"].cur

    wsp0 = [("wld", v_wD)]
    for p_ in range(34):
        src = SB[:, o_rw1 + p_ * TS:o_rw1 + p_ * TS + TS]
        hh_ = SB[:, o_rw1h + p_ * 256:o_rw1h + p_ * 256 + 256].bitcast(F16)
        hl_ = SB[:, o_rw1l + p_ * 256:o_rw1l + p_ * 256 + 256].bitcast(F16)
        k.dve(lambda e, s_=src, d_=hh_: e.tensor_copy(d_, s_), waits=wsp0)
        wsp0 = []
        k.dve(lambda e, s_=src, a=hh_, d_=hl_: e.tensor_tensor(
            d_, s_, a, op=OP.subtract), inc=(p_ == 33))
    v_wsp = S["dv"].cur

    st_d = {"v_mml": 0, "v_sp": [0, 0]}
    for st in range(NST):
        tok0 = st * TS
        for pr in range(4):
            sl = pr % 2
            k.dma("sync", SB[:, o_hD + sl * 1024:o_hD + sl * 1024 + 1024]
                  .rearrange("p (c t) -> p c t", c=2),
                  h_sp.rearrange("c p l -> p c l")[:, 2 * pr:2 * pr + 2,
                                                   tok0:tok0 + TS],
                  waits=[("dv", st_d["v_sp"][sl])] +
                  ([k.w("pe"), k.w("ac"), k.w("dv")] if st == 0 and pr < 2
                   else []))
            v_hld = S["ld"].cur
            for c2 in range(2):
                c = 2 * pr + c2
                src = SB[:, o_hD + sl * 1024 + c2 * TS:
                         o_hD + sl * 1024 + c2 * TS + TS]
                hh_ = SB[:, o_hh + c * 256:o_hh + c * 256 + 256].bitcast(F16)
                hl_ = SB[:, o_hl + c * 256:o_hl + c * 256 + 256].bitcast(F16)
                k.dve(lambda e, s_=src, d_=hh_: e.tensor_copy(d_, s_),
                      waits=[("ld", v_hld), ("pe", st_d["v_mml"])]
                      if c2 == 0 else [])
                k.dve(lambda e, s_=src, a=hh_, d_=hl_: e.tensor_tensor(
                    d_, s_, a, op=OP.subtract), inc=(c2 == 1))
            st_d["v_sp"][sl] = S["dv"].cur
        v_hsp = S["dv"].cur
        lg_hist = []
        for m in range(17):
            bank = m % 2
            first = True
            for wa, ha in (("h", "h"), ("h", "l"), ("l", "h")):
                wb = o_rw1h if wa == "h" else o_rw1l
                hb = o_hh if ha == "h" else o_hl
                for kc in range(8):
                    lh = SB[:, wb + kc * 1088 + 64 * m:
                            wb + kc * 1088 + 64 * m + 64].bitcast(F16)
                    rh = SB[:, hb + kc * 256:hb + kc * 256 + 256].bitcast(F16)
                    k.mm(PB[bank][:, :], lh, rh, start=first, stop=False,
                         waits=([("dv", v_hsp), ("dv", v_wsp), k.bg(bank)]
                                if first else []))
                    first = False
            k.mm(PB[bank][:, :],
                 SB[0:64, o_rw1 + 8 * 2176 + 128 * m:
                    o_rw1 + 8 * 2176 + 128 * m + 128],
                 SB[0:64, o_feats + tok0:o_feats + tok0 + TS],
                 start=False, stop=True, inc=True)
            st_d["v_mml"] = S["pe"].cur
            v_pm = S["pe"].cur
            db = m % 2
            va = k.act(SB[:, o_h1 + db * TS:o_h1 + (db + 1) * TS], PB[bank][:, :],
                       AF.Silu, bias=SB[:, o_rb1 + m:o_rb1 + m + 1],
                       waits=[("pe", v_pm)] +
                       ([("pe", lg_hist[m - 2])] if m >= 2 else []),
                       inc=True)
            k.setbg(bank, "ac", va)
            k.mm(PB[2][0:16, :], SB[:, o_rw2 + m * 16:o_rw2 + m * 16 + 16],
                 SB[:, o_h1 + db * TS:o_h1 + (db + 1) * TS],
                 start=(m == 0), stop=(m == 16),
                 waits=[("ac", va)] + ([k.bg(2)] if m == 0 else []),
                 inc=True)
            lg_hist.append(S["pe"].cur)
        v_lgm = S["pe"].cur
        va = k.act(SB[0:16, o_lg:o_lg + TS], PB[2][0:16, :], AF.Identity,
                   bias=SB[0:16, o_rb2:o_rb2 + 1], waits=[("pe", v_lgm)],
                   inc=True)
        k.setbg(2, "ac", va)
        for blk in range(4):
            B_ = 4 * st + blk
            k.transp(PB[3][0:128, 0:16], SB[0:16, o_lg + blk * 128:o_lg + blk * 128 + 128],
                     SB[0:16, o_id:o_id + 16], waits=[("ac", va), k.bg(3)], inc=True)
            v_t = S["pe"].cur
            lgT = SB[:, o_lgT:o_lgT + 16]
            k.dve(lambda e: e.tensor_copy(lgT, PB[3][:, 0:16]),
                  waits=[("pe", v_t)], inc=True)
            k.setbg(3, "dv")
            mx = SB[:, o_sm:o_sm + 4]
            ex = SB[:, o_sm + 16:o_sm + 32]
            s4 = SB[:, o_sm + 32:o_sm + 36]
            r4 = SB[:, o_sm + 48:o_sm + 52]
            k.dve(lambda e: e.tensor_tensor(mx, lgT[:, 0:4], lgT[:, 4:8],
                                            op=OP.max), waits=[k.w("dv")], inc=True)
            k.dve(lambda e: e.tensor_tensor(s4, lgT[:, 8:12], lgT[:, 12:16],
                                            op=OP.max), inc=True)
            k.dve(lambda e: e.tensor_tensor(mx, mx, s4, op=OP.max),
                  waits=[k.w("dv")], inc=True)
            for i in range(4):
                k.dve(lambda e, i=i: e.tensor_tensor(
                    ex[:, 4 * i:4 * i + 4], lgT[:, 4 * i:4 * i + 4], mx,
                    op=OP.subtract), waits=[k.w("dv")], inc=(i == 3))
            va2 = k.act(ex, ex, AF.Exp, waits=[k.w("dv")], inc=True)
            k.dve(lambda e: e.tensor_tensor(s4, ex[:, 0:4], ex[:, 4:8],
                                            op=OP.add), waits=[("ac", va2)],
                  inc=True)
            k.dve(lambda e: e.tensor_tensor(r4, ex[:, 8:12], ex[:, 12:16],
                                            op=OP.add), inc=True)
            k.dve(lambda e: e.tensor_tensor(s4, s4, r4, op=OP.add),
                  waits=[k.w("dv")], inc=True)
            k.dve(lambda e: e.reciprocal(r4, s4), waits=[k.w("dv")], inc=True)
            pdst = SB[:, o_pT + 16 * B_:o_pT + 16 * B_ + 16]
            for i in range(4):
                k.dve(lambda e, i=i, p_=pdst: e.tensor_tensor(
                    p_[:, 4 * i:4 * i + 4], ex[:, 4 * i:4 * i + 4], r4,
                    op=OP.mult), waits=[k.w("dv")], inc=(i == 3))
            k.dve(lambda e, p_=pdst: e.tensor_scalar(
                p_, p_, 1.0 - 4 * 0.01, 0.01, op0=OP.mult, op1=OP.add),
                waits=[k.w("dv")], inc=True)
    k.pop()
    k.pop()
    if "D" in phases:
        k.flush()

    k.push()
    o_A = k.take(NCH * DM)
    o_B = k.take(NCH * DM)
    o_C = k.take(NCH * DM)
    o_mixw = k.take(DM)
    o_onr = k.take(DM)
    o_xE = k.take(NCH * TS)
    o_onF = k.take(NCH * TS)
    o_x1 = k.take(NCH * TS)
    o_t4 = [k.take(DM) for _ in range(4)]
    o_mx = k.take(DM)
    o_mw2 = k.take(DM)
    o_t3 = k.take(DM)
    o_s4 = k.take(64)
    o_fb = k.take(16)
    o_fch = k.take(TS)
    o_och = k.take(2 * TS)
    o_mrE = k.take(2 * TS)
    mrow = SB[0:1, o_mrE:o_mrE + TS]
    rrow = SB[0:1, o_mrE + TS:o_mrE + 2 * TS]

    k.dma("sync", SB[:, o_A:o_A + NCH * DM].rearrange("p (c m) -> p c m", c=NCH),
          ffn_wT_d.rearrange("(c p) m -> p c m", p=128), sem="wld",
          waits=[k.w("pe"), k.w("dv"), k.w("ac"), k.w("sp")])
    k.dma("sync", SB[:, o_B:o_B + NCH * DM].rearrange("p (c m) -> p c m", c=NCH),
          ffn_a1_d.rearrange("(c p) m -> p c m", p=128), sem="wld")
    k.dma("sync", SB[:, o_mixw:o_mixw + DM], mixw_d, sem="wld")
    k.dma("sync", SB[:, o_onr:o_onr + DM], onormr_d, sem="wld")
    k.dma("sync", SB[:, o_fb:o_fb + 8].rearrange("p c -> p c"),
          ffn_b_d.rearrange("(c p) -> p c", p=128), sem="wld", slow=True)
    v_wE = S["wld"].cur

    def mm1024(dst_off, lhs_off, rhs_off, wv, lhT_colmaj=True):
        for m in range(NCH):
            bank = m % 2
            for nn in range(2):
                for kc in range(NCH):
                    k.mm(PB[bank][:, :],
                         SB[:, lhs_off + kc * DM + 128 * m:
                            lhs_off + kc * DM + 128 * m + 128],
                         SB[:, rhs_off + kc * DM + 512 * nn:
                            rhs_off + kc * DM + 512 * nn + 512],
                         start=(kc == 0), stop=(kc == NCH - 1),
                         waits=([("wld", wv), k.bg(bank)] if kc == 0 else []),
                         inc=(kc == NCH - 1))
                v_ = S["pe"].cur
                k.dve(lambda e, b=bank, mm_=m, n_=nn, d=dst_off: e.tensor_copy(
                    SB[:, d + mm_ * DM + 512 * n_:d + mm_ * DM + 512 * n_ + 512],
                    PB[b][:, :]), waits=[("pe", v_)], inc=True)
                k.setbg(bank, "dv")

    mm1024(o_C, o_A, o_B, v_wE)
    v_T1 = S["dv"].cur
    for cc in range(NCH):
        for m in range(NCH):
            bank = 2 + (m % 2)
            k.transp(PB[bank][:, 0:128],
                     SB[:, o_C + cc * DM + 128 * m:o_C + cc * DM + 128 * m + 128],
                     SB[:, o_id:o_id + 128],
                     waits=[("dv", v_T1), k.bg(bank)] if cc == 0 and m < 2 else
                     [k.bg(bank)], inc=True)
            v_t = S["pe"].cur
            k.dve(lambda e, b=bank, c_=cc, m_=m: e.tensor_copy(
                SB[:, o_A + m_ * DM + 128 * c_:o_A + m_ * DM + 128 * c_ + 128],
                PB[b][:, 0:128]), waits=[("pe", v_t)], inc=True)
            k.setbg(bank, "dv")
    v_T1T = S["dv"].cur
    k.dma("sync", SB[:, o_B:o_B + NCH * DM].rearrange("p (c m) -> p c m", c=NCH),
          ffn_a2_d.rearrange("(c p) m -> p c m", p=128), sem="wld",
          waits=[("dv", v_T1T)])
    v_wE2 = S["wld"].cur
    mm1024(o_C, o_A, o_B, v_wE2)
    k.dma("sync", SB[:, o_A:o_A + NCH * DM].rearrange("p (c m) -> p c m", c=NCH),
          wo_d.rearrange("(c p) m -> p c m", p=128), sem="wld",
          waits=[k.w("dv")])
    v_wo = S["wld"].cur
    och_spill = [0, 0]

    onF_o = [o_onF, o_B]
    xE_o = [o_xE, o_B + NCH * TS]
    v_onF_p = [0, 0]
    v_x1_p = [0, 0]
    v_wp_p = [0, 0]
    v_xld_p = [0, 0]
    st_e = {"v_tr": 0}

    def e_front_load(st):
        p = st % 2
        tok0 = st * TS
        k.dma("sync", SB[:, xE_o[p]:xE_o[p] + NCH * TS].rearrange(
            "p (c t) -> p c t", c=NCH),
            x_fm.rearrange("(c p) l -> p c l", p=128)[:, :, tok0:tok0 + TS],
            waits=[("dv", v_x1_p[p])] +
            ([k.w("dv"), k.w("pe"), k.w("ac"), k.w("sp")] if st < 2 else []))
        v_xld_p[p] = S["ld"].cur

    def e_chain(st, blk):
        B_ = 4 * st + blk
        for ti, src in enumerate((lT_sp, mT_sp, dT_sp, vT_sp)):
            k.dma("sync", SB[:, o_t4[ti]:o_t4[ti] + DM], src[B_],
                  waits=[k.w("dv")] if ti == 0 else [])
        v_tld = S["ld"].cur
        for hd in range(H):
            for i in range(4):
                pcol = SB[:, o_pT + 16 * B_ + 4 * i + hd:
                          o_pT + 16 * B_ + 4 * i + hd + 1]
                src = SB[:, o_t4[i] + 256 * hd:o_t4[i] + 256 * hd + 256]
                dst = SB[:, o_mx + 256 * hd:o_mx + 256 * hd + 256]
                if i == 0:
                    k.dve(lambda e, s_=src, d_=dst, p_=pcol: e.tensor_scalar(
                        d_, s_, p_, None, op0=OP.mult),
                        waits=[("ld", v_tld)] if hd == 0 else [])
                else:
                    k.dve(lambda e, s_=src, d_=dst, p_=pcol:
                          e.scalar_tensor_tensor(d_, s_, p_, d_,
                                                 op0=OP.mult, op1=OP.add),
                          inc=(hd == H - 1 and i == 3))
        v_mx = S["dv"].cur
        k.dve(lambda e: e.tensor_tensor(SB[:, o_t3:o_t3 + DM],
                                        SB[:, o_mx:o_mx + DM],
                                        SB[:, o_mx:o_mx + DM], op=OP.mult),
              waits=[("dv", v_mx), ("pe", st_e["v_tr"])])
        k.dve(lambda e: e.tensor_reduce(
            SB[:, o_s4:o_s4 + 4],
            SB[:, o_t3:o_t3 + DM].rearrange("p (hh dd) -> p hh dd", hh=4),
            axis=mybir.AxisListType.X, op=OP.add), waits=[k.w("dv")],
            inc=True)
        va = k.act(SB[:, o_s4 + 8:o_s4 + 12], SB[:, o_s4:o_s4 + 4], AF.Sqrt,
                   scale=1.0 / DH, bias=1e-5, waits=[k.w("dv")], inc=True)
        k.dve(lambda e: e.reciprocal(SB[:, o_s4 + 16:o_s4 + 20],
                                     SB[:, o_s4 + 8:o_s4 + 12]),
              waits=[("ac", va)], inc=True)
        k.dve(lambda e: e.tensor_tensor(SB[:, o_mw2:o_mw2 + DM],
                                        SB[:, o_mx:o_mx + DM],
                                        SB[:, o_mixw:o_mixw + DM],
                                        op=OP.mult), inc=True)
        k.dve(lambda e: e.tensor_tensor(SB[:, o_t3:o_t3 + DM],
                                        SB[:, o_mw2:o_mw2 + DM],
                                        SB[:, o_mw2:o_mw2 + DM],
                                        op=OP.mult), waits=[k.w("dv")])
        k.dve(lambda e: e.tensor_reduce(
            SB[:, o_s4 + 24:o_s4 + 28],
            SB[:, o_t3:o_t3 + DM].rearrange("p (hh dd) -> p hh dd", hh=4),
            axis=mybir.AxisListType.X, op=OP.add), waits=[k.w("dv")],
            inc=True)
        k.dve(lambda e: e.tensor_tensor(
            SB[:, o_s4 + 32:o_s4 + 36], SB[:, o_s4 + 16:o_s4 + 20],
            SB[:, o_s4 + 16:o_s4 + 20], op=OP.mult), waits=[k.w("dv")],
            inc=True)
        k.dve(lambda e: e.tensor_tensor(
            SB[:, o_s4 + 40:o_s4 + 44], SB[:, o_s4 + 24:o_s4 + 28],
            SB[:, o_s4 + 32:o_s4 + 36], op=OP.mult), waits=[k.w("dv")],
            inc=True)
        va = k.act(SB[:, o_s4 + 44:o_s4 + 48], SB[:, o_s4 + 40:o_s4 + 44],
                   AF.Sqrt, scale=1.0 / DH, bias=1e-5, waits=[k.w("dv")],
                   inc=True)
        k.dve(lambda e: e.reciprocal(SB[:, o_s4 + 48:o_s4 + 52],
                                     SB[:, o_s4 + 44:o_s4 + 48]),
              waits=[("ac", va)], inc=True)
        k.dve(lambda e: e.tensor_tensor(
            SB[:, o_s4 + 52:o_s4 + 56], SB[:, o_s4 + 16:o_s4 + 20],
            SB[:, o_s4 + 48:o_s4 + 52], op=OP.mult), waits=[k.w("dv")],
            inc=True)
        k.dve(lambda e: e.tensor_tensor(SB[:, o_t3:o_t3 + DM],
                                        SB[:, o_mw2:o_mw2 + DM],
                                        SB[:, o_onr:o_onr + DM],
                                        op=OP.mult), waits=[k.w("dv")])
        for hd in range(H):
            k.dve(lambda e, h=hd: e.tensor_scalar(
                SB[:, o_t3 + 256 * h:o_t3 + 256 * h + 256],
                SB[:, o_t3 + 256 * h:o_t3 + 256 * h + 256],
                SB[:, o_s4 + 52 + h:o_s4 + 53 + h], None, op0=OP.mult),
                waits=[k.w("dv")] if hd == 0 else [], inc=(hd == H - 1))
        st_e["v_on"] = S["dv"].cur

    def e_transp(st, blk):
        p = st % 2
        v_on = st_e["v_on"]
        for cc in range(NCH):
            bank = 6 + (cc % 2)
            k.transp(PB[bank][:, 0:128],
                     SB[:, o_t3 + 128 * cc:o_t3 + 128 * cc + 128],
                     SB[:, o_id:o_id + 128],
                     waits=[("dv", v_on), k.bg(bank)], inc=True)
            v_t = S["pe"].cur
            k.dve(lambda e, b=bank, c_=cc, bl=blk, p_=p: e.tensor_copy(
                SB[:, onF_o[p_] + c_ * TS + 128 * bl:
                   onF_o[p_] + c_ * TS + 128 * bl + 128],
                PB[b][:, 0:128]),
                waits=[("pe", v_t)] +
                ([("pe", v_wp_p[p])] if cc == 0 and blk == 0 else []),
                inc=True)
            k.setbg(bank, "dv")
        st_e["v_tr"] = S["pe"].cur
        if blk == 3:
            v_onF_p[p] = S["dv"].cur

    def e_back_wo(stb, ms):
        p = stb % 2
        for m in ms:
            bank = m % 4
            for kc in range(NCH):
                k.mm(PB[bank][:, :],
                     SB[:, o_A + kc * DM + 128 * m:o_A + kc * DM + 128 * m + 128],
                     SB[:, onF_o[p] + kc * TS:onF_o[p] + (kc + 1) * TS],
                     start=(kc == 0), stop=(kc == NCH - 1),
                     waits=([("dv", v_onF_p[p]), ("wld", v_wo), k.bg(bank)]
                            if kc == 0 else []),
                     inc=(kc == NCH - 1))
            v_ = S["pe"].cur
            k.dve(lambda e, b=bank, m_=m, p_=p: e.scalar_tensor_tensor(
                SB[:, o_x1 + m_ * TS:o_x1 + (m_ + 1) * TS], PB[b][:, :],
                modcol(2, m_), SB[:, xE_o[p_] + m_ * TS:xE_o[p_] + (m_ + 1) * TS],
                op0=OP.mult, op1=OP.add),
                waits=[("pe", v_)] + ([("ld", v_xld_p[p])] if m == ms[0] else []),
                inc=True)
            k.setbg(bank, "dv")
        if ms[-1] == NCH - 1:
            v_x1_p[p] = S["dv"].cur
            st_e["v_x1"] = S["dv"].cur

    def e_back_ln2(stb):
        p = stb % 2
        v_x1 = st_e["v_x1"]
        for c in range(NCH):
            k.mm(PB[4][0:1, :], SB[:, o_onesc:o_onesc + 1],
                 SB[:, o_x1 + c * TS:o_x1 + (c + 1) * TS],
                 start=(c == 0), stop=(c == NCH - 1),
                 waits=[("dv", v_x1), k.bg(4)] if c == 0 else [],
                 inc=(c == NCH - 1))
        pe_hist2 = []
        for c in range(NCH):
            wv_ = ([("dv", v_x1), ("dv", st_e["v_outlast"])] if c == 0
                   else [k.w("pe", pe_hist2[-1])])
            va = k.act(SB[:, o_fch:o_fch + TS],
                       SB[:, o_x1 + c * TS:o_x1 + (c + 1) * TS], AF.Square,
                       waits=wv_, inc=True)
            k.mm(PB[5][0:1, :], SB[:, o_onesc:o_onesc + 1], SB[:, o_fch:o_fch + TS],
                 start=(c == 0), stop=(c == NCH - 1),
                 waits=[("ac", va)] + ([k.bg(5)] if c == 0 else []), inc=True)
            pe_hist2.append(S["pe"].cur)
        v_st2 = S["pe"].cur
        k.dve(lambda e: e.tensor_scalar(mrow, PB[4][0:1, :], 1.0 / DM, None,
                                        op0=OP.mult), waits=[("pe", v_st2)])
        k.dve(lambda e: e.tensor_tensor(rrow, mrow, mrow, op=OP.mult))
        k.dve(lambda e: e.scalar_tensor_tensor(
            rrow, PB[5][0:1, :], 1.0 / DM, rrow, op0=OP.mult, op1=OP.subtract),
            inc=True)
        v_var2 = S["dv"].cur
        k.setbg(4, "dv", v_var2)
        k.setbg(5, "dv", v_var2)
        va = k.act(rrow, rrow, AF.Sqrt, bias=1e-5, waits=[("dv", v_var2)],
                   inc=True)
        k.dve(lambda e: e.reciprocal(rrow, rrow), waits=[("ac", va)], inc=True)
        v_r2 = S["dv"].cur
        k.mm(PB[4][:, :], SB[0:1, o_onesr:o_onesr + 128], mrow,
             start=True, stop=True, waits=[("dv", v_r2), k.bg(4)])
        k.mm(PB[5][:, :], SB[0:1, o_onesr:o_onesr + 128], rrow,
             start=True, stop=True, waits=[k.bg(5)], inc=True)
        v_bc2 = S["pe"].cur
        for c in range(NCH):
            x1c = SB[:, o_x1 + c * TS:o_x1 + (c + 1) * TS]
            hc = SB[:, onF_o[p] + c * TS:onF_o[p] + (c + 1) * TS]
            k.dve(lambda e, x=x1c, h=hc: e.tensor_tensor(h, x, PB[4][:, :],
                                                         op=OP.subtract),
                  waits=[("pe", v_bc2)] if c == 0 else [])
            k.dve(lambda e, h=hc: e.tensor_tensor(h, h, PB[5][:, :], op=OP.mult))
            k.dve(lambda e, h=hc, c=c: e.scalar_tensor_tensor(
                h, h, modcol(4, c), h, op0=OP.mult, op1=OP.add))
            k.dve(lambda e, h=hc, c=c: e.tensor_scalar(
                h, h, modcol(3, c), None, op0=OP.add), inc=True)
        st_e["v_h2"] = S["dv"].cur
        k.setbg(4, "dv", st_e["v_h2"])
        k.setbg(5, "dv", st_e["v_h2"])

    def e_back_ffn(stb, ms):
        p = stb % 2
        tok0 = stb * TS
        v_h2 = st_e["v_h2"]
        for m in ms:
            bank = m % 4
            for kc in range(NCH):
                k.mm(PB[bank][:, :],
                     SB[:, o_C + kc * DM + 128 * m:o_C + kc * DM + 128 * m + 128],
                     SB[:, onF_o[p] + kc * TS:onF_o[p] + (kc + 1) * TS],
                     start=(kc == 0), stop=(kc == NCH - 1),
                     waits=[("dv", v_h2), k.bg(bank)] if kc == 0 else [],
                     inc=(kc == NCH - 1))
            v_ = S["pe"].cur
            va = k.act(SB[:, o_fch:o_fch + TS], PB[bank][:, :], AF.Tanh,
                       bias=SB[:, o_fb + m:o_fb + m + 1],
                       waits=[("pe", v_), k.w("dv")], inc=True)
            k.setbg(bank, "ac", va)
            db = m % 2
            k.dve(lambda e, m_=m, d_=db: e.scalar_tensor_tensor(
                SB[:, o_och + d_ * TS:o_och + (d_ + 1) * TS],
                SB[:, o_fch:o_fch + TS], modcol(5, m_),
                SB[:, o_x1 + m_ * TS:o_x1 + (m_ + 1) * TS],
                op0=OP.mult, op1=OP.add),
                waits=[("ac", va), ("sp", och_spill[db])], inc=True)
            v_out = S["dv"].cur
            och_spill[db] = k.dma(
                "sync",
                out_fm.rearrange("(c p) l -> p c l", p=128)[:, m:m + 1,
                                                            tok0:tok0 + TS],
                SB[:, o_och + db * TS:o_och + (db + 1) * TS].unsqueeze(1),
                waits=[("dv", v_out)], sem="sp")
        if ms[-1] == NCH - 1:
            v_wp_p[p] = S["pe"].cur
            st_e["v_outlast"] = S["dv"].cur

    st_e["v_x1"] = S["dv"].cur
    st_e["v_outlast"] = S["dv"].cur
    for st in range(NST + 1):
        back = None
        if st > 0:
            stb = st - 1
            back = [lambda s=stb: e_back_wo(s, (0, 1, 2, 3)),
                    lambda s=stb: e_back_wo(s, (4, 5, 6, 7)),
                    lambda s=stb: e_back_ln2(s),
                    lambda s=stb: e_back_ffn(s, (0, 1, 2, 3, 4, 5, 6, 7))]
        if st < NST:
            e_front_load(st)
            for blk in range(4):
                e_chain(st, blk)
                if back is not None:
                    back[blk]()
                e_transp(st, blk)
        else:
            for pc in back:
                pc()
    k.pop()
    if "E" in phases:
        k.flush()

    k.op("sync", lambda e: e.nop(),
         waits=[k.w("sp"), k.w("ld"), k.w("wld")])
    k.flush()
    return k


def build_C(k, env):
    raise NotImplementedError


def build_D(k, env):
    raise NotImplementedError


def build_E(k, env):
    raise NotImplementedError



def make_consts():
    ident = np.eye(128, dtype=np.float32)
    onesrow = np.ones((1, 128), np.float32)
    onescol = np.ones((128, 1), np.float32)
    indrow = np.zeros((128, 512), np.float32)
    indhead = np.zeros((128, 512), np.float32)
    for h in range(4):
        indrow[32 * h, 128 * h:128 * (h + 1)] = 1.0
        indhead[:, 128 * h + 32 * h] = 1.0
    masks = np.zeros((128, 1536), np.float32)
    sl = np.tril(np.ones((CC, CC), np.float32), -1)
    su = np.triu(np.ones((CC, CC), np.float32), 1)
    iu = np.triu(np.ones((CC, CC), np.float32), 0)
    i64 = np.eye(CC, dtype=np.float32)
    for r in range(2):
        rs = slice(r * CC, (r + 1) * CC)
        masks[rs, 0:CC] = sl
        masks[rs, CC:2 * CC] = su
        masks[rs, 2 * CC:3 * CC] = iu
        masks[rs, 3 * CC:4 * CC] = i64
        for t in range(4):
            masks[rs, 256 + t * CC:256 + (t + 1) * CC] = sl
            masks[rs, 512 + t * CC:512 + (t + 1) * CC] = su
            masks[rs, 768 + t * CC:768 + (t + 1) * CC] = iu
        for t in range(8):
            masks[rs, 1024 + t * CC:1024 + (t + 1) * CC] = i64
    return dict(ident=ident, onesrow=onesrow, onescol=onescol,
                indrow=indrow, indhead=indhead, masks=masks)


def make_inputs(d, b):
    f32 = np.float32
    c = make_consts()
    x = np.ascontiguousarray(np.asarray(d["L0_attn_hidden_states"][b]).T).astype(f32)
    wbp = np.zeros((DM, 128), f32)
    for h in range(4):
        wbp[:, 32 * h] = np.asarray(d["L0_attn_Wb"])[:, h]
    convw = np.concatenate([
        np.asarray(d["L0_attn_conv_q_w"]), np.asarray(d["L0_attn_conv_k_w"]),
        np.asarray(d["L0_attn_conv_v_w"]), np.asarray(d["L0_attn_local_w"]),
        np.asarray(d["L0_attn_mid_w"])], axis=1).astype(f32)
    rw1 = np.zeros((1088, 2176), f32)
    rw1[:1080, :2160] = np.asarray(d["L0_attn_r_w1"])
    rb1 = np.zeros((2176,), f32)
    rb1[:2160] = np.asarray(d["L0_attn_r_b1"])
    rw2 = np.asarray(d["L0_attn_r_w2"]).astype(f32)
    rb2 = np.asarray(d["L0_attn_r_b2"]).astype(f32)
    perm = [4 * h + i for i in range(4) for h in range(4)]
    rw2p = np.zeros((2176, 16), f32)
    rw2p[:2160] = rw2[:, perm]
    rb2p = np.ascontiguousarray(rb2[perm])
    mixw = np.asarray(d["L0_attn_mix_w"]).astype(f32).reshape(-1)
    onormt = np.tile(np.asarray(d["L0_attn_onorm_w"]).astype(f32), 4)
    return {
        "x_fm": x,
        "wq": np.asarray(d["L0_attn_Wq"]).astype(f32),
        "wk": np.asarray(d["L0_attn_Wk"]).astype(f32),
        "wv": np.asarray(d["L0_attn_Wv"]).astype(f32),
        "wbp": wbp,
        "convw": convw,
        "rw1": rw1, "rb1": rb1, "rw2p": rw2p, "rb2p": rb2p,
        "onorm_r": np.broadcast_to(onormt, (128, DM)).copy(),
        "mixw_r": np.broadcast_to(mixw, (128, DM)).copy(),
        "wo": np.asarray(d["L0_attn_Wo"]).astype(f32),
        "ffn_wT": np.ascontiguousarray(np.asarray(d["L0_ffn_weight"]).T).astype(f32),
        "ffn_a1": np.asarray(d["L0_ffn_aux_weight1"]).astype(f32),
        "ffn_a2": np.asarray(d["L0_ffn_aux_weight2"]).astype(f32),
        "ffn_b": np.asarray(d["L0_ffn_bias"]).astype(f32),
        "cvec": np.asarray(d["L0_c"][b]).astype(f32),
        "wmod": np.asarray(d["L0_wmod"]).astype(f32),
        **c,
    }


def kernel(**inputs):
    kk = build_program()
    in_maps = [make_inputs(inputs, b) for b in range(8)]
    res = run_bass_kernel_spmd(kk.nc, in_maps, core_ids=list(range(8)))
    out = np.stack([np.asarray(r["out_fm"]).T for r in res.results])
    return out.astype(np.float32)
